# version 01
# baseline (speedup 1.0000x reference)
import numpy as np
import ml_dtypes
import concourse.bass as bass
import concourse.mybir as mybir
from concourse.bass_utils import run_bass_kernel_spmd

F32 = mybir.dt.float32
BF16 = mybir.dt.bfloat16
ALU = mybir.AluOpType
ACT = mybir.ActivationFunctionType
AX = mybir.AxisListType.X

D = 1024
S = 2048
NBATCH = 2
NCH = 8
H = 16
DH = 64
NE = 8
HID = 256
TS = 512
NTS = S // TS
QB = 256
EPS_LN = 1e-5
EPS_RMS = 1e-6
KB = 1024
BIG1024 = [1024, 1024]


class Ctr:
    __slots__ = ("h", "n", "name")
    def __init__(self, h, name):
        self.h = h; self.n = 0; self.name = name


class Build:
    def __init__(self, phases=("ffn1", "attn", "ffn2"), nbatch=NBATCH, debug=()):
        self.phases = phases
        self.nbatch = nbatch
        self.debug = list(debug)
        self.dbg_slots = {}
        self.nc = bass.Bass("TRN2", target_bir_lowering=False)
        self.plan = []
        self.last_wait = {}

    def op(self, eng, fn, waits=(), inc=None, inc_k=1, tag=""):
        waits2 = []
        for (c, v) in waits:
            if v <= 0:
                continue
            assert v <= c.n, f"forward wait on {c.name}: {v} > {c.n} ({tag})"
            key = (eng, id(c))
            if self.last_wait.get(key, -1) < v:
                self.last_wait[key] = v
                waits2.append((c.h, v))
        if inc is not None:
            inc.n += inc_k
        ic = inc
        def run(eh, waits2=waits2, fn=fn, ic=ic, inc_k=inc_k):
            for (hh, v) in waits2:
                eh.wait_ge(hh, v)
            r = fn(eh)
            if ic is not None:
                r.then_inc(ic.h, inc_k)
        self.plan.append((eng, run))
        return inc.n if inc is not None else None

    def dma(self, q, out_ap, in_ap, waits=(), tag="", **kw):
        c = self.sd if q == "sync" else self.sg
        return self.op(q, lambda e: e.dma_start(out_ap, in_ap, **kw),
                       waits=waits, inc=c, inc_k=16, tag=tag)

    def tmm(self, out, lhsT, rhs, start, stop, waits=(), inc=False, tp=None, tag=""):
        kw = {}
        if tp is not None:
            kw["tile_position"] = tp
        return self.op("tensor",
                       lambda e: e.matmul(out, lhsT, rhs, start=start, stop=stop, **kw),
                       waits=waits, inc=self.st if (inc or stop) else None, tag=tag)

    def ttr(self, out, in_, ident, waits=(), tag=""):
        return self.op("tensor", lambda e: e.transpose(out, in_, ident),
                       waits=waits, inc=self.st, tag=tag)

    def vop(self, fn, waits=(), tag=""):
        return self.op("vector", fn, waits=waits, inc=self.sv, tag=tag)

    def sop(self, fn, waits=(), tag=""):
        return self.op("scalar", fn, waits=waits, inc=self.ss, tag=tag)

    def claim_bank(self, k):
        w = self.bank_free[k]
        self.bank_free[k] = []
        return list(w)

    def bank_read(self, k, ctr, val):
        self.bank_free[k].append((ctr, val))

    def PSb(self, k, c0=0, c1=512):
        return self.PS[:, 512 * k + c0:512 * k + c1]

    def declare(self):
        nc = self.nc
        self.ins = {}
        specs = {
            "L0_attn_x": [self.nbatch, S, D], "L0_attn_wq": BIG1024,
            "L0_attn_bq": [D], "L0_attn_wk": BIG1024, "L0_attn_bk": [D],
            "L0_attn_wv": BIG1024, "L0_attn_bv": [D], "L0_attn_wo": BIG1024,
            "L0_attn_bo": [D], "L0_ffn_dw_kernel": [D, 1, 3], "L0_ffn_dw_bias": [D],
            "L0_ffn_pw_w": BIG1024, "L0_ffn_pw_b": [D], "L0_ffn_router_w": [8, D],
            "L0_ffn_router_b": [8], "L0_ffn_expert_w": [NE, D, D],
            "L0_ffn_expert_b": [NE, D], "L0_ffn_out_w": BIG1024, "L0_ffn_out_b": [D],
            "L0_ffn_norm_w": [D], "L0_ffn_ff_down_w": [HID, D],
            "L0_ffn_ff_down_b": [HID], "L0_ffn_ff_up_w": [D, HID], "L0_ffn_ff_up_b": [D],
            "c_cos": [128, S], "c_sin": [128, S], "c_perm": [128, 128],
            "c_ones": [128, 128], "c_idf": [128, 128], "c_idb": [128, 128],
        }
        bf16_names = {"c_cos", "c_sin", "c_perm", "c_ones", "c_idb"}
        for k, sh in specs.items():
            dt = BF16 if k in bf16_names else F32
            self.ins[k] = nc.dram_tensor(k, sh, dt, kind="ExternalInput").ap()
        self.out = nc.dram_tensor("y", [self.nbatch, S, D], F32, kind="ExternalOutput").ap()
        if self.debug:
            self.dbgt = nc.dram_tensor("dbg", [len(self.debug), 128, 512], F32,
                                       kind="ExternalOutput").ap()
        self.nat = {}
        self.img = {}
        for w in ["wq", "wk", "wv", "wo", "pw", "ow"] + [f"e{i}" for i in range(NE)]:
            self.nat[w] = nc.dram_tensor("nat_" + w, BIG1024, BF16, kind="Internal").ap()
            self.img[w] = nc.dram_tensor("img_" + w, [128, 8 * KB], BF16, kind="Internal").ap()
        self.nat["ffd"] = nc.dram_tensor("nat_ffd", [HID, D], BF16, kind="Internal").ap()
        self.img["ffd"] = nc.dram_tensor("img_ffd", [128, 8 * HID], BF16, kind="Internal").ap()
        self.nat["ffu"] = nc.dram_tensor("nat_ffu", [D, HID], BF16, kind="Internal").ap()
        self.img["ffu"] = nc.dram_tensor("img_ffu", [128, 2 * KB], BF16, kind="Internal").ap()

    def alloc(self):
        nc = self.nc
        A = nc.alloc_sbuf_tensor_at
        arena = nc.alloc_sbuf_tensor("ARENA", [128, 212800], mybir.dt.uint8)
        amloc = nc.lookup_mloc(arena)
        o = [amloc.addr]
        def at(name, shape, dtype):
            t = A(name, shape, dtype, offset=o[0])
            o[0] += int(np.prod(shape[1:])) * mybir.dt.size(dtype)
            o[0] = (o[0] + 31) & ~31
            return t
        self.R = at("R", [128, NCH * S], F32)
        self.cos = at("cosT", [128, S], BF16)
        self.sin = at("sinT", [128, S], BF16)
        self.perm = at("permT", [128, 128], BF16)
        self.onesb = at("onesb", [128, 128], BF16)
        self.idf = at("idf", [128, 128], F32)
        self.idb = at("idb", [128, 128], BF16)
        self.bias = at("biasc", [128, 240], F32)
        self.routimg = at("routimg", [128, 64], BF16)
        self.rbcol = at("rbcol", [128, 1], F32)
        self.bvrow = at("bvrow", [1, KB], BF16)
        self.S0 = at("S0", [128, 8 * KB], BF16)
        self.S1 = at("S1", [128, 8 * KB], BF16)
        self.S2 = at("S2", [128, 8 * KB], BF16)
        ua_base = o[0]
        self.UA = at("UA", [128, 16 * KB], BF16)
        ub = o[0]
        self.yb = A("yb", [128, NCH * (TS + 2)], F32, offset=ub)
        self.sres = A("sres", [128, NCH * TS], F32, offset=ub)
        self.comb = A("comb", [128, NCH * TS], F32, offset=ub + 16448)
        self.probs0 = A("probs0", [128, 2 * TS], BF16, offset=ub)
        self.probs1 = A("probs1", [128, 2 * TS], BF16, offset=ub + 2048)
        self.vhat = A("vhat", [128, 16 * 130], BF16, offset=ub + 4096)
        self.q1 = A("q1", [128, S], BF16, offset=ub + 8320)
        self.k1 = A("k1", [128, S], BF16, offset=ub + 12416)
        self.rsc = A("ropesc", [128, S], BF16, offset=ub + 16512)
        o[0] = ((ub + 33248) + 31) & ~31
        uc = o[0]
        self.row_m = A("row_m", [1, TS], F32, offset=uc)
        self.row_q = A("row_q", [1, TS], F32, offset=uc + 2048)
        self.row_t = A("row_t", [1, TS], F32, offset=uc + 4096)
        self.row_bf = A("row_bf", [1, 4 * TS], BF16, offset=uc + 6144)
        self.mb = A("mb", [128, TS], F32, offset=uc + 10240)
        self.rb_ = A("rb_", [128, TS], F32, offset=uc + 12288)
        self.silu = A("siluf", [128, TS], F32, offset=uc + 14336)
        self.rwrow = A("rwrow", [1, TS], BF16, offset=uc + 10240)
        self.rwb = A("rwb", [128, TS], BF16, offset=uc + 11264)
        self.lgf = A("lgf", [128, 32], F32, offset=uc + 16384)
        self.rwtok = A("rwtok", [128, 32], BF16, offset=uc + 16512)
        self.m1n = A("m1n", [128, 24], F32, offset=uc + 16576)
        self.cstage = A("cstage", [128, TS], BF16, offset=uc + 16672)
        self.quarter = A("quarter", [128, 2 * S], BF16, offset=uc)
        self.wotsl = A("wotsl", [128, S], BF16, offset=uc + 8192)
        self.stage = A("stage", [128, S], BF16, offset=uc + 12288)
        self.recipf = A("recipf", [128, 512], F32, offset=uc + 16384)
        self.denb = A("denb", [128, 512], BF16, offset=uc + 18432)
        self.xstage = A("xstage", [128, KB], F32, offset=uc + 12288)
        o[0] = uc + 19456
        if self.debug:
            self.dbgstage = A("dbgstage", [128, 512], F32, offset=ua_base + 28672)
            self.dbgstage2 = A("dbgstage2", [128, 512], F32, offset=ub + 20608)
        assert o[0] <= amloc.addr + 212800, o[0]
        self.PS = nc.alloc_psum_tensor("PS", [128, 4096], F32)
        self.bc = {}
        names = ["bq", "bk", "bv", "bo", "dwb", "pwb", "outb", "nw", "nwh",
                 "ffub", "ffubh"]
        for i, name in enumerate(names):
            self.bc[name] = (8 * i, 8)
        self.bc["ffdb"] = (88, 2)
        self.bc["ebc"] = (90, 64)
        self.bc["kern"] = (154, 24)
        self.bc["rbb"] = (178, 8)
        self.bc["c0"] = (186, 2)

    def bcol(self, name, j=0, w=1):
        c0, n = self.bc[name]
        assert j + w <= n, (name, j, w)
        return self.bias[:, c0 + j:c0 + j + w]

    def sems(self):
        nc = self.nc
        self._semctx = []
        def mk(name):
            g = nc.semaphore(name)
            h = g.__enter__()
            self._semctx.append(g)
            return Ctr(h, name)
        self.sd = mk("sd"); self.sg = mk("sg"); self.st = mk("st")
        self.sv = mk("sv"); self.ss = mk("ss"); self.so = mk("so")
        self.bank_free = {k: [] for k in range(8)}
        self.slot_free = {0: (self.st, 0), 1: (self.st, 0), 2: (self.st, 0)}
        self.eready = {1: 0, 2: 0}

    def dump(self, name, ap, bf16=False, stage2=False):
        if name not in self.debug:
            return
        slot = self.debug.index(name)
        stg = self.dbgstage2 if stage2 else self.dbgstage
        waits = [(c, c.n) for c in (self.st, self.sv, self.ss, self.sg)]
        nr, ncols = ap.shape[0], ap.shape[-1]
        self.vop(lambda e: e.tensor_copy(stg[0:nr, 0:ncols], ap),
                 waits=waits, tag="dbgc_" + name)
        v = self.sv.n
        self.dma("sync", self.dbgt[slot], stg[:, :],
                 waits=[(self.sv, v)], tag="dbg_" + name)
        self.vop(lambda e: e.tensor_copy(self.m1n[:, 3:4], self.m1n[:, 3:4]),
                 waits=[(self.sd, self.sd.n)], tag="dbgw_" + name)

    def prologue(self):
        ins = self.ins
        self.dma("sync", self.cos[:], ins["c_cos"], tag="ccos")
        self.dma("sync", self.sin[:], ins["c_sin"], tag="csin")
        self.dma("sync", self.perm[:], ins["c_perm"], tag="cperm")
        self.dma("sync", self.onesb[:], ins["c_ones"], tag="cones")
        self.dma("sync", self.idf[:], ins["c_idf"], tag="cidf")
        self.dma("sync", self.idb[:], ins["c_idb"], tag="cidb")
        for name, src in [("bq", "L0_attn_bq"), ("bk", "L0_attn_bk"),
                          ("bv", "L0_attn_bv"), ("bo", "L0_attn_bo"),
                          ("dwb", "L0_ffn_dw_bias"), ("pwb", "L0_ffn_pw_b"),
                          ("outb", "L0_ffn_out_b"), ("nw", "L0_ffn_norm_w"),
                          ("ffub", "L0_ffn_ff_up_b")]:
            c0, _ = self.bc[name]
            self.dma("sync", self.bias[:, c0:c0 + 8],
                     ins[src].rearrange("(c p) -> p c", p=128), tag="b_" + name, allow_slow_non_contiguous=True)
        c0, _ = self.bc["ffdb"]
        self.dma("sync", self.bias[:, c0:c0 + 2],
                 ins["L0_ffn_ff_down_b"].rearrange("(c p) -> p c", p=128), tag="b_ffdb", allow_slow_non_contiguous=True)
        c0, _ = self.bc["ebc"]
        self.dma("sync", self.bias[:, c0:c0 + 64].rearrange("p (e c) -> p e c", e=8),
                 ins["L0_ffn_expert_b"].rearrange("e (c p) -> p e c", p=128), tag="b_ebc", allow_slow_non_contiguous=True)
        c0, _ = self.bc["kern"]
        self.dma("sync", self.bias[:, c0:c0 + 24].rearrange("p (c k) -> p c k", c=8),
                 ins["L0_ffn_dw_kernel"].rearrange("(c p) o k -> p c (o k)", p=128),
                 tag="b_kern", allow_slow_non_contiguous=True)
        self.dma("sync", self.rbcol[0:8, 0:1],
                 ins["L0_ffn_router_b"].rearrange("(e o) -> e o", o=1), tag="b_rb",
                 allow_slow_non_contiguous=True)
        n_const = self.sd.n
        self.const_ready = n_const

        self.sop(lambda e: e.mul(self.bcol("nwh", 0, 8), self.bcol("nw", 0, 8), 0.5),
                 waits=[(self.sd, n_const)], tag="nwh")
        self.sop(lambda e: e.mul(self.bcol("ffubh", 0, 8), self.bcol("ffub", 0, 8), 0.5),
                 tag="ffubh")

        self.dma("sync", self.rbcol[0:8, 0:1],
                 ins["L0_ffn_router_b"].rearrange("(e o) -> e o", o=1), tag="b_rb",
                 allow_slow_non_contiguous=True)
        n_const = self.sd.n
        order = ["pw"] + [f"e{i}" for i in range(NE)] + ["ow", "ffd", "ffu",
                                                         "wq", "wk", "wv", "wo"]
        src = {"pw": self.ins["L0_ffn_pw_w"], "ow": self.ins["L0_ffn_out_w"],
               "ffd": self.ins["L0_ffn_ff_down_w"], "ffu": self.ins["L0_ffn_ff_up_w"],
               "wq": self.ins["L0_attn_wq"], "wk": self.ins["L0_attn_wk"],
               "wv": self.ins["L0_attn_wv"], "wo": self.ins["L0_attn_wo"]}
        for i in range(NE):
            src[f"e{i}"] = self.ins["L0_ffn_expert_w"][i]
        self.img_ready = {}
        s1_free = 0
        s2_free = 0
        for wname in order:
            a = src[wname]
            rows, ncol = a.shape
            nch = rows // 128
            cast_view = self.S1[:, 0:nch * ncol].rearrange("p (c n) -> p c n", c=nch)
            self.dma("gpsimd", cast_view, a.rearrange("(c p) n -> p c n", p=128),
                     waits=[(self.sd, s1_free)], tag=f"cast_{wname}")
            g_cast = self.sg.n
            if wname == "ffu":
                self.sop(lambda e: e.mul(self.S1[:, 0:2 * KB], self.S1[:, 0:2 * KB], 0.5),
                         waits=[(self.sg, g_cast)], tag="ffu_half")
                store_waits = [(self.ss, self.ss.n)]
            else:
                store_waits = [(self.sg, g_cast)]
            self.dma("sync", self.nat[wname].rearrange("(c p) n -> p c n", p=128),
                     cast_view, waits=store_waits, tag=f"nat_{wname}")
            d_nat = self.sd.n
            s1_free = d_nat
            nkc = ncol // 128
            for c in range(nkc):
                self.dma("sync", self.S2[:, c * rows:(c + 1) * rows],
                         self.nat[wname][:, c * 128:(c + 1) * 128], transpose=True,
                         waits=[(self.sd, max(d_nat, s2_free))] if c == 0 else (),
                         tag=f"tr_{wname}_{c}")
            d_tr = self.sd.n
            self.dma("sync", self.img[wname][:, 0:nkc * rows], self.S2[:, 0:nkc * rows],
                     waits=[(self.sd, d_tr)], tag=f"img_{wname}")
            s2_free = self.sd.n
            self.img_ready[wname] = self.sd.n
        self.dma("gpsimd", self.S1[0:8, 0:KB], self.ins["L0_ffn_router_w"],
                 waits=[(self.sd, s1_free)], tag="cast_rout")
        g_r = self.sg.n
        for c in range(8):
            bk = 2 + (c % 2)
            w = self.claim_bank(bk)
            self.tmm(self.PS[:, bk * 512:bk * 512 + 8],
                     self.S1[0:8, c * 128:(c + 1) * 128], self.idb[0:8, 0:8],
                     True, True, waits=w + [(self.sg, g_r)], tag=f"routT{c}")
            t_n = self.st.n
            self.sop(lambda e, c=c, bk=bk: e.copy(self.routimg[:, c * 8:(c + 1) * 8],
                                                  self.PS[:, bk * 512:bk * 512 + 8]),
                     waits=[(self.st, t_n)], tag=f"routE{c}")
            self.bank_read(bk, self.ss, self.ss.n)
        self.dma("gpsimd", self.bvrow[0:1, :],
                 self.ins["L0_attn_bv"].rearrange("(o d) -> o d", o=1), tag="bvrow")
        self.prologue_done = self.sd.n
        self.slot_free[1] = (self.st, self.st.n)
        self.prol_s1_free = (self.sd, s1_free)
        self.prol_s2_free = (self.sd, s2_free)
        self.prol_g = (self.sg, self.sg.n)

    def load_R(self, b):
        first = True
        for tt in range(16):
            waits = []
            if first and b > 0:
                waits.append((self.so, self.so.n))
            if not first:
                waits.append((self.st, self.last_xT))
            d_x = self.dma("sync", self.xstage[:],
                           self.ins["L0_attn_x"][b, tt * 128:(tt + 1) * 128, :],
                           waits=waits, tag=f"xin{b}_{tt}")
            first = False
            for c in range(NCH):
                bk = c % 4
                w = self.claim_bank(bk)
                self.ttr(self.PS[:, bk * 512:bk * 512 + 128],
                         self.xstage[:, c * 128:(c + 1) * 128], self.idf[:],
                         waits=w + [(self.sd, d_x)], tag=f"xT{tt}_{c}")
                t_n = self.st.n
                self.vop(lambda e, c=c, tt=tt, bk=bk: e.tensor_copy(
                    self.R[:, c * S + tt * 128:c * S + (tt + 1) * 128],
                    self.PS[:, bk * 512:bk * 512 + 128]),
                    waits=[(self.st, t_n)], tag=f"xE{tt}_{c}")
                self.bank_read(bk, self.sv, self.sv.n)
            self.last_xT = self.st.n

    def store_R(self, b):
        prev_store = self.so.n
        for tt in range(16):
            for c in range(NCH):
                bk = c % 4
                w = self.claim_bank(bk)
                self.ttr(self.PS[:, bk * 512:bk * 512 + 128],
                         self.R[:, c * S + tt * 128:c * S + (tt + 1) * 128], self.idf[:],
                         waits=w + ([(self.sv, self.sv.n)] if (tt == 0 and c == 0) else []),
                         tag=f"oT{tt}_{c}")
                t_n = self.st.n
                self.vop(lambda e, c=c, bk=bk: e.tensor_copy(
                    self.xstage[:, c * 128:(c + 1) * 128],
                    self.PS[:, bk * 512:bk * 512 + 128]),
                    waits=[(self.st, t_n)] + ([(self.so, prev_store)] if c == 0 else []),
                    tag=f"oE{tt}_{c}")
                self.bank_read(bk, self.sv, self.sv.n)
            v_n = self.sv.n
            self.op("sync", lambda e, tt=tt, b=b: e.dma_start(
                self.out[b, tt * 128:(tt + 1) * 128, :], self.xstage[:]),
                waits=[(self.sv, v_n)], inc=self.so, inc_k=16, tag=f"xout{tt}")
            prev_store = self.so.n

    def stats(self, get_chunk, want_mean, eps, tag, entry_waits=(), scratch=None):
        ua = self.UA
        if scratch is None:
            xbf = ua[:, 13 * KB:13 * KB + TS]
            sq = ua[:, 13 * KB + TS:13 * KB + 2 * TS]
        else:
            xbf, sq = scratch
        bsum, bsq = 4, 5
        wsum = self.claim_bank(bsum)
        wsq = self.claim_bank(bsq)
        t_prev = 0
        for c in range(NCH):
            src = get_chunk(c)
            cw = list(entry_waits) if c == 0 else [(self.st, t_prev)]
            self.vop(lambda e, src=src, xbf=xbf: e.tensor_copy(xbf, src),
                     waits=cw, tag=f"{tag}cast{c}")
            v_c = self.sv.n
            self.vop(lambda e, xbf=xbf, sq=sq: e.tensor_mul(sq, xbf, xbf),
                     tag=f"{tag}sq{c}")
            v_s = self.sv.n
            if want_mean:
                self.tmm(self.PSb(bsum)[0:1, 0:TS], self.onesb[:, 0:1], xbf,
                         c == 0, c == NCH - 1,
                         waits=(wsum if c == 0 else []) + [(self.sv, v_c)],
                         inc=True, tag=f"{tag}ms{c}")
            self.tmm(self.PSb(bsq)[0:1, 0:TS], self.onesb[:, 0:1], sq,
                     c == 0, c == NCH - 1,
                     waits=(wsq if c == 0 else []) + [(self.sv, v_s)],
                     inc=True, tag=f"{tag}qs{c}")
            t_prev = self.st.n
        t_done = self.st.n
        rm, rq, rt, rbf = self.row_m, self.row_q, self.row_t, self.row_bf
        if tag == "F00_0" and self.debug:
            self.sop(lambda e: e.copy(self.silu[0:1, :], self.PSb(bsq)[0:1, 0:TS]),
                     waits=[(self.st, t_done)], tag="dbgq")
            self.bank_read(bsq, self.ss, self.ss.n)
            self.dump("qsumP", self.silu[0:1, :])
            self.sop(lambda e: e.copy(self.silu[0:1, :], self.PSb(bsum)[0:1, 0:TS]),
                     waits=[(self.st, t_done)], tag="dbgm")
            self.bank_read(bsum, self.ss, self.ss.n)
            self.dump("msumP", self.silu[0:1, :])
        if want_mean:
            self.vop(lambda e: e.tensor_scalar_mul(rm[0:1, :], self.PSb(bsum)[0:1, 0:TS],
                                                   1.0 / D),
                     waits=[(self.st, t_done)], tag=f"{tag}m")
        self.vop(lambda e: e.tensor_scalar_mul(rq[0:1, :], self.PSb(bsq)[0:1, 0:TS],
                                               1.0 / D),
                 waits=[(self.st, t_done)], tag=f"{tag}q")
        self.bank_read(bsum, self.sv, self.sv.n)
        self.bank_read(bsq, self.sv, self.sv.n)
        if want_mean:
            self.vop(lambda e: e.tensor_mul(rt[0:1, :], rm[0:1, :], rm[0:1, :]),
                     tag=f"{tag}mm")
            self.vop(lambda e: e.tensor_sub(rq[0:1, :], rq[0:1, :], rt[0:1, :]),
                     tag=f"{tag}var")
        self.vop(lambda e: e.tensor_scalar_add(rq[0:1, :], rq[0:1, :], eps),
                 tag=f"{tag}eps")
        v_var = self.sv.n
        if tag == "F00_0":
            self.dump("varrow", rq[0:1, :])
        self.sop(lambda e: e.activation(rt[0:1, :], rq[0:1, :], ACT.Sqrt),
                 waits=[(self.sv, v_var)], tag=f"{tag}sd")
        s_sd = self.ss.n
        if tag == "F00_0" and self.debug:
            self.dump("sdrow", rt[0:1, :])
        self.vop(lambda e: e.reciprocal(rq[0:1, :], rt[0:1, :]),
                 waits=[(self.ss, s_sd)], tag=f"{tag}rc")
        self.vop(lambda e: e.tensor_copy(rbf[0:1, 2 * TS:3 * TS], rq[0:1, :]),
                 tag=f"{tag}rhi")
        self.vop(lambda e: e.tensor_sub(rbf[0:1, 3 * TS:4 * TS], rq[0:1, :],
                                        rbf[0:1, 2 * TS:3 * TS]), tag=f"{tag}rlo")
        if want_mean:
            self.vop(lambda e: e.tensor_copy(rbf[0:1, 0:TS], rm[0:1, :]),
                     tag=f"{tag}mhi")
            self.vop(lambda e: e.tensor_sub(rbf[0:1, TS:2 * TS], rm[0:1, :],
                                            rbf[0:1, 0:TS]), tag=f"{tag}mlo")
        v_rows = self.sv.n
        if tag == "F00_0":
            self.dump("rrow", rq[0:1, :])
            self.dump("rhilo", rbf[0:1, 2 * TS:3 * TS], bf16=True)
        plan = [(self.rb_, 2 * TS, 3 * TS, 6)]
        if want_mean:
            plan.append((self.mb, 0, TS, 7))
        for (dst, hi, lo, bk) in plan:
            w = self.claim_bank(bk)
            self.tmm(self.PSb(bk), self.onesb[0:1, :], rbf[0:1, hi:hi + TS],
                     True, False, waits=w + [(self.sv, v_rows)], inc=True,
                     tag=f"{tag}bc1")
            self.tmm(self.PSb(bk), self.onesb[0:1, :], rbf[0:1, lo:lo + TS],
                     False, True, tag=f"{tag}bc2")
            t_b = self.st.n
            self.vop(lambda e, dst=dst, bk=bk: e.tensor_copy(dst[:, :], self.PSb(bk)),
                     waits=[(self.st, t_b)], tag=f"{tag}bce")
            self.bank_read(bk, self.sv, self.sv.n)
        return self.sv.n

    def ffn(self, b, call):
        ua = self.UA
        conv_r = lambda c: ua[:, c * TS:(c + 1) * TS]
        x1_r = lambda c: ua[:, 4 * KB + c * TS:4 * KB + (c + 1) * TS]
        h_r = lambda c: ua[:, 8 * KB + c * TS:8 * KB + (c + 1) * TS]
        g_r = lambda m: ua[:, 12 * KB + m * TS:12 * KB + (m + 1) * TS]
        lgbf = ua[:, 15 * KB:15 * KB + 512]

        sem, val = self.slot_free[0]
        self.dma("sync", self.S0[:], self.img["pw"],
                 waits=[(self.sd, self.img_ready["pw"]), (sem, val)], tag=f"pwld{call}")
        pw_rdy = self.sd.n

        for ts in range(NTS):
            t0 = ts * TS
            Rc = lambda c: self.R[:, c * S + t0:c * S + t0 + TS]
            entry = [(self.st, self.st.n), (self.ss, self.ss.n)] if ts == 0 else ()
            if b == 0 and call == 0 and ts == 0:
                self.dump("R0", Rc(0))
            self.stats(Rc, True, EPS_LN, f"F{call}{b}_{ts}", entry_waits=entry)
            if b == 0 and call == 0 and ts == 0:
                self.dump("mb", self.mb[:, :])
                self.dump("rb", self.rb_[:, :])
            if ts == 0:
                self.vop(lambda e: e.memset(
                    self.yb[:, :].rearrange("p (c n) -> p c n", c=NCH)[:, :, 0:2], 0.0),
                    tag="halo0")
            for c in range(NCH):
                yv = self.yb[:, c * (TS + 2) + 2:c * (TS + 2) + 2 + TS]
                Rsrc = Rc(c)
                self.vop(lambda e, Rsrc=Rsrc, yv=yv: e.tensor_sub(yv, Rsrc, self.mb[:, :]),
                         tag=f"ysub{c}")
                self.vop(lambda e, yv=yv: e.tensor_mul(yv, yv, self.rb_[:, :]),
                         tag=f"ymul{c}")
            for c in range(NCH):
                yb0 = self.yb[:, c * (TS + 2):c * (TS + 2) + TS]
                yb1 = self.yb[:, c * (TS + 2) + 1:c * (TS + 2) + 1 + TS]
                yb2 = self.yb[:, c * (TS + 2) + 2:c * (TS + 2) + 2 + TS]
                self.vop(lambda e, yb0=yb0, c=c: e.tensor_scalar(
                    self.silu[:, :], yb0, self.bcol("kern", 3 * c), self.bcol("dwb", c),
                    ALU.mult, ALU.add), tag=f"cv0_{c}")
                self.vop(lambda e, yb1=yb1, c=c: e.scalar_tensor_tensor(
                    self.silu[:, :], yb1, self.bcol("kern", 3 * c + 1), self.silu[:, :],
                    ALU.mult, ALU.add), tag=f"cv1_{c}")
                self.vop(lambda e, yb2=yb2, c=c: e.scalar_tensor_tensor(
                    conv_r(c), yb2, self.bcol("kern", 3 * c + 2), self.silu[:, :],
                    ALU.mult, ALU.add), tag=f"cv2_{c}")
            v_conv = self.sv.n
            if b == 0 and call == 0 and ts == 0:
                self.dump("y0", self.yb[:, 2:2 + TS])
                self.dump("cv0", conv_r(0), bf16=True)
            if ts < NTS - 1:
                for c in range(NCH):
                    self.vop(lambda e, c=c: e.tensor_copy(
                        self.yb[:, c * (TS + 2):c * (TS + 2) + 2],
                        self.yb[:, c * (TS + 2) + TS:c * (TS + 2) + TS + 2]),
                        tag=f"halo{c}")
            v_silu_free = self.sv.n
            for oc in range(NCH):
                bk = oc % 4
                w = self.claim_bank(bk)
                for kc in range(NCH):
                    self.tmm(self.PSb(bk),
                             self.S0[:, kc * KB + oc * 128:kc * KB + (oc + 1) * 128],
                             conv_r(kc), kc == 0, kc == NCH - 1,
                             waits=(w + [(self.sv, v_conv), (self.sd, pw_rdy)])
                             if kc == 0 else [], tag=f"pw{oc}")
                t_pw = self.st.n
                self.sop(lambda e, bk=bk, oc=oc: e.activation(
                    self.silu[:, :], self.PSb(bk), ACT.Silu, bias=self.bcol("pwb", oc)),
                    waits=[(self.st, t_pw), (self.sv, v_silu_free)], tag=f"psl{oc}")
                s_si = self.ss.n
                self.bank_read(bk, self.ss, s_si)
                yv = self.yb[:, oc * (TS + 2) + 2:oc * (TS + 2) + 2 + TS]
                self.vop(lambda e, oc=oc, yv=yv: e.tensor_add(x1_r(oc), yv,
                                                              self.silu[:, :]),
                         waits=[(self.ss, s_si)], tag=f"x1_{oc}")
                v_silu_free = self.sv.n
            v_x1 = self.sv.n
            if b == 0 and call == 0 and ts == 0:
                self.dump("x10", x1_r(0), bf16=True)
            self._dbg_rt = (b == 0 and call == 0 and ts == 0)
            self._router(v_x1, x1_r, lgbf)
            self._experts(call, ts, x1_r, conv_r)
            self._moe_tail(b, call, ts, x1_r, h_r, g_r)

    def _router(self, v_x1, x1_r, lgbf):
        blg, blt = 4, 5
        if getattr(self, "_dbg_rt", False):
            self.dump("rimg", self.routimg[:, :], bf16=True)
            self.dump("rbbD", self.bias[:, self.bc["rbb"][0]:self.bc["rbb"][0] + 8])
        for tt in range(4):
            w = self.claim_bank(blg)
            for kc in range(NCH):
                self.tmm(self.PS[0:8, blg * 512:blg * 512 + 128],
                         self.routimg[:, kc * 8:(kc + 1) * 8],
                         x1_r(kc)[:, tt * 128:(tt + 1) * 128],
                         kc == 0, kc == NCH - 1,
                         waits=(w + [(self.sv, v_x1)]) if kc == 0 else [],
                         tag=f"rt{tt}")
            t_lg = self.st.n
            self.sop(lambda e, tt=tt: e.add(
                lgbf[0:8, tt * 128:(tt + 1) * 128],
                self.PS[0:8, blg * 512:blg * 512 + 128],
                self.rbcol[0:8, 0:1]),
                waits=[(self.st, t_lg)], tag=f"lgev{tt}")
            self.bank_read(blg, self.ss, self.ss.n)
        s_lg = self.ss.n
        if getattr(self, "_dbg_rt", False):
            self.dump("lgF", lgbf[0:8, :], bf16=True)
        for tt in range(4):
            w = self.claim_bank(blt)
            self.tmm(self.PS[:, blt * 512:blt * 512 + 8],
                     lgbf[0:8, tt * 128:(tt + 1) * 128], self.idb[0:8, 0:8],
                     True, True, waits=(w + [(self.ss, s_lg)]) if tt == 0 else w,
                     tag=f"lgT{tt}")
            t_lgT = self.st.n
            lg = self.lgf[:, tt * 8:(tt + 1) * 8]
            m1n = self.m1n[:, 0:1]
            m2n = self.m1n[:, 1:2]
            rec = self.m1n[:, 2:3]
            rec2 = self.m1n[:, 12:13]
            msk = self.m1n[:, 4:12]
            msk2 = self.m1n[:, 16:24]
            self.vop(lambda e, lg=lg: e.tensor_copy(
                lg, self.PS[:, blt * 512:blt * 512 + 8]),
                waits=[(self.st, t_lgT)], tag=f"lgadd{tt}")
            self.bank_read(blt, self.sv, self.sv.n)
            W = lambda: [(self.sv, self.sv.n)]
            self.vop(lambda e, lg=lg, m1n=m1n: e.reduce_max(m1n, lg, AX, negate=True),
                     waits=W(), tag=f"t1_{tt}")
            self.vop(lambda e, lg=lg, m1n=m1n, msk=msk: e.tensor_scalar(
                msk, lg, m1n, 0.0, ALU.add, ALU.is_ge), waits=W(), tag=f"i1_{tt}")
            self.vop(lambda e, lg=lg, msk=msk, msk2=msk2: e.scalar_tensor_tensor(
                msk2, msk, -1e30, lg, ALU.mult, ALU.add), waits=W(), tag=f"zz_{tt}")
            self.vop(lambda e, m2n=m2n, msk2=msk2: e.reduce_max(m2n, msk2, AX,
                                                               negate=True),
                     waits=W(), tag=f"t2_{tt}")
            self.vop(lambda e, lg=lg, m2n=m2n, msk=msk: e.tensor_scalar(
                msk, lg, m2n, 0.0, ALU.add, ALU.is_ge), waits=W(), tag=f"i2_{tt}")
            v_i = self.sv.n
            self.sop(lambda e, lg=lg, m1n=m1n: e.activation(lg, lg, ACT.Exp, bias=m1n),
                     waits=[(self.sv, v_i)], tag=f"gt_{tt}")
            s_g = self.ss.n
            self.vop(lambda e, lg=lg, msk=msk: e.tensor_mul(lg, lg, msk),
                     waits=[(self.ss, s_g)], tag=f"gm_{tt}")
            self.vop(lambda e, lg=lg, rec=rec: e.reduce_sum(rec, lg, AX),
                     waits=W(), tag=f"dn_{tt}")
            self.vop(lambda e, rec=rec, rec2=rec2: e.reciprocal(rec2, rec),
                     waits=W(), tag=f"rc_{tt}")
            self.vop(lambda e, lg=lg, rec2=rec2, tt=tt: e.tensor_scalar(
                self.rwtok[:, tt * 8:(tt + 1) * 8], lg, rec2, None, ALU.mult,
                ALU.bypass), waits=W(), tag=f"rw_{tt}")
        self.vop(lambda e: e.tensor_copy(self.lgf[:, 0:8], self.lgf[:, 0:8]),
                 waits=[(self.sv, self.sv.n)], tag="rwflush")
        self.v_rw = self.sv.n
        if getattr(self, "_dbg_rt", False):
            self.dump("lg", self.lgf[:, :])
            self.dump("rwt", self.rwtok[:, :], bf16=True)

    def _experts(self, call, ts, x1_r, conv_r):
        v_rw = self.v_rw
        v_x1 = v_rw
        def eload(e_i):
            slot = self.S1 if e_i % 2 == 0 else self.S2
            k = 1 if e_i % 2 == 0 else 2
            sem, val = self.slot_free[k]
            wl = [(self.sd, self.img_ready[f"e{e_i}"]), (sem, val)]
            if not getattr(self, "_prol_clear", False):
                wl += [self.prol_s1_free, self.prol_s2_free, self.prol_g]
            self.dma("sync", slot[:], self.img[f"e{e_i}"], waits=wl, tag=f"el{e_i}")
            self.eready[k] = self.sd.n
            if e_i >= 6:
                self._prol_clear = True
        eload(0)
        eload(1)
        self._esilu_free = self.sv.n
        for e_i in range(NE):
            slot = self.S1 if e_i % 2 == 0 else self.S2
            k = 1 if e_i % 2 == 0 else 2
            brw, brb = 4, 5
            for tt in range(4):
                w = self.claim_bank(brw)
                self.tmm(self.PS[0:1, brw * 512:brw * 512 + 128],
                         self.rwtok[:, tt * 8 + e_i:tt * 8 + e_i + 1], self.idb[:],
                         True, True, waits=(w + [(self.sv, v_rw)]) if tt == 0 else w,
                         tag=f"rwT{e_i}")
                t_rw = self.st.n
                self.sop(lambda e, tt=tt: e.copy(
                    self.rwrow[0:1, tt * 128:(tt + 1) * 128],
                    self.PS[0:1, brw * 512:brw * 512 + 128]),
                    waits=[(self.st, t_rw)], tag=f"rwr{e_i}")
                self.bank_read(brw, self.ss, self.ss.n)
            s_rw = self.ss.n
            w = self.claim_bank(brb)
            self.tmm(self.PSb(brb), self.onesb[0:1, :], self.rwrow[0:1, :], True, True,
                     waits=w + [(self.ss, s_rw)], tag=f"rwb{e_i}")
            t_rb = self.st.n
            self.vop(lambda e: e.tensor_copy(self.rwb[:, :], self.PSb(brb)),
                     waits=[(self.st, t_rb)], tag=f"rwbe{e_i}")
            v_rwb = self.sv.n
            self.bank_read(brb, self.sv, v_rwb)
            if getattr(self, "_dbg_rt", False) and e_i == 0:
                self.dump("rwb0", self.rwb[:, :], bf16=True)
            eri = self.eready[k]
            for oc in range(NCH):
                bk = oc % 4
                w = self.claim_bank(bk)
                for kc in range(NCH):
                    self.tmm(self.PSb(bk),
                             slot[:, kc * KB + oc * 128:kc * KB + (oc + 1) * 128],
                             x1_r(kc), kc == 0, kc == NCH - 1,
                             waits=(w + [(self.sd, eri), (self.sv, v_x1)])
                             if kc == 0 else [], tag=f"ex{e_i}_{oc}")
                t_e = self.st.n
                if oc == NCH - 1:
                    self.slot_free[k] = (self.st, t_e)
                    if e_i + 2 < NE:
                        eload(e_i + 2)
                self.sop(lambda e, bk=bk, e_i=e_i, oc=oc: e.activation(
                    self.silu[:, :], self.PSb(bk), ACT.Silu,
                    bias=self.bcol("ebc", e_i * 8 + oc)),
                    waits=[(self.st, t_e), (self.sv, self._esilu_free)],
                    tag=f"es{e_i}_{oc}")
                s_si = self.ss.n
                self.bank_read(bk, self.ss, s_si)
                sc = conv_r(oc)
                self.vop(lambda e, sc=sc: e.tensor_mul(sc, self.silu[:, :],
                                                       self.rwb[:, :]),
                         waits=[(self.ss, s_si), (self.sv, v_rwb)], tag=f"em{e_i}_{oc}")
                self._esilu_free = self.sv.n
                cb = self.comb[:, oc * TS:(oc + 1) * TS]
                if e_i == 0:
                    self.vop(lambda e, sc=sc, cb=cb: e.tensor_copy(cb, sc),
                             tag=f"ec{e_i}_{oc}")
                else:
                    self.vop(lambda e, sc=sc, cb=cb: e.tensor_add(cb, cb, sc),
                             tag=f"ea{e_i}_{oc}")

    def _moe_tail(self, b, call, ts, x1_r, h_r, g_r):
        v_comb = self.sv.n
        if getattr(self, "_dbg_rt", False):
            self.dump("cb0", self.comb[:, 0:TS])
        sem, val = self.slot_free[1]
        self.dma("sync", self.S1[:], self.img["ow"],
                 waits=[(self.sd, self.img_ready["ow"]), (sem, val)], tag="owld")
        ow_rdy = self.sd.n
        for wave in range(2):
            wbanks = []
            for bk in range(4):
                wbanks += self.claim_bank(bk)
            t_prev_kc = 0
            for kc in range(NCH):
                cs = self.cstage[:, 0:TS]
                cbsrc = self.comb[:, kc * TS:(kc + 1) * TS]
                cw = [(self.sv, v_comb)] if kc == 0 else [(self.st, t_prev_kc)]
                self.vop(lambda e, cs=cs, cbsrc=cbsrc: e.tensor_copy(cs, cbsrc),
                         waits=cw, tag=f"mc{wave}_{kc}")
                v_cs = self.sv.n
                for j in range(4):
                    oc = wave * 4 + j
                    self.tmm(self.PSb(j),
                             self.S1[:, kc * KB + oc * 128:kc * KB + (oc + 1) * 128],
                             cs, kc == 0, kc == NCH - 1,
                             waits=(wbanks + [(self.sd, ow_rdy), (self.sv, v_cs)])
                             if (kc == 0 and j == 0) else
                             ([(self.sv, v_cs)] if j == 0 else []),
                             inc=(j == 3), tag=f"mm{wave}_{kc}_{j}")
                t_prev_kc = self.st.n
            t_moe = self.st.n
            self.slot_free[1] = (self.st, t_moe)
            for j in range(4):
                oc = wave * 4 + j
                ps = self.PSb(j)
                sr = self.sres[:, oc * TS:(oc + 1) * TS]
                sb = h_r(oc)
                x1s = x1_r(oc)
                self.vop(lambda e, ps=ps, sr=sr, oc=oc, x1s=x1s: e.scalar_tensor_tensor(
                    sr, ps, self.bcol("outb", oc), x1s, ALU.add, ALU.add),
                    waits=[(self.st, t_moe)], tag=f"sr{oc}")
                self.vop(lambda e, sr=sr, sb=sb: e.tensor_copy(sb, sr), tag=f"sb{oc}")
                self.bank_read(j, self.sv, self.sv.n)
        if getattr(self, "_dbg_rt", False):
            self.dump("sr0", self.sres[:, 0:TS])
        ua = self.UA
        sq = ua[:, 13 * KB + TS:13 * KB + 2 * TS]
        bsq = 5
        w = self.claim_bank(bsq)
        t_prev = 0
        for c in range(NCH):
            self.vop(lambda e, c=c, sq=sq: e.tensor_mul(sq, h_r(c), h_r(c)),
                     waits=[(self.st, t_prev)] if c > 0 else [], tag=f"rq{c}")
            v_s = self.sv.n
            self.tmm(self.PSb(bsq)[0:1, 0:TS], self.onesb[:, 0:1], sq,
                     c == 0, c == NCH - 1,
                     waits=(w if c == 0 else []) + [(self.sv, v_s)], inc=True,
                     tag=f"rqs{c}")
            t_prev = self.st.n
        t_done = self.st.n
        rq, rt, rbf = self.row_q, self.row_t, self.row_bf
        self.vop(lambda e: e.tensor_scalar_mul(rq[0:1, :], self.PSb(bsq)[0:1, 0:TS],
                                               1.0 / D),
                 waits=[(self.st, t_done)], tag="rmq")
        self.bank_read(bsq, self.sv, self.sv.n)
        self.vop(lambda e: e.tensor_scalar_add(rq[0:1, :], rq[0:1, :], EPS_RMS),
                 tag="rme")
        v_var = self.sv.n
        self.sop(lambda e: e.activation(rt[0:1, :], rq[0:1, :], ACT.Sqrt),
                 waits=[(self.sv, v_var)], tag="rmsd")
        s_sd = self.ss.n
        self.vop(lambda e: e.reciprocal(rq[0:1, :], rt[0:1, :]),
                 waits=[(self.ss, s_sd)], tag="rmrc")
        if getattr(self, "_dbg_rt", False):
            self.dump("rmsrow", rq[0:1, :])
        self.vop(lambda e: e.tensor_copy(rbf[0:1, 2 * TS:3 * TS], rq[0:1, :]),
                 waits=[(self.st, getattr(self, "rbf_free", 0))], tag="rmhi")
        self.vop(lambda e: e.tensor_sub(rbf[0:1, 3 * TS:4 * TS], rq[0:1, :],
                                        rbf[0:1, 2 * TS:3 * TS]), tag="rmlo")
        v_rows = self.sv.n
        bk = 6
        w = self.claim_bank(bk)
        self.tmm(self.PSb(bk), self.onesb[0:1, :], rbf[0:1, 2 * TS:3 * TS], True, False,
                 waits=w + [(self.sv, v_rows)], inc=True, tag="rmb1")
        self.tmm(self.PSb(bk), self.onesb[0:1, :], rbf[0:1, 3 * TS:4 * TS], False, True,
                 tag="rmb2")
        t_b = self.st.n
        self.rbf_free = t_b
        psb = self.PSb(bk)
        self.vop(lambda e, psb=psb: e.tensor_copy(self.rb_[:, :], psb),
                 waits=[(self.st, t_b)], tag="rmbe")
        self.bank_read(bk, self.sv, self.sv.n)
        if getattr(self, "_dbg_rt", False):
            self.sop(lambda e, psb=psb: e.copy(self.mb[:, :], psb),
                     waits=[(self.st, t_b)], tag="dbgb6")
            self.bank_read(bk, self.ss, self.ss.n)
            self.dump("bk6D", self.mb[:, :])
            self.dump("rmhiD", rbf[0:1, 2 * TS:3 * TS], bf16=True)
            self.dump("rb2b", self.rb_[:, :])
        for oc in range(NCH):
            sr = self.sres[:, oc * TS:(oc + 1) * TS]
            self.vop(lambda e, sr=sr: e.tensor_mul(self.silu[:, :], sr, self.rb_[:, :]),
                     tag=f"h{oc}")
            self.vop(lambda e, oc=oc: e.tensor_scalar(
                h_r(oc), self.silu[:, :], self.bcol("nw", oc), None, ALU.mult,
                ALU.bypass), tag=f"hb{oc}")
            Rv = self.R[:, oc * S + ts * TS:oc * S + (ts + 1) * TS]
            self.vop(lambda e, Rv=Rv, oc=oc: e.scalar_tensor_tensor(
                Rv, self.silu[:, :], self.bcol("nwh", oc), Rv, ALU.mult, ALU.add),
                tag=f"Rh{oc}")
        v_h = self.sv.n
        if getattr(self, "_dbg_rt", False):
            self.dump("rb2", self.rb_[:, :])
            self.dump("h0", h_r(0), bf16=True)
        sem, val = self.slot_free[1]
        self.dma("sync", self.S1[:, 0:8 * HID], self.img["ffd"],
                 waits=[(self.sd, self.img_ready["ffd"]), (sem, val)], tag="ffdld")
        self.dma("sync", self.S1[:, 8 * HID:8 * HID + 2 * KB], self.img["ffu"],
                 waits=[(self.sd, self.img_ready["ffu"])], tag="ffuld")
        ff_rdy = self.sd.n
        s_gs = []
        for mc in range(2):
            bk = 4 + mc
            w = self.claim_bank(bk)
            for kc in range(NCH):
                self.tmm(self.PSb(bk),
                         self.S1[:, kc * HID + mc * 128:kc * HID + (mc + 1) * 128],
                         h_r(kc), kc == 0, kc == NCH - 1,
                         waits=(w + [(self.sd, ff_rdy), (self.sv, v_h)])
                         if kc == 0 else [], tag=f"fd{mc}")
            t_f = self.st.n
            self.sop(lambda e, bk=bk, mc=mc: e.activation(
                g_r(mc), self.PSb(bk), ACT.Gelu, bias=self.bcol("ffdb", mc)),
                waits=[(self.st, t_f)], tag=f"ge{mc}")
            self.bank_read(bk, self.ss, self.ss.n)
        s_g = self.ss.n
        if getattr(self, "_dbg_rt", False):
            self.dump("g0", g_r(0), bf16=True)
        for oc in range(NCH):
            bk = oc % 4
            w = self.claim_bank(bk)
            for hc in range(2):
                self.tmm(self.PSb(bk),
                         self.S1[:, 8 * HID + hc * KB + oc * 128:
                                 8 * HID + hc * KB + (oc + 1) * 128],
                         g_r(hc), hc == 0, hc == 1,
                         waits=(w + [(self.ss, s_g)]) if hc == 0 else [], tag=f"fu{oc}")
            t_u = self.st.n
            Rv = self.R[:, oc * S + ts * TS:oc * S + (ts + 1) * TS]
            self.vop(lambda e, bk=bk, Rv=Rv, oc=oc: e.scalar_tensor_tensor(
                Rv, self.PSb(bk), self.bcol("ffubh", oc), Rv, ALU.add, ALU.add),
                waits=[(self.st, t_u)], tag=f"Rf{oc}")
            self.bank_read(bk, self.sv, self.sv.n)
        self.slot_free[1] = (self.st, self.st.n)
        if getattr(self, "_dbg_rt", False):
            self.dump("Rf0", self.R[:, 0:TS])

    def attn(self, b):
        ua = self.UA
        yln = lambda c: ua[:, c * S:(c + 1) * S]
        q1, k1, rsc = self.q1, self.k1, self.rsc
        rf, db = self.recipf, self.denb
        self.vop(lambda e: e.memset(self.vhat[:, :], 1.0),
                 waits=[(self.st, self.st.n), (self.ss, self.ss.n)], tag="vhat1")
        for ts in range(NTS):
            t0 = ts * TS
            Rc = lambda c, t0=t0: self.R[:, c * S + t0:c * S + t0 + TS]
            entry = [(self.st, self.st.n), (self.ss, self.ss.n)] if ts == 0 else ()
            self.stats(Rc, True, EPS_LN, f"A{b}_{ts}", entry_waits=entry,
                       scratch=(self.probs0[:, 0:TS], self.probs0[:, TS:2 * TS]))
            for c in range(NCH):
                Rsrc = Rc(c)
                dst = yln(c)[:, t0:t0 + TS]
                self.vop(lambda e, Rsrc=Rsrc: e.tensor_sub(self.silu[:, :], Rsrc,
                                                           self.mb[:, :]),
                         tag=f"alns{c}")
                self.vop(lambda e, dst=dst: e.tensor_mul(dst, self.silu[:, :],
                                                         self.rb_[:, :]),
                         tag=f"alnm{c}")
        v_yln = self.sv.n
        slots = {}
        for i, w in enumerate(["wq", "wk", "wv"]):
            sl = [self.S0, self.S1, self.S2][i]
            sem, val = self.slot_free[i]
            self.dma("sync", sl[:], self.img[w],
                     waits=[(self.sd, self.img_ready[w]), (sem, val)],
                     tag=f"a_{w}ld")
            slots[w] = sl
        w_rdy = self.sd.n
        pair_idx = 0
        for hp in range(NCH):
            for (wname, buf, bs, bias) in [("wq", q1, 0, "bq"), ("wk", k1, 4, "bk")]:
                for nb in range(4):
                    wb = self.claim_bank(bs + nb)
                    for kc in range(NCH):
                        self.tmm(self.PSb(bs + nb),
                                 slots[wname][:, kc * KB + hp * 128:kc * KB + (hp + 1) * 128],
                                 yln(kc)[:, nb * TS:(nb + 1) * TS],
                                 kc == 0, kc == NCH - 1,
                                 waits=(wb + [(self.sd, w_rdy), (self.sv, v_yln)])
                                 if kc == 0 else [], tag=f"{wname}p{hp}_{nb}")
                t_p = self.st.n
                self.sop(lambda e, buf=buf, bs=bs, hp=hp, bias=bias: e.add(
                    buf[:, :], self.PS[:, bs * 512:bs * 512 + 2048],
                    self.bcol(bias, hp)),
                    waits=[(self.st, t_p)], tag=f"{wname}e{hp}")
                s_e = self.ss.n
                for nb in range(4):
                    self.bank_read(bs + nb, self.ss, s_e)
                for nb in range(4):
                    wb = self.claim_bank(bs + nb)
                    self.tmm(self.PSb(bs + nb), self.perm[:, :],
                             buf[:, nb * TS:(nb + 1) * TS], True, True,
                             waits=(wb + [(self.ss, s_e)]) if nb == 0 else wb,
                             tag=f"pm{hp}_{bs}_{nb}")
                t_pm = self.st.n
                ps = self.PS[:, bs * 512:bs * 512 + 2048]
                self.vop(lambda e, ps=ps: e.tensor_tensor(rsc[:, :], ps,
                                                          self.sin[:, :], ALU.mult),
                         waits=[(self.st, t_pm)], tag=f"rs{hp}_{bs}")
                for nb in range(4):
                    self.bank_read(bs + nb, self.sv, self.sv.n)
                self.vop(lambda e, buf=buf: e.tensor_mul(buf[:, :], buf[:, :],
                                                         self.cos[:, :]),
                         waits=[(self.st, t_pm)], tag=f"rc{hp}_{bs}")
                self.vop(lambda e, buf=buf: e.tensor_add(buf[:, :], buf[:, :],
                                                         rsc[:, :]),
                         tag=f"ra{hp}_{bs}")
            v_rope = self.sv.n
            if b == 0 and hp == 0 and self.debug:
                self.dump("ylnD", yln(0)[:, 0:512], stage2=True)
                self.dump("q1D", q1[:, 0:512], stage2=True)
                self.dump("k1D", k1[:, 0:512], stage2=True)
            for kt in range(16):
                bk = kt % 4
                wb = self.claim_bank(bk)
                for kc in range(NCH):
                    self.tmm(self.PS[:, bk * 512:bk * 512 + 128],
                             yln(kc)[:, kt * 128:(kt + 1) * 128],
                             slots["wv"][:, kc * KB + hp * 128:kc * KB + (hp + 1) * 128],
                             kc == 0, False,
                             waits=wb if kc == 0 else [], tag=f"vp{hp}_{kt}")
                self.tmm(self.PS[:, bk * 512:bk * 512 + 128], self.onesb[0:1, :],
                         self.bvrow[0:1, hp * 128:(hp + 1) * 128], False, True,
                         tag=f"vb{hp}_{kt}")
                t_v = self.st.n
                for h in range(2):
                    self.sop(lambda e, bk=bk, kt=kt, h=h: e.copy(
                        self.vhat[:, kt * 130 + 65 * h:kt * 130 + 65 * h + 64],
                        self.PS[:, bk * 512 + 64 * h:bk * 512 + 64 * h + 64]),
                        waits=[(self.st, t_v)] if h == 0 else [], tag=f"ve{hp}_{kt}_{h}")
                self.bank_read(bk, self.ss, self.ss.n)
            s_vhat = self.ss.n
            W = lambda: [(self.sv, self.sv.n)]
            bk6 = 6
            for i, (buf, h) in enumerate([(q1, 0), (q1, 1), (k1, 0), (k1, 1)]):
                self.vop(lambda e, buf=buf, h=h: e.tensor_mul(
                    rsc[64 * h:64 * h + 64, :], buf[64 * h:64 * h + 64, :],
                    buf[64 * h:64 * h + 64, :]),
                    waits=W(), tag=f"nsq{hp}_{i}")
                v_sq = self.sv.n
                wbs = []
                for j in range(4):
                    wbs += self.claim_bank(4 + j)
                for j in range(4):
                    self.tmm(self.PS[0:1, (4 + j) * 512:(4 + j) * 512 + 512],
                             self.onesb[64 * h:64 * h + 64, 0:1],
                             rsc[64 * h:64 * h + 64, j * 512:(j + 1) * 512],
                             True, True,
                             waits=(wbs + [(self.sv, v_sq)]) if j == 0 else [],
                             inc=(j == 3), tag=f"nrm{hp}_{i}_{j}")
                t_n = self.st.n
                self.vop(lambda e, i=i: e.reduce_max(rf[0:1, 4 + i:5 + i],
                                                     self.PS[0:1, 2048:4096], AX),
                         waits=[(self.st, t_n)], tag=f"nmx{hp}_{i}")
                for j in range(4):
                    self.bank_read(4 + j, self.sv, self.sv.n)
            s_cells = self.sv.n
            self.vop(lambda e: e.tensor_copy(rf[0:1, 4:8], rf[0:1, 4:8]),
                     waits=[(self.sv, s_cells)], tag=f"nflush{hp}")
            if b == 0 and hp == 0 and self.debug:
                self.dump("celD", rf[0:1, 0:16], stage2=True)
            self.vop(lambda e: e.tensor_mul(rf[0:1, 8:10], rf[0:1, 4:6],
                                            rf[0:1, 6:8]),
                     waits=[(self.sv, self.sv.n)], tag=f"cm{hp}")
            v_cm = self.sv.n
            self.sop(lambda e: e.activation(rf[0:1, 10:12], rf[0:1, 8:10], ACT.Sqrt),
                     waits=[(self.sv, v_cm)], tag=f"cs{hp}")
            s_cs = self.ss.n
            self.vop(lambda e: e.tensor_scalar(db[0:1, 4:6], rf[0:1, 10:12],
                                               -0.125, None, ALU.mult, ALU.bypass),
                     waits=[(self.ss, s_cs)], tag=f"cnn{hp}")
            v_cn = self.sv.n
            for h in range(2):
                wb = self.claim_bank(bk6)
                self.tmm(self.PS[:, bk6 * 512:bk6 * 512 + 1], self.onesb[0:1, :],
                         db[0:1, 4 + h:5 + h], True, True,
                         waits=wb + [(self.sv, v_cn)], tag=f"cb{hp}_{h}")
                t_c = self.st.n
                self.vop(lambda e, h=h: e.tensor_copy(
                    self.bcol("c0", h), self.PS[:, bk6 * 512:bk6 * 512 + 1]),
                    waits=[(self.st, t_c), (self.ss, getattr(self, "c0_free", 0))],
                    tag=f"ce{hp}_{h}")
                self.bank_read(bk6, self.sv, self.sv.n)
            v_c0 = self.sv.n
            if b == 0 and hp == 0 and self.debug:
                self.dump("vhD", self.vhat[:, 0:260], stage2=True)
                self.dump("c0D", self.bcol("c0", 0, 2), stage2=True)
            pt = lambda h, p: (self.probs0 if h == 0 else self.probs1)[:, p * 512:(p + 1) * 512]
            probs_free = {(h, p): (self.st, 0) for h in range(2) for p in range(2)}
            for qb in range(4):
                avw = {0: self.claim_bank(2), 1: self.claim_bank(3)}
                exp_done = {}
                for kt in range(16):
                    p = kt % 2
                    for h in range(2):
                        wb = self.claim_bank(h)
                        self.tmm(self.PS[0:128, h * 512:(h + 1) * 512],
                                 k1[64 * h:64 * h + 64, kt * 128:(kt + 1) * 128],
                                 q1[64 * h:64 * h + 64, qb * 512:(qb + 1) * 512],
                                 True, True,
                                 waits=wb + ([(self.sv, v_c0), (self.sv, v_rope)]
                                             if (qb == 0 and kt == 0) else []),
                                 tp=(64 * h, 0), tag=f"sc{hp}_{qb}_{kt}_{h}", inc=True)
                        t_s = self.st.n
                        sem, val = probs_free[(h, p)]
                        self.sop(lambda e, h=h, p=p: e.activation(
                            pt(h, p), self.PS[0:128, h * 512:(h + 1) * 512],
                            ACT.Exp, bias=self.bcol("c0", h), scale=0.125),
                            waits=[(self.st, t_s), (sem, val), (self.ss, s_vhat)],
                            tag=f"ex{hp}_{qb}_{kt}_{h}")
                        self.bank_read(h, self.ss, self.ss.n)
                        exp_done[(h, p)] = self.ss.n
                    for h in range(2):
                        p = kt % 2
                        self.tmm(self.PS[0:65, (2 + h) * 512:(3 + h) * 512],
                                 self.vhat[:, kt * 130 + 65 * h:kt * 130 + 65 * (h + 1)],
                                 pt(h, p), kt == 0, kt == 15,
                                 waits=(avw[h] if kt == 0 else []) +
                                 [(self.ss, exp_done[(h, p)])],
                                 tag=f"av{hp}_{qb}_{kt}_{h}", inc=True)
                        probs_free[(h, p)] = (self.st, self.st.n)
                t_av = self.st.n
                if b == 0 and hp == 0 and qb == 0 and self.debug:
                    self.dump("pr0", pt(0, 1), stage2=True)
                for h in range(2):
                    avps = self.PS[:, (2 + h) * 512:(3 + h) * 512]
                    self.sop(lambda e, avps=avps: e.activation(
                        rf[64:65, :], avps[64:65, :], ACT.Ln),
                        waits=[(self.st, t_av)], tag=f"dn{hp}_{qb}_{h}")
                    self.sop(lambda e: e.activation(db[64:65, 0:512], rf[64:65, :],
                                                    ACT.Exp, scale=-1.0),
                             tag=f"dx{hp}_{qb}_{h}")
                    s_dn = self.ss.n
                    bkb = 5
                    wb = self.claim_bank(bkb)
                    self.tmm(self.PS[0:64, bkb * 512:bkb * 512 + 512],
                             self.onesb[64:65, 0:64], db[64:65, 0:512],
                             True, True, waits=wb + [(self.ss, s_dn)],
                             tag=f"dbc{hp}_{qb}_{h}", inc=True)
                    t_db = self.st.n
                    self.sop(lambda e: e.copy(rf[0:64, :],
                                              self.PS[0:64, bkb * 512:bkb * 512 + 512]),
                             waits=[(self.st, t_db)], tag=f"dbe{hp}_{qb}_{h}")
                    s_dbe = self.ss.n
                    self.bank_read(bkb, self.ss, s_dbe)
                    if h == 0:
                        dst = self.quarter[0:64,
                                           (hp % 2) * S + qb * 512:(hp % 2) * S + qb * 512 + 512]
                    else:
                        dst = self.stage[0:64, qb * 512:qb * 512 + 512]
                    self.vop(lambda e, avps=avps, dst=dst: e.tensor_mul(
                        dst, avps[0:64, :], rf[0:64, :]),
                        waits=[(self.ss, s_dbe)], tag=f"qn{hp}_{qb}_{h}")
                    self.bank_read(2 + h, self.sv, self.sv.n)
            self.c0_free = self.ss.n
            if b == 0 and hp == 0 and self.debug:
                self.dump("dnD", db[64:65, 0:512], stage2=True)
                self.dump("qrD", self.quarter[0:64, 0:512], stage2=True)
            v_q = self.sv.n
            self.dma("gpsimd",
                     self.quarter[64:128, (hp % 2) * S:(hp % 2) * S + S],
                     self.stage[0:64, 0:S],
                     waits=[(self.sv, v_q), (self.sd, self.prologue_done)],
                     tag=f"hshift{hp}")
            self.qshift_done = self.sg.n
            if hp % 2 == 1:
                wlist = [(self.sd, self.img_ready["wo"]), (self.sg, self.qshift_done)]
                if hasattr(self, "wotsl_free"):
                    wlist.append((self.st, self.wotsl_free))
                self.dma("sync", self.wotsl[:, 0:KB],
                         self.img["wo"][:, (hp - 1) * KB:hp * KB],
                         waits=wlist, tag=f"wold{hp}a")
                self.dma("sync", self.wotsl[:, KB:2 * KB],
                         self.img["wo"][:, hp * KB:(hp + 1) * KB], tag=f"wold{hp}b")
                wo_rdy = self.sd.n
                for oc in range(NCH):
                    for tb in range(4):
                        bk = (oc * 4 + tb) % 2 + 6
                        wb = self.claim_bank(bk)
                        for kc2 in range(2):
                            self.tmm(self.PSb(bk),
                                     self.wotsl[:, kc2 * KB + oc * 128:kc2 * KB + (oc + 1) * 128],
                                     self.quarter[:, kc2 * S + tb * 512:kc2 * S + (tb + 1) * 512],
                                     kc2 == 0, kc2 == 1,
                                     waits=(wb + [(self.sd, wo_rdy),
                                                  (self.sg, self.qshift_done)])
                                     if kc2 == 0 else [], tag=f"op{hp}_{oc}_{tb}")
                        t_o = self.st.n
                        Rv = self.R[:, oc * S + tb * 512:oc * S + (tb + 1) * 512]
                        ps = self.PSb(bk)
                        if pair_idx == 0:
                            self.vop(lambda e, ps=ps, Rv=Rv, oc=oc: e.scalar_tensor_tensor(
                                Rv, ps, self.bcol("bo", oc), Rv, ALU.add, ALU.add),
                                waits=[(self.st, t_o)], tag=f"Ro{hp}_{oc}_{tb}")
                        else:
                            self.vop(lambda e, ps=ps, Rv=Rv: e.tensor_add(Rv, Rv, ps),
                                     waits=[(self.st, t_o)], tag=f"Ro{hp}_{oc}_{tb}")
                        self.bank_read(bk, self.sv, self.sv.n)
                self.wotsl_free = self.st.n
                pair_idx += 1
        self.slot_free[0] = (self.st, self.st.n)
        self.slot_free[1] = (self.st, self.st.n)
        self.slot_free[2] = (self.st, self.st.n)
        if b == 0 and self.debug:
            self.dump("RaD", self.R[:, 0:512], stage2=True)

    def build(self):
        self.declare()
        self.alloc()
        self.sems()
        self.prologue()
        for b in range(self.nbatch):
            self.load_R(b)
            for ph in self.phases:
                if ph == "ffn1":
                    self.ffn(b, 0)
                elif ph == "attn":
                    self.attn(b)
                elif ph == "ffn2":
                    self.ffn(b, 1)
            self.store_R(b)
        self.plan.append(("sync", lambda e: e.wait_ge(self.so.h, self.so.n)))
        nc = self.nc
        with nc.Block() as block:
            cur = 0
            while cur < len(self.plan):
                eng = self.plan[cur][0]
                group = []
                while cur < len(self.plan) and self.plan[cur][0] == eng:
                    group.append(self.plan[cur][1])
                    cur += 1
                def section(eh, group=group):
                    for f in group:
                        f(eh)
                getattr(block, eng)(section)
        for g in reversed(self._semctx):
            g.__exit__(None, None, None)
        return nc


def make_consts():
    j = np.arange(0, DH, 2, dtype=np.float64) / DH
    invf = 1.0 / (10000.0 ** j)
    t = np.arange(S, dtype=np.float64)
    fr = t[None, :] * invf[:, None]
    cos = np.repeat(np.cos(fr), 2, axis=0)
    sin = np.repeat(np.sin(fr), 2, axis=0)
    sin_eff = sin.copy()
    sin_eff[:32] *= -1.0
    cosT = np.concatenate([cos, cos], axis=0).astype(ml_dtypes.bfloat16)
    sinT = np.concatenate([sin_eff, sin_eff], axis=0).astype(ml_dtypes.bfloat16)
    perm = np.zeros((128, 128), dtype=np.float32)
    for blk in range(2):
        for i in range(64):
            perm[blk * 64 + (i + 32) % 64, blk * 64 + i] = 1.0
    return {
        "c_cos": np.ascontiguousarray(cosT),
        "c_sin": np.ascontiguousarray(sinT),
        "c_perm": perm.astype(ml_dtypes.bfloat16),
        "c_ones": np.ones((128, 128), dtype=ml_dtypes.bfloat16),
        "c_idf": np.eye(128, dtype=np.float32),
        "c_idb": np.eye(128, dtype=ml_dtypes.bfloat16),
    }


_BUILD_CACHE = {}


def get_nc(phases=("ffn1", "attn", "ffn2")):
    key = tuple(phases)
    if key not in _BUILD_CACHE:
        _BUILD_CACHE[key] = Build(phases=phases).build()
    return _BUILD_CACHE[key]


def kernel(**inputs):
    consts = make_consts()
    nc = get_nc()
    in_maps = []
    for core in range(8):
        m = dict(consts)
        for k, v in inputs.items():
            v = np.ascontiguousarray(np.asarray(v))
            m[k] = v[2 * core:2 * core + 2] if k == "L0_attn_x" else v
        in_maps.append(m)
    res = run_bass_kernel_spmd(nc, in_maps, core_ids=list(range(8)))
    return np.concatenate([r["y"] for r in res.results], axis=0)


if __name__ == "__main__":
    bld = Build(phases=("ffn1",))
    nc = bld.build()
    print("build ok; instructions:", nc.n_instructions())


# version 03
# speedup vs baseline: 1.0886x; 1.0886x over previous
import numpy as np
import ml_dtypes
import concourse.bass as bass
import concourse.mybir as mybir
from concourse.bass_utils import run_bass_kernel_spmd

F32 = mybir.dt.float32
BF16 = mybir.dt.bfloat16
ALU = mybir.AluOpType
ACT = mybir.ActivationFunctionType
AX = mybir.AxisListType.X

D = 1024
S = 2048
NBATCH = 2
NCH = 8
H = 16
DH = 64
NE = 8
HID = 256
TS = 512
NTS = S // TS
QB = 256
EPS_LN = 1e-5
EPS_RMS = 1e-6
KB = 1024
BIG1024 = [1024, 1024]


class Ctr:
    __slots__ = ("h", "n", "name")
    def __init__(self, h, name):
        self.h = h; self.n = 0; self.name = name


class Build:
    def __init__(self, phases=("ffn1", "attn", "ffn2"), nbatch=NBATCH, debug=()):
        self.phases = phases
        self.nbatch = nbatch
        self.debug = list(debug)
        self.dbg_slots = {}
        self.nc = bass.Bass("TRN2", target_bir_lowering=False)
        self.plan = []
        self.segs = []
        self.last_wait = {}

    def op(self, eng, fn, waits=(), inc=None, inc_k=1, tag=""):
        waits2 = []
        for (c, v) in waits:
            if v <= 0:
                continue
            assert v <= c.n, f"forward wait on {c.name}: {v} > {c.n} ({tag})"
            key = (eng, id(c))
            if self.last_wait.get(key, -1) < v:
                self.last_wait[key] = v
                waits2.append((c.h, v))
        if inc is not None:
            inc.n += inc_k
        ic = inc
        def run(eh, waits2=waits2, fn=fn, ic=ic, inc_k=inc_k):
            for (hh, v) in waits2:
                eh.wait_ge(hh, v)
            r = fn(eh)
            if ic is not None:
                r.then_inc(ic.h, inc_k)
        self.plan.append((eng, run))
        return inc.n if inc is not None else None

    def dma(self, q, out_ap, in_ap, waits=(), tag="", **kw):
        c = self.sd if q == "sync" else self.sg
        return self.op(q, lambda e: e.dma_start(out_ap, in_ap, **kw),
                       waits=waits, inc=c, inc_k=16, tag=tag)

    def tmm(self, out, lhsT, rhs, start, stop, waits=(), inc=False, tp=None, tag=""):
        kw = {}
        if tp is not None:
            kw["tile_position"] = tp
        return self.op("tensor",
                       lambda e: e.matmul(out, lhsT, rhs, start=start, stop=stop, **kw),
                       waits=waits, inc=self.st if (inc or stop) else None, tag=tag)

    def ttr(self, out, in_, ident, waits=(), tag=""):
        return self.op("tensor", lambda e: e.transpose(out, in_, ident),
                       waits=waits, inc=self.st, tag=tag)

    def vop(self, fn, waits=(), tag=""):
        return self.op("vector", fn, waits=waits, inc=self.sv, tag=tag)

    def sop(self, fn, waits=(), tag=""):
        return self.op("scalar", fn, waits=waits, inc=self.ss, tag=tag)

    def claim_bank(self, k):
        w = self.bank_free[k]
        self.bank_free[k] = []
        return list(w)

    def bank_read(self, k, ctr, val):
        self.bank_free[k].append((ctr, val))

    def PSb(self, k, c0=0, c1=512):
        return self.PS[:, 512 * k + c0:512 * k + c1]

    def declare(self):
        nc = self.nc
        self.ins = {}
        specs = {
            "L0_attn_x": [self.nbatch, S, D], "L0_attn_wq": BIG1024,
            "L0_attn_bq": [D], "L0_attn_wk": BIG1024, "L0_attn_bk": [D],
            "L0_attn_wv": BIG1024, "L0_attn_bv": [D], "L0_attn_wo": BIG1024,
            "L0_attn_bo": [D], "L0_ffn_dw_kernel": [D, 1, 3], "L0_ffn_dw_bias": [D],
            "L0_ffn_pw_w": BIG1024, "L0_ffn_pw_b": [D], "L0_ffn_router_w": [8, D],
            "L0_ffn_router_b": [8], "L0_ffn_expert_w": [NE, D, D],
            "L0_ffn_expert_b": [NE, D], "L0_ffn_out_w": BIG1024, "L0_ffn_out_b": [D],
            "L0_ffn_norm_w": [D], "L0_ffn_ff_down_w": [HID, D],
            "L0_ffn_ff_down_b": [HID], "L0_ffn_ff_up_w": [D, HID], "L0_ffn_ff_up_b": [D],
            "c_cos": [128, S], "c_sin": [128, S], "c_perm": [128, 128],
            "c_ones": [128, 128], "c_idf": [128, 128], "c_idb": [128, 128],
        }
        bf16_names = {"c_cos", "c_sin", "c_perm", "c_ones", "c_idb"}
        for k, sh in specs.items():
            dt = BF16 if k in bf16_names else F32
            self.ins[k] = nc.dram_tensor(k, sh, dt, kind="ExternalInput").ap()
        self.out = nc.dram_tensor("y", [self.nbatch, S, D], F32, kind="ExternalOutput").ap()
        if self.debug:
            self.dbgt = nc.dram_tensor("dbg", [len(self.debug), 128, 512], F32,
                                       kind="ExternalOutput").ap()
        self.nat = {}
        self.img = {}
        for w in ["wq", "wk", "wv", "wo", "pw", "ow"] + [f"e{i}" for i in range(NE)]:
            self.nat[w] = nc.dram_tensor("nat_" + w, BIG1024, BF16, kind="Internal").ap()
            self.img[w] = nc.dram_tensor("img_" + w, [128, 8 * KB], BF16, kind="Internal").ap()
        self.nat["ffd"] = nc.dram_tensor("nat_ffd", [HID, D], BF16, kind="Internal").ap()
        self.img["ffd"] = nc.dram_tensor("img_ffd", [128, 8 * HID], BF16, kind="Internal").ap()
        self.nat["ffu"] = nc.dram_tensor("nat_ffu", [D, HID], BF16, kind="Internal").ap()
        self.img["ffu"] = nc.dram_tensor("img_ffu", [128, 2 * KB], BF16, kind="Internal").ap()

    def alloc(self):
        nc = self.nc
        A = nc.alloc_sbuf_tensor_at
        arena = nc.alloc_sbuf_tensor("ARENA", [128, 212800], mybir.dt.uint8)
        amloc = nc.lookup_mloc(arena)
        o = [amloc.addr]
        def at(name, shape, dtype):
            t = A(name, shape, dtype, offset=o[0])
            o[0] += int(np.prod(shape[1:])) * mybir.dt.size(dtype)
            o[0] = (o[0] + 31) & ~31
            return t
        self.R = at("R", [128, NCH * S], F32)
        self.cos = at("cosT", [128, S], BF16)
        self.sin = at("sinT", [128, S], BF16)
        self.perm = at("permT", [128, 128], BF16)
        self.onesb = at("onesb", [128, 128], BF16)
        self.idf = at("idf", [128, 128], F32)
        self.idb = at("idb", [128, 128], BF16)
        self.bias = at("biasc", [128, 240], F32)
        self.routimg = at("routimg", [128, 64], BF16)
        self.rbcol = at("rbcol", [128, 1], F32)
        self.bvrow = at("bvrow", [1, KB], BF16)
        self.S0 = at("S0", [128, 8 * KB], BF16)
        self.S1 = at("S1", [128, 8 * KB], BF16)
        self.S2 = at("S2", [128, 8 * KB], BF16)
        ua_base = o[0]
        self.UA = at("UA", [128, 16 * KB], BF16)
        ub = o[0]
        self.yb = A("yb", [128, NCH * (TS + 2)], F32, offset=ub)
        self.sres = A("sres", [128, NCH * TS], F32, offset=ub)
        self.comb = A("comb", [128, NCH * TS], F32, offset=ub + 16448)
        self.probs0 = A("probs0", [128, 2 * TS], BF16, offset=ub)
        self.probs1 = A("probs1", [128, 2 * TS], BF16, offset=ub + 2048)
        self.vhat = A("vhat", [128, 16 * 130], BF16, offset=ub + 4096)
        self.q1 = A("q1", [128, S], BF16, offset=ub + 8320)
        self.k1 = A("k1", [128, S], BF16, offset=ub + 12416)
        self.rsc = A("ropesc", [128, S], BF16, offset=ub + 16512)
        o[0] = ((ub + 33248) + 31) & ~31
        uc = o[0]
        self.row_m = A("row_m", [1, TS], F32, offset=uc)
        self.row_q = A("row_q", [1, TS], F32, offset=uc + 2048)
        self.row_t = A("row_t", [1, TS], F32, offset=uc + 4096)
        self.row_bf = A("row_bf", [1, 4 * TS], BF16, offset=uc + 6144)
        self.mb = A("mb", [128, TS], F32, offset=uc + 10240)
        self.rb_ = A("rb_", [128, TS], F32, offset=uc + 12288)
        self.silu = A("siluf", [128, TS], F32, offset=uc + 14336)
        self.rwrow = A("rwrow", [1, TS], BF16, offset=uc + 10240)
        self.rwb = A("rwb", [128, TS], BF16, offset=uc + 11264)
        self.lgf = A("lgf", [128, 32], F32, offset=uc + 16384)
        self.rwtok = A("rwtok", [128, 32], BF16, offset=uc + 16512)
        self.m1n = A("m1n", [128, 24], F32, offset=uc + 16576)
        self.cstage = A("cstage", [128, TS], BF16, offset=uc + 16672)
        self.quarter = A("quarter", [128, 2 * S], BF16, offset=uc)
        self.wotsl = A("wotsl", [128, S], BF16, offset=uc + 8192)
        self.stage = A("stage", [128, S], BF16, offset=uc + 12288)
        self.recipf = A("recipf", [128, 512], F32, offset=uc + 16384)
        self.denb = A("denb", [128, 512], BF16, offset=uc + 18432)
        self.xstage = A("xstage", [128, KB], F32, offset=uc + 12288)
        o[0] = uc + 19456
        if self.debug:
            self.dbgstage = A("dbgstage", [128, 512], F32, offset=ua_base + 28672)
            self.dbgstage2 = A("dbgstage2", [128, 512], F32, offset=ub + 20608)
        assert o[0] <= amloc.addr + 212800, o[0]
        self.PS = nc.alloc_psum_tensor("PS", [128, 4096], F32)
        self.bc = {}
        names = ["bq", "bk", "bv", "bo", "dwb", "pwb", "outb", "nw", "nwh",
                 "ffub", "ffubh"]
        for i, name in enumerate(names):
            self.bc[name] = (8 * i, 8)
        self.bc["ffdb"] = (88, 2)
        self.bc["ebc"] = (90, 64)
        self.bc["kern"] = (154, 24)
        self.bc["rbb"] = (178, 8)
        self.bc["c0"] = (186, 2)

    def bcol(self, name, j=0, w=1):
        c0, n = self.bc[name]
        assert j + w <= n, (name, j, w)
        return self.bias[:, c0 + j:c0 + j + w]

    def sems(self):
        nc = self.nc
        self._semctx = []
        def mk(name):
            g = nc.semaphore(name)
            h = g.__enter__()
            self._semctx.append(g)
            return Ctr(h, name)
        self.sd = mk("sd"); self.sg = mk("sg"); self.st = mk("st")
        self.sv = mk("sv"); self.ss = mk("ss"); self.so = mk("so")
        self.bank_free = {k: [] for k in range(8)}
        self.slot_free = {0: (self.st, 0), 1: (self.st, 0), 2: (self.st, 0)}
        self.eready = {1: 0, 2: 0}

    def dump(self, name, ap, bf16=False, stage2=False):
        if name not in self.debug:
            return
        slot = self.debug.index(name)
        stg = self.dbgstage2 if stage2 else self.dbgstage
        waits = [(c, c.n) for c in (self.st, self.sv, self.ss, self.sg)]
        nr, ncols = ap.shape[0], ap.shape[-1]
        self.vop(lambda e: e.tensor_copy(stg[0:nr, 0:ncols], ap),
                 waits=waits, tag="dbgc_" + name)
        v = self.sv.n
        self.dma("sync", self.dbgt[slot], stg[:, :],
                 waits=[(self.sv, v)], tag="dbg_" + name)
        self.vop(lambda e: e.tensor_copy(self.m1n[:, 3:4], self.m1n[:, 3:4]),
                 waits=[(self.sd, self.sd.n)], tag="dbgw_" + name)

    def prologue(self):
        ins = self.ins
        self.dma("sync", self.cos[:], ins["c_cos"], tag="ccos")
        self.dma("sync", self.sin[:], ins["c_sin"], tag="csin")
        self.dma("sync", self.perm[:], ins["c_perm"], tag="cperm")
        self.dma("sync", self.onesb[:], ins["c_ones"], tag="cones")
        self.dma("sync", self.idf[:], ins["c_idf"], tag="cidf")
        self.dma("sync", self.idb[:], ins["c_idb"], tag="cidb")
        for name, src in [("bq", "L0_attn_bq"), ("bk", "L0_attn_bk"),
                          ("bv", "L0_attn_bv"), ("bo", "L0_attn_bo"),
                          ("dwb", "L0_ffn_dw_bias"), ("pwb", "L0_ffn_pw_b"),
                          ("outb", "L0_ffn_out_b"), ("nw", "L0_ffn_norm_w"),
                          ("ffub", "L0_ffn_ff_up_b")]:
            c0, _ = self.bc[name]
            self.dma("sync", self.bias[:, c0:c0 + 8],
                     ins[src].rearrange("(c p) -> p c", p=128), tag="b_" + name, allow_slow_non_contiguous=True)
        c0, _ = self.bc["ffdb"]
        self.dma("sync", self.bias[:, c0:c0 + 2],
                 ins["L0_ffn_ff_down_b"].rearrange("(c p) -> p c", p=128), tag="b_ffdb", allow_slow_non_contiguous=True)
        c0, _ = self.bc["ebc"]
        self.dma("sync", self.bias[:, c0:c0 + 64].rearrange("p (e c) -> p e c", e=8),
                 ins["L0_ffn_expert_b"].rearrange("e (c p) -> p e c", p=128), tag="b_ebc", allow_slow_non_contiguous=True)
        c0, _ = self.bc["kern"]
        self.dma("sync", self.bias[:, c0:c0 + 24].rearrange("p (c k) -> p c k", c=8),
                 ins["L0_ffn_dw_kernel"].rearrange("(c p) o k -> p c (o k)", p=128),
                 tag="b_kern", allow_slow_non_contiguous=True)
        self.dma("sync", self.rbcol[0:8, 0:1],
                 ins["L0_ffn_router_b"].rearrange("(e o) -> e o", o=1), tag="b_rb",
                 allow_slow_non_contiguous=True)
        n_const = self.sd.n
        self.const_ready = n_const

        self.sop(lambda e: e.mul(self.bcol("nwh", 0, 8), self.bcol("nw", 0, 8), 0.5),
                 waits=[(self.sd, n_const)], tag="nwh")
        self.sop(lambda e: e.mul(self.bcol("ffubh", 0, 8), self.bcol("ffub", 0, 8), 0.5),
                 tag="ffubh")

        self.dma("sync", self.rbcol[0:8, 0:1],
                 ins["L0_ffn_router_b"].rearrange("(e o) -> e o", o=1), tag="b_rb",
                 allow_slow_non_contiguous=True)
        n_const = self.sd.n
        order = ["pw"] + [f"e{i}" for i in range(NE)] + ["ow", "ffd", "ffu",
                                                         "wq", "wk", "wv", "wo"]
        src = {"pw": self.ins["L0_ffn_pw_w"], "ow": self.ins["L0_ffn_out_w"],
               "ffd": self.ins["L0_ffn_ff_down_w"], "ffu": self.ins["L0_ffn_ff_up_w"],
               "wq": self.ins["L0_attn_wq"], "wk": self.ins["L0_attn_wk"],
               "wv": self.ins["L0_attn_wv"], "wo": self.ins["L0_attn_wo"]}
        for i in range(NE):
            src[f"e{i}"] = self.ins["L0_ffn_expert_w"][i]
        self.img_ready = {}
        s1_free = 0
        s2_free = 0
        for wname in order:
            a = src[wname]
            rows, ncol = a.shape
            nch = rows // 128
            cast_view = self.S1[:, 0:nch * ncol].rearrange("p (c n) -> p c n", c=nch)
            self.dma("gpsimd", cast_view, a.rearrange("(c p) n -> p c n", p=128),
                     waits=[(self.sd, s1_free)], tag=f"cast_{wname}")
            g_cast = self.sg.n
            if wname == "ffu":
                self.sop(lambda e: e.mul(self.S1[:, 0:2 * KB], self.S1[:, 0:2 * KB], 0.5),
                         waits=[(self.sg, g_cast)], tag="ffu_half")
                store_waits = [(self.ss, self.ss.n)]
            else:
                store_waits = [(self.sg, g_cast)]
            self.dma("sync", self.nat[wname].rearrange("(c p) n -> p c n", p=128),
                     cast_view, waits=store_waits, tag=f"nat_{wname}")
            d_nat = self.sd.n
            s1_free = d_nat
            nkc = ncol // 128
            for c in range(nkc):
                self.dma("sync", self.S2[:, c * rows:(c + 1) * rows],
                         self.nat[wname][:, c * 128:(c + 1) * 128], transpose=True,
                         waits=[(self.sd, max(d_nat, s2_free))] if c == 0 else (),
                         tag=f"tr_{wname}_{c}")
            d_tr = self.sd.n
            self.dma("sync", self.img[wname][:, 0:nkc * rows], self.S2[:, 0:nkc * rows],
                     waits=[(self.sd, d_tr)], tag=f"img_{wname}")
            s2_free = self.sd.n
            self.img_ready[wname] = self.sd.n
        self.dma("gpsimd", self.S1[0:8, 0:KB], self.ins["L0_ffn_router_w"],
                 waits=[(self.sd, s1_free)], tag="cast_rout")
        g_r = self.sg.n
        for c in range(8):
            bk = 2 + (c % 2)
            w = self.claim_bank(bk)
            self.tmm(self.PS[:, bk * 512:bk * 512 + 8],
                     self.S1[0:8, c * 128:(c + 1) * 128], self.idb[0:8, 0:8],
                     True, True, waits=w + [(self.sg, g_r)], tag=f"routT{c}")
            t_n = self.st.n
            self.sop(lambda e, c=c, bk=bk: e.copy(self.routimg[:, c * 8:(c + 1) * 8],
                                                  self.PS[:, bk * 512:bk * 512 + 8]),
                     waits=[(self.st, t_n)], tag=f"routE{c}")
            self.bank_read(bk, self.ss, self.ss.n)
        self.dma("gpsimd", self.bvrow[0:1, :],
                 self.ins["L0_attn_bv"].rearrange("(o d) -> o d", o=1), tag="bvrow")
        self.prologue_done = self.sd.n
        self.slot_free[1] = (self.st, self.st.n)
        self.prol_s1_free = (self.sd, s1_free)
        self.prol_s2_free = (self.sd, s2_free)
        self.prol_g = (self.sg, self.sg.n)

    def load_R(self, b):
        first = True
        for tt in range(16):
            waits = []
            if first and b > 0:
                waits.append((self.so, self.so.n))
            if not first:
                waits.append((self.st, self.last_xT))
            d_x = self.dma("sync", self.xstage[:],
                           self.ins["L0_attn_x"][b, tt * 128:(tt + 1) * 128, :],
                           waits=waits, tag=f"xin{b}_{tt}")
            first = False
            for c in range(NCH):
                bk = c % 4
                w = self.claim_bank(bk)
                self.ttr(self.PS[:, bk * 512:bk * 512 + 128],
                         self.xstage[:, c * 128:(c + 1) * 128], self.idf[:],
                         waits=w + [(self.sd, d_x)], tag=f"xT{tt}_{c}")
                t_n = self.st.n
                self.vop(lambda e, c=c, tt=tt, bk=bk: e.tensor_copy(
                    self.R[:, c * S + tt * 128:c * S + (tt + 1) * 128],
                    self.PS[:, bk * 512:bk * 512 + 128]),
                    waits=[(self.st, t_n)], tag=f"xE{tt}_{c}")
                self.bank_read(bk, self.sv, self.sv.n)
            self.last_xT = self.st.n

    def store_R(self, b):
        prev_store = self.so.n
        for tt in range(16):
            for c in range(NCH):
                bk = c % 4
                w = self.claim_bank(bk)
                self.ttr(self.PS[:, bk * 512:bk * 512 + 128],
                         self.R[:, c * S + tt * 128:c * S + (tt + 1) * 128], self.idf[:],
                         waits=w + ([(self.sv, self.sv.n)] if (tt == 0 and c == 0) else []),
                         tag=f"oT{tt}_{c}")
                t_n = self.st.n
                self.vop(lambda e, c=c, bk=bk: e.tensor_copy(
                    self.xstage[:, c * 128:(c + 1) * 128],
                    self.PS[:, bk * 512:bk * 512 + 128]),
                    waits=[(self.st, t_n)] + ([(self.so, prev_store)] if c == 0 else []),
                    tag=f"oE{tt}_{c}")
                self.bank_read(bk, self.sv, self.sv.n)
            v_n = self.sv.n
            self.op("sync", lambda e, tt=tt, b=b: e.dma_start(
                self.out[b, tt * 128:(tt + 1) * 128, :], self.xstage[:]),
                waits=[(self.sv, v_n)], inc=self.so, inc_k=16, tag=f"xout{tt}")
            prev_store = self.so.n

    def stats(self, get_chunk, want_mean, eps, tag, entry_waits=(), scratch=None):
        ua = self.UA
        if scratch is None:
            xbf = ua[:, 13 * KB:13 * KB + TS]
            sq = ua[:, 13 * KB + TS:13 * KB + 2 * TS]
        else:
            xbf, sq = scratch
        bsum, bsq = 4, 5
        wsum = self.claim_bank(bsum)
        wsq = self.claim_bank(bsq)
        t_prev = 0
        for c in range(NCH):
            src = get_chunk(c)
            cw = list(entry_waits) if c == 0 else [(self.st, t_prev)]
            self.vop(lambda e, src=src, xbf=xbf: e.tensor_copy(xbf, src),
                     waits=cw, tag=f"{tag}cast{c}")
            v_c = self.sv.n
            self.vop(lambda e, xbf=xbf, sq=sq: e.tensor_mul(sq, xbf, xbf),
                     tag=f"{tag}sq{c}")
            v_s = self.sv.n
            if want_mean:
                self.tmm(self.PSb(bsum)[0:1, 0:TS], self.onesb[:, 0:1], xbf,
                         c == 0, c == NCH - 1,
                         waits=(wsum if c == 0 else []) + [(self.sv, v_c)],
                         inc=True, tag=f"{tag}ms{c}")
            self.tmm(self.PSb(bsq)[0:1, 0:TS], self.onesb[:, 0:1], sq,
                     c == 0, c == NCH - 1,
                     waits=(wsq if c == 0 else []) + [(self.sv, v_s)],
                     inc=True, tag=f"{tag}qs{c}")
            t_prev = self.st.n
        t_done = self.st.n
        rm, rq, rt, rbf = self.row_m, self.row_q, self.row_t, self.row_bf
        if tag == "F00_0" and self.debug:
            self.sop(lambda e: e.copy(self.silu[0:1, :], self.PSb(bsq)[0:1, 0:TS]),
                     waits=[(self.st, t_done)], tag="dbgq")
            self.bank_read(bsq, self.ss, self.ss.n)
            self.dump("qsumP", self.silu[0:1, :])
            self.sop(lambda e: e.copy(self.silu[0:1, :], self.PSb(bsum)[0:1, 0:TS]),
                     waits=[(self.st, t_done)], tag="dbgm")
            self.bank_read(bsum, self.ss, self.ss.n)
            self.dump("msumP", self.silu[0:1, :])
        if want_mean:
            self.vop(lambda e: e.tensor_scalar_mul(rm[0:1, :], self.PSb(bsum)[0:1, 0:TS],
                                                   1.0 / D),
                     waits=[(self.st, t_done)], tag=f"{tag}m")
        self.vop(lambda e: e.tensor_scalar_mul(rq[0:1, :], self.PSb(bsq)[0:1, 0:TS],
                                               1.0 / D),
                 waits=[(self.st, t_done)], tag=f"{tag}q")
        self.bank_read(bsum, self.sv, self.sv.n)
        self.bank_read(bsq, self.sv, self.sv.n)
        if want_mean:
            self.vop(lambda e: e.tensor_mul(rt[0:1, :], rm[0:1, :], rm[0:1, :]),
                     tag=f"{tag}mm")
            self.vop(lambda e: e.tensor_sub(rq[0:1, :], rq[0:1, :], rt[0:1, :]),
                     tag=f"{tag}var")
        self.vop(lambda e: e.tensor_scalar_add(rq[0:1, :], rq[0:1, :], eps),
                 tag=f"{tag}eps")
        v_var = self.sv.n
        if tag == "F00_0":
            self.dump("varrow", rq[0:1, :])
        self.sop(lambda e: e.activation(rt[0:1, :], rq[0:1, :], ACT.Sqrt),
                 waits=[(self.sv, v_var)], tag=f"{tag}sd")
        s_sd = self.ss.n
        if tag == "F00_0" and self.debug:
            self.dump("sdrow", rt[0:1, :])
        self.vop(lambda e: e.reciprocal(rq[0:1, :], rt[0:1, :]),
                 waits=[(self.ss, s_sd)], tag=f"{tag}rc")
        self.vop(lambda e: e.tensor_copy(rbf[0:1, 2 * TS:3 * TS], rq[0:1, :]),
                 tag=f"{tag}rhi")
        self.vop(lambda e: e.tensor_sub(rbf[0:1, 3 * TS:4 * TS], rq[0:1, :],
                                        rbf[0:1, 2 * TS:3 * TS]), tag=f"{tag}rlo")
        if want_mean:
            self.vop(lambda e: e.tensor_copy(rbf[0:1, 0:TS], rm[0:1, :]),
                     tag=f"{tag}mhi")
            self.vop(lambda e: e.tensor_sub(rbf[0:1, TS:2 * TS], rm[0:1, :],
                                            rbf[0:1, 0:TS]), tag=f"{tag}mlo")
        v_rows = self.sv.n
        if tag == "F00_0":
            self.dump("rrow", rq[0:1, :])
            self.dump("rhilo", rbf[0:1, 2 * TS:3 * TS], bf16=True)
        plan = [(self.rb_, 2 * TS, 3 * TS, 6)]
        if want_mean:
            plan.append((self.mb, 0, TS, 7))
        for (dst, hi, lo, bk) in plan:
            w = self.claim_bank(bk)
            self.tmm(self.PSb(bk), self.onesb[0:1, :], rbf[0:1, hi:hi + TS],
                     True, False, waits=w + [(self.sv, v_rows)], inc=True,
                     tag=f"{tag}bc1")
            self.tmm(self.PSb(bk), self.onesb[0:1, :], rbf[0:1, lo:lo + TS],
                     False, True, tag=f"{tag}bc2")
            t_b = self.st.n
            self.vop(lambda e, dst=dst, bk=bk: e.tensor_copy(dst[:, :], self.PSb(bk)),
                     waits=[(self.st, t_b)], tag=f"{tag}bce")
            self.bank_read(bk, self.sv, self.sv.n)
        return self.sv.n

    def ffn(self, b, call):
        ua = self.UA
        conv_r = lambda c: ua[:, c * TS:(c + 1) * TS]
        x1_r = lambda c: ua[:, 4 * KB + c * TS:4 * KB + (c + 1) * TS]
        h_r = lambda c: ua[:, 8 * KB + c * TS:8 * KB + (c + 1) * TS]
        g_r = lambda m: ua[:, 12 * KB + m * TS:12 * KB + (m + 1) * TS]
        lgbf = ua[:, 15 * KB:15 * KB + 512]

        sem, val = self.slot_free[0]
        self.dma("sync", self.S0[:], self.img["pw"],
                 waits=[(self.sd, self.img_ready["pw"]), (sem, val)], tag=f"pwld{call}")
        pw_rdy = self.sd.n

        for ts in range(NTS):
            t0 = ts * TS
            Rc = lambda c: self.R[:, c * S + t0:c * S + t0 + TS]
            entry = [(self.st, self.st.n), (self.ss, self.ss.n)] if ts == 0 else ()
            if b == 0 and call == 0 and ts == 0:
                self.dump("R0", Rc(0))
            self.stats(Rc, True, EPS_LN, f"F{call}{b}_{ts}", entry_waits=entry)
            if b == 0 and call == 0 and ts == 0:
                self.dump("mb", self.mb[:, :])
                self.dump("rb", self.rb_[:, :])
            if ts == 0:
                self.vop(lambda e: e.memset(
                    self.yb[:, :].rearrange("p (c n) -> p c n", c=NCH)[:, :, 0:2], 0.0),
                    tag="halo0")
            for c in range(NCH):
                yv = self.yb[:, c * (TS + 2) + 2:c * (TS + 2) + 2 + TS]
                Rsrc = Rc(c)
                self.vop(lambda e, Rsrc=Rsrc, yv=yv: e.tensor_sub(yv, Rsrc, self.mb[:, :]),
                         tag=f"ysub{c}")
                self.vop(lambda e, yv=yv: e.tensor_mul(yv, yv, self.rb_[:, :]),
                         tag=f"ymul{c}")
            for c in range(NCH):
                yb0 = self.yb[:, c * (TS + 2):c * (TS + 2) + TS]
                yb1 = self.yb[:, c * (TS + 2) + 1:c * (TS + 2) + 1 + TS]
                yb2 = self.yb[:, c * (TS + 2) + 2:c * (TS + 2) + 2 + TS]
                self.vop(lambda e, yb0=yb0, c=c: e.tensor_scalar(
                    self.silu[:, :], yb0, self.bcol("kern", 3 * c), self.bcol("dwb", c),
                    ALU.mult, ALU.add), tag=f"cv0_{c}")
                self.vop(lambda e, yb1=yb1, c=c: e.scalar_tensor_tensor(
                    self.silu[:, :], yb1, self.bcol("kern", 3 * c + 1), self.silu[:, :],
                    ALU.mult, ALU.add), tag=f"cv1_{c}")
                self.vop(lambda e, yb2=yb2, c=c: e.scalar_tensor_tensor(
                    conv_r(c), yb2, self.bcol("kern", 3 * c + 2), self.silu[:, :],
                    ALU.mult, ALU.add), tag=f"cv2_{c}")
            v_conv = self.sv.n
            if b == 0 and call == 0 and ts == 0:
                self.dump("y0", self.yb[:, 2:2 + TS])
                self.dump("cv0", conv_r(0), bf16=True)
            if ts < NTS - 1:
                for c in range(NCH):
                    self.vop(lambda e, c=c: e.tensor_copy(
                        self.yb[:, c * (TS + 2):c * (TS + 2) + 2],
                        self.yb[:, c * (TS + 2) + TS:c * (TS + 2) + TS + 2]),
                        tag=f"halo{c}")
            v_silu_free = self.sv.n
            for half in range(2):
                t_pw = {}
                for j in range(4):
                    oc = half * 4 + j
                    w = self.claim_bank(j)
                    for kc in range(NCH):
                        self.tmm(self.PSb(j),
                                 self.S0[:, kc * KB + oc * 128:kc * KB + (oc + 1) * 128],
                                 conv_r(kc), kc == 0, kc == NCH - 1,
                                 waits=(w + [(self.sv, v_conv), (self.sd, pw_rdy)])
                                 if kc == 0 else [], tag=f"pw{oc}")
                    t_pw[j] = self.st.n
                s_si = {}
                for j in range(4):
                    oc = half * 4 + j
                    dst = self.UA[:, 13 * KB + j * TS:13 * KB + (j + 1) * TS]
                    self.sop(lambda e, j=j, oc=oc, dst=dst: e.activation(
                        dst, self.PSb(j), ACT.Silu, bias=self.bcol("pwb", oc)),
                        waits=[(self.st, t_pw[j]), (self.sv, v_silu_free)],
                        tag=f"psl{oc}")
                    s_si[j] = self.ss.n
                    self.bank_read(j, self.ss, s_si[j])
                for j in range(4):
                    oc = half * 4 + j
                    yv = self.yb[:, oc * (TS + 2) + 2:oc * (TS + 2) + 2 + TS]
                    srcb = self.UA[:, 13 * KB + j * TS:13 * KB + (j + 1) * TS]
                    self.vop(lambda e, oc=oc, yv=yv, srcb=srcb: e.tensor_add(
                        x1_r(oc), yv, srcb),
                        waits=[(self.ss, s_si[j])], tag=f"x1_{oc}")
                v_silu_free = self.sv.n
            v_x1 = self.sv.n
            if b == 0 and call == 0 and ts == 0:
                self.dump("x10", x1_r(0), bf16=True)
            self._dbg_rt = (b == 0 and call == 0 and ts == 0)
            self._router(v_x1, x1_r, lgbf)
            self._experts(call, ts, x1_r, conv_r, v_x1)
            self._moe_tail(b, call, ts, x1_r, h_r, g_r)

    def _router(self, v_x1, x1_r, lgbf):
        blg, blt = 4, 5
        if getattr(self, "_dbg_rt", False):
            self.dump("rimg", self.routimg[:, :], bf16=True)
            self.dump("rbbD", self.bias[:, self.bc["rbb"][0]:self.bc["rbb"][0] + 8])
        for tt in range(4):
            w = self.claim_bank(blg)
            for kc in range(NCH):
                self.tmm(self.PS[0:8, blg * 512:blg * 512 + 128],
                         self.routimg[:, kc * 8:(kc + 1) * 8],
                         x1_r(kc)[:, tt * 128:(tt + 1) * 128],
                         kc == 0, kc == NCH - 1,
                         waits=(w + [(self.sv, v_x1)]) if kc == 0 else [],
                         tag=f"rt{tt}")
            t_lg = self.st.n
            self.sop(lambda e, tt=tt: e.add(
                lgbf[0:8, tt * 128:(tt + 1) * 128],
                self.PS[0:8, blg * 512:blg * 512 + 128],
                self.rbcol[0:8, 0:1]),
                waits=[(self.st, t_lg)], tag=f"lgev{tt}")
            self.bank_read(blg, self.ss, self.ss.n)
        s_lg = self.ss.n
        if getattr(self, "_dbg_rt", False):
            self.dump("lgF", lgbf[0:8, :], bf16=True)
        for tt in range(4):
            w = self.claim_bank(blt)
            self.tmm(self.PS[:, blt * 512:blt * 512 + 8],
                     lgbf[0:8, tt * 128:(tt + 1) * 128], self.idb[0:8, 0:8],
                     True, True, waits=(w + [(self.ss, s_lg)]) if tt == 0 else w,
                     tag=f"lgT{tt}")
            t_lgT = self.st.n
            lg = self.lgf[:, tt * 8:(tt + 1) * 8]
            m1n = self.m1n[:, 0:1]
            m2n = self.m1n[:, 1:2]
            rec = self.m1n[:, 2:3]
            rec2 = self.m1n[:, 12:13]
            msk = self.m1n[:, 4:12]
            msk2 = self.m1n[:, 16:24]
            self.vop(lambda e, lg=lg: e.tensor_copy(
                lg, self.PS[:, blt * 512:blt * 512 + 8]),
                waits=[(self.st, t_lgT)], tag=f"lgadd{tt}")
            self.bank_read(blt, self.sv, self.sv.n)
            W = lambda: [(self.sv, self.sv.n)]
            self.vop(lambda e, lg=lg, m1n=m1n: e.reduce_max(m1n, lg, AX, negate=True),
                     waits=W(), tag=f"t1_{tt}")
            self.vop(lambda e, lg=lg, m1n=m1n, msk=msk: e.tensor_scalar(
                msk, lg, m1n, 0.0, ALU.add, ALU.is_ge), waits=W(), tag=f"i1_{tt}")
            self.vop(lambda e, lg=lg, msk=msk, msk2=msk2: e.scalar_tensor_tensor(
                msk2, msk, -1e30, lg, ALU.mult, ALU.add), waits=W(), tag=f"zz_{tt}")
            self.vop(lambda e, m2n=m2n, msk2=msk2: e.reduce_max(m2n, msk2, AX,
                                                               negate=True),
                     waits=W(), tag=f"t2_{tt}")
            self.vop(lambda e, lg=lg, m2n=m2n, msk=msk: e.tensor_scalar(
                msk, lg, m2n, 0.0, ALU.add, ALU.is_ge), waits=W(), tag=f"i2_{tt}")
            v_i = self.sv.n
            self.sop(lambda e, lg=lg, m1n=m1n: e.activation(lg, lg, ACT.Exp, bias=m1n),
                     waits=[(self.sv, v_i)], tag=f"gt_{tt}")
            s_g = self.ss.n
            self.vop(lambda e, lg=lg, msk=msk: e.tensor_mul(lg, lg, msk),
                     waits=[(self.ss, s_g)], tag=f"gm_{tt}")
            self.vop(lambda e, lg=lg, rec=rec: e.reduce_sum(rec, lg, AX),
                     waits=W(), tag=f"dn_{tt}")
            self.vop(lambda e, rec=rec, rec2=rec2: e.reciprocal(rec2, rec),
                     waits=W(), tag=f"rc_{tt}")
            self.vop(lambda e, lg=lg, rec2=rec2, tt=tt: e.tensor_scalar(
                self.rwtok[:, tt * 8:(tt + 1) * 8], lg, rec2, None, ALU.mult,
                ALU.bypass), waits=W(), tag=f"rw_{tt}")
        self.vop(lambda e: e.tensor_copy(self.lgf[:, 0:8], self.lgf[:, 0:8]),
                 waits=[(self.sv, self.sv.n)], tag="rwflush")
        self.v_rw = self.sv.n
        if getattr(self, "_dbg_rt", False):
            self.dump("lg", self.lgf[:, :])
            self.dump("rwt", self.rwtok[:, :], bf16=True)

    def _experts(self, call, ts, x1_r, conv_r, v_x1):
        v_rw = self.v_rw
        def eload(e_i):
            slot = self.S1 if e_i % 2 == 0 else self.S2
            k = 1 if e_i % 2 == 0 else 2
            sem, val = self.slot_free[k]
            wl = [(self.sd, self.img_ready[f"e{e_i}"]), (sem, val)]
            if not getattr(self, "_prol_clear", False):
                wl += [self.prol_s1_free, self.prol_s2_free, self.prol_g]
            self.dma("sync", slot[:], self.img[f"e{e_i}"], waits=wl, tag=f"el{e_i}")
            self.eready[k] = self.sd.n
            if e_i >= 6:
                self._prol_clear = True
        eload(0)
        eload(1)
        self._esilu_free = self.sv.n
        for e_i in range(NE):
            slot = self.S1 if e_i % 2 == 0 else self.S2
            k = 1 if e_i % 2 == 0 else 2
            brw, brb = 4, 5
            for tt in range(4):
                w = self.claim_bank(brw)
                self.tmm(self.PS[0:1, brw * 512:brw * 512 + 128],
                         self.rwtok[:, tt * 8 + e_i:tt * 8 + e_i + 1], self.idb[:],
                         True, True, waits=(w + [(self.sv, v_rw)]) if tt == 0 else w,
                         tag=f"rwT{e_i}")
                t_rw = self.st.n
                self.sop(lambda e, tt=tt: e.copy(
                    self.rwrow[0:1, tt * 128:(tt + 1) * 128],
                    self.PS[0:1, brw * 512:brw * 512 + 128]),
                    waits=[(self.st, t_rw)], tag=f"rwr{e_i}")
                self.bank_read(brw, self.ss, self.ss.n)
            s_rw = self.ss.n
            w = self.claim_bank(brb)
            self.tmm(self.PSb(brb), self.onesb[0:1, :], self.rwrow[0:1, :], True, True,
                     waits=w + [(self.ss, s_rw)], tag=f"rwb{e_i}")
            t_rb = self.st.n
            self.vop(lambda e: e.tensor_copy(self.rwb[:, :], self.PSb(brb)),
                     waits=[(self.st, t_rb)], tag=f"rwbe{e_i}")
            v_rwb = self.sv.n
            self.bank_read(brb, self.sv, v_rwb)
            if getattr(self, "_dbg_rt", False) and e_i == 0:
                self.dump("rwb0", self.rwb[:, :], bf16=True)
            eri = self.eready[k]
            for half in range(2):
                t_e = {}
                for j in range(4):
                    oc = half * 4 + j
                    w = self.claim_bank(j)
                    for kc in range(NCH):
                        self.tmm(self.PSb(j),
                                 slot[:, kc * KB + oc * 128:kc * KB + (oc + 1) * 128],
                                 x1_r(kc), kc == 0, kc == NCH - 1,
                                 waits=(w + [(self.sd, eri), (self.sv, v_x1)])
                                 if kc == 0 else [], tag=f"ex{e_i}_{oc}")
                    t_e[j] = self.st.n
                if half == 1:
                    self.slot_free[k] = (self.st, self.st.n)
                    if e_i + 2 < NE:
                        eload(e_i + 2)
                s_si = {}
                for j in range(4):
                    oc = half * 4 + j
                    dst = self.UA[:, 13 * KB + j * TS:13 * KB + (j + 1) * TS]
                    self.sop(lambda e, j=j, e_i=e_i, oc=oc, dst=dst: e.activation(
                        dst, self.PSb(j), ACT.Silu,
                        bias=self.bcol("ebc", e_i * 8 + oc)),
                        waits=[(self.st, t_e[j]), (self.sv, self._esilu_free)],
                        tag=f"es{e_i}_{oc}")
                    s_si[j] = self.ss.n
                    self.bank_read(j, self.ss, s_si[j])
                for j in range(4):
                    oc = half * 4 + j
                    sc = conv_r(oc)
                    srcb = self.UA[:, 13 * KB + j * TS:13 * KB + (j + 1) * TS]
                    self.vop(lambda e, sc=sc, srcb=srcb: e.tensor_mul(
                        sc, srcb, self.rwb[:, :]),
                        waits=[(self.ss, s_si[j]), (self.sv, v_rwb)],
                        tag=f"em{e_i}_{oc}")
                    cb = self.comb[:, oc * TS:(oc + 1) * TS]
                    if e_i == 0:
                        self.vop(lambda e, sc=sc, cb=cb: e.tensor_copy(cb, sc),
                                 tag=f"ec{e_i}_{oc}")
                    else:
                        self.vop(lambda e, sc=sc, cb=cb: e.tensor_add(cb, cb, sc),
                                 tag=f"ea{e_i}_{oc}")
                self._esilu_free = self.sv.n

    def _moe_tail(self, b, call, ts, x1_r, h_r, g_r):
        v_comb = self.sv.n
        if getattr(self, "_dbg_rt", False):
            self.dump("cb0", self.comb[:, 0:TS])
        sem, val = self.slot_free[1]
        self.dma("sync", self.S1[:], self.img["ow"],
                 waits=[(self.sd, self.img_ready["ow"]), (sem, val)], tag="owld")
        ow_rdy = self.sd.n
        for wave in range(2):
            wbanks = []
            for bk in range(4):
                wbanks += self.claim_bank(bk)
            t_prev_kc = 0
            for kc in range(NCH):
                cs = self.cstage[:, 0:TS]
                cbsrc = self.comb[:, kc * TS:(kc + 1) * TS]
                cw = [(self.sv, v_comb)] if kc == 0 else [(self.st, t_prev_kc)]
                self.vop(lambda e, cs=cs, cbsrc=cbsrc: e.tensor_copy(cs, cbsrc),
                         waits=cw, tag=f"mc{wave}_{kc}")
                v_cs = self.sv.n
                for j in range(4):
                    oc = wave * 4 + j
                    self.tmm(self.PSb(j),
                             self.S1[:, kc * KB + oc * 128:kc * KB + (oc + 1) * 128],
                             cs, kc == 0, kc == NCH - 1,
                             waits=(wbanks + [(self.sd, ow_rdy), (self.sv, v_cs)])
                             if (kc == 0 and j == 0) else
                             ([(self.sv, v_cs)] if j == 0 else []),
                             inc=(j == 3), tag=f"mm{wave}_{kc}_{j}")
                t_prev_kc = self.st.n
            t_moe = self.st.n
            self.slot_free[1] = (self.st, t_moe)
            for j in range(4):
                oc = wave * 4 + j
                ps = self.PSb(j)
                sr = self.sres[:, oc * TS:(oc + 1) * TS]
                sb = h_r(oc)
                x1s = x1_r(oc)
                self.vop(lambda e, ps=ps, sr=sr, oc=oc, x1s=x1s: e.scalar_tensor_tensor(
                    sr, ps, self.bcol("outb", oc), x1s, ALU.add, ALU.add),
                    waits=[(self.st, t_moe)], tag=f"sr{oc}")
                self.vop(lambda e, sr=sr, sb=sb: e.tensor_copy(sb, sr), tag=f"sb{oc}")
                self.bank_read(j, self.sv, self.sv.n)
        if getattr(self, "_dbg_rt", False):
            self.dump("sr0", self.sres[:, 0:TS])
        ua = self.UA
        sq = ua[:, 13 * KB + TS:13 * KB + 2 * TS]
        bsq = 5
        w = self.claim_bank(bsq)
        t_prev = 0
        for c in range(NCH):
            self.vop(lambda e, c=c, sq=sq: e.tensor_mul(sq, h_r(c), h_r(c)),
                     waits=[(self.st, t_prev)] if c > 0 else [], tag=f"rq{c}")
            v_s = self.sv.n
            self.tmm(self.PSb(bsq)[0:1, 0:TS], self.onesb[:, 0:1], sq,
                     c == 0, c == NCH - 1,
                     waits=(w if c == 0 else []) + [(self.sv, v_s)], inc=True,
                     tag=f"rqs{c}")
            t_prev = self.st.n
        t_done = self.st.n
        rq, rt, rbf = self.row_q, self.row_t, self.row_bf
        self.vop(lambda e: e.tensor_scalar_mul(rq[0:1, :], self.PSb(bsq)[0:1, 0:TS],
                                               1.0 / D),
                 waits=[(self.st, t_done)], tag="rmq")
        self.bank_read(bsq, self.sv, self.sv.n)
        self.vop(lambda e: e.tensor_scalar_add(rq[0:1, :], rq[0:1, :], EPS_RMS),
                 tag="rme")
        v_var = self.sv.n
        self.sop(lambda e: e.activation(rt[0:1, :], rq[0:1, :], ACT.Sqrt),
                 waits=[(self.sv, v_var)], tag="rmsd")
        s_sd = self.ss.n
        self.vop(lambda e: e.reciprocal(rq[0:1, :], rt[0:1, :]),
                 waits=[(self.ss, s_sd)], tag="rmrc")
        if getattr(self, "_dbg_rt", False):
            self.dump("rmsrow", rq[0:1, :])
        self.vop(lambda e: e.tensor_copy(rbf[0:1, 2 * TS:3 * TS], rq[0:1, :]),
                 waits=[(self.st, getattr(self, "rbf_free", 0))], tag="rmhi")
        self.vop(lambda e: e.tensor_sub(rbf[0:1, 3 * TS:4 * TS], rq[0:1, :],
                                        rbf[0:1, 2 * TS:3 * TS]), tag="rmlo")
        v_rows = self.sv.n
        bk = 6
        w = self.claim_bank(bk)
        self.tmm(self.PSb(bk), self.onesb[0:1, :], rbf[0:1, 2 * TS:3 * TS], True, False,
                 waits=w + [(self.sv, v_rows)], inc=True, tag="rmb1")
        self.tmm(self.PSb(bk), self.onesb[0:1, :], rbf[0:1, 3 * TS:4 * TS], False, True,
                 tag="rmb2")
        t_b = self.st.n
        self.rbf_free = t_b
        psb = self.PSb(bk)
        self.vop(lambda e, psb=psb: e.tensor_copy(self.rb_[:, :], psb),
                 waits=[(self.st, t_b)], tag="rmbe")
        self.bank_read(bk, self.sv, self.sv.n)
        if getattr(self, "_dbg_rt", False):
            self.sop(lambda e, psb=psb: e.copy(self.mb[:, :], psb),
                     waits=[(self.st, t_b)], tag="dbgb6")
            self.bank_read(bk, self.ss, self.ss.n)
            self.dump("bk6D", self.mb[:, :])
            self.dump("rmhiD", rbf[0:1, 2 * TS:3 * TS], bf16=True)
            self.dump("rb2b", self.rb_[:, :])
        for oc in range(NCH):
            sr = self.sres[:, oc * TS:(oc + 1) * TS]
            self.vop(lambda e, sr=sr: e.tensor_mul(self.silu[:, :], sr, self.rb_[:, :]),
                     tag=f"h{oc}")
            self.vop(lambda e, oc=oc: e.tensor_scalar(
                h_r(oc), self.silu[:, :], self.bcol("nw", oc), None, ALU.mult,
                ALU.bypass), tag=f"hb{oc}")
            Rv = self.R[:, oc * S + ts * TS:oc * S + (ts + 1) * TS]
            self.vop(lambda e, Rv=Rv, oc=oc: e.scalar_tensor_tensor(
                Rv, self.silu[:, :], self.bcol("nwh", oc), Rv, ALU.mult, ALU.add),
                tag=f"Rh{oc}")
        v_h = self.sv.n
        if getattr(self, "_dbg_rt", False):
            self.dump("rb2", self.rb_[:, :])
            self.dump("h0", h_r(0), bf16=True)
        sem, val = self.slot_free[1]
        self.dma("sync", self.S1[:, 0:8 * HID], self.img["ffd"],
                 waits=[(self.sd, self.img_ready["ffd"]), (sem, val)], tag="ffdld")
        self.dma("sync", self.S1[:, 8 * HID:8 * HID + 2 * KB], self.img["ffu"],
                 waits=[(self.sd, self.img_ready["ffu"])], tag="ffuld")
        ff_rdy = self.sd.n
        s_gs = []
        for mc in range(2):
            bk = 4 + mc
            w = self.claim_bank(bk)
            for kc in range(NCH):
                self.tmm(self.PSb(bk),
                         self.S1[:, kc * HID + mc * 128:kc * HID + (mc + 1) * 128],
                         h_r(kc), kc == 0, kc == NCH - 1,
                         waits=(w + [(self.sd, ff_rdy), (self.sv, v_h)])
                         if kc == 0 else [], tag=f"fd{mc}")
            t_f = self.st.n
            self.sop(lambda e, bk=bk, mc=mc: e.activation(
                g_r(mc), self.PSb(bk), ACT.Gelu, bias=self.bcol("ffdb", mc)),
                waits=[(self.st, t_f)], tag=f"ge{mc}")
            self.bank_read(bk, self.ss, self.ss.n)
        s_g = self.ss.n
        if getattr(self, "_dbg_rt", False):
            self.dump("g0", g_r(0), bf16=True)
        for oc in range(NCH):
            bk = oc % 4
            w = self.claim_bank(bk)
            for hc in range(2):
                self.tmm(self.PSb(bk),
                         self.S1[:, 8 * HID + hc * KB + oc * 128:
                                 8 * HID + hc * KB + (oc + 1) * 128],
                         g_r(hc), hc == 0, hc == 1,
                         waits=(w + [(self.ss, s_g)]) if hc == 0 else [], tag=f"fu{oc}")
            t_u = self.st.n
            Rv = self.R[:, oc * S + ts * TS:oc * S + (ts + 1) * TS]
            self.vop(lambda e, bk=bk, Rv=Rv, oc=oc: e.scalar_tensor_tensor(
                Rv, self.PSb(bk), self.bcol("ffubh", oc), Rv, ALU.add, ALU.add),
                waits=[(self.st, t_u)], tag=f"Rf{oc}")
            self.bank_read(bk, self.sv, self.sv.n)
        self.slot_free[1] = (self.st, self.st.n)
        if getattr(self, "_dbg_rt", False):
            self.dump("Rf0", self.R[:, 0:TS])

    def attn(self, b):
        ua = self.UA
        yln = lambda c: ua[:, c * S:(c + 1) * S]
        q1, k1, rsc = self.q1, self.k1, self.rsc
        rf, db = self.recipf, self.denb
        self.vop(lambda e: e.memset(self.vhat[:, :], 1.0),
                 waits=[(self.st, self.st.n), (self.ss, self.ss.n)], tag="vhat1")
        for ts in range(NTS):
            t0 = ts * TS
            Rc = lambda c, t0=t0: self.R[:, c * S + t0:c * S + t0 + TS]
            entry = [(self.st, self.st.n), (self.ss, self.ss.n)] if ts == 0 else ()
            self.stats(Rc, True, EPS_LN, f"A{b}_{ts}", entry_waits=entry,
                       scratch=(self.probs0[:, 0:TS], self.probs0[:, TS:2 * TS]))
            for c in range(NCH):
                Rsrc = Rc(c)
                dst = yln(c)[:, t0:t0 + TS]
                self.vop(lambda e, Rsrc=Rsrc: e.tensor_sub(self.silu[:, :], Rsrc,
                                                           self.mb[:, :]),
                         tag=f"alns{c}")
                self.vop(lambda e, dst=dst: e.tensor_mul(dst, self.silu[:, :],
                                                         self.rb_[:, :]),
                         tag=f"alnm{c}")
        v_yln = self.sv.n
        slots = {}
        for i, w in enumerate(["wq", "wk", "wv"]):
            sl = [self.S0, self.S1, self.S2][i]
            sem, val = self.slot_free[i]
            self.dma("sync", sl[:], self.img[w],
                     waits=[(self.sd, self.img_ready[w]), (sem, val)],
                     tag=f"a_{w}ld")
            slots[w] = sl
        w_rdy = self.sd.n
        pair_idx = 0
        for hp in range(NCH):
            for (wname, buf, bs, bias) in [("wq", q1, 0, "bq"), ("wk", k1, 4, "bk")]:
                for nb in range(4):
                    wb = self.claim_bank(bs + nb)
                    for kc in range(NCH):
                        self.tmm(self.PSb(bs + nb),
                                 slots[wname][:, kc * KB + hp * 128:kc * KB + (hp + 1) * 128],
                                 yln(kc)[:, nb * TS:(nb + 1) * TS],
                                 kc == 0, kc == NCH - 1,
                                 waits=(wb + [(self.sd, w_rdy), (self.sv, v_yln)])
                                 if kc == 0 else [], tag=f"{wname}p{hp}_{nb}")
                t_p = self.st.n
                self.sop(lambda e, buf=buf, bs=bs, hp=hp, bias=bias: e.add(
                    buf[:, :], self.PS[:, bs * 512:bs * 512 + 2048],
                    self.bcol(bias, hp)),
                    waits=[(self.st, t_p)], tag=f"{wname}e{hp}")
                s_e = self.ss.n
                for nb in range(4):
                    self.bank_read(bs + nb, self.ss, s_e)
                for nb in range(4):
                    wb = self.claim_bank(bs + nb)
                    self.tmm(self.PSb(bs + nb), self.perm[:, :],
                             buf[:, nb * TS:(nb + 1) * TS], True, True,
                             waits=(wb + [(self.ss, s_e)]) if nb == 0 else wb,
                             tag=f"pm{hp}_{bs}_{nb}")
                t_pm = self.st.n
                ps = self.PS[:, bs * 512:bs * 512 + 2048]
                self.vop(lambda e, ps=ps: e.tensor_tensor(rsc[:, :], ps,
                                                          self.sin[:, :], ALU.mult),
                         waits=[(self.st, t_pm)], tag=f"rs{hp}_{bs}")
                for nb in range(4):
                    self.bank_read(bs + nb, self.sv, self.sv.n)
                self.vop(lambda e, buf=buf: e.tensor_mul(buf[:, :], buf[:, :],
                                                         self.cos[:, :]),
                         waits=[(self.st, t_pm)], tag=f"rc{hp}_{bs}")
                self.vop(lambda e, buf=buf: e.tensor_add(buf[:, :], buf[:, :],
                                                         rsc[:, :]),
                         tag=f"ra{hp}_{bs}")
            v_rope = self.sv.n
            if b == 0 and hp == 0 and self.debug:
                self.dump("ylnD", yln(0)[:, 0:512], stage2=True)
                self.dump("q1D", q1[:, 0:512], stage2=True)
                self.dump("k1D", k1[:, 0:512], stage2=True)
            for kt in range(16):
                bk = kt % 4
                wb = self.claim_bank(bk)
                for kc in range(NCH):
                    self.tmm(self.PS[:, bk * 512:bk * 512 + 128],
                             yln(kc)[:, kt * 128:(kt + 1) * 128],
                             slots["wv"][:, kc * KB + hp * 128:kc * KB + (hp + 1) * 128],
                             kc == 0, False,
                             waits=wb if kc == 0 else [], tag=f"vp{hp}_{kt}")
                self.tmm(self.PS[:, bk * 512:bk * 512 + 128], self.onesb[0:1, :],
                         self.bvrow[0:1, hp * 128:(hp + 1) * 128], False, True,
                         tag=f"vb{hp}_{kt}")
                t_v = self.st.n
                for h in range(2):
                    self.sop(lambda e, bk=bk, kt=kt, h=h: e.copy(
                        self.vhat[:, kt * 130 + 65 * h:kt * 130 + 65 * h + 64],
                        self.PS[:, bk * 512 + 64 * h:bk * 512 + 64 * h + 64]),
                        waits=[(self.st, t_v)] if h == 0 else [], tag=f"ve{hp}_{kt}_{h}")
                self.bank_read(bk, self.ss, self.ss.n)
            s_vhat = self.ss.n
            W = lambda: [(self.sv, self.sv.n)]
            bk6 = 6
            for i, (buf, h) in enumerate([(q1, 0), (q1, 1), (k1, 0), (k1, 1)]):
                self.vop(lambda e, buf=buf, h=h: e.tensor_mul(
                    rsc[64 * h:64 * h + 64, :], buf[64 * h:64 * h + 64, :],
                    buf[64 * h:64 * h + 64, :]),
                    waits=W(), tag=f"nsq{hp}_{i}")
                v_sq = self.sv.n
                wbs = []
                for j in range(4):
                    wbs += self.claim_bank(4 + j)
                for j in range(4):
                    self.tmm(self.PS[0:1, (4 + j) * 512:(4 + j) * 512 + 512],
                             self.onesb[64 * h:64 * h + 64, 0:1],
                             rsc[64 * h:64 * h + 64, j * 512:(j + 1) * 512],
                             True, True,
                             waits=(wbs + [(self.sv, v_sq)]) if j == 0 else [],
                             inc=(j == 3), tag=f"nrm{hp}_{i}_{j}")
                t_n = self.st.n
                self.vop(lambda e, i=i: e.reduce_max(rf[0:1, 4 + i:5 + i],
                                                     self.PS[0:1, 2048:4096], AX),
                         waits=[(self.st, t_n)], tag=f"nmx{hp}_{i}")
                for j in range(4):
                    self.bank_read(4 + j, self.sv, self.sv.n)
            s_cells = self.sv.n
            self.vop(lambda e: e.tensor_copy(rf[0:1, 4:8], rf[0:1, 4:8]),
                     waits=[(self.sv, s_cells)], tag=f"nflush{hp}")
            if b == 0 and hp == 0 and self.debug:
                self.dump("celD", rf[0:1, 0:16], stage2=True)
            self.vop(lambda e: e.tensor_mul(rf[0:1, 8:10], rf[0:1, 4:6],
                                            rf[0:1, 6:8]),
                     waits=[(self.sv, self.sv.n)], tag=f"cm{hp}")
            v_cm = self.sv.n
            self.sop(lambda e: e.activation(rf[0:1, 10:12], rf[0:1, 8:10], ACT.Sqrt),
                     waits=[(self.sv, v_cm)], tag=f"cs{hp}")
            s_cs = self.ss.n
            self.vop(lambda e: e.tensor_scalar(db[0:1, 4:6], rf[0:1, 10:12],
                                               -0.125, None, ALU.mult, ALU.bypass),
                     waits=[(self.ss, s_cs)], tag=f"cnn{hp}")
            v_cn = self.sv.n
            for h in range(2):
                wb = self.claim_bank(bk6)
                self.tmm(self.PS[:, bk6 * 512:bk6 * 512 + 1], self.onesb[0:1, :],
                         db[0:1, 4 + h:5 + h], True, True,
                         waits=wb + [(self.sv, v_cn)], tag=f"cb{hp}_{h}")
                t_c = self.st.n
                self.vop(lambda e, h=h: e.tensor_copy(
                    self.bcol("c0", h), self.PS[:, bk6 * 512:bk6 * 512 + 1]),
                    waits=[(self.st, t_c), (self.ss, getattr(self, "c0_free", 0))],
                    tag=f"ce{hp}_{h}")
                self.bank_read(bk6, self.sv, self.sv.n)
            v_c0 = self.sv.n
            if b == 0 and hp == 0 and self.debug:
                self.dump("vhD", self.vhat[:, 0:260], stage2=True)
                self.dump("c0D", self.bcol("c0", 0, 2), stage2=True)
            pt = lambda h, p: (self.probs0 if h == 0 else self.probs1)[:, p * 512:(p + 1) * 512]
            probs_free = {(h, p): (self.st, 0) for h in range(2) for p in range(2)}
            for qb in range(4):
                avw = {0: self.claim_bank(2), 1: self.claim_bank(3)}
                exp_done = {}
                for kt in range(16):
                    p = kt % 2
                    for h in range(2):
                        wb = self.claim_bank(h)
                        self.tmm(self.PS[0:128, h * 512:(h + 1) * 512],
                                 k1[64 * h:64 * h + 64, kt * 128:(kt + 1) * 128],
                                 q1[64 * h:64 * h + 64, qb * 512:(qb + 1) * 512],
                                 True, True,
                                 waits=wb + ([(self.sv, v_c0), (self.sv, v_rope)]
                                             if (qb == 0 and kt == 0) else []),
                                 tp=(64 * h, 0), tag=f"sc{hp}_{qb}_{kt}_{h}", inc=True)
                        t_s = self.st.n
                        sem, val = probs_free[(h, p)]
                        self.sop(lambda e, h=h, p=p: e.activation(
                            pt(h, p), self.PS[0:128, h * 512:(h + 1) * 512],
                            ACT.Exp, bias=self.bcol("c0", h), scale=0.125),
                            waits=[(self.st, t_s), (sem, val), (self.ss, s_vhat)],
                            tag=f"ex{hp}_{qb}_{kt}_{h}")
                        self.bank_read(h, self.ss, self.ss.n)
                        exp_done[(h, p)] = self.ss.n
                    for h in range(2):
                        p = kt % 2
                        self.tmm(self.PS[0:65, (2 + h) * 512:(3 + h) * 512],
                                 self.vhat[:, kt * 130 + 65 * h:kt * 130 + 65 * (h + 1)],
                                 pt(h, p), kt == 0, kt == 15,
                                 waits=(avw[h] if kt == 0 else []) +
                                 [(self.ss, exp_done[(h, p)])],
                                 tag=f"av{hp}_{qb}_{kt}_{h}", inc=True)
                        probs_free[(h, p)] = (self.st, self.st.n)
                t_av = self.st.n
                if b == 0 and hp == 0 and qb == 0 and self.debug:
                    self.dump("pr0", pt(0, 1), stage2=True)
                for h in range(2):
                    avps = self.PS[:, (2 + h) * 512:(3 + h) * 512]
                    self.sop(lambda e, avps=avps: e.activation(
                        rf[64:65, :], avps[64:65, :], ACT.Ln),
                        waits=[(self.st, t_av)], tag=f"dn{hp}_{qb}_{h}")
                    self.sop(lambda e: e.activation(db[64:65, 0:512], rf[64:65, :],
                                                    ACT.Exp, scale=-1.0),
                             tag=f"dx{hp}_{qb}_{h}")
                    s_dn = self.ss.n
                    bkb = 5
                    wb = self.claim_bank(bkb)
                    self.tmm(self.PS[0:64, bkb * 512:bkb * 512 + 512],
                             self.onesb[64:65, 0:64], db[64:65, 0:512],
                             True, True, waits=wb + [(self.ss, s_dn)],
                             tag=f"dbc{hp}_{qb}_{h}", inc=True)
                    t_db = self.st.n
                    self.sop(lambda e: e.copy(rf[0:64, :],
                                              self.PS[0:64, bkb * 512:bkb * 512 + 512]),
                             waits=[(self.st, t_db)], tag=f"dbe{hp}_{qb}_{h}")
                    s_dbe = self.ss.n
                    self.bank_read(bkb, self.ss, s_dbe)
                    if h == 0:
                        dst = self.quarter[0:64,
                                           (hp % 2) * S + qb * 512:(hp % 2) * S + qb * 512 + 512]
                    else:
                        dst = self.stage[0:64, qb * 512:qb * 512 + 512]
                    self.vop(lambda e, avps=avps, dst=dst: e.tensor_mul(
                        dst, avps[0:64, :], rf[0:64, :]),
                        waits=[(self.ss, s_dbe)], tag=f"qn{hp}_{qb}_{h}")
                    self.bank_read(2 + h, self.sv, self.sv.n)
            self.c0_free = self.ss.n
            if b == 0 and hp == 0 and self.debug:
                self.dump("dnD", db[64:65, 0:512], stage2=True)
                self.dump("qrD", self.quarter[0:64, 0:512], stage2=True)
            v_q = self.sv.n
            self.dma("gpsimd",
                     self.quarter[64:128, (hp % 2) * S:(hp % 2) * S + S],
                     self.stage[0:64, 0:S],
                     waits=[(self.sv, v_q), (self.sd, self.prologue_done)],
                     tag=f"hshift{hp}")
            self.qshift_done = self.sg.n
            if hp % 2 == 1:
                wlist = [(self.sd, self.img_ready["wo"]), (self.sg, self.qshift_done)]
                if hasattr(self, "wotsl_free"):
                    wlist.append((self.st, self.wotsl_free))
                self.dma("sync", self.wotsl[:, 0:KB],
                         self.img["wo"][:, (hp - 1) * KB:hp * KB],
                         waits=wlist, tag=f"wold{hp}a")
                self.dma("sync", self.wotsl[:, KB:2 * KB],
                         self.img["wo"][:, hp * KB:(hp + 1) * KB], tag=f"wold{hp}b")
                wo_rdy = self.sd.n
                for oc in range(NCH):
                    for tb in range(4):
                        bk = (oc * 4 + tb) % 2 + 6
                        wb = self.claim_bank(bk)
                        for kc2 in range(2):
                            self.tmm(self.PSb(bk),
                                     self.wotsl[:, kc2 * KB + oc * 128:kc2 * KB + (oc + 1) * 128],
                                     self.quarter[:, kc2 * S + tb * 512:kc2 * S + (tb + 1) * 512],
                                     kc2 == 0, kc2 == 1,
                                     waits=(wb + [(self.sd, wo_rdy),
                                                  (self.sg, self.qshift_done)])
                                     if kc2 == 0 else [], tag=f"op{hp}_{oc}_{tb}")
                        t_o = self.st.n
                        Rv = self.R[:, oc * S + tb * 512:oc * S + (tb + 1) * 512]
                        ps = self.PSb(bk)
                        if pair_idx == 0:
                            self.vop(lambda e, ps=ps, Rv=Rv, oc=oc: e.scalar_tensor_tensor(
                                Rv, ps, self.bcol("bo", oc), Rv, ALU.add, ALU.add),
                                waits=[(self.st, t_o)], tag=f"Ro{hp}_{oc}_{tb}")
                        else:
                            self.vop(lambda e, ps=ps, Rv=Rv: e.tensor_add(Rv, Rv, ps),
                                     waits=[(self.st, t_o)], tag=f"Ro{hp}_{oc}_{tb}")
                        self.bank_read(bk, self.sv, self.sv.n)
                self.wotsl_free = self.st.n
                pair_idx += 1
        self.slot_free[0] = (self.st, self.st.n)
        self.slot_free[1] = (self.st, self.st.n)
        self.slot_free[2] = (self.st, self.st.n)
        if b == 0 and self.debug:
            self.dump("RaD", self.R[:, 0:512], stage2=True)

    def seg(self):
        self.segs.append(len(self.plan))

    def build(self):
        self.declare()
        self.alloc()
        self.sems()
        self.prologue()
        self.seg()
        for b in range(self.nbatch):
            self.load_R(b)
            self.seg()
            for ph in self.phases:
                if ph == "ffn1":
                    self.ffn(b, 0)
                elif ph == "attn":
                    self.attn(b)
                elif ph == "ffn2":
                    self.ffn(b, 1)
                self.seg()
            self.store_R(b)
            self.seg()
        self.plan.append(("sync", lambda e: e.wait_ge(self.so.h, self.so.n)))
        nc = self.nc
        with nc.Block() as block:
            cur = 0
            while cur < len(self.plan):
                eng = self.plan[cur][0]
                group = []
                while cur < len(self.plan) and self.plan[cur][0] == eng:
                    group.append(self.plan[cur][1])
                    cur += 1
                def section(eh, group=group):
                    for f in group:
                        f(eh)
                getattr(block, eng)(section)
        for g in reversed(self._semctx):
            g.__exit__(None, None, None)
        return nc


def make_consts():
    j = np.arange(0, DH, 2, dtype=np.float64) / DH
    invf = 1.0 / (10000.0 ** j)
    t = np.arange(S, dtype=np.float64)
    fr = t[None, :] * invf[:, None]
    cos = np.repeat(np.cos(fr), 2, axis=0)
    sin = np.repeat(np.sin(fr), 2, axis=0)
    sin_eff = sin.copy()
    sin_eff[:32] *= -1.0
    cosT = np.concatenate([cos, cos], axis=0).astype(ml_dtypes.bfloat16)
    sinT = np.concatenate([sin_eff, sin_eff], axis=0).astype(ml_dtypes.bfloat16)
    perm = np.zeros((128, 128), dtype=np.float32)
    for blk in range(2):
        for i in range(64):
            perm[blk * 64 + (i + 32) % 64, blk * 64 + i] = 1.0
    return {
        "c_cos": np.ascontiguousarray(cosT),
        "c_sin": np.ascontiguousarray(sinT),
        "c_perm": perm.astype(ml_dtypes.bfloat16),
        "c_ones": np.ones((128, 128), dtype=ml_dtypes.bfloat16),
        "c_idf": np.eye(128, dtype=np.float32),
        "c_idb": np.eye(128, dtype=ml_dtypes.bfloat16),
    }


_BUILD_CACHE = {}


def get_nc(phases=("ffn1", "attn", "ffn2")):
    key = tuple(phases)
    if key not in _BUILD_CACHE:
        _BUILD_CACHE[key] = Build(phases=phases).build()
    return _BUILD_CACHE[key]


def kernel(**inputs):
    consts = make_consts()
    nc = get_nc()
    in_maps = []
    for core in range(8):
        m = dict(consts)
        for k, v in inputs.items():
            v = np.ascontiguousarray(np.asarray(v))
            m[k] = v[2 * core:2 * core + 2] if k == "L0_attn_x" else v
        in_maps.append(m)
    res = run_bass_kernel_spmd(nc, in_maps, core_ids=list(range(8)))
    return np.concatenate([r["y"] for r in res.results], axis=0)


if __name__ == "__main__":
    bld = Build(phases=("ffn1",))
    nc = bld.build()
    print("build ok; instructions:", nc.n_instructions())


# version 04
# speedup vs baseline: 1.4355x; 1.3187x over previous
import numpy as np
import ml_dtypes
import concourse.bass as bass
import concourse.mybir as mybir
from concourse.bass_utils import run_bass_kernel_spmd

F32 = mybir.dt.float32
BF16 = mybir.dt.bfloat16
ALU = mybir.AluOpType
ACT = mybir.ActivationFunctionType
AX = mybir.AxisListType.X

D = 1024
S = 2048
NBATCH = 2
NCH = 8
H = 16
DH = 64
NE = 8
HID = 256
TS = 512
NTS = S // TS
QB = 256
EPS_LN = 1e-5
EPS_RMS = 1e-6
KB = 1024
BIG1024 = [1024, 1024]


class Ctr:
    __slots__ = ("h", "n", "name")
    def __init__(self, h, name):
        self.h = h; self.n = 0; self.name = name


class Build:
    def __init__(self, phases=("ffn1", "attn", "ffn2"), nbatch=NBATCH, debug=()):
        self.phases = phases
        self.nbatch = nbatch
        self.debug = list(debug)
        self.dbg_slots = {}
        self.nc = bass.Bass("TRN2", target_bir_lowering=False)
        self.plan = []
        self.segs = []
        self.last_wait = {}

    def op(self, eng, fn, waits=(), inc=None, inc_k=1, tag=""):
        waits2 = []
        for (c, v) in waits:
            if v <= 0:
                continue
            assert v <= c.n, f"forward wait on {c.name}: {v} > {c.n} ({tag})"
            key = (eng, id(c))
            if self.last_wait.get(key, -1) < v:
                self.last_wait[key] = v
                waits2.append((c.h, v))
        if inc is not None:
            inc.n += inc_k
        ic = inc
        def run(eh, waits2=waits2, fn=fn, ic=ic, inc_k=inc_k):
            for (hh, v) in waits2:
                eh.wait_ge(hh, v)
            r = fn(eh)
            if ic is not None:
                r.then_inc(ic.h, inc_k)
        self.plan.append((eng, run))
        return inc.n if inc is not None else None

    def dma(self, q, out_ap, in_ap, waits=(), tag="", **kw):
        c = self.sd if q == "sync" else self.sg
        return self.op(q, lambda e: e.dma_start(out_ap, in_ap, **kw),
                       waits=waits, inc=c, inc_k=16, tag=tag)

    def tmm(self, out, lhsT, rhs, start, stop, waits=(), inc=False, tp=None, tag=""):
        kw = {}
        if tp is not None:
            kw["tile_position"] = tp
        return self.op("tensor",
                       lambda e: e.matmul(out, lhsT, rhs, start=start, stop=stop, **kw),
                       waits=waits, inc=self.st if (inc or stop) else None, tag=tag)

    def ttr(self, out, in_, ident, waits=(), tag=""):
        return self.op("tensor", lambda e: e.transpose(out, in_, ident),
                       waits=waits, inc=self.st, tag=tag)

    def vop(self, fn, waits=(), tag=""):
        return self.op("vector", fn, waits=waits, inc=self.sv, tag=tag)

    def sop(self, fn, waits=(), tag=""):
        return self.op("scalar", fn, waits=waits, inc=self.ss, tag=tag)

    def claim_bank(self, k):
        w = self.bank_free[k]
        self.bank_free[k] = []
        return list(w)

    def bank_read(self, k, ctr, val):
        self.bank_free[k].append((ctr, val))

    def PSb(self, k, c0=0, c1=512):
        return self.PS[:, 512 * k + c0:512 * k + c1]

    def declare(self):
        nc = self.nc
        self.ins = {}
        specs = {
            "L0_attn_x": [self.nbatch, S, D], "L0_attn_wq": BIG1024,
            "L0_attn_bq": [D], "L0_attn_wk": BIG1024, "L0_attn_bk": [D],
            "L0_attn_wv": BIG1024, "L0_attn_bv": [D], "L0_attn_wo": BIG1024,
            "L0_attn_bo": [D], "L0_ffn_dw_kernel": [D, 1, 3], "L0_ffn_dw_bias": [D],
            "L0_ffn_pw_w": BIG1024, "L0_ffn_pw_b": [D], "L0_ffn_router_w": [8, D],
            "L0_ffn_router_b": [8], "L0_ffn_expert_w": [NE, D, D],
            "L0_ffn_expert_b": [NE, D], "L0_ffn_out_w": BIG1024, "L0_ffn_out_b": [D],
            "L0_ffn_norm_w": [D], "L0_ffn_ff_down_w": [HID, D],
            "L0_ffn_ff_down_b": [HID], "L0_ffn_ff_up_w": [D, HID], "L0_ffn_ff_up_b": [D],
            "c_cos": [128, S], "c_sin": [128, S], "c_perm": [128, 128],
            "c_ones": [128, 128], "c_idf": [128, 128], "c_idb": [128, 128],
        }
        bf16_names = {"c_cos", "c_sin", "c_perm", "c_ones", "c_idb"}
        for k, sh in specs.items():
            dt = BF16 if k in bf16_names else F32
            self.ins[k] = nc.dram_tensor(k, sh, dt, kind="ExternalInput").ap()
        self.out = nc.dram_tensor("y", [self.nbatch, S, D], F32, kind="ExternalOutput").ap()
        if self.debug:
            self.dbgt = nc.dram_tensor("dbg", [len(self.debug), 128, 512], F32,
                                       kind="ExternalOutput").ap()
        self.nat = {}
        self.img = {}
        for w in ["wq", "wk", "wv", "wo", "pw", "ow"] + [f"e{i}" for i in range(NE)]:
            self.nat[w] = nc.dram_tensor("nat_" + w, BIG1024, BF16, kind="Internal").ap()
            self.img[w] = nc.dram_tensor("img_" + w, [128, 8 * KB], BF16, kind="Internal").ap()
        self.nat["ffd"] = nc.dram_tensor("nat_ffd", [HID, D], BF16, kind="Internal").ap()
        self.img["ffd"] = nc.dram_tensor("img_ffd", [128, 8 * HID], BF16, kind="Internal").ap()
        self.nat["ffu"] = nc.dram_tensor("nat_ffu", [D, HID], BF16, kind="Internal").ap()
        self.img["ffu"] = nc.dram_tensor("img_ffu", [128, 2 * KB], BF16, kind="Internal").ap()

    def alloc(self):
        nc = self.nc
        A = nc.alloc_sbuf_tensor_at
        arena = nc.alloc_sbuf_tensor("ARENA", [128, 212800], mybir.dt.uint8)
        amloc = nc.lookup_mloc(arena)
        o = [amloc.addr]
        def at(name, shape, dtype):
            t = A(name, shape, dtype, offset=o[0])
            o[0] += int(np.prod(shape[1:])) * mybir.dt.size(dtype)
            o[0] = (o[0] + 31) & ~31
            return t
        self.R = at("R", [128, NCH * S], F32)
        self.cos = at("cosT", [128, S], BF16)
        self.sin = at("sinT", [128, S], BF16)
        self.perm = at("permT", [128, 128], BF16)
        self.onesb = at("onesb", [128, 128], BF16)
        self.idf = at("idf", [128, 128], F32)
        self.idb = at("idb", [128, 128], BF16)
        self.bias = at("biasc", [128, 240], F32)
        self.routimg = at("routimg", [128, 64], BF16)
        self.rbcol = at("rbcol", [128, 1], F32)
        self.bvrow = at("bvrow", [1, KB], BF16)
        self.S0 = at("S0", [128, 8 * KB], BF16)
        self.S1 = at("S1", [128, 8 * KB], BF16)
        self.S2 = at("S2", [128, 8 * KB], BF16)
        ua_base = o[0]
        self.UA = at("UA", [128, 16 * KB], BF16)
        ub = o[0]
        self.yb = A("yb", [128, NCH * (TS + 2)], F32, offset=ub)
        self.sres = A("sres", [128, NCH * TS], F32, offset=ub)
        self.comb = A("comb", [128, NCH * TS], F32, offset=ub + 16448)
        self.probs0 = A("probs0", [128, 2 * TS], BF16, offset=ub)
        self.probs1 = A("probs1", [128, 2 * TS], BF16, offset=ub + 2048)
        self.vhat = A("vhat", [128, 16 * 130], BF16, offset=ub + 4096)
        self.q1 = A("q1", [128, S], BF16, offset=ub + 8320)
        self.k1 = A("k1", [128, S], BF16, offset=ub + 12416)
        self.rsc = A("ropesc", [128, S], BF16, offset=ub + 16512)
        o[0] = ((ub + 33248) + 31) & ~31
        uc = o[0]
        self.row_m = A("row_m", [1, TS], F32, offset=uc)
        self.row_q = A("row_q", [1, TS], F32, offset=uc + 2048)
        self.row_t = A("row_t", [1, TS], F32, offset=uc + 4096)
        self.row_bf = A("row_bf", [1, 4 * TS], BF16, offset=uc + 6144)
        self.mb = A("mb", [128, TS], F32, offset=uc + 10240)
        self.rb_ = A("rb_", [128, TS], F32, offset=uc + 12288)
        self.silu = A("siluf", [128, TS], F32, offset=uc + 14336)
        self.rwrow = A("rwrow", [1, TS], BF16, offset=uc + 10240)
        self.rwb = A("rwb", [128, TS], BF16, offset=uc + 11264)
        self.lgf = A("lgf", [128, 32], F32, offset=uc + 16384)
        self.rwtok = A("rwtok", [128, 32], BF16, offset=uc + 16512)
        self.m1n = A("m1n", [128, 24], F32, offset=uc + 16576)
        self.cstage = A("cstage", [128, TS], BF16, offset=uc + 16672)
        self.quarter = A("quarter", [128, 2 * S], BF16, offset=uc)
        self.wotsl = A("wotsl", [128, S], BF16, offset=uc + 8192)
        self.stage = A("stage", [128, S], BF16, offset=uc + 12288)
        self.recipf = A("recipf", [128, 512], F32, offset=uc + 16384)
        self.denb = A("denb", [128, 512], BF16, offset=uc + 18432)
        self.xstage = A("xstage", [128, KB], F32, offset=uc + 12288)
        o[0] = uc + 19456
        if self.debug:
            self.dbgstage = A("dbgstage", [128, 512], F32, offset=ua_base + 28672)
            self.dbgstage2 = A("dbgstage2", [128, 512], F32, offset=ub + 20608)
        assert o[0] <= amloc.addr + 212800, o[0]
        self.PS = nc.alloc_psum_tensor("PS", [128, 4096], F32)
        self.bc = {}
        names = ["bq", "bk", "bv", "bo", "dwb", "pwb", "outb", "nw", "nwh",
                 "ffub", "ffubh"]
        for i, name in enumerate(names):
            self.bc[name] = (8 * i, 8)
        self.bc["ffdb"] = (88, 2)
        self.bc["ebc"] = (90, 64)
        self.bc["kern"] = (154, 24)
        self.bc["rbb"] = (178, 8)
        self.bc["c0"] = (186, 2)

    def bcol(self, name, j=0, w=1):
        c0, n = self.bc[name]
        assert j + w <= n, (name, j, w)
        return self.bias[:, c0 + j:c0 + j + w]

    def sems(self):
        nc = self.nc
        self._semctx = []
        def mk(name):
            g = nc.semaphore(name)
            h = g.__enter__()
            self._semctx.append(g)
            return Ctr(h, name)
        self.sd = mk("sd"); self.sg = mk("sg"); self.st = mk("st")
        self.sv = mk("sv"); self.ss = mk("ss"); self.so = mk("so")
        self.bank_free = {k: [] for k in range(8)}
        self.slot_free = {0: (self.st, 0), 1: (self.st, 0), 2: (self.st, 0)}
        self.eready = {1: 0, 2: 0}

    def dump(self, name, ap, bf16=False, stage2=False):
        if name not in self.debug:
            return
        slot = self.debug.index(name)
        stg = self.dbgstage2 if stage2 else self.dbgstage
        waits = [(c, c.n) for c in (self.st, self.sv, self.ss, self.sg)]
        nr, ncols = ap.shape[0], ap.shape[-1]
        self.vop(lambda e: e.tensor_copy(stg[0:nr, 0:ncols], ap),
                 waits=waits, tag="dbgc_" + name)
        v = self.sv.n
        self.dma("sync", self.dbgt[slot], stg[:, :],
                 waits=[(self.sv, v)], tag="dbg_" + name)
        self.vop(lambda e: e.tensor_copy(self.m1n[:, 3:4], self.m1n[:, 3:4]),
                 waits=[(self.sd, self.sd.n)], tag="dbgw_" + name)

    def prologue(self):
        ins = self.ins
        self.dma("sync", self.cos[:], ins["c_cos"], tag="ccos")
        self.dma("sync", self.sin[:], ins["c_sin"], tag="csin")
        self.dma("sync", self.perm[:], ins["c_perm"], tag="cperm")
        self.dma("sync", self.onesb[:], ins["c_ones"], tag="cones")
        self.dma("sync", self.idf[:], ins["c_idf"], tag="cidf")
        self.dma("sync", self.idb[:], ins["c_idb"], tag="cidb")
        for name, src in [("bq", "L0_attn_bq"), ("bk", "L0_attn_bk"),
                          ("bv", "L0_attn_bv"), ("bo", "L0_attn_bo"),
                          ("dwb", "L0_ffn_dw_bias"), ("pwb", "L0_ffn_pw_b"),
                          ("outb", "L0_ffn_out_b"), ("nw", "L0_ffn_norm_w"),
                          ("ffub", "L0_ffn_ff_up_b")]:
            c0, _ = self.bc[name]
            self.dma("sync", self.bias[:, c0:c0 + 8],
                     ins[src].rearrange("(c p) -> p c", p=128), tag="b_" + name, allow_slow_non_contiguous=True)
        c0, _ = self.bc["ffdb"]
        self.dma("sync", self.bias[:, c0:c0 + 2],
                 ins["L0_ffn_ff_down_b"].rearrange("(c p) -> p c", p=128), tag="b_ffdb", allow_slow_non_contiguous=True)
        c0, _ = self.bc["ebc"]
        self.dma("sync", self.bias[:, c0:c0 + 64].rearrange("p (e c) -> p e c", e=8),
                 ins["L0_ffn_expert_b"].rearrange("e (c p) -> p e c", p=128), tag="b_ebc", allow_slow_non_contiguous=True)
        c0, _ = self.bc["kern"]
        self.dma("sync", self.bias[:, c0:c0 + 24].rearrange("p (c k) -> p c k", c=8),
                 ins["L0_ffn_dw_kernel"].rearrange("(c p) o k -> p c (o k)", p=128),
                 tag="b_kern", allow_slow_non_contiguous=True)
        self.dma("sync", self.rbcol[0:8, 0:1],
                 ins["L0_ffn_router_b"].rearrange("(e o) -> e o", o=1), tag="b_rb",
                 allow_slow_non_contiguous=True)
        n_const = self.sd.n
        self.const_ready = n_const

        self.sop(lambda e: e.mul(self.bcol("nwh", 0, 8), self.bcol("nw", 0, 8), 0.5),
                 waits=[(self.sd, n_const)], tag="nwh")
        self.sop(lambda e: e.mul(self.bcol("ffubh", 0, 8), self.bcol("ffub", 0, 8), 0.5),
                 tag="ffubh")

        self.dma("sync", self.rbcol[0:8, 0:1],
                 ins["L0_ffn_router_b"].rearrange("(e o) -> e o", o=1), tag="b_rb",
                 allow_slow_non_contiguous=True)
        n_const = self.sd.n
        order = ["pw"] + [f"e{i}" for i in range(NE)] + ["ow", "ffd", "ffu",
                                                         "wq", "wk", "wv", "wo"]
        src = {"pw": self.ins["L0_ffn_pw_w"], "ow": self.ins["L0_ffn_out_w"],
               "ffd": self.ins["L0_ffn_ff_down_w"], "ffu": self.ins["L0_ffn_ff_up_w"],
               "wq": self.ins["L0_attn_wq"], "wk": self.ins["L0_attn_wk"],
               "wv": self.ins["L0_attn_wv"], "wo": self.ins["L0_attn_wo"]}
        for i in range(NE):
            src[f"e{i}"] = self.ins["L0_ffn_expert_w"][i]
        self.img_ready = {}
        s1_free = 0
        s2_free = 0
        for wname in order:
            a = src[wname]
            rows, ncol = a.shape
            nch = rows // 128
            cast_view = self.S1[:, 0:nch * ncol].rearrange("p (c n) -> p c n", c=nch)
            self.dma("gpsimd", cast_view, a.rearrange("(c p) n -> p c n", p=128),
                     waits=[(self.sd, s1_free)], tag=f"cast_{wname}")
            g_cast = self.sg.n
            if wname == "ffu":
                self.sop(lambda e: e.mul(self.S1[:, 0:2 * KB], self.S1[:, 0:2 * KB], 0.5),
                         waits=[(self.sg, g_cast)], tag="ffu_half")
                store_waits = [(self.ss, self.ss.n)]
            else:
                store_waits = [(self.sg, g_cast)]
            self.dma("sync", self.nat[wname].rearrange("(c p) n -> p c n", p=128),
                     cast_view, waits=store_waits, tag=f"nat_{wname}")
            d_nat = self.sd.n
            s1_free = d_nat
            nkc = ncol // 128
            for c in range(nkc):
                self.dma("sync", self.S2[:, c * rows:(c + 1) * rows],
                         self.nat[wname][:, c * 128:(c + 1) * 128], transpose=True,
                         waits=[(self.sd, max(d_nat, s2_free))] if c == 0 else (),
                         tag=f"tr_{wname}_{c}")
            d_tr = self.sd.n
            self.dma("sync", self.img[wname][:, 0:nkc * rows], self.S2[:, 0:nkc * rows],
                     waits=[(self.sd, d_tr)], tag=f"img_{wname}")
            s2_free = self.sd.n
            self.img_ready[wname] = self.sd.n
        self.dma("gpsimd", self.S1[0:8, 0:KB], self.ins["L0_ffn_router_w"],
                 waits=[(self.sd, s1_free)], tag="cast_rout")
        g_r = self.sg.n
        for c in range(8):
            bk = 2 + (c % 2)
            w = self.claim_bank(bk)
            self.tmm(self.PS[:, bk * 512:bk * 512 + 8],
                     self.S1[0:8, c * 128:(c + 1) * 128], self.idb[0:8, 0:8],
                     True, True, waits=w + [(self.sg, g_r)], tag=f"routT{c}")
            t_n = self.st.n
            self.sop(lambda e, c=c, bk=bk: e.copy(self.routimg[:, c * 8:(c + 1) * 8],
                                                  self.PS[:, bk * 512:bk * 512 + 8]),
                     waits=[(self.st, t_n)], tag=f"routE{c}")
            self.bank_read(bk, self.ss, self.ss.n)
        self.dma("gpsimd", self.bvrow[0:1, :],
                 self.ins["L0_attn_bv"].rearrange("(o d) -> o d", o=1), tag="bvrow")
        self.prologue_done = self.sd.n
        self.slot_free[1] = (self.st, self.st.n)
        self.prol_s1_free = (self.sd, s1_free)
        self.prol_s2_free = (self.sd, s2_free)
        self.prol_g = (self.sg, self.sg.n)

    def load_R(self, b):
        first = True
        for tt in range(16):
            waits = []
            if first and b > 0:
                waits.append((self.so, self.so.n))
            if not first:
                waits.append((self.st, self.last_xT))
            d_x = self.dma("sync", self.xstage[:],
                           self.ins["L0_attn_x"][b, tt * 128:(tt + 1) * 128, :],
                           waits=waits, tag=f"xin{b}_{tt}")
            first = False
            for c in range(NCH):
                bk = c % 4
                w = self.claim_bank(bk)
                self.ttr(self.PS[:, bk * 512:bk * 512 + 128],
                         self.xstage[:, c * 128:(c + 1) * 128], self.idf[:],
                         waits=w + [(self.sd, d_x)], tag=f"xT{tt}_{c}")
                t_n = self.st.n
                self.vop(lambda e, c=c, tt=tt, bk=bk: e.tensor_copy(
                    self.R[:, c * S + tt * 128:c * S + (tt + 1) * 128],
                    self.PS[:, bk * 512:bk * 512 + 128]),
                    waits=[(self.st, t_n)], tag=f"xE{tt}_{c}")
                self.bank_read(bk, self.sv, self.sv.n)
            self.last_xT = self.st.n

    def store_R(self, b):
        prev_store = self.so.n
        for tt in range(16):
            for c in range(NCH):
                bk = c % 4
                w = self.claim_bank(bk)
                self.ttr(self.PS[:, bk * 512:bk * 512 + 128],
                         self.R[:, c * S + tt * 128:c * S + (tt + 1) * 128], self.idf[:],
                         waits=w + ([(self.sv, self.sv.n)] if (tt == 0 and c == 0) else []),
                         tag=f"oT{tt}_{c}")
                t_n = self.st.n
                self.vop(lambda e, c=c, bk=bk: e.tensor_copy(
                    self.xstage[:, c * 128:(c + 1) * 128],
                    self.PS[:, bk * 512:bk * 512 + 128]),
                    waits=[(self.st, t_n)] + ([(self.so, prev_store)] if c == 0 else []),
                    tag=f"oE{tt}_{c}")
                self.bank_read(bk, self.sv, self.sv.n)
            v_n = self.sv.n
            self.op("sync", lambda e, tt=tt, b=b: e.dma_start(
                self.out[b, tt * 128:(tt + 1) * 128, :], self.xstage[:]),
                waits=[(self.sv, v_n)], inc=self.so, inc_k=16, tag=f"xout{tt}")
            prev_store = self.so.n

    def stats(self, get_chunk, want_mean, eps, tag, entry_waits=(), scratch=None):
        ua = self.UA
        if scratch is None:
            xbf = ua[:, 13 * KB:13 * KB + TS]
            sq = ua[:, 13 * KB + TS:13 * KB + 2 * TS]
        else:
            xbf, sq = scratch
        bsum, bsq = 4, 5
        wsum = self.claim_bank(bsum)
        wsq = self.claim_bank(bsq)
        t_prev = 0
        for c in range(NCH):
            src = get_chunk(c)
            cw = list(entry_waits) if c == 0 else [(self.st, t_prev)]
            self.vop(lambda e, src=src, xbf=xbf: e.tensor_copy(xbf, src),
                     waits=cw, tag=f"{tag}cast{c}")
            v_c = self.sv.n
            self.vop(lambda e, xbf=xbf, sq=sq: e.tensor_mul(sq, xbf, xbf),
                     tag=f"{tag}sq{c}")
            v_s = self.sv.n
            if want_mean:
                self.tmm(self.PSb(bsum)[0:1, 0:TS], self.onesb[:, 0:1], xbf,
                         c == 0, c == NCH - 1,
                         waits=(wsum if c == 0 else []) + [(self.sv, v_c)],
                         inc=True, tag=f"{tag}ms{c}")
            self.tmm(self.PSb(bsq)[0:1, 0:TS], self.onesb[:, 0:1], sq,
                     c == 0, c == NCH - 1,
                     waits=(wsq if c == 0 else []) + [(self.sv, v_s)],
                     inc=True, tag=f"{tag}qs{c}")
            t_prev = self.st.n
        t_done = self.st.n
        rm, rq, rt, rbf = self.row_m, self.row_q, self.row_t, self.row_bf
        if tag == "F00_0" and self.debug:
            self.sop(lambda e: e.copy(self.silu[0:1, :], self.PSb(bsq)[0:1, 0:TS]),
                     waits=[(self.st, t_done)], tag="dbgq")
            self.bank_read(bsq, self.ss, self.ss.n)
            self.dump("qsumP", self.silu[0:1, :])
            self.sop(lambda e: e.copy(self.silu[0:1, :], self.PSb(bsum)[0:1, 0:TS]),
                     waits=[(self.st, t_done)], tag="dbgm")
            self.bank_read(bsum, self.ss, self.ss.n)
            self.dump("msumP", self.silu[0:1, :])
        if want_mean:
            self.vop(lambda e: e.tensor_scalar_mul(rm[0:1, :], self.PSb(bsum)[0:1, 0:TS],
                                                   1.0 / D),
                     waits=[(self.st, t_done)], tag=f"{tag}m")
        self.vop(lambda e: e.tensor_scalar_mul(rq[0:1, :], self.PSb(bsq)[0:1, 0:TS],
                                               1.0 / D),
                 waits=[(self.st, t_done)], tag=f"{tag}q")
        self.bank_read(bsum, self.sv, self.sv.n)
        self.bank_read(bsq, self.sv, self.sv.n)
        if want_mean:
            self.vop(lambda e: e.tensor_mul(rt[0:1, :], rm[0:1, :], rm[0:1, :]),
                     tag=f"{tag}mm")
            self.vop(lambda e: e.tensor_sub(rq[0:1, :], rq[0:1, :], rt[0:1, :]),
                     tag=f"{tag}var")
        self.vop(lambda e: e.tensor_scalar_add(rq[0:1, :], rq[0:1, :], eps),
                 tag=f"{tag}eps")
        v_var = self.sv.n
        if tag == "F00_0":
            self.dump("varrow", rq[0:1, :])
        self.sop(lambda e: e.activation(rt[0:1, :], rq[0:1, :], ACT.Sqrt),
                 waits=[(self.sv, v_var)], tag=f"{tag}sd")
        s_sd = self.ss.n
        if tag == "F00_0" and self.debug:
            self.dump("sdrow", rt[0:1, :])
        self.vop(lambda e: e.reciprocal(rq[0:1, :], rt[0:1, :]),
                 waits=[(self.ss, s_sd)], tag=f"{tag}rc")
        self.vop(lambda e: e.tensor_copy(rbf[0:1, 2 * TS:3 * TS], rq[0:1, :]),
                 tag=f"{tag}rhi")
        self.vop(lambda e: e.tensor_sub(rbf[0:1, 3 * TS:4 * TS], rq[0:1, :],
                                        rbf[0:1, 2 * TS:3 * TS]), tag=f"{tag}rlo")
        if want_mean:
            self.vop(lambda e: e.tensor_copy(rbf[0:1, 0:TS], rm[0:1, :]),
                     tag=f"{tag}mhi")
            self.vop(lambda e: e.tensor_sub(rbf[0:1, TS:2 * TS], rm[0:1, :],
                                            rbf[0:1, 0:TS]), tag=f"{tag}mlo")
        v_rows = self.sv.n
        if tag == "F00_0":
            self.dump("rrow", rq[0:1, :])
            self.dump("rhilo", rbf[0:1, 2 * TS:3 * TS], bf16=True)
        plan = [(self.rb_, 2 * TS, 3 * TS, 6)]
        if want_mean:
            plan.append((self.mb, 0, TS, 7))
        for (dst, hi, lo, bk) in plan:
            w = self.claim_bank(bk)
            self.tmm(self.PSb(bk), self.onesb[0:1, :], rbf[0:1, hi:hi + TS],
                     True, False, waits=w + [(self.sv, v_rows)], inc=True,
                     tag=f"{tag}bc1")
            self.tmm(self.PSb(bk), self.onesb[0:1, :], rbf[0:1, lo:lo + TS],
                     False, True, tag=f"{tag}bc2")
            t_b = self.st.n
            self.vop(lambda e, dst=dst, bk=bk: e.tensor_copy(dst[:, :], self.PSb(bk)),
                     waits=[(self.st, t_b)], tag=f"{tag}bce")
            self.bank_read(bk, self.sv, self.sv.n)
        return self.sv.n

    def ffn(self, b, call):
        ua = self.UA
        conv_r = lambda c: ua[:, c * TS:(c + 1) * TS]
        x1_r = lambda c: ua[:, 4 * KB + c * TS:4 * KB + (c + 1) * TS]
        h_r = lambda c: ua[:, 8 * KB + c * TS:8 * KB + (c + 1) * TS]
        g_r = lambda m: ua[:, 12 * KB + m * TS:12 * KB + (m + 1) * TS]
        lgbf = ua[:, 15 * KB:15 * KB + 512]

        sem, val = self.slot_free[0]
        self.dma("sync", self.S0[:], self.img["pw"],
                 waits=[(self.sd, self.img_ready["pw"]), (sem, val)], tag=f"pwld{call}")
        pw_rdy = self.sd.n

        for ts in range(NTS):
            t0 = ts * TS
            Rc = lambda c: self.R[:, c * S + t0:c * S + t0 + TS]
            entry = [(self.st, self.st.n), (self.ss, self.ss.n)] if ts == 0 else ()
            if b == 0 and call == 0 and ts == 0:
                self.dump("R0", Rc(0))
            self.stats(Rc, True, EPS_LN, f"F{call}{b}_{ts}", entry_waits=entry)
            if b == 0 and call == 0 and ts == 0:
                self.dump("mb", self.mb[:, :])
                self.dump("rb", self.rb_[:, :])
            if ts == 0:
                self.vop(lambda e: e.memset(
                    self.yb[:, :].rearrange("p (c n) -> p c n", c=NCH)[:, :, 0:2], 0.0),
                    tag="halo0")
            for c in range(NCH):
                yv = self.yb[:, c * (TS + 2) + 2:c * (TS + 2) + 2 + TS]
                Rsrc = Rc(c)
                self.vop(lambda e, Rsrc=Rsrc, yv=yv: e.tensor_sub(yv, Rsrc, self.mb[:, :]),
                         tag=f"ysub{c}")
                self.vop(lambda e, yv=yv: e.tensor_mul(yv, yv, self.rb_[:, :]),
                         tag=f"ymul{c}")
            for c in range(NCH):
                yb0 = self.yb[:, c * (TS + 2):c * (TS + 2) + TS]
                yb1 = self.yb[:, c * (TS + 2) + 1:c * (TS + 2) + 1 + TS]
                yb2 = self.yb[:, c * (TS + 2) + 2:c * (TS + 2) + 2 + TS]
                self.vop(lambda e, yb0=yb0, c=c: e.tensor_scalar(
                    self.silu[:, :], yb0, self.bcol("kern", 3 * c), self.bcol("dwb", c),
                    ALU.mult, ALU.add), tag=f"cv0_{c}")
                self.vop(lambda e, yb1=yb1, c=c: e.scalar_tensor_tensor(
                    self.silu[:, :], yb1, self.bcol("kern", 3 * c + 1), self.silu[:, :],
                    ALU.mult, ALU.add), tag=f"cv1_{c}")
                self.vop(lambda e, yb2=yb2, c=c: e.scalar_tensor_tensor(
                    conv_r(c), yb2, self.bcol("kern", 3 * c + 2), self.silu[:, :],
                    ALU.mult, ALU.add), tag=f"cv2_{c}")
            v_conv = self.sv.n
            if b == 0 and call == 0 and ts == 0:
                self.dump("y0", self.yb[:, 2:2 + TS])
                self.dump("cv0", conv_r(0), bf16=True)
            if ts < NTS - 1:
                for c in range(NCH):
                    self.vop(lambda e, c=c: e.tensor_copy(
                        self.yb[:, c * (TS + 2):c * (TS + 2) + 2],
                        self.yb[:, c * (TS + 2) + TS:c * (TS + 2) + TS + 2]),
                        tag=f"halo{c}")
            v_silu_free = self.sv.n
            for half in range(2):
                t_pw = {}
                for j in range(4):
                    oc = half * 4 + j
                    w = self.claim_bank(j)
                    for kc in range(NCH):
                        self.tmm(self.PSb(j),
                                 self.S0[:, kc * KB + oc * 128:kc * KB + (oc + 1) * 128],
                                 conv_r(kc), kc == 0, kc == NCH - 1,
                                 waits=(w + [(self.sv, v_conv), (self.sd, pw_rdy)])
                                 if kc == 0 else [], tag=f"pw{oc}")
                    t_pw[j] = self.st.n
                s_si = {}
                for j in range(4):
                    oc = half * 4 + j
                    dst = self.UA[:, 13 * KB + j * TS:13 * KB + (j + 1) * TS]
                    self.sop(lambda e, j=j, oc=oc, dst=dst: e.activation(
                        dst, self.PSb(j), ACT.Silu, bias=self.bcol("pwb", oc)),
                        waits=[(self.st, t_pw[j]), (self.sv, v_silu_free)],
                        tag=f"psl{oc}")
                    s_si[j] = self.ss.n
                    self.bank_read(j, self.ss, s_si[j])
                for j in range(4):
                    oc = half * 4 + j
                    yv = self.yb[:, oc * (TS + 2) + 2:oc * (TS + 2) + 2 + TS]
                    srcb = self.UA[:, 13 * KB + j * TS:13 * KB + (j + 1) * TS]
                    self.vop(lambda e, oc=oc, yv=yv, srcb=srcb: e.tensor_add(
                        x1_r(oc), yv, srcb),
                        waits=[(self.ss, s_si[j])], tag=f"x1_{oc}")
                v_silu_free = self.sv.n
            v_x1 = self.sv.n
            if b == 0 and call == 0 and ts == 0:
                self.dump("x10", x1_r(0), bf16=True)
            self._dbg_rt = (b == 0 and call == 0 and ts == 0)
            self._router(v_x1, x1_r, lgbf)
            self._experts(call, ts, x1_r, conv_r, v_x1)
            self._moe_tail(b, call, ts, x1_r, h_r, g_r)

    def _router(self, v_x1, x1_r, lgbf):
        blg, blt = 4, 5
        if getattr(self, "_dbg_rt", False):
            self.dump("rimg", self.routimg[:, :], bf16=True)
            self.dump("rbbD", self.bias[:, self.bc["rbb"][0]:self.bc["rbb"][0] + 8])
        for tt in range(4):
            w = self.claim_bank(blg)
            for kc in range(NCH):
                self.tmm(self.PS[0:8, blg * 512:blg * 512 + 128],
                         self.routimg[:, kc * 8:(kc + 1) * 8],
                         x1_r(kc)[:, tt * 128:(tt + 1) * 128],
                         kc == 0, kc == NCH - 1,
                         waits=(w + [(self.sv, v_x1)]) if kc == 0 else [],
                         tag=f"rt{tt}")
            t_lg = self.st.n
            self.sop(lambda e, tt=tt: e.add(
                lgbf[0:8, tt * 128:(tt + 1) * 128],
                self.PS[0:8, blg * 512:blg * 512 + 128],
                self.rbcol[0:8, 0:1]),
                waits=[(self.st, t_lg)], tag=f"lgev{tt}")
            self.bank_read(blg, self.ss, self.ss.n)
        s_lg = self.ss.n
        if getattr(self, "_dbg_rt", False):
            self.dump("lgF", lgbf[0:8, :], bf16=True)
        for tt in range(4):
            w = self.claim_bank(blt)
            self.tmm(self.PS[:, blt * 512:blt * 512 + 8],
                     lgbf[0:8, tt * 128:(tt + 1) * 128], self.idb[0:8, 0:8],
                     True, True, waits=(w + [(self.ss, s_lg)]) if tt == 0 else w,
                     tag=f"lgT{tt}")
            t_lgT = self.st.n
            lg = self.lgf[:, tt * 8:(tt + 1) * 8]
            m1n = self.m1n[:, 0:1]
            m2n = self.m1n[:, 1:2]
            rec = self.m1n[:, 2:3]
            rec2 = self.m1n[:, 12:13]
            msk = self.m1n[:, 4:12]
            msk2 = self.m1n[:, 16:24]
            self.vop(lambda e, lg=lg: e.tensor_copy(
                lg, self.PS[:, blt * 512:blt * 512 + 8]),
                waits=[(self.st, t_lgT)], tag=f"lgadd{tt}")
            self.bank_read(blt, self.sv, self.sv.n)
            W = lambda: [(self.sv, self.sv.n)]
            self.vop(lambda e, lg=lg, m1n=m1n: e.reduce_max(m1n, lg, AX, negate=True),
                     waits=W(), tag=f"t1_{tt}")
            self.vop(lambda e, lg=lg, m1n=m1n, msk=msk: e.tensor_scalar(
                msk, lg, m1n, 0.0, ALU.add, ALU.is_ge), waits=W(), tag=f"i1_{tt}")
            self.vop(lambda e, lg=lg, msk=msk, msk2=msk2: e.scalar_tensor_tensor(
                msk2, msk, -1e30, lg, ALU.mult, ALU.add), waits=W(), tag=f"zz_{tt}")
            self.vop(lambda e, m2n=m2n, msk2=msk2: e.reduce_max(m2n, msk2, AX,
                                                               negate=True),
                     waits=W(), tag=f"t2_{tt}")
            self.vop(lambda e, lg=lg, m2n=m2n, msk=msk: e.tensor_scalar(
                msk, lg, m2n, 0.0, ALU.add, ALU.is_ge), waits=W(), tag=f"i2_{tt}")
            v_i = self.sv.n
            self.sop(lambda e, lg=lg, m1n=m1n: e.activation(lg, lg, ACT.Exp, bias=m1n),
                     waits=[(self.sv, v_i)], tag=f"gt_{tt}")
            s_g = self.ss.n
            self.vop(lambda e, lg=lg, msk=msk: e.tensor_mul(lg, lg, msk),
                     waits=[(self.ss, s_g)], tag=f"gm_{tt}")
            self.vop(lambda e, lg=lg, rec=rec: e.reduce_sum(rec, lg, AX),
                     waits=W(), tag=f"dn_{tt}")
            self.vop(lambda e, rec=rec, rec2=rec2: e.reciprocal(rec2, rec),
                     waits=W(), tag=f"rc_{tt}")
            self.vop(lambda e, lg=lg, rec2=rec2, tt=tt: e.tensor_scalar(
                self.rwtok[:, tt * 8:(tt + 1) * 8], lg, rec2, None, ALU.mult,
                ALU.bypass), waits=W(), tag=f"rw_{tt}")
        self.vop(lambda e: e.tensor_copy(self.lgf[:, 0:8], self.lgf[:, 0:8]),
                 waits=[(self.sv, self.sv.n)], tag="rwflush")
        self.v_rw = self.sv.n
        if getattr(self, "_dbg_rt", False):
            self.dump("lg", self.lgf[:, :])
            self.dump("rwt", self.rwtok[:, :], bf16=True)

    def _experts(self, call, ts, x1_r, conv_r, v_x1):
        v_rw = self.v_rw
        def eload(e_i):
            slot = self.S1 if e_i % 2 == 0 else self.S2
            k = 1 if e_i % 2 == 0 else 2
            sem, val = self.slot_free[k]
            wl = [(self.sd, self.img_ready[f"e{e_i}"]), (sem, val)]
            if not getattr(self, "_prol_clear", False):
                wl += [self.prol_s1_free, self.prol_s2_free, self.prol_g]
            self.dma("sync", slot[:], self.img[f"e{e_i}"], waits=wl, tag=f"el{e_i}")
            self.eready[k] = self.sd.n
            if e_i >= 6:
                self._prol_clear = True
        eload(0)
        eload(1)
        self._esilu_free = self.sv.n
        for e_i in range(NE):
            slot = self.S1 if e_i % 2 == 0 else self.S2
            k = 1 if e_i % 2 == 0 else 2
            brb = 5
            t_rw = {}
            for tt in range(4):
                bkt = 4 + (tt + 2) % 4
                w = self.claim_bank(bkt)
                self.tmm(self.PS[0:1, bkt * 512:bkt * 512 + 128],
                         self.rwtok[:, tt * 8 + e_i:tt * 8 + e_i + 1], self.idb[:],
                         True, True, waits=(w + [(self.sv, v_rw)]) if tt == 0 else w,
                         tag=f"rwT{e_i}")
                t_rw[tt] = self.st.n
            for tt in range(4):
                bkt = 4 + (tt + 2) % 4
                self.sop(lambda e, tt=tt, bkt=bkt: e.copy(
                    self.rwrow[0:1, tt * 128:(tt + 1) * 128],
                    self.PS[0:1, bkt * 512:bkt * 512 + 128]),
                    waits=[(self.st, t_rw[tt])], tag=f"rwr{e_i}")
                self.bank_read(bkt, self.ss, self.ss.n)
            s_rw = self.ss.n
            w = self.claim_bank(brb)
            self.tmm(self.PSb(brb), self.onesb[0:1, :], self.rwrow[0:1, :], True, True,
                     waits=w + [(self.ss, s_rw)], tag=f"rwb{e_i}")
            t_rb = self.st.n
            self.vop(lambda e: e.tensor_copy(self.rwb[:, :], self.PSb(brb)),
                     waits=[(self.st, t_rb)], tag=f"rwbe{e_i}")
            v_rwb = self.sv.n
            self.bank_read(brb, self.sv, v_rwb)
            if getattr(self, "_dbg_rt", False) and e_i == 0:
                self.dump("rwb0", self.rwb[:, :], bf16=True)
            eri = self.eready[k]
            for half in range(2):
                t_e = {}
                for j in range(4):
                    oc = half * 4 + j
                    w = self.claim_bank(j)
                    for kc in range(NCH):
                        self.tmm(self.PSb(j),
                                 slot[:, kc * KB + oc * 128:kc * KB + (oc + 1) * 128],
                                 x1_r(kc), kc == 0, kc == NCH - 1,
                                 waits=(w + [(self.sd, eri), (self.sv, v_x1)])
                                 if kc == 0 else [], tag=f"ex{e_i}_{oc}")
                    t_e[j] = self.st.n
                if half == 1:
                    self.slot_free[k] = (self.st, self.st.n)
                    if e_i + 2 < NE:
                        eload(e_i + 2)
                s_si = {}
                for j in range(4):
                    oc = half * 4 + j
                    dst = self.UA[:, 13 * KB + j * TS:13 * KB + (j + 1) * TS]
                    self.sop(lambda e, j=j, e_i=e_i, oc=oc, dst=dst: e.activation(
                        dst, self.PSb(j), ACT.Silu,
                        bias=self.bcol("ebc", e_i * 8 + oc)),
                        waits=[(self.st, t_e[j]), (self.sv, self._esilu_free)],
                        tag=f"es{e_i}_{oc}")
                    s_si[j] = self.ss.n
                    self.bank_read(j, self.ss, s_si[j])
                for j in range(4):
                    oc = half * 4 + j
                    sc = conv_r(oc)
                    srcb = self.UA[:, 13 * KB + j * TS:13 * KB + (j + 1) * TS]
                    self.vop(lambda e, sc=sc, srcb=srcb: e.tensor_mul(
                        sc, srcb, self.rwb[:, :]),
                        waits=[(self.ss, s_si[j]), (self.sv, v_rwb)],
                        tag=f"em{e_i}_{oc}")
                    cb = self.comb[:, oc * TS:(oc + 1) * TS]
                    if e_i == 0:
                        self.vop(lambda e, sc=sc, cb=cb: e.tensor_copy(cb, sc),
                                 tag=f"ec{e_i}_{oc}")
                    else:
                        self.vop(lambda e, sc=sc, cb=cb: e.tensor_add(cb, cb, sc),
                                 tag=f"ea{e_i}_{oc}")
                self._esilu_free = self.sv.n

    def _moe_tail(self, b, call, ts, x1_r, h_r, g_r):
        v_comb = self.sv.n
        if getattr(self, "_dbg_rt", False):
            self.dump("cb0", self.comb[:, 0:TS])
        sem, val = self.slot_free[1]
        self.dma("sync", self.S1[:], self.img["ow"],
                 waits=[(self.sd, self.img_ready["ow"]), (sem, val)], tag="owld")
        ow_rdy = self.sd.n
        for wave in range(2):
            wbanks = []
            for bk in range(4):
                wbanks += self.claim_bank(bk)
            t_prev_kc = 0
            for kc in range(NCH):
                cs = self.cstage[:, 0:TS]
                cbsrc = self.comb[:, kc * TS:(kc + 1) * TS]
                cw = [(self.sv, v_comb)] if kc == 0 else [(self.st, t_prev_kc)]
                self.vop(lambda e, cs=cs, cbsrc=cbsrc: e.tensor_copy(cs, cbsrc),
                         waits=cw, tag=f"mc{wave}_{kc}")
                v_cs = self.sv.n
                for j in range(4):
                    oc = wave * 4 + j
                    self.tmm(self.PSb(j),
                             self.S1[:, kc * KB + oc * 128:kc * KB + (oc + 1) * 128],
                             cs, kc == 0, kc == NCH - 1,
                             waits=(wbanks + [(self.sd, ow_rdy), (self.sv, v_cs)])
                             if (kc == 0 and j == 0) else
                             ([(self.sv, v_cs)] if j == 0 else []),
                             inc=(j == 3), tag=f"mm{wave}_{kc}_{j}")
                t_prev_kc = self.st.n
            t_moe = self.st.n
            self.slot_free[1] = (self.st, t_moe)
            for j in range(4):
                oc = wave * 4 + j
                ps = self.PSb(j)
                sr = self.sres[:, oc * TS:(oc + 1) * TS]
                sb = h_r(oc)
                x1s = x1_r(oc)
                self.vop(lambda e, ps=ps, sr=sr, oc=oc, x1s=x1s: e.scalar_tensor_tensor(
                    sr, ps, self.bcol("outb", oc), x1s, ALU.add, ALU.add),
                    waits=[(self.st, t_moe)], tag=f"sr{oc}")
                self.vop(lambda e, sr=sr, sb=sb: e.tensor_copy(sb, sr), tag=f"sb{oc}")
                self.bank_read(j, self.sv, self.sv.n)
        if getattr(self, "_dbg_rt", False):
            self.dump("sr0", self.sres[:, 0:TS])
        ua = self.UA
        sq = ua[:, 13 * KB + TS:13 * KB + 2 * TS]
        bsq = 5
        w = self.claim_bank(bsq)
        t_prev = 0
        for c in range(NCH):
            self.vop(lambda e, c=c, sq=sq: e.tensor_mul(sq, h_r(c), h_r(c)),
                     waits=[(self.st, t_prev)] if c > 0 else [], tag=f"rq{c}")
            v_s = self.sv.n
            self.tmm(self.PSb(bsq)[0:1, 0:TS], self.onesb[:, 0:1], sq,
                     c == 0, c == NCH - 1,
                     waits=(w if c == 0 else []) + [(self.sv, v_s)], inc=True,
                     tag=f"rqs{c}")
            t_prev = self.st.n
        t_done = self.st.n
        rq, rt, rbf = self.row_q, self.row_t, self.row_bf
        self.vop(lambda e: e.tensor_scalar_mul(rq[0:1, :], self.PSb(bsq)[0:1, 0:TS],
                                               1.0 / D),
                 waits=[(self.st, t_done)], tag="rmq")
        self.bank_read(bsq, self.sv, self.sv.n)
        self.vop(lambda e: e.tensor_scalar_add(rq[0:1, :], rq[0:1, :], EPS_RMS),
                 tag="rme")
        v_var = self.sv.n
        self.sop(lambda e: e.activation(rt[0:1, :], rq[0:1, :], ACT.Sqrt),
                 waits=[(self.sv, v_var)], tag="rmsd")
        s_sd = self.ss.n
        self.vop(lambda e: e.reciprocal(rq[0:1, :], rt[0:1, :]),
                 waits=[(self.ss, s_sd)], tag="rmrc")
        if getattr(self, "_dbg_rt", False):
            self.dump("rmsrow", rq[0:1, :])
        self.vop(lambda e: e.tensor_copy(rbf[0:1, 2 * TS:3 * TS], rq[0:1, :]),
                 waits=[(self.st, getattr(self, "rbf_free", 0))], tag="rmhi")
        self.vop(lambda e: e.tensor_sub(rbf[0:1, 3 * TS:4 * TS], rq[0:1, :],
                                        rbf[0:1, 2 * TS:3 * TS]), tag="rmlo")
        v_rows = self.sv.n
        bk = 6
        w = self.claim_bank(bk)
        self.tmm(self.PSb(bk), self.onesb[0:1, :], rbf[0:1, 2 * TS:3 * TS], True, False,
                 waits=w + [(self.sv, v_rows)], inc=True, tag="rmb1")
        self.tmm(self.PSb(bk), self.onesb[0:1, :], rbf[0:1, 3 * TS:4 * TS], False, True,
                 tag="rmb2")
        t_b = self.st.n
        self.rbf_free = t_b
        psb = self.PSb(bk)
        self.vop(lambda e, psb=psb: e.tensor_copy(self.rb_[:, :], psb),
                 waits=[(self.st, t_b)], tag="rmbe")
        self.bank_read(bk, self.sv, self.sv.n)
        if getattr(self, "_dbg_rt", False):
            self.sop(lambda e, psb=psb: e.copy(self.mb[:, :], psb),
                     waits=[(self.st, t_b)], tag="dbgb6")
            self.bank_read(bk, self.ss, self.ss.n)
            self.dump("bk6D", self.mb[:, :])
            self.dump("rmhiD", rbf[0:1, 2 * TS:3 * TS], bf16=True)
            self.dump("rb2b", self.rb_[:, :])
        for oc in range(NCH):
            sr = self.sres[:, oc * TS:(oc + 1) * TS]
            self.vop(lambda e, sr=sr: e.tensor_mul(self.silu[:, :], sr, self.rb_[:, :]),
                     tag=f"h{oc}")
            self.vop(lambda e, oc=oc: e.tensor_scalar(
                h_r(oc), self.silu[:, :], self.bcol("nw", oc), None, ALU.mult,
                ALU.bypass), tag=f"hb{oc}")
            Rv = self.R[:, oc * S + ts * TS:oc * S + (ts + 1) * TS]
            self.vop(lambda e, Rv=Rv, oc=oc: e.scalar_tensor_tensor(
                Rv, self.silu[:, :], self.bcol("nwh", oc), Rv, ALU.mult, ALU.add),
                tag=f"Rh{oc}")
        v_h = self.sv.n
        if getattr(self, "_dbg_rt", False):
            self.dump("rb2", self.rb_[:, :])
            self.dump("h0", h_r(0), bf16=True)
        sem, val = self.slot_free[1]
        self.dma("sync", self.S1[:, 0:8 * HID], self.img["ffd"],
                 waits=[(self.sd, self.img_ready["ffd"]), (sem, val)], tag="ffdld")
        self.dma("sync", self.S1[:, 8 * HID:8 * HID + 2 * KB], self.img["ffu"],
                 waits=[(self.sd, self.img_ready["ffu"])], tag="ffuld")
        ff_rdy = self.sd.n
        s_gs = []
        for mc in range(2):
            bk = 4 + mc
            w = self.claim_bank(bk)
            for kc in range(NCH):
                self.tmm(self.PSb(bk),
                         self.S1[:, kc * HID + mc * 128:kc * HID + (mc + 1) * 128],
                         h_r(kc), kc == 0, kc == NCH - 1,
                         waits=(w + [(self.sd, ff_rdy), (self.sv, v_h)])
                         if kc == 0 else [], tag=f"fd{mc}")
            t_f = self.st.n
            self.sop(lambda e, bk=bk, mc=mc: e.activation(
                g_r(mc), self.PSb(bk), ACT.Gelu, bias=self.bcol("ffdb", mc)),
                waits=[(self.st, t_f)], tag=f"ge{mc}")
            self.bank_read(bk, self.ss, self.ss.n)
        s_g = self.ss.n
        if getattr(self, "_dbg_rt", False):
            self.dump("g0", g_r(0), bf16=True)
        for oc in range(NCH):
            bk = oc % 4
            w = self.claim_bank(bk)
            for hc in range(2):
                self.tmm(self.PSb(bk),
                         self.S1[:, 8 * HID + hc * KB + oc * 128:
                                 8 * HID + hc * KB + (oc + 1) * 128],
                         g_r(hc), hc == 0, hc == 1,
                         waits=(w + [(self.ss, s_g)]) if hc == 0 else [], tag=f"fu{oc}")
            t_u = self.st.n
            Rv = self.R[:, oc * S + ts * TS:oc * S + (ts + 1) * TS]
            self.vop(lambda e, bk=bk, Rv=Rv, oc=oc: e.scalar_tensor_tensor(
                Rv, self.PSb(bk), self.bcol("ffubh", oc), Rv, ALU.add, ALU.add),
                waits=[(self.st, t_u)], tag=f"Rf{oc}")
            self.bank_read(bk, self.sv, self.sv.n)
        self.slot_free[1] = (self.st, self.st.n)
        if getattr(self, "_dbg_rt", False):
            self.dump("Rf0", self.R[:, 0:TS])

    def attn(self, b):
        ua = self.UA
        yln = lambda c: ua[:, c * S:(c + 1) * S]
        q1, k1, rsc = self.q1, self.k1, self.rsc
        rf, db = self.recipf, self.denb
        self.vop(lambda e: e.memset(self.vhat[:, :], 1.0),
                 waits=[(self.st, self.st.n), (self.ss, self.ss.n)], tag="vhat1")
        for ts in range(NTS):
            t0 = ts * TS
            Rc = lambda c, t0=t0: self.R[:, c * S + t0:c * S + t0 + TS]
            entry = [(self.st, self.st.n), (self.ss, self.ss.n)] if ts == 0 else ()
            self.stats(Rc, True, EPS_LN, f"A{b}_{ts}", entry_waits=entry,
                       scratch=(self.probs0[:, 0:TS], self.probs0[:, TS:2 * TS]))
            for c in range(NCH):
                Rsrc = Rc(c)
                dst = yln(c)[:, t0:t0 + TS]
                self.vop(lambda e, Rsrc=Rsrc: e.tensor_sub(self.silu[:, :], Rsrc,
                                                           self.mb[:, :]),
                         tag=f"alns{c}")
                self.vop(lambda e, dst=dst: e.tensor_mul(dst, self.silu[:, :],
                                                         self.rb_[:, :]),
                         tag=f"alnm{c}")
        v_yln = self.sv.n
        slots = {}
        for i, w in enumerate(["wq", "wk", "wv"]):
            sl = [self.S0, self.S1, self.S2][i]
            sem, val = self.slot_free[i]
            self.dma("sync", sl[:], self.img[w],
                     waits=[(self.sd, self.img_ready[w]), (sem, val)],
                     tag=f"a_{w}ld")
            slots[w] = sl
        w_rdy = self.sd.n
        pair_idx = 0
        for hp in range(NCH):
            for (wname, buf, bs, bias) in [("wq", q1, 0, "bq"), ("wk", k1, 4, "bk")]:
                for nb in range(4):
                    wb = self.claim_bank(bs + nb)
                    for kc in range(NCH):
                        self.tmm(self.PSb(bs + nb),
                                 slots[wname][:, kc * KB + hp * 128:kc * KB + (hp + 1) * 128],
                                 yln(kc)[:, nb * TS:(nb + 1) * TS],
                                 kc == 0, kc == NCH - 1,
                                 waits=(wb + [(self.sd, w_rdy), (self.sv, v_yln)])
                                 if kc == 0 else [], tag=f"{wname}p{hp}_{nb}")
                t_p = self.st.n
                self.sop(lambda e, buf=buf, bs=bs, hp=hp, bias=bias: e.add(
                    buf[:, :], self.PS[:, bs * 512:bs * 512 + 2048],
                    self.bcol(bias, hp)),
                    waits=[(self.st, t_p)], tag=f"{wname}e{hp}")
                s_e = self.ss.n
                for nb in range(4):
                    self.bank_read(bs + nb, self.ss, s_e)
                for nb in range(4):
                    wb = self.claim_bank(bs + nb)
                    self.tmm(self.PSb(bs + nb), self.perm[:, :],
                             buf[:, nb * TS:(nb + 1) * TS], True, True,
                             waits=(wb + [(self.ss, s_e)]) if nb == 0 else wb,
                             tag=f"pm{hp}_{bs}_{nb}")
                t_pm = self.st.n
                ps = self.PS[:, bs * 512:bs * 512 + 2048]
                self.vop(lambda e, ps=ps: e.tensor_tensor(rsc[:, :], ps,
                                                          self.sin[:, :], ALU.mult),
                         waits=[(self.st, t_pm)], tag=f"rs{hp}_{bs}")
                for nb in range(4):
                    self.bank_read(bs + nb, self.sv, self.sv.n)
                self.vop(lambda e, buf=buf: e.tensor_mul(buf[:, :], buf[:, :],
                                                         self.cos[:, :]),
                         waits=[(self.st, t_pm)], tag=f"rc{hp}_{bs}")
                self.vop(lambda e, buf=buf: e.tensor_add(buf[:, :], buf[:, :],
                                                         rsc[:, :]),
                         tag=f"ra{hp}_{bs}")
            v_rope = self.sv.n
            if b == 0 and hp == 0 and self.debug:
                self.dump("ylnD", yln(0)[:, 0:512], stage2=True)
                self.dump("q1D", q1[:, 0:512], stage2=True)
                self.dump("k1D", k1[:, 0:512], stage2=True)
            for ktq in range(4):
                t_v = {}
                for dk in range(4):
                    kt = 4 * ktq + dk
                    wb = self.claim_bank(dk)
                    for kc in range(NCH):
                        self.tmm(self.PS[:, dk * 512:dk * 512 + 128],
                                 yln(kc)[:, kt * 128:(kt + 1) * 128],
                                 slots["wv"][:, kc * KB + hp * 128:kc * KB + (hp + 1) * 128],
                                 kc == 0, False,
                                 waits=wb if kc == 0 else [], tag=f"vp{hp}_{kt}")
                    self.tmm(self.PS[:, dk * 512:dk * 512 + 128], self.onesb[0:1, :],
                             self.bvrow[0:1, hp * 128:(hp + 1) * 128], False, True,
                             tag=f"vb{hp}_{kt}")
                    t_v[dk] = self.st.n
                for dk in range(4):
                    kt = 4 * ktq + dk
                    for h in range(2):
                        self.sop(lambda e, dk=dk, kt=kt, h=h: e.copy(
                            self.vhat[:, kt * 130 + 65 * h:kt * 130 + 65 * h + 64],
                            self.PS[:, dk * 512 + 64 * h:dk * 512 + 64 * h + 64]),
                            waits=[(self.st, t_v[dk])] if h == 0 else [],
                            tag=f"ve{hp}_{kt}_{h}")
                    self.bank_read(dk, self.ss, self.ss.n)
            s_vhat = self.ss.n
            W = lambda: [(self.sv, self.sv.n)]
            bk6 = 6
            for i, (buf, h) in enumerate([(q1, 0), (q1, 1), (k1, 0), (k1, 1)]):
                self.vop(lambda e, buf=buf, h=h: e.tensor_mul(
                    rsc[64 * h:64 * h + 64, :], buf[64 * h:64 * h + 64, :],
                    buf[64 * h:64 * h + 64, :]),
                    waits=W(), tag=f"nsq{hp}_{i}")
                v_sq = self.sv.n
                wbs = []
                for j in range(4):
                    wbs += self.claim_bank(4 + j)
                for j in range(4):
                    self.tmm(self.PS[0:1, (4 + j) * 512:(4 + j) * 512 + 512],
                             self.onesb[64 * h:64 * h + 64, 0:1],
                             rsc[64 * h:64 * h + 64, j * 512:(j + 1) * 512],
                             True, True,
                             waits=(wbs + [(self.sv, v_sq)]) if j == 0 else [],
                             inc=(j == 3), tag=f"nrm{hp}_{i}_{j}")
                t_n = self.st.n
                self.vop(lambda e, i=i: e.reduce_max(rf[0:1, 4 + i:5 + i],
                                                     self.PS[0:1, 2048:4096], AX),
                         waits=[(self.st, t_n)], tag=f"nmx{hp}_{i}")
                for j in range(4):
                    self.bank_read(4 + j, self.sv, self.sv.n)
            s_cells = self.sv.n
            self.vop(lambda e: e.tensor_copy(rf[0:1, 4:8], rf[0:1, 4:8]),
                     waits=[(self.sv, s_cells)], tag=f"nflush{hp}")
            if b == 0 and hp == 0 and self.debug:
                self.dump("celD", rf[0:1, 0:16], stage2=True)
            self.vop(lambda e: e.tensor_mul(rf[0:1, 8:10], rf[0:1, 4:6],
                                            rf[0:1, 6:8]),
                     waits=[(self.sv, self.sv.n)], tag=f"cm{hp}")
            v_cm = self.sv.n
            self.sop(lambda e: e.activation(rf[0:1, 10:12], rf[0:1, 8:10], ACT.Sqrt),
                     waits=[(self.sv, v_cm)], tag=f"cs{hp}")
            s_cs = self.ss.n
            self.vop(lambda e: e.tensor_scalar(db[0:1, 4:6], rf[0:1, 10:12],
                                               -0.125, None, ALU.mult, ALU.bypass),
                     waits=[(self.ss, s_cs)], tag=f"cnn{hp}")
            v_cn = self.sv.n
            for h in range(2):
                wb = self.claim_bank(bk6)
                self.tmm(self.PS[:, bk6 * 512:bk6 * 512 + 1], self.onesb[0:1, :],
                         db[0:1, 4 + h:5 + h], True, True,
                         waits=wb + [(self.sv, v_cn)], tag=f"cb{hp}_{h}")
                t_c = self.st.n
                self.vop(lambda e, h=h: e.tensor_copy(
                    self.bcol("c0", h), self.PS[:, bk6 * 512:bk6 * 512 + 1]),
                    waits=[(self.st, t_c), (self.ss, getattr(self, "c0_free", 0))],
                    tag=f"ce{hp}_{h}")
                self.bank_read(bk6, self.sv, self.sv.n)
            v_c0 = self.sv.n
            if b == 0 and hp == 0 and self.debug:
                self.dump("vhD", self.vhat[:, 0:260], stage2=True)
                self.dump("c0D", self.bcol("c0", 0, 2), stage2=True)
            probs_free = {h: (self.st, 0) for h in range(2)}
            for qb in range(4):
                avw = {0: self.claim_bank(4), 1: self.claim_bank(5)}
                for m in range(8):
                    for dk in range(2):
                        kt = 2 * m + dk
                        for h in range(2):
                            bk = 2 * h + dk
                            wb = self.claim_bank(bk)
                            self.tmm(self.PS[0:128, bk * 512:(bk + 1) * 512],
                                     k1[64 * h:64 * h + 64, kt * 128:(kt + 1) * 128],
                                     q1[64 * h:64 * h + 64, qb * 512:(qb + 1) * 512],
                                     True, True,
                                     waits=wb + ([(self.sv, v_c0), (self.sv, v_rope)]
                                                 if (qb == 0 and m == 0) else []),
                                     tp=(64 * h, 0), tag=f"sc{hp}_{qb}_{kt}_{h}",
                                     inc=True)
                    t_s = self.st.n
                    exp_done = {}
                    for h in range(2):
                        prb = self.probs0 if h == 0 else self.probs1
                        sem, val = probs_free[h]
                        self.sop(lambda e, h=h, prb=prb: e.activation(
                            prb[:, 0:1024], self.PS[0:128, h * 1024:(h + 1) * 1024],
                            ACT.Exp, bias=self.bcol("c0", h), scale=0.125),
                            waits=[(self.st, t_s), (sem, val), (self.ss, s_vhat)],
                            tag=f"ex{hp}_{qb}_{m}_{h}")
                        exp_done[h] = self.ss.n
                        self.bank_read(2 * h, self.ss, self.ss.n)
                        self.bank_read(2 * h + 1, self.ss, self.ss.n)
                    for h in range(2):
                        prb = self.probs0 if h == 0 else self.probs1
                        for dk in range(2):
                            kt = 2 * m + dk
                            self.tmm(self.PS[0:65, (4 + h) * 512:(5 + h) * 512],
                                     self.vhat[:, kt * 130 + 65 * h:kt * 130 + 65 * (h + 1)],
                                     prb[:, dk * 512:(dk + 1) * 512],
                                     kt == 0, kt == 15,
                                     waits=(avw[h] if kt == 0 else []) +
                                     ([(self.ss, exp_done[h])] if dk == 0 else []),
                                     tag=f"av{hp}_{qb}_{kt}_{h}", inc=True)
                        probs_free[h] = (self.st, self.st.n)
                t_av = self.st.n
                if b == 0 and hp == 0 and qb == 0 and self.debug:
                    self.dump("pr0", self.probs0[:, 512:1024], stage2=True)
                for h in range(2):
                    avps = self.PS[:, (4 + h) * 512:(5 + h) * 512]
                    self.sop(lambda e, avps=avps: e.activation(
                        rf[64:65, :], avps[64:65, :], ACT.Ln),
                        waits=[(self.st, t_av)], tag=f"dn{hp}_{qb}_{h}")
                    self.sop(lambda e: e.activation(db[64:65, 0:512], rf[64:65, :],
                                                    ACT.Exp, scale=-1.0),
                             tag=f"dx{hp}_{qb}_{h}")
                    s_dn = self.ss.n
                    bkb = 6
                    wb = self.claim_bank(bkb)
                    self.tmm(self.PS[0:64, bkb * 512:bkb * 512 + 512],
                             self.onesb[64:65, 0:64], db[64:65, 0:512],
                             True, True, waits=wb + [(self.ss, s_dn)],
                             tag=f"dbc{hp}_{qb}_{h}", inc=True)
                    t_db = self.st.n
                    self.vop(lambda e: e.tensor_copy(rf[0:64, :],
                             self.PS[0:64, bkb * 512:bkb * 512 + 512]),
                             waits=[(self.st, t_db)], tag=f"dbe{hp}_{qb}_{h}")
                    v_dbe = self.sv.n
                    self.bank_read(bkb, self.sv, v_dbe)
                    if h == 0:
                        dst = self.quarter[0:64,
                                           (hp % 2) * S + qb * 512:(hp % 2) * S + qb * 512 + 512]
                    else:
                        dst = self.stage[0:64, qb * 512:qb * 512 + 512]
                    self.vop(lambda e, avps=avps, dst=dst: e.tensor_mul(
                        dst, avps[0:64, :], rf[0:64, :]),
                        waits=[(self.sv, v_dbe)], tag=f"qn{hp}_{qb}_{h}")
                    self.bank_read(4 + h, self.sv, self.sv.n)
            self.c0_free = self.ss.n
            if b == 0 and hp == 0 and self.debug:
                self.dump("dnD", db[64:65, 0:512], stage2=True)
                self.dump("qrD", self.quarter[0:64, 0:512], stage2=True)
            v_q = self.sv.n
            self.dma("gpsimd",
                     self.quarter[64:128, (hp % 2) * S:(hp % 2) * S + S],
                     self.stage[0:64, 0:S],
                     waits=[(self.sv, v_q), (self.sd, self.prologue_done)],
                     tag=f"hshift{hp}")
            self.qshift_done = self.sg.n
            if hp % 2 == 1:
                wlist = [(self.sd, self.img_ready["wo"]), (self.sg, self.qshift_done)]
                if hasattr(self, "wotsl_free"):
                    wlist.append((self.st, self.wotsl_free))
                self.dma("sync", self.wotsl[:, 0:KB],
                         self.img["wo"][:, (hp - 1) * KB:hp * KB],
                         waits=wlist, tag=f"wold{hp}a")
                self.dma("sync", self.wotsl[:, KB:2 * KB],
                         self.img["wo"][:, hp * KB:(hp + 1) * KB], tag=f"wold{hp}b")
                wo_rdy = self.sd.n
                for oc in range(NCH):
                    for tb in range(4):
                        bk = (oc * 4 + tb) % 2 + 6
                        wb = self.claim_bank(bk)
                        for kc2 in range(2):
                            self.tmm(self.PSb(bk),
                                     self.wotsl[:, kc2 * KB + oc * 128:kc2 * KB + (oc + 1) * 128],
                                     self.quarter[:, kc2 * S + tb * 512:kc2 * S + (tb + 1) * 512],
                                     kc2 == 0, kc2 == 1,
                                     waits=(wb + [(self.sd, wo_rdy),
                                                  (self.sg, self.qshift_done)])
                                     if kc2 == 0 else [], tag=f"op{hp}_{oc}_{tb}")
                        t_o = self.st.n
                        Rv = self.R[:, oc * S + tb * 512:oc * S + (tb + 1) * 512]
                        ps = self.PSb(bk)
                        if pair_idx == 0:
                            self.vop(lambda e, ps=ps, Rv=Rv, oc=oc: e.scalar_tensor_tensor(
                                Rv, ps, self.bcol("bo", oc), Rv, ALU.add, ALU.add),
                                waits=[(self.st, t_o)], tag=f"Ro{hp}_{oc}_{tb}")
                        else:
                            self.vop(lambda e, ps=ps, Rv=Rv: e.tensor_add(Rv, Rv, ps),
                                     waits=[(self.st, t_o)], tag=f"Ro{hp}_{oc}_{tb}")
                        self.bank_read(bk, self.sv, self.sv.n)
                self.wotsl_free = self.st.n
                pair_idx += 1
        self.slot_free[0] = (self.st, self.st.n)
        self.slot_free[1] = (self.st, self.st.n)
        self.slot_free[2] = (self.st, self.st.n)
        if b == 0 and self.debug:
            self.dump("RaD", self.R[:, 0:512], stage2=True)

    def seg(self):
        self.segs.append(len(self.plan))

    def build(self):
        self.declare()
        self.alloc()
        self.sems()
        self.prologue()
        self.seg()
        for b in range(self.nbatch):
            self.load_R(b)
            self.seg()
            for ph in self.phases:
                if ph == "ffn1":
                    self.ffn(b, 0)
                elif ph == "attn":
                    self.attn(b)
                elif ph == "ffn2":
                    self.ffn(b, 1)
                self.seg()
            self.store_R(b)
            self.seg()
        self.plan.append(("sync", lambda e: e.wait_ge(self.so.h, self.so.n)))
        nc = self.nc
        with nc.Block() as block:
            cur = 0
            while cur < len(self.plan):
                eng = self.plan[cur][0]
                group = []
                while cur < len(self.plan) and self.plan[cur][0] == eng:
                    group.append(self.plan[cur][1])
                    cur += 1
                def section(eh, group=group):
                    for f in group:
                        f(eh)
                getattr(block, eng)(section)
        for g in reversed(self._semctx):
            g.__exit__(None, None, None)
        return nc


def make_consts():
    j = np.arange(0, DH, 2, dtype=np.float64) / DH
    invf = 1.0 / (10000.0 ** j)
    t = np.arange(S, dtype=np.float64)
    fr = t[None, :] * invf[:, None]
    cos = np.repeat(np.cos(fr), 2, axis=0)
    sin = np.repeat(np.sin(fr), 2, axis=0)
    sin_eff = sin.copy()
    sin_eff[:32] *= -1.0
    cosT = np.concatenate([cos, cos], axis=0).astype(ml_dtypes.bfloat16)
    sinT = np.concatenate([sin_eff, sin_eff], axis=0).astype(ml_dtypes.bfloat16)
    perm = np.zeros((128, 128), dtype=np.float32)
    for blk in range(2):
        for i in range(64):
            perm[blk * 64 + (i + 32) % 64, blk * 64 + i] = 1.0
    return {
        "c_cos": np.ascontiguousarray(cosT),
        "c_sin": np.ascontiguousarray(sinT),
        "c_perm": perm.astype(ml_dtypes.bfloat16),
        "c_ones": np.ones((128, 128), dtype=ml_dtypes.bfloat16),
        "c_idf": np.eye(128, dtype=np.float32),
        "c_idb": np.eye(128, dtype=ml_dtypes.bfloat16),
    }


_BUILD_CACHE = {}


def get_nc(phases=("ffn1", "attn", "ffn2")):
    key = tuple(phases)
    if key not in _BUILD_CACHE:
        _BUILD_CACHE[key] = Build(phases=phases).build()
    return _BUILD_CACHE[key]


def kernel(**inputs):
    consts = make_consts()
    nc = get_nc()
    in_maps = []
    for core in range(8):
        m = dict(consts)
        for k, v in inputs.items():
            v = np.ascontiguousarray(np.asarray(v))
            m[k] = v[2 * core:2 * core + 2] if k == "L0_attn_x" else v
        in_maps.append(m)
    res = run_bass_kernel_spmd(nc, in_maps, core_ids=list(range(8)))
    return np.concatenate([r["y"] for r in res.results], axis=0)


if __name__ == "__main__":
    bld = Build(phases=("ffn1",))
    nc = bld.build()
    print("build ok; instructions:", nc.n_instructions())


# version 05
# speedup vs baseline: 1.5129x; 1.0539x over previous
import numpy as np
import ml_dtypes
import concourse.bass as bass
import concourse.mybir as mybir
from concourse.bass_utils import run_bass_kernel_spmd

F32 = mybir.dt.float32
BF16 = mybir.dt.bfloat16
ALU = mybir.AluOpType
ACT = mybir.ActivationFunctionType
AX = mybir.AxisListType.X

D = 1024
S = 2048
NBATCH = 2
NCH = 8
H = 16
DH = 64
NE = 8
HID = 256
TS = 512
NTS = S // TS
QB = 256
EPS_LN = 1e-5
EPS_RMS = 1e-6
KB = 1024
BIG1024 = [1024, 1024]


class Ctr:
    __slots__ = ("h", "n", "name")
    def __init__(self, h, name):
        self.h = h; self.n = 0; self.name = name


class Build:
    def __init__(self, phases=("ffn1", "attn", "ffn2"), nbatch=NBATCH, debug=()):
        self.phases = phases
        self.nbatch = nbatch
        self.debug = list(debug)
        self.dbg_slots = {}
        self.nc = bass.Bass("TRN2", target_bir_lowering=False)
        self.plan = []
        self.segs = []
        self.last_wait = {}

    def op(self, eng, fn, waits=(), inc=None, inc_k=1, tag=""):
        waits2 = []
        for (c, v) in waits:
            if v <= 0:
                continue
            assert v <= c.n, f"forward wait on {c.name}: {v} > {c.n} ({tag})"
            key = (eng, id(c))
            if self.last_wait.get(key, -1) < v:
                self.last_wait[key] = v
                waits2.append((c.h, v))
        if inc is not None:
            inc.n += inc_k
        ic = inc
        def run(eh, waits2=waits2, fn=fn, ic=ic, inc_k=inc_k):
            for (hh, v) in waits2:
                eh.wait_ge(hh, v)
            r = fn(eh)
            if ic is not None:
                r.then_inc(ic.h, inc_k)
        self.plan.append((eng, run))
        return inc.n if inc is not None else None

    def dma(self, q, out_ap, in_ap, waits=(), tag="", **kw):
        c = self.sd if q == "sync" else self.sg
        return self.op(q, lambda e: e.dma_start(out_ap, in_ap, **kw),
                       waits=waits, inc=c, inc_k=16, tag=tag)

    def tmm(self, out, lhsT, rhs, start, stop, waits=(), inc=False, tp=None, tag=""):
        kw = {}
        if tp is not None:
            kw["tile_position"] = tp
        return self.op("tensor",
                       lambda e: e.matmul(out, lhsT, rhs, start=start, stop=stop, **kw),
                       waits=waits, inc=self.st if (inc or stop) else None, tag=tag)

    def ttr(self, out, in_, ident, waits=(), tag=""):
        return self.op("tensor", lambda e: e.transpose(out, in_, ident),
                       waits=waits, inc=self.st, tag=tag)

    def vop(self, fn, waits=(), tag=""):
        return self.op("vector", fn, waits=waits, inc=self.sv, tag=tag)

    def sop(self, fn, waits=(), tag=""):
        return self.op("scalar", fn, waits=waits, inc=self.ss, tag=tag)

    def claim_bank(self, k):
        w = self.bank_free[k]
        self.bank_free[k] = []
        return list(w)

    def bank_read(self, k, ctr, val):
        self.bank_free[k].append((ctr, val))

    def PSb(self, k, c0=0, c1=512):
        return self.PS[:, 512 * k + c0:512 * k + c1]

    def declare(self):
        nc = self.nc
        self.ins = {}
        specs = {
            "L0_attn_x": [self.nbatch, S, D], "L0_attn_wq": BIG1024,
            "L0_attn_bq": [D], "L0_attn_wk": BIG1024, "L0_attn_bk": [D],
            "L0_attn_wv": BIG1024, "L0_attn_bv": [D], "L0_attn_wo": BIG1024,
            "L0_attn_bo": [D], "L0_ffn_dw_kernel": [D, 1, 3], "L0_ffn_dw_bias": [D],
            "L0_ffn_pw_w": BIG1024, "L0_ffn_pw_b": [D], "L0_ffn_router_w": [8, D],
            "L0_ffn_router_b": [8], "L0_ffn_expert_w": [NE, D, D],
            "L0_ffn_expert_b": [NE, D], "L0_ffn_out_w": BIG1024, "L0_ffn_out_b": [D],
            "L0_ffn_norm_w": [D], "L0_ffn_ff_down_w": [HID, D],
            "L0_ffn_ff_down_b": [HID], "L0_ffn_ff_up_w": [D, HID], "L0_ffn_ff_up_b": [D],
            "c_cos": [128, S], "c_sin": [128, S], "c_perm": [128, 128],
            "c_ones": [128, 128], "c_idf": [128, 128], "c_idb": [128, 128],
        }
        bf16_names = {"c_cos", "c_sin", "c_perm", "c_ones", "c_idb"}
        for k, sh in specs.items():
            dt = BF16 if k in bf16_names else F32
            self.ins[k] = nc.dram_tensor(k, sh, dt, kind="ExternalInput").ap()
        self.out = nc.dram_tensor("y", [self.nbatch, S, D], F32, kind="ExternalOutput").ap()
        if self.debug:
            self.dbgt = nc.dram_tensor("dbg", [len(self.debug), 128, 512], F32,
                                       kind="ExternalOutput").ap()
        self.nat = {}
        self.img = {}
        for w in ["wq", "wk", "wv", "wo", "pw", "ow"] + [f"e{i}" for i in range(NE)]:
            self.nat[w] = nc.dram_tensor("nat_" + w, BIG1024, BF16, kind="Internal").ap()
            self.img[w] = nc.dram_tensor("img_" + w, [128, 8 * KB], BF16, kind="Internal").ap()
        self.nat["ffd"] = nc.dram_tensor("nat_ffd", [HID, D], BF16, kind="Internal").ap()
        self.img["ffd"] = nc.dram_tensor("img_ffd", [128, 8 * HID], BF16, kind="Internal").ap()
        self.nat["ffu"] = nc.dram_tensor("nat_ffu", [D, HID], BF16, kind="Internal").ap()
        self.img["ffu"] = nc.dram_tensor("img_ffu", [128, 2 * KB], BF16, kind="Internal").ap()

    def alloc(self):
        nc = self.nc
        A = nc.alloc_sbuf_tensor_at
        arena = nc.alloc_sbuf_tensor("ARENA", [128, 212800], mybir.dt.uint8)
        amloc = nc.lookup_mloc(arena)
        o = [amloc.addr]
        def at(name, shape, dtype):
            t = A(name, shape, dtype, offset=o[0])
            o[0] += int(np.prod(shape[1:])) * mybir.dt.size(dtype)
            o[0] = (o[0] + 31) & ~31
            return t
        self.R = at("R", [128, NCH * S], F32)
        self.cos = at("cosT", [128, S], BF16)
        self.sin = at("sinT", [128, S], BF16)
        self.perm = at("permT", [128, 128], BF16)
        self.onesb = at("onesb", [128, 128], BF16)
        self.idf = at("idf", [128, 128], F32)
        self.idb = at("idb", [128, 128], BF16)
        self.bias = at("biasc", [128, 240], F32)
        self.routimg = at("routimg", [128, 64], BF16)
        self.rbcol = at("rbcol", [128, 1], F32)
        self.bvrow = at("bvrow", [1, KB], BF16)
        self.S0 = at("S0", [128, 8 * KB], BF16)
        self.S1 = at("S1", [128, 8 * KB], BF16)
        self.S2 = at("S2", [128, 8 * KB], BF16)
        ua_base = o[0]
        self.UA = at("UA", [128, 16 * KB], BF16)
        ub = o[0]
        self.yb = A("yb", [128, NCH * (TS + 2)], F32, offset=ub)
        self.sres = A("sres", [128, NCH * TS], F32, offset=ub)
        self.comb = A("comb", [128, NCH * TS], F32, offset=ub + 16448)
        self.probs0 = A("probs0", [128, 2 * TS], BF16, offset=ub)
        self.probs1 = A("probs1", [128, 2 * TS], BF16, offset=ub + 2048)
        self.vhat = A("vhat", [128, 16 * 130], BF16, offset=ub + 4096)
        self.q1 = A("q1", [128, S], BF16, offset=ub + 8320)
        self.k1 = A("k1", [128, S], BF16, offset=ub + 12416)
        self.rsc = A("ropesc", [128, S], BF16, offset=ub + 16512)
        o[0] = ((ub + 33248) + 31) & ~31
        uc = o[0]
        self.row_m = A("row_m", [1, TS], F32, offset=uc)
        self.row_q = A("row_q", [1, TS], F32, offset=uc + 2048)
        self.row_t = A("row_t", [1, TS], F32, offset=uc + 4096)
        self.row_bf = A("row_bf", [1, 4 * TS], BF16, offset=uc + 6144)
        self.mb = A("mb", [128, TS], F32, offset=uc + 10240)
        self.rb_ = A("rb_", [128, TS], F32, offset=uc + 12288)
        self.silu = A("siluf", [128, TS], F32, offset=uc + 14336)
        self.rwrow = A("rwrow", [1, TS], BF16, offset=uc + 10240)
        self.rwb = A("rwb", [128, TS], BF16, offset=uc + 11264)
        self.lgf = A("lgf", [128, 32], F32, offset=uc + 16384)
        self.rwtok = A("rwtok", [128, 32], BF16, offset=uc + 16512)
        self.m1n = A("m1n", [128, 24], F32, offset=uc + 16576)
        self.cstage = A("cstage", [128, TS], BF16, offset=uc + 16672)
        self.quarter = A("quarter", [128, 2 * S], BF16, offset=uc)
        self.wotsl = A("wotsl", [128, S], BF16, offset=uc + 8192)
        self.stage = A("stage", [128, S], BF16, offset=uc + 12288)
        self.recipf = A("recipf", [128, 512], F32, offset=uc + 16384)
        self.denb = A("denb", [128, 512], BF16, offset=uc + 18432)
        self.xstage = A("xstage", [128, KB], F32, offset=uc + 12288)
        o[0] = uc + 19456
        if self.debug:
            self.dbgstage = A("dbgstage", [128, 512], F32, offset=ua_base + 28672)
            self.dbgstage2 = A("dbgstage2", [128, 512], F32, offset=ub + 20608)
        assert o[0] <= amloc.addr + 212800, o[0]
        self.PS = nc.alloc_psum_tensor("PS", [128, 4096], F32)
        self.bc = {}
        names = ["bq", "bk", "bv", "bo", "dwb", "pwb", "outb", "nw", "nwh",
                 "ffub", "ffubh"]
        for i, name in enumerate(names):
            self.bc[name] = (8 * i, 8)
        self.bc["ffdb"] = (88, 2)
        self.bc["ebc"] = (90, 64)
        self.bc["kern"] = (154, 24)
        self.bc["rbb"] = (178, 8)
        self.bc["c0"] = (186, 2)

    def bcol(self, name, j=0, w=1):
        c0, n = self.bc[name]
        assert j + w <= n, (name, j, w)
        return self.bias[:, c0 + j:c0 + j + w]

    def sems(self):
        nc = self.nc
        self._semctx = []
        def mk(name):
            g = nc.semaphore(name)
            h = g.__enter__()
            self._semctx.append(g)
            return Ctr(h, name)
        self.sd = mk("sd"); self.sg = mk("sg"); self.st = mk("st")
        self.sv = mk("sv"); self.ss = mk("ss"); self.so = mk("so")
        self.bank_free = {k: [] for k in range(8)}
        self.slot_free = {0: (self.st, 0), 1: (self.st, 0), 2: (self.st, 0)}
        self.eready = {1: 0, 2: 0}

    def dump(self, name, ap, bf16=False, stage2=False):
        if name not in self.debug:
            return
        slot = self.debug.index(name)
        stg = self.dbgstage2 if stage2 else self.dbgstage
        waits = [(c, c.n) for c in (self.st, self.sv, self.ss, self.sg)]
        nr, ncols = ap.shape[0], ap.shape[-1]
        self.vop(lambda e: e.tensor_copy(stg[0:nr, 0:ncols], ap),
                 waits=waits, tag="dbgc_" + name)
        v = self.sv.n
        self.dma("sync", self.dbgt[slot], stg[:, :],
                 waits=[(self.sv, v)], tag="dbg_" + name)
        self.vop(lambda e: e.tensor_copy(self.m1n[:, 3:4], self.m1n[:, 3:4]),
                 waits=[(self.sd, self.sd.n)], tag="dbgw_" + name)

    def prologue(self):
        ins = self.ins
        self.dma("sync", self.cos[:], ins["c_cos"], tag="ccos")
        self.dma("sync", self.sin[:], ins["c_sin"], tag="csin")
        self.dma("sync", self.perm[:], ins["c_perm"], tag="cperm")
        self.dma("sync", self.onesb[:], ins["c_ones"], tag="cones")
        self.dma("sync", self.idf[:], ins["c_idf"], tag="cidf")
        self.dma("sync", self.idb[:], ins["c_idb"], tag="cidb")
        for name, src in [("bq", "L0_attn_bq"), ("bk", "L0_attn_bk"),
                          ("bv", "L0_attn_bv"), ("bo", "L0_attn_bo"),
                          ("dwb", "L0_ffn_dw_bias"), ("pwb", "L0_ffn_pw_b"),
                          ("outb", "L0_ffn_out_b"), ("nw", "L0_ffn_norm_w"),
                          ("ffub", "L0_ffn_ff_up_b")]:
            c0, _ = self.bc[name]
            self.dma("sync", self.bias[:, c0:c0 + 8],
                     ins[src].rearrange("(c p) -> p c", p=128), tag="b_" + name, allow_slow_non_contiguous=True)
        c0, _ = self.bc["ffdb"]
        self.dma("sync", self.bias[:, c0:c0 + 2],
                 ins["L0_ffn_ff_down_b"].rearrange("(c p) -> p c", p=128), tag="b_ffdb", allow_slow_non_contiguous=True)
        c0, _ = self.bc["ebc"]
        self.dma("sync", self.bias[:, c0:c0 + 64].rearrange("p (e c) -> p e c", e=8),
                 ins["L0_ffn_expert_b"].rearrange("e (c p) -> p e c", p=128), tag="b_ebc", allow_slow_non_contiguous=True)
        c0, _ = self.bc["kern"]
        self.dma("sync", self.bias[:, c0:c0 + 24].rearrange("p (c k) -> p c k", c=8),
                 ins["L0_ffn_dw_kernel"].rearrange("(c p) o k -> p c (o k)", p=128),
                 tag="b_kern", allow_slow_non_contiguous=True)
        self.dma("sync", self.rbcol[0:8, 0:1],
                 ins["L0_ffn_router_b"].rearrange("(e o) -> e o", o=1), tag="b_rb",
                 allow_slow_non_contiguous=True)
        n_const = self.sd.n
        self.const_ready = n_const

        self.sop(lambda e: e.mul(self.bcol("nwh", 0, 8), self.bcol("nw", 0, 8), 0.5),
                 waits=[(self.sd, n_const)], tag="nwh")
        self.sop(lambda e: e.mul(self.bcol("ffubh", 0, 8), self.bcol("ffub", 0, 8), 0.5),
                 tag="ffubh")

        self.dma("sync", self.rbcol[0:8, 0:1],
                 ins["L0_ffn_router_b"].rearrange("(e o) -> e o", o=1), tag="b_rb",
                 allow_slow_non_contiguous=True)
        n_const = self.sd.n
        order = ["pw"] + [f"e{i}" for i in range(NE)] + ["ow", "ffd", "ffu",
                                                         "wq", "wk", "wv", "wo"]
        src = {"pw": self.ins["L0_ffn_pw_w"], "ow": self.ins["L0_ffn_out_w"],
               "ffd": self.ins["L0_ffn_ff_down_w"], "ffu": self.ins["L0_ffn_ff_up_w"],
               "wq": self.ins["L0_attn_wq"], "wk": self.ins["L0_attn_wk"],
               "wv": self.ins["L0_attn_wv"], "wo": self.ins["L0_attn_wo"]}
        for i in range(NE):
            src[f"e{i}"] = self.ins["L0_ffn_expert_w"][i]
        self.img_ready = {}
        s1_free = 0
        s2_free = 0
        for wname in order:
            a = src[wname]
            rows, ncol = a.shape
            nch = rows // 128
            cast_view = self.S1[:, 0:nch * ncol].rearrange("p (c n) -> p c n", c=nch)
            self.dma("gpsimd", cast_view, a.rearrange("(c p) n -> p c n", p=128),
                     waits=[(self.sd, s1_free)], tag=f"cast_{wname}")
            g_cast = self.sg.n
            if wname == "ffu":
                self.sop(lambda e: e.mul(self.S1[:, 0:2 * KB], self.S1[:, 0:2 * KB], 0.5),
                         waits=[(self.sg, g_cast)], tag="ffu_half")
                store_waits = [(self.ss, self.ss.n)]
            else:
                store_waits = [(self.sg, g_cast)]
            self.dma("sync", self.nat[wname].rearrange("(c p) n -> p c n", p=128),
                     cast_view, waits=store_waits, tag=f"nat_{wname}")
            d_nat = self.sd.n
            s1_free = d_nat
            nkc = ncol // 128
            for c in range(nkc):
                self.dma("sync", self.S2[:, c * rows:(c + 1) * rows],
                         self.nat[wname][:, c * 128:(c + 1) * 128], transpose=True,
                         waits=[(self.sd, max(d_nat, s2_free))] if c == 0 else (),
                         tag=f"tr_{wname}_{c}")
            d_tr = self.sd.n
            self.dma("sync", self.img[wname][:, 0:nkc * rows], self.S2[:, 0:nkc * rows],
                     waits=[(self.sd, d_tr)], tag=f"img_{wname}")
            s2_free = self.sd.n
            self.img_ready[wname] = self.sd.n
        self.dma("gpsimd", self.S1[0:8, 0:KB], self.ins["L0_ffn_router_w"],
                 waits=[(self.sd, s1_free)], tag="cast_rout")
        g_r = self.sg.n
        for c in range(8):
            bk = 2 + (c % 2)
            w = self.claim_bank(bk)
            self.tmm(self.PS[:, bk * 512:bk * 512 + 8],
                     self.S1[0:8, c * 128:(c + 1) * 128], self.idb[0:8, 0:8],
                     True, True, waits=w + [(self.sg, g_r)], tag=f"routT{c}")
            t_n = self.st.n
            self.sop(lambda e, c=c, bk=bk: e.copy(self.routimg[:, c * 8:(c + 1) * 8],
                                                  self.PS[:, bk * 512:bk * 512 + 8]),
                     waits=[(self.st, t_n)], tag=f"routE{c}")
            self.bank_read(bk, self.ss, self.ss.n)
        self.dma("gpsimd", self.bvrow[0:1, :],
                 self.ins["L0_attn_bv"].rearrange("(o d) -> o d", o=1), tag="bvrow")
        self.prologue_done = self.sd.n
        self.slot_free[1] = (self.st, self.st.n)
        self.prol_s1_free = (self.sd, s1_free)
        self.prol_s2_free = (self.sd, s2_free)
        self.prol_g = (self.sg, self.sg.n)

    def load_R(self, b):
        first = True
        for tt in range(16):
            waits = []
            if first and b > 0:
                waits.append((self.so, self.so.n))
            if not first:
                waits.append((self.st, self.last_xT))
            d_x = self.dma("sync", self.xstage[:],
                           self.ins["L0_attn_x"][b, tt * 128:(tt + 1) * 128, :],
                           waits=waits, tag=f"xin{b}_{tt}")
            first = False
            for c in range(NCH):
                bk = c % 4
                w = self.claim_bank(bk)
                self.ttr(self.PS[:, bk * 512:bk * 512 + 128],
                         self.xstage[:, c * 128:(c + 1) * 128], self.idf[:],
                         waits=w + [(self.sd, d_x)], tag=f"xT{tt}_{c}")
                t_n = self.st.n
                self.vop(lambda e, c=c, tt=tt, bk=bk: e.tensor_copy(
                    self.R[:, c * S + tt * 128:c * S + (tt + 1) * 128],
                    self.PS[:, bk * 512:bk * 512 + 128]),
                    waits=[(self.st, t_n)], tag=f"xE{tt}_{c}")
                self.bank_read(bk, self.sv, self.sv.n)
            self.last_xT = self.st.n

    def store_R(self, b):
        prev_store = self.so.n
        for tt in range(16):
            for c in range(NCH):
                bk = c % 4
                w = self.claim_bank(bk)
                self.ttr(self.PS[:, bk * 512:bk * 512 + 128],
                         self.R[:, c * S + tt * 128:c * S + (tt + 1) * 128], self.idf[:],
                         waits=w + ([(self.sv, self.sv.n)] if (tt == 0 and c == 0) else []),
                         tag=f"oT{tt}_{c}")
                t_n = self.st.n
                self.vop(lambda e, c=c, bk=bk: e.tensor_copy(
                    self.xstage[:, c * 128:(c + 1) * 128],
                    self.PS[:, bk * 512:bk * 512 + 128]),
                    waits=[(self.st, t_n)] + ([(self.so, prev_store)] if c == 0 else []),
                    tag=f"oE{tt}_{c}")
                self.bank_read(bk, self.sv, self.sv.n)
            v_n = self.sv.n
            self.op("sync", lambda e, tt=tt, b=b: e.dma_start(
                self.out[b, tt * 128:(tt + 1) * 128, :], self.xstage[:]),
                waits=[(self.sv, v_n)], inc=self.so, inc_k=16, tag=f"xout{tt}")
            prev_store = self.so.n

    def stats(self, get_chunk, want_mean, eps, tag, entry_waits=(), scratch=None):
        ua = self.UA
        if scratch is None:
            xbf = ua[:, 13 * KB:13 * KB + TS]
            sq = ua[:, 13 * KB + TS:13 * KB + 2 * TS]
        else:
            xbf, sq = scratch
        bsum, bsq = 4, 5
        wsum = self.claim_bank(bsum)
        wsq = self.claim_bank(bsq)
        t_prev = 0
        for c in range(NCH):
            src = get_chunk(c)
            cw = list(entry_waits) if c == 0 else [(self.st, t_prev)]
            self.vop(lambda e, src=src, xbf=xbf: e.tensor_copy(xbf, src),
                     waits=cw, tag=f"{tag}cast{c}")
            v_c = self.sv.n
            self.vop(lambda e, xbf=xbf, sq=sq: e.tensor_mul(sq, xbf, xbf),
                     tag=f"{tag}sq{c}")
            v_s = self.sv.n
            if want_mean:
                self.tmm(self.PSb(bsum)[0:1, 0:TS], self.onesb[:, 0:1], xbf,
                         c == 0, c == NCH - 1,
                         waits=(wsum if c == 0 else []) + [(self.sv, v_c)],
                         inc=True, tag=f"{tag}ms{c}")
            self.tmm(self.PSb(bsq)[0:1, 0:TS], self.onesb[:, 0:1], sq,
                     c == 0, c == NCH - 1,
                     waits=(wsq if c == 0 else []) + [(self.sv, v_s)],
                     inc=True, tag=f"{tag}qs{c}")
            t_prev = self.st.n
        t_done = self.st.n
        rm, rq, rt, rbf = self.row_m, self.row_q, self.row_t, self.row_bf
        if tag == "F00_0" and self.debug:
            self.sop(lambda e: e.copy(self.silu[0:1, :], self.PSb(bsq)[0:1, 0:TS]),
                     waits=[(self.st, t_done)], tag="dbgq")
            self.bank_read(bsq, self.ss, self.ss.n)
            self.dump("qsumP", self.silu[0:1, :])
            self.sop(lambda e: e.copy(self.silu[0:1, :], self.PSb(bsum)[0:1, 0:TS]),
                     waits=[(self.st, t_done)], tag="dbgm")
            self.bank_read(bsum, self.ss, self.ss.n)
            self.dump("msumP", self.silu[0:1, :])
        if want_mean:
            self.vop(lambda e: e.tensor_scalar_mul(rm[0:1, :], self.PSb(bsum)[0:1, 0:TS],
                                                   1.0 / D),
                     waits=[(self.st, t_done)], tag=f"{tag}m")
        self.vop(lambda e: e.tensor_scalar_mul(rq[0:1, :], self.PSb(bsq)[0:1, 0:TS],
                                               1.0 / D),
                 waits=[(self.st, t_done)], tag=f"{tag}q")
        self.bank_read(bsum, self.sv, self.sv.n)
        self.bank_read(bsq, self.sv, self.sv.n)
        if want_mean:
            self.vop(lambda e: e.tensor_mul(rt[0:1, :], rm[0:1, :], rm[0:1, :]),
                     tag=f"{tag}mm")
            self.vop(lambda e: e.tensor_sub(rq[0:1, :], rq[0:1, :], rt[0:1, :]),
                     tag=f"{tag}var")
        self.vop(lambda e: e.tensor_scalar_add(rq[0:1, :], rq[0:1, :], eps),
                 tag=f"{tag}eps")
        v_var = self.sv.n
        if tag == "F00_0":
            self.dump("varrow", rq[0:1, :])
        self.sop(lambda e: e.activation(rt[0:1, :], rq[0:1, :], ACT.Sqrt),
                 waits=[(self.sv, v_var)], tag=f"{tag}sd")
        s_sd = self.ss.n
        if tag == "F00_0" and self.debug:
            self.dump("sdrow", rt[0:1, :])
        self.vop(lambda e: e.reciprocal(rq[0:1, :], rt[0:1, :]),
                 waits=[(self.ss, s_sd)], tag=f"{tag}rc")
        self.vop(lambda e: e.tensor_copy(rbf[0:1, 2 * TS:3 * TS], rq[0:1, :]),
                 tag=f"{tag}rhi")
        self.vop(lambda e: e.tensor_sub(rbf[0:1, 3 * TS:4 * TS], rq[0:1, :],
                                        rbf[0:1, 2 * TS:3 * TS]), tag=f"{tag}rlo")
        if want_mean:
            self.vop(lambda e: e.tensor_copy(rbf[0:1, 0:TS], rm[0:1, :]),
                     tag=f"{tag}mhi")
            self.vop(lambda e: e.tensor_sub(rbf[0:1, TS:2 * TS], rm[0:1, :],
                                            rbf[0:1, 0:TS]), tag=f"{tag}mlo")
        v_rows = self.sv.n
        if tag == "F00_0":
            self.dump("rrow", rq[0:1, :])
            self.dump("rhilo", rbf[0:1, 2 * TS:3 * TS], bf16=True)
        plan = [(self.rb_, 2 * TS, 3 * TS, 6)]
        if want_mean:
            plan.append((self.mb, 0, TS, 7))
        for (dst, hi, lo, bk) in plan:
            w = self.claim_bank(bk)
            self.tmm(self.PSb(bk), self.onesb[0:1, :], rbf[0:1, hi:hi + TS],
                     True, False, waits=w + [(self.sv, v_rows)], inc=True,
                     tag=f"{tag}bc1")
            self.tmm(self.PSb(bk), self.onesb[0:1, :], rbf[0:1, lo:lo + TS],
                     False, True, tag=f"{tag}bc2")
            t_b = self.st.n
            self.vop(lambda e, dst=dst, bk=bk: e.tensor_copy(dst[:, :], self.PSb(bk)),
                     waits=[(self.st, t_b)], tag=f"{tag}bce")
            self.bank_read(bk, self.sv, self.sv.n)
        return self.sv.n

    def ffn(self, b, call):
        ua = self.UA
        conv_r = lambda c: ua[:, c * TS:(c + 1) * TS]
        x1_r = lambda c: ua[:, 4 * KB + c * TS:4 * KB + (c + 1) * TS]
        h_r = lambda c: ua[:, 8 * KB + c * TS:8 * KB + (c + 1) * TS]
        g_r = lambda m: ua[:, 12 * KB + m * TS:12 * KB + (m + 1) * TS]
        lgbf = ua[:, 15 * KB:15 * KB + 512]

        sem, val = self.slot_free[0]
        self.dma("sync", self.S0[:], self.img["pw"],
                 waits=[(self.sd, self.img_ready["pw"]), (sem, val)], tag=f"pwld{call}")
        pw_rdy = self.sd.n

        for ts in range(NTS):
            t0 = ts * TS
            Rc = lambda c: self.R[:, c * S + t0:c * S + t0 + TS]
            entry = [(self.st, self.st.n), (self.ss, self.ss.n)] if ts == 0 else ()
            if b == 0 and call == 0 and ts == 0:
                self.dump("R0", Rc(0))
            self.stats(Rc, True, EPS_LN, f"F{call}{b}_{ts}", entry_waits=entry)
            if b == 0 and call == 0 and ts == 0:
                self.dump("mb", self.mb[:, :])
                self.dump("rb", self.rb_[:, :])
            if ts == 0:
                self.vop(lambda e: e.memset(
                    self.yb[:, :].rearrange("p (c n) -> p c n", c=NCH)[:, :, 0:2], 0.0),
                    tag="halo0")
            for c in range(NCH):
                yv = self.yb[:, c * (TS + 2) + 2:c * (TS + 2) + 2 + TS]
                Rsrc = Rc(c)
                self.vop(lambda e, Rsrc=Rsrc, yv=yv: e.tensor_sub(yv, Rsrc, self.mb[:, :]),
                         tag=f"ysub{c}")
                self.vop(lambda e, yv=yv: e.tensor_mul(yv, yv, self.rb_[:, :]),
                         tag=f"ymul{c}")
            for c in range(NCH):
                yb0 = self.yb[:, c * (TS + 2):c * (TS + 2) + TS]
                yb1 = self.yb[:, c * (TS + 2) + 1:c * (TS + 2) + 1 + TS]
                yb2 = self.yb[:, c * (TS + 2) + 2:c * (TS + 2) + 2 + TS]
                self.vop(lambda e, yb0=yb0, c=c: e.tensor_scalar(
                    self.silu[:, :], yb0, self.bcol("kern", 3 * c), self.bcol("dwb", c),
                    ALU.mult, ALU.add), tag=f"cv0_{c}")
                self.vop(lambda e, yb1=yb1, c=c: e.scalar_tensor_tensor(
                    self.silu[:, :], yb1, self.bcol("kern", 3 * c + 1), self.silu[:, :],
                    ALU.mult, ALU.add), tag=f"cv1_{c}")
                self.vop(lambda e, yb2=yb2, c=c: e.scalar_tensor_tensor(
                    conv_r(c), yb2, self.bcol("kern", 3 * c + 2), self.silu[:, :],
                    ALU.mult, ALU.add), tag=f"cv2_{c}")
            v_conv = self.sv.n
            if b == 0 and call == 0 and ts == 0:
                self.dump("y0", self.yb[:, 2:2 + TS])
                self.dump("cv0", conv_r(0), bf16=True)
            if ts < NTS - 1:
                for c in range(NCH):
                    self.vop(lambda e, c=c: e.tensor_copy(
                        self.yb[:, c * (TS + 2):c * (TS + 2) + 2],
                        self.yb[:, c * (TS + 2) + TS:c * (TS + 2) + TS + 2]),
                        tag=f"halo{c}")
            v_silu_free = self.sv.n
            for half in range(2):
                t_pw = {}
                for j in range(4):
                    oc = half * 4 + j
                    w = self.claim_bank(j)
                    for kc in range(NCH):
                        self.tmm(self.PSb(j),
                                 self.S0[:, kc * KB + oc * 128:kc * KB + (oc + 1) * 128],
                                 conv_r(kc), kc == 0, kc == NCH - 1,
                                 waits=(w + [(self.sv, v_conv), (self.sd, pw_rdy)])
                                 if kc == 0 else [], tag=f"pw{oc}")
                    t_pw[j] = self.st.n
                s_si = {}
                for j in range(4):
                    oc = half * 4 + j
                    dst = self.UA[:, 13 * KB + j * TS:13 * KB + (j + 1) * TS]
                    self.sop(lambda e, j=j, oc=oc, dst=dst: e.activation(
                        dst, self.PSb(j), ACT.Silu, bias=self.bcol("pwb", oc)),
                        waits=[(self.st, t_pw[j]), (self.sv, v_silu_free)],
                        tag=f"psl{oc}")
                    s_si[j] = self.ss.n
                    self.bank_read(j, self.ss, s_si[j])
                for j in range(4):
                    oc = half * 4 + j
                    yv = self.yb[:, oc * (TS + 2) + 2:oc * (TS + 2) + 2 + TS]
                    srcb = self.UA[:, 13 * KB + j * TS:13 * KB + (j + 1) * TS]
                    self.vop(lambda e, oc=oc, yv=yv, srcb=srcb: e.tensor_add(
                        x1_r(oc), yv, srcb),
                        waits=[(self.ss, s_si[j])], tag=f"x1_{oc}")
                v_silu_free = self.sv.n
            v_x1 = self.sv.n
            if b == 0 and call == 0 and ts == 0:
                self.dump("x10", x1_r(0), bf16=True)
            self._dbg_rt = (b == 0 and call == 0 and ts == 0)
            self._router(v_x1, x1_r, lgbf)
            self._experts(call, ts, x1_r, conv_r, v_x1)
            self._moe_tail(b, call, ts, x1_r, h_r, g_r)

    def _router(self, v_x1, x1_r, lgbf):
        blg, blt = 4, 5
        if getattr(self, "_dbg_rt", False):
            self.dump("rimg", self.routimg[:, :], bf16=True)
            self.dump("rbbD", self.bias[:, self.bc["rbb"][0]:self.bc["rbb"][0] + 8])
        for tt in range(4):
            w = self.claim_bank(blg)
            for kc in range(NCH):
                self.tmm(self.PS[0:8, blg * 512:blg * 512 + 128],
                         self.routimg[:, kc * 8:(kc + 1) * 8],
                         x1_r(kc)[:, tt * 128:(tt + 1) * 128],
                         kc == 0, kc == NCH - 1,
                         waits=(w + [(self.sv, v_x1)]) if kc == 0 else [],
                         tag=f"rt{tt}")
            t_lg = self.st.n
            self.sop(lambda e, tt=tt: e.add(
                lgbf[0:8, tt * 128:(tt + 1) * 128],
                self.PS[0:8, blg * 512:blg * 512 + 128],
                self.rbcol[0:8, 0:1]),
                waits=[(self.st, t_lg)], tag=f"lgev{tt}")
            self.bank_read(blg, self.ss, self.ss.n)
        s_lg = self.ss.n
        if getattr(self, "_dbg_rt", False):
            self.dump("lgF", lgbf[0:8, :], bf16=True)
        for tt in range(4):
            w = self.claim_bank(blt)
            self.tmm(self.PS[:, blt * 512:blt * 512 + 8],
                     lgbf[0:8, tt * 128:(tt + 1) * 128], self.idb[0:8, 0:8],
                     True, True, waits=(w + [(self.ss, s_lg)]) if tt == 0 else w,
                     tag=f"lgT{tt}")
            t_lgT = self.st.n
            lg = self.lgf[:, tt * 8:(tt + 1) * 8]
            m1n = self.m1n[:, 0:1]
            m2n = self.m1n[:, 1:2]
            rec = self.m1n[:, 2:3]
            rec2 = self.m1n[:, 12:13]
            msk = self.m1n[:, 4:12]
            msk2 = self.m1n[:, 16:24]
            self.vop(lambda e, lg=lg: e.tensor_copy(
                lg, self.PS[:, blt * 512:blt * 512 + 8]),
                waits=[(self.st, t_lgT)], tag=f"lgadd{tt}")
            self.bank_read(blt, self.sv, self.sv.n)
            W = lambda: [(self.sv, self.sv.n)]
            self.vop(lambda e, lg=lg, m1n=m1n: e.reduce_max(m1n, lg, AX, negate=True),
                     waits=W(), tag=f"t1_{tt}")
            self.vop(lambda e, lg=lg, m1n=m1n, msk=msk: e.tensor_scalar(
                msk, lg, m1n, 0.0, ALU.add, ALU.is_ge), waits=W(), tag=f"i1_{tt}")
            self.vop(lambda e, lg=lg, msk=msk, msk2=msk2: e.scalar_tensor_tensor(
                msk2, msk, -1e30, lg, ALU.mult, ALU.add), waits=W(), tag=f"zz_{tt}")
            self.vop(lambda e, m2n=m2n, msk2=msk2: e.reduce_max(m2n, msk2, AX,
                                                               negate=True),
                     waits=W(), tag=f"t2_{tt}")
            self.vop(lambda e, lg=lg, m2n=m2n, msk=msk: e.tensor_scalar(
                msk, lg, m2n, 0.0, ALU.add, ALU.is_ge), waits=W(), tag=f"i2_{tt}")
            v_i = self.sv.n
            self.sop(lambda e, lg=lg, m1n=m1n: e.activation(lg, lg, ACT.Exp, bias=m1n),
                     waits=[(self.sv, v_i)], tag=f"gt_{tt}")
            s_g = self.ss.n
            self.vop(lambda e, lg=lg, msk=msk: e.tensor_mul(lg, lg, msk),
                     waits=[(self.ss, s_g)], tag=f"gm_{tt}")
            self.vop(lambda e, lg=lg, rec=rec: e.reduce_sum(rec, lg, AX),
                     waits=W(), tag=f"dn_{tt}")
            self.vop(lambda e, rec=rec, rec2=rec2: e.reciprocal(rec2, rec),
                     waits=W(), tag=f"rc_{tt}")
            self.vop(lambda e, lg=lg, rec2=rec2, tt=tt: e.tensor_scalar(
                self.rwtok[:, tt * 8:(tt + 1) * 8], lg, rec2, None, ALU.mult,
                ALU.bypass), waits=W(), tag=f"rw_{tt}")
        self.vop(lambda e: e.tensor_copy(self.lgf[:, 0:8], self.lgf[:, 0:8]),
                 waits=[(self.sv, self.sv.n)], tag="rwflush")
        self.v_rw = self.sv.n
        if getattr(self, "_dbg_rt", False):
            self.dump("lg", self.lgf[:, :])
            self.dump("rwt", self.rwtok[:, :], bf16=True)

    def _experts(self, call, ts, x1_r, conv_r, v_x1):
        v_rw = self.v_rw
        def eload(e_i):
            slot = self.S1 if e_i % 2 == 0 else self.S2
            k = 1 if e_i % 2 == 0 else 2
            sem, val = self.slot_free[k]
            wl = [(self.sd, self.img_ready[f"e{e_i}"]), (sem, val)]
            if not getattr(self, "_prol_clear", False):
                wl += [self.prol_s1_free, self.prol_s2_free, self.prol_g]
            self.dma("sync", slot[:], self.img[f"e{e_i}"], waits=wl, tag=f"el{e_i}")
            self.eready[k] = self.sd.n
            if e_i >= 6:
                self._prol_clear = True
        eload(0)
        eload(1)
        self._esilu_free = self.sv.n
        brb = 5
        rw_trw, rw_s, rw_tb, rw_v = {}, {}, {}, {}
        def emit_rwT(e_i):
            t_rw = {}
            for tt in range(4):
                bkt = 4 + (tt + 2) % 4
                w = self.claim_bank(bkt)
                self.tmm(self.PS[0:1, bkt * 512:bkt * 512 + 128],
                         self.rwtok[:, tt * 8 + e_i:tt * 8 + e_i + 1], self.idb[:],
                         True, True, waits=(w + [(self.sv, v_rw)]) if tt == 0 else w,
                         tag=f"rwT{e_i}")
                t_rw[tt] = self.st.n
            rw_trw[e_i] = t_rw
        def emit_rwr(e_i):
            t_rw = rw_trw[e_i]
            for tt in range(4):
                bkt = 4 + (tt + 2) % 4
                self.sop(lambda e, tt=tt, bkt=bkt: e.copy(
                    self.rwrow[0:1, tt * 128:(tt + 1) * 128],
                    self.PS[0:1, bkt * 512:bkt * 512 + 128]),
                    waits=[(self.st, t_rw[tt])], tag=f"rwr{e_i}")
                self.bank_read(bkt, self.ss, self.ss.n)
            rw_s[e_i] = self.ss.n
        def emit_rwbMM(e_i):
            w = self.claim_bank(brb)
            self.tmm(self.PSb(brb), self.onesb[0:1, :], self.rwrow[0:1, :], True, True,
                     waits=w + [(self.ss, rw_s[e_i])], tag=f"rwb{e_i}")
            rw_tb[e_i] = self.st.n
        def emit_rwbe(e_i):
            self.vop(lambda e: e.tensor_copy(self.rwb[:, :], self.PSb(brb)),
                     waits=[(self.st, rw_tb[e_i])], tag=f"rwbe{e_i}")
            rw_v[e_i] = self.sv.n
            self.bank_read(brb, self.sv, rw_v[e_i])
        emit_rwT(0)
        emit_rwr(0)
        emit_rwbMM(0)
        emit_rwbe(0)
        for e_i in range(NE):
            slot = self.S1 if e_i % 2 == 0 else self.S2
            k = 1 if e_i % 2 == 0 else 2
            v_rwb = rw_v[e_i]
            if getattr(self, "_dbg_rt", False) and e_i == 0:
                self.dump("rwb0", self.rwb[:, :], bf16=True)
            eri = self.eready[k]
            for half in range(2):
                t_e = {}
                for j in range(4):
                    oc = half * 4 + j
                    w = self.claim_bank(j)
                    for kc in range(NCH):
                        self.tmm(self.PSb(j),
                                 slot[:, kc * KB + oc * 128:kc * KB + (oc + 1) * 128],
                                 x1_r(kc), kc == 0, kc == NCH - 1,
                                 waits=(w + [(self.sd, eri), (self.sv, v_x1)])
                                 if kc == 0 else [], tag=f"ex{e_i}_{oc}")
                    t_e[j] = self.st.n
                if half == 0 and e_i + 1 < NE:
                    emit_rwT(e_i + 1)
                if half == 1:
                    self.slot_free[k] = (self.st, self.st.n)
                    if e_i + 2 < NE:
                        eload(e_i + 2)
                    if e_i + 1 < NE:
                        emit_rwbMM(e_i + 1)
                s_si = {}
                for j in range(4):
                    oc = half * 4 + j
                    dst = self.UA[:, 13 * KB + j * TS:13 * KB + (j + 1) * TS]
                    self.sop(lambda e, j=j, e_i=e_i, oc=oc, dst=dst: e.activation(
                        dst, self.PSb(j), ACT.Silu,
                        bias=self.bcol("ebc", e_i * 8 + oc)),
                        waits=[(self.st, t_e[j]), (self.sv, self._esilu_free)],
                        tag=f"es{e_i}_{oc}")
                    s_si[j] = self.ss.n
                    self.bank_read(j, self.ss, s_si[j])
                if half == 0 and e_i + 1 < NE:
                    emit_rwr(e_i + 1)
                for j in range(4):
                    oc = half * 4 + j
                    sc = conv_r(oc)
                    srcb = self.UA[:, 13 * KB + j * TS:13 * KB + (j + 1) * TS]
                    self.vop(lambda e, sc=sc, srcb=srcb: e.tensor_mul(
                        sc, srcb, self.rwb[:, :]),
                        waits=[(self.ss, s_si[j]), (self.sv, v_rwb)],
                        tag=f"em{e_i}_{oc}")
                    cb = self.comb[:, oc * TS:(oc + 1) * TS]
                    if e_i == 0:
                        self.vop(lambda e, sc=sc, cb=cb: e.tensor_copy(cb, sc),
                                 tag=f"ec{e_i}_{oc}")
                    else:
                        self.vop(lambda e, sc=sc, cb=cb: e.tensor_add(cb, cb, sc),
                                 tag=f"ea{e_i}_{oc}")
                if half == 1 and e_i + 1 < NE:
                    emit_rwbe(e_i + 1)
                self._esilu_free = self.sv.n

    def _moe_tail(self, b, call, ts, x1_r, h_r, g_r):
        v_comb = self.sv.n
        if getattr(self, "_dbg_rt", False):
            self.dump("cb0", self.comb[:, 0:TS])
        sem, val = self.slot_free[1]
        self.dma("sync", self.S1[:], self.img["ow"],
                 waits=[(self.sd, self.img_ready["ow"]), (sem, val)], tag="owld")
        ow_rdy = self.sd.n
        for wave in range(2):
            wbanks = []
            for bk in range(4):
                wbanks += self.claim_bank(bk)
            t_prev_kc = 0
            for kc in range(NCH):
                cs = self.cstage[:, 0:TS]
                cbsrc = self.comb[:, kc * TS:(kc + 1) * TS]
                cw = [(self.sv, v_comb)] if kc == 0 else [(self.st, t_prev_kc)]
                self.vop(lambda e, cs=cs, cbsrc=cbsrc: e.tensor_copy(cs, cbsrc),
                         waits=cw, tag=f"mc{wave}_{kc}")
                v_cs = self.sv.n
                for j in range(4):
                    oc = wave * 4 + j
                    self.tmm(self.PSb(j),
                             self.S1[:, kc * KB + oc * 128:kc * KB + (oc + 1) * 128],
                             cs, kc == 0, kc == NCH - 1,
                             waits=(wbanks + [(self.sd, ow_rdy), (self.sv, v_cs)])
                             if (kc == 0 and j == 0) else
                             ([(self.sv, v_cs)] if j == 0 else []),
                             inc=(j == 3), tag=f"mm{wave}_{kc}_{j}")
                t_prev_kc = self.st.n
            t_moe = self.st.n
            self.slot_free[1] = (self.st, t_moe)
            for j in range(4):
                oc = wave * 4 + j
                ps = self.PSb(j)
                sr = self.sres[:, oc * TS:(oc + 1) * TS]
                sb = h_r(oc)
                x1s = x1_r(oc)
                self.vop(lambda e, ps=ps, sr=sr, oc=oc, x1s=x1s: e.scalar_tensor_tensor(
                    sr, ps, self.bcol("outb", oc), x1s, ALU.add, ALU.add),
                    waits=[(self.st, t_moe)], tag=f"sr{oc}")
                self.vop(lambda e, sr=sr, sb=sb: e.tensor_copy(sb, sr), tag=f"sb{oc}")
                self.bank_read(j, self.sv, self.sv.n)
        if getattr(self, "_dbg_rt", False):
            self.dump("sr0", self.sres[:, 0:TS])
        ua = self.UA
        sq = ua[:, 13 * KB + TS:13 * KB + 2 * TS]
        bsq = 5
        w = self.claim_bank(bsq)
        t_prev = 0
        for c in range(NCH):
            self.vop(lambda e, c=c, sq=sq: e.tensor_mul(sq, h_r(c), h_r(c)),
                     waits=[(self.st, t_prev)] if c > 0 else [], tag=f"rq{c}")
            v_s = self.sv.n
            self.tmm(self.PSb(bsq)[0:1, 0:TS], self.onesb[:, 0:1], sq,
                     c == 0, c == NCH - 1,
                     waits=(w if c == 0 else []) + [(self.sv, v_s)], inc=True,
                     tag=f"rqs{c}")
            t_prev = self.st.n
        t_done = self.st.n
        rq, rt, rbf = self.row_q, self.row_t, self.row_bf
        self.vop(lambda e: e.tensor_scalar_mul(rq[0:1, :], self.PSb(bsq)[0:1, 0:TS],
                                               1.0 / D),
                 waits=[(self.st, t_done)], tag="rmq")
        self.bank_read(bsq, self.sv, self.sv.n)
        self.vop(lambda e: e.tensor_scalar_add(rq[0:1, :], rq[0:1, :], EPS_RMS),
                 tag="rme")
        v_var = self.sv.n
        self.sop(lambda e: e.activation(rt[0:1, :], rq[0:1, :], ACT.Sqrt),
                 waits=[(self.sv, v_var)], tag="rmsd")
        s_sd = self.ss.n
        self.vop(lambda e: e.reciprocal(rq[0:1, :], rt[0:1, :]),
                 waits=[(self.ss, s_sd)], tag="rmrc")
        if getattr(self, "_dbg_rt", False):
            self.dump("rmsrow", rq[0:1, :])
        self.vop(lambda e: e.tensor_copy(rbf[0:1, 2 * TS:3 * TS], rq[0:1, :]),
                 waits=[(self.st, getattr(self, "rbf_free", 0))], tag="rmhi")
        self.vop(lambda e: e.tensor_sub(rbf[0:1, 3 * TS:4 * TS], rq[0:1, :],
                                        rbf[0:1, 2 * TS:3 * TS]), tag="rmlo")
        v_rows = self.sv.n
        bk = 6
        w = self.claim_bank(bk)
        self.tmm(self.PSb(bk), self.onesb[0:1, :], rbf[0:1, 2 * TS:3 * TS], True, False,
                 waits=w + [(self.sv, v_rows)], inc=True, tag="rmb1")
        self.tmm(self.PSb(bk), self.onesb[0:1, :], rbf[0:1, 3 * TS:4 * TS], False, True,
                 tag="rmb2")
        t_b = self.st.n
        self.rbf_free = t_b
        psb = self.PSb(bk)
        self.vop(lambda e, psb=psb: e.tensor_copy(self.rb_[:, :], psb),
                 waits=[(self.st, t_b)], tag="rmbe")
        self.bank_read(bk, self.sv, self.sv.n)
        if getattr(self, "_dbg_rt", False):
            self.sop(lambda e, psb=psb: e.copy(self.mb[:, :], psb),
                     waits=[(self.st, t_b)], tag="dbgb6")
            self.bank_read(bk, self.ss, self.ss.n)
            self.dump("bk6D", self.mb[:, :])
            self.dump("rmhiD", rbf[0:1, 2 * TS:3 * TS], bf16=True)
            self.dump("rb2b", self.rb_[:, :])
        for oc in range(NCH):
            sr = self.sres[:, oc * TS:(oc + 1) * TS]
            self.vop(lambda e, sr=sr: e.tensor_mul(self.silu[:, :], sr, self.rb_[:, :]),
                     tag=f"h{oc}")
            self.vop(lambda e, oc=oc: e.tensor_scalar(
                h_r(oc), self.silu[:, :], self.bcol("nw", oc), None, ALU.mult,
                ALU.bypass), tag=f"hb{oc}")
            Rv = self.R[:, oc * S + ts * TS:oc * S + (ts + 1) * TS]
            self.vop(lambda e, Rv=Rv, oc=oc: e.scalar_tensor_tensor(
                Rv, self.silu[:, :], self.bcol("nwh", oc), Rv, ALU.mult, ALU.add),
                tag=f"Rh{oc}")
        v_h = self.sv.n
        if getattr(self, "_dbg_rt", False):
            self.dump("rb2", self.rb_[:, :])
            self.dump("h0", h_r(0), bf16=True)
        sem, val = self.slot_free[1]
        self.dma("sync", self.S1[:, 0:8 * HID], self.img["ffd"],
                 waits=[(self.sd, self.img_ready["ffd"]), (sem, val)], tag="ffdld")
        self.dma("sync", self.S1[:, 8 * HID:8 * HID + 2 * KB], self.img["ffu"],
                 waits=[(self.sd, self.img_ready["ffu"])], tag="ffuld")
        ff_rdy = self.sd.n
        s_gs = []
        for mc in range(2):
            bk = 4 + mc
            w = self.claim_bank(bk)
            for kc in range(NCH):
                self.tmm(self.PSb(bk),
                         self.S1[:, kc * HID + mc * 128:kc * HID + (mc + 1) * 128],
                         h_r(kc), kc == 0, kc == NCH - 1,
                         waits=(w + [(self.sd, ff_rdy), (self.sv, v_h)])
                         if kc == 0 else [], tag=f"fd{mc}")
            t_f = self.st.n
            self.sop(lambda e, bk=bk, mc=mc: e.activation(
                g_r(mc), self.PSb(bk), ACT.Gelu, bias=self.bcol("ffdb", mc)),
                waits=[(self.st, t_f)], tag=f"ge{mc}")
            self.bank_read(bk, self.ss, self.ss.n)
        s_g = self.ss.n
        if getattr(self, "_dbg_rt", False):
            self.dump("g0", g_r(0), bf16=True)
        for oc in range(NCH):
            bk = oc % 4
            w = self.claim_bank(bk)
            for hc in range(2):
                self.tmm(self.PSb(bk),
                         self.S1[:, 8 * HID + hc * KB + oc * 128:
                                 8 * HID + hc * KB + (oc + 1) * 128],
                         g_r(hc), hc == 0, hc == 1,
                         waits=(w + [(self.ss, s_g)]) if hc == 0 else [], tag=f"fu{oc}")
            t_u = self.st.n
            Rv = self.R[:, oc * S + ts * TS:oc * S + (ts + 1) * TS]
            self.vop(lambda e, bk=bk, Rv=Rv, oc=oc: e.scalar_tensor_tensor(
                Rv, self.PSb(bk), self.bcol("ffubh", oc), Rv, ALU.add, ALU.add),
                waits=[(self.st, t_u)], tag=f"Rf{oc}")
            self.bank_read(bk, self.sv, self.sv.n)
        self.slot_free[1] = (self.st, self.st.n)
        if getattr(self, "_dbg_rt", False):
            self.dump("Rf0", self.R[:, 0:TS])

    def attn(self, b):
        ua = self.UA
        yln = lambda c: ua[:, c * S:(c + 1) * S]
        q1, k1, rsc = self.q1, self.k1, self.rsc
        rf, db = self.recipf, self.denb
        self.vop(lambda e: e.memset(self.vhat[:, :], 1.0),
                 waits=[(self.st, self.st.n), (self.ss, self.ss.n)], tag="vhat1")
        for ts in range(NTS):
            t0 = ts * TS
            Rc = lambda c, t0=t0: self.R[:, c * S + t0:c * S + t0 + TS]
            entry = [(self.st, self.st.n), (self.ss, self.ss.n)] if ts == 0 else ()
            self.stats(Rc, True, EPS_LN, f"A{b}_{ts}", entry_waits=entry,
                       scratch=(self.probs0[:, 0:TS], self.probs0[:, TS:2 * TS]))
            for c in range(NCH):
                Rsrc = Rc(c)
                dst = yln(c)[:, t0:t0 + TS]
                self.vop(lambda e, Rsrc=Rsrc: e.tensor_sub(self.silu[:, :], Rsrc,
                                                           self.mb[:, :]),
                         tag=f"alns{c}")
                self.vop(lambda e, dst=dst: e.tensor_mul(dst, self.silu[:, :],
                                                         self.rb_[:, :]),
                         tag=f"alnm{c}")
        v_yln = self.sv.n
        slots = {}
        for i, w in enumerate(["wq", "wk", "wv"]):
            sl = [self.S0, self.S1, self.S2][i]
            sem, val = self.slot_free[i]
            self.dma("sync", sl[:], self.img[w],
                     waits=[(self.sd, self.img_ready[w]), (sem, val)],
                     tag=f"a_{w}ld")
            slots[w] = sl
        w_rdy = self.sd.n
        pair_idx = 0
        for hp in range(NCH):
            for (wname, buf, bs, bias) in [("wq", q1, 0, "bq"), ("wk", k1, 4, "bk")]:
                for nb in range(4):
                    wb = self.claim_bank(bs + nb)
                    for kc in range(NCH):
                        self.tmm(self.PSb(bs + nb),
                                 slots[wname][:, kc * KB + hp * 128:kc * KB + (hp + 1) * 128],
                                 yln(kc)[:, nb * TS:(nb + 1) * TS],
                                 kc == 0, kc == NCH - 1,
                                 waits=(wb + [(self.sd, w_rdy), (self.sv, v_yln)])
                                 if kc == 0 else [], tag=f"{wname}p{hp}_{nb}")
                t_p = self.st.n
                self.sop(lambda e, buf=buf, bs=bs, hp=hp, bias=bias: e.add(
                    buf[:, :], self.PS[:, bs * 512:bs * 512 + 2048],
                    self.bcol(bias, hp)),
                    waits=[(self.st, t_p)], tag=f"{wname}e{hp}")
                s_e = self.ss.n
                for nb in range(4):
                    self.bank_read(bs + nb, self.ss, s_e)
                for nb in range(4):
                    wb = self.claim_bank(bs + nb)
                    self.tmm(self.PSb(bs + nb), self.perm[:, :],
                             buf[:, nb * TS:(nb + 1) * TS], True, True,
                             waits=(wb + [(self.ss, s_e)]) if nb == 0 else wb,
                             tag=f"pm{hp}_{bs}_{nb}")
                t_pm = self.st.n
                ps = self.PS[:, bs * 512:bs * 512 + 2048]
                self.vop(lambda e, ps=ps: e.tensor_tensor(rsc[:, :], ps,
                                                          self.sin[:, :], ALU.mult),
                         waits=[(self.st, t_pm)], tag=f"rs{hp}_{bs}")
                for nb in range(4):
                    self.bank_read(bs + nb, self.sv, self.sv.n)
                self.vop(lambda e, buf=buf: e.tensor_mul(buf[:, :], buf[:, :],
                                                         self.cos[:, :]),
                         waits=[(self.st, t_pm)], tag=f"rc{hp}_{bs}")
                self.vop(lambda e, buf=buf: e.tensor_add(buf[:, :], buf[:, :],
                                                         rsc[:, :]),
                         tag=f"ra{hp}_{bs}")
            v_rope = self.sv.n
            if b == 0 and hp == 0 and self.debug:
                self.dump("ylnD", yln(0)[:, 0:512], stage2=True)
                self.dump("q1D", q1[:, 0:512], stage2=True)
                self.dump("k1D", k1[:, 0:512], stage2=True)
            for ktq in range(4):
                t_v = {}
                for dk in range(4):
                    kt = 4 * ktq + dk
                    wb = self.claim_bank(dk)
                    for kc in range(NCH):
                        self.tmm(self.PS[:, dk * 512:dk * 512 + 128],
                                 yln(kc)[:, kt * 128:(kt + 1) * 128],
                                 slots["wv"][:, kc * KB + hp * 128:kc * KB + (hp + 1) * 128],
                                 kc == 0, False,
                                 waits=wb if kc == 0 else [], tag=f"vp{hp}_{kt}")
                    self.tmm(self.PS[:, dk * 512:dk * 512 + 128], self.onesb[0:1, :],
                             self.bvrow[0:1, hp * 128:(hp + 1) * 128], False, True,
                             tag=f"vb{hp}_{kt}")
                    t_v[dk] = self.st.n
                for dk in range(4):
                    kt = 4 * ktq + dk
                    for h in range(2):
                        self.sop(lambda e, dk=dk, kt=kt, h=h: e.copy(
                            self.vhat[:, kt * 130 + 65 * h:kt * 130 + 65 * h + 64],
                            self.PS[:, dk * 512 + 64 * h:dk * 512 + 64 * h + 64]),
                            waits=[(self.st, t_v[dk])] if h == 0 else [],
                            tag=f"ve{hp}_{kt}_{h}")
                    self.bank_read(dk, self.ss, self.ss.n)
            s_vhat = self.ss.n
            W = lambda: [(self.sv, self.sv.n)]
            bk6 = 6
            for i, (buf, h) in enumerate([(q1, 0), (q1, 1), (k1, 0), (k1, 1)]):
                self.vop(lambda e, buf=buf, h=h: e.tensor_mul(
                    rsc[64 * h:64 * h + 64, :], buf[64 * h:64 * h + 64, :],
                    buf[64 * h:64 * h + 64, :]),
                    waits=W(), tag=f"nsq{hp}_{i}")
                v_sq = self.sv.n
                wbs = []
                for j in range(4):
                    wbs += self.claim_bank(4 + j)
                for j in range(4):
                    self.tmm(self.PS[0:1, (4 + j) * 512:(4 + j) * 512 + 512],
                             self.onesb[64 * h:64 * h + 64, 0:1],
                             rsc[64 * h:64 * h + 64, j * 512:(j + 1) * 512],
                             True, True,
                             waits=(wbs + [(self.sv, v_sq)]) if j == 0 else [],
                             inc=(j == 3), tag=f"nrm{hp}_{i}_{j}")
                t_n = self.st.n
                self.vop(lambda e, i=i: e.reduce_max(rf[0:1, 4 + i:5 + i],
                                                     self.PS[0:1, 2048:4096], AX),
                         waits=[(self.st, t_n)], tag=f"nmx{hp}_{i}")
                for j in range(4):
                    self.bank_read(4 + j, self.sv, self.sv.n)
            s_cells = self.sv.n
            self.vop(lambda e: e.tensor_copy(rf[0:1, 4:8], rf[0:1, 4:8]),
                     waits=[(self.sv, s_cells)], tag=f"nflush{hp}")
            if b == 0 and hp == 0 and self.debug:
                self.dump("celD", rf[0:1, 0:16], stage2=True)
            self.vop(lambda e: e.tensor_mul(rf[0:1, 8:10], rf[0:1, 4:6],
                                            rf[0:1, 6:8]),
                     waits=[(self.sv, self.sv.n)], tag=f"cm{hp}")
            v_cm = self.sv.n
            self.sop(lambda e: e.activation(rf[0:1, 10:12], rf[0:1, 8:10], ACT.Sqrt),
                     waits=[(self.sv, v_cm)], tag=f"cs{hp}")
            s_cs = self.ss.n
            self.vop(lambda e: e.tensor_scalar(db[0:1, 4:6], rf[0:1, 10:12],
                                               -0.125, None, ALU.mult, ALU.bypass),
                     waits=[(self.ss, s_cs)], tag=f"cnn{hp}")
            v_cn = self.sv.n
            for h in range(2):
                wb = self.claim_bank(bk6)
                self.tmm(self.PS[:, bk6 * 512:bk6 * 512 + 1], self.onesb[0:1, :],
                         db[0:1, 4 + h:5 + h], True, True,
                         waits=wb + [(self.sv, v_cn)], tag=f"cb{hp}_{h}")
                t_c = self.st.n
                self.vop(lambda e, h=h: e.tensor_copy(
                    self.bcol("c0", h), self.PS[:, bk6 * 512:bk6 * 512 + 1]),
                    waits=[(self.st, t_c), (self.ss, getattr(self, "c0_free", 0))],
                    tag=f"ce{hp}_{h}")
                self.bank_read(bk6, self.sv, self.sv.n)
            v_c0 = self.sv.n
            if b == 0 and hp == 0 and self.debug:
                self.dump("vhD", self.vhat[:, 0:260], stage2=True)
                self.dump("c0D", self.bcol("c0", 0, 2), stage2=True)
            probs_free = {h: (self.st, 0) for h in range(2)}
            for qb in range(4):
                avw = {0: self.claim_bank(4), 1: self.claim_bank(5)}
                for m in range(8):
                    for dk in range(2):
                        kt = 2 * m + dk
                        for h in range(2):
                            bk = 2 * h + dk
                            wb = self.claim_bank(bk)
                            self.tmm(self.PS[0:128, bk * 512:(bk + 1) * 512],
                                     k1[64 * h:64 * h + 64, kt * 128:(kt + 1) * 128],
                                     q1[64 * h:64 * h + 64, qb * 512:(qb + 1) * 512],
                                     True, True,
                                     waits=wb + ([(self.sv, v_c0), (self.sv, v_rope)]
                                                 if (qb == 0 and m == 0) else []),
                                     tp=(64 * h, 0), tag=f"sc{hp}_{qb}_{kt}_{h}",
                                     inc=True)
                    t_s = self.st.n
                    exp_done = {}
                    for h in range(2):
                        prb = self.probs0 if h == 0 else self.probs1
                        sem, val = probs_free[h]
                        self.sop(lambda e, h=h, prb=prb: e.activation(
                            prb[:, 0:1024], self.PS[0:128, h * 1024:(h + 1) * 1024],
                            ACT.Exp, bias=self.bcol("c0", h), scale=0.125),
                            waits=[(self.st, t_s), (sem, val), (self.ss, s_vhat)],
                            tag=f"ex{hp}_{qb}_{m}_{h}")
                        exp_done[h] = self.ss.n
                        self.bank_read(2 * h, self.ss, self.ss.n)
                        self.bank_read(2 * h + 1, self.ss, self.ss.n)
                    for h in range(2):
                        prb = self.probs0 if h == 0 else self.probs1
                        for dk in range(2):
                            kt = 2 * m + dk
                            self.tmm(self.PS[0:65, (4 + h) * 512:(5 + h) * 512],
                                     self.vhat[:, kt * 130 + 65 * h:kt * 130 + 65 * (h + 1)],
                                     prb[:, dk * 512:(dk + 1) * 512],
                                     kt == 0, kt == 15,
                                     waits=(avw[h] if kt == 0 else []) +
                                     ([(self.ss, exp_done[h])] if dk == 0 else []),
                                     tag=f"av{hp}_{qb}_{kt}_{h}", inc=True)
                        probs_free[h] = (self.st, self.st.n)
                t_av = self.st.n
                if b == 0 and hp == 0 and qb == 0 and self.debug:
                    self.dump("pr0", self.probs0[:, 512:1024], stage2=True)
                for h in range(2):
                    avps = self.PS[:, (4 + h) * 512:(5 + h) * 512]
                    self.sop(lambda e, avps=avps: e.activation(
                        rf[64:65, :], avps[64:65, :], ACT.Ln),
                        waits=[(self.st, t_av)], tag=f"dn{hp}_{qb}_{h}")
                    self.sop(lambda e: e.activation(db[64:65, 0:512], rf[64:65, :],
                                                    ACT.Exp, scale=-1.0),
                             tag=f"dx{hp}_{qb}_{h}")
                    s_dn = self.ss.n
                    bkb = 6
                    wb = self.claim_bank(bkb)
                    self.tmm(self.PS[0:64, bkb * 512:bkb * 512 + 512],
                             self.onesb[64:65, 0:64], db[64:65, 0:512],
                             True, True, waits=wb + [(self.ss, s_dn)],
                             tag=f"dbc{hp}_{qb}_{h}", inc=True)
                    t_db = self.st.n
                    self.vop(lambda e: e.tensor_copy(rf[0:64, :],
                             self.PS[0:64, bkb * 512:bkb * 512 + 512]),
                             waits=[(self.st, t_db)], tag=f"dbe{hp}_{qb}_{h}")
                    v_dbe = self.sv.n
                    self.bank_read(bkb, self.sv, v_dbe)
                    if h == 0:
                        dst = self.quarter[0:64,
                                           (hp % 2) * S + qb * 512:(hp % 2) * S + qb * 512 + 512]
                    else:
                        dst = self.stage[0:64, qb * 512:qb * 512 + 512]
                    self.vop(lambda e, avps=avps, dst=dst: e.tensor_mul(
                        dst, avps[0:64, :], rf[0:64, :]),
                        waits=[(self.sv, v_dbe)], tag=f"qn{hp}_{qb}_{h}")
                    self.bank_read(4 + h, self.sv, self.sv.n)
            self.c0_free = self.ss.n
            if b == 0 and hp == 0 and self.debug:
                self.dump("dnD", db[64:65, 0:512], stage2=True)
                self.dump("qrD", self.quarter[0:64, 0:512], stage2=True)
            v_q = self.sv.n
            self.dma("gpsimd",
                     self.quarter[64:128, (hp % 2) * S:(hp % 2) * S + S],
                     self.stage[0:64, 0:S],
                     waits=[(self.sv, v_q), (self.sd, self.prologue_done)],
                     tag=f"hshift{hp}")
            self.qshift_done = self.sg.n
            if hp % 2 == 1:
                wlist = [(self.sd, self.img_ready["wo"]), (self.sg, self.qshift_done)]
                if hasattr(self, "wotsl_free"):
                    wlist.append((self.st, self.wotsl_free))
                self.dma("sync", self.wotsl[:, 0:KB],
                         self.img["wo"][:, (hp - 1) * KB:hp * KB],
                         waits=wlist, tag=f"wold{hp}a")
                self.dma("sync", self.wotsl[:, KB:2 * KB],
                         self.img["wo"][:, hp * KB:(hp + 1) * KB], tag=f"wold{hp}b")
                wo_rdy = self.sd.n
                for oc in range(NCH):
                    for tb in range(4):
                        bk = (oc * 4 + tb) % 2 + 6
                        wb = self.claim_bank(bk)
                        for kc2 in range(2):
                            self.tmm(self.PSb(bk),
                                     self.wotsl[:, kc2 * KB + oc * 128:kc2 * KB + (oc + 1) * 128],
                                     self.quarter[:, kc2 * S + tb * 512:kc2 * S + (tb + 1) * 512],
                                     kc2 == 0, kc2 == 1,
                                     waits=(wb + [(self.sd, wo_rdy),
                                                  (self.sg, self.qshift_done)])
                                     if kc2 == 0 else [], tag=f"op{hp}_{oc}_{tb}")
                        t_o = self.st.n
                        Rv = self.R[:, oc * S + tb * 512:oc * S + (tb + 1) * 512]
                        ps = self.PSb(bk)
                        if pair_idx == 0:
                            self.vop(lambda e, ps=ps, Rv=Rv, oc=oc: e.scalar_tensor_tensor(
                                Rv, ps, self.bcol("bo", oc), Rv, ALU.add, ALU.add),
                                waits=[(self.st, t_o)], tag=f"Ro{hp}_{oc}_{tb}")
                        else:
                            self.vop(lambda e, ps=ps, Rv=Rv: e.tensor_add(Rv, Rv, ps),
                                     waits=[(self.st, t_o)], tag=f"Ro{hp}_{oc}_{tb}")
                        self.bank_read(bk, self.sv, self.sv.n)
                self.wotsl_free = self.st.n
                pair_idx += 1
        self.slot_free[0] = (self.st, self.st.n)
        self.slot_free[1] = (self.st, self.st.n)
        self.slot_free[2] = (self.st, self.st.n)
        if b == 0 and self.debug:
            self.dump("RaD", self.R[:, 0:512], stage2=True)

    def seg(self):
        self.segs.append(len(self.plan))

    def build(self):
        self.declare()
        self.alloc()
        self.sems()
        self.prologue()
        self.seg()
        for b in range(self.nbatch):
            self.load_R(b)
            self.seg()
            for ph in self.phases:
                if ph == "ffn1":
                    self.ffn(b, 0)
                elif ph == "attn":
                    self.attn(b)
                elif ph == "ffn2":
                    self.ffn(b, 1)
                self.seg()
            self.store_R(b)
            self.seg()
        self.plan.append(("sync", lambda e: e.wait_ge(self.so.h, self.so.n)))
        nc = self.nc
        with nc.Block() as block:
            cur = 0
            while cur < len(self.plan):
                eng = self.plan[cur][0]
                group = []
                while cur < len(self.plan) and self.plan[cur][0] == eng:
                    group.append(self.plan[cur][1])
                    cur += 1
                def section(eh, group=group):
                    for f in group:
                        f(eh)
                getattr(block, eng)(section)
        for g in reversed(self._semctx):
            g.__exit__(None, None, None)
        return nc


def make_consts():
    j = np.arange(0, DH, 2, dtype=np.float64) / DH
    invf = 1.0 / (10000.0 ** j)
    t = np.arange(S, dtype=np.float64)
    fr = t[None, :] * invf[:, None]
    cos = np.repeat(np.cos(fr), 2, axis=0)
    sin = np.repeat(np.sin(fr), 2, axis=0)
    sin_eff = sin.copy()
    sin_eff[:32] *= -1.0
    cosT = np.concatenate([cos, cos], axis=0).astype(ml_dtypes.bfloat16)
    sinT = np.concatenate([sin_eff, sin_eff], axis=0).astype(ml_dtypes.bfloat16)
    perm = np.zeros((128, 128), dtype=np.float32)
    for blk in range(2):
        for i in range(64):
            perm[blk * 64 + (i + 32) % 64, blk * 64 + i] = 1.0
    return {
        "c_cos": np.ascontiguousarray(cosT),
        "c_sin": np.ascontiguousarray(sinT),
        "c_perm": perm.astype(ml_dtypes.bfloat16),
        "c_ones": np.ones((128, 128), dtype=ml_dtypes.bfloat16),
        "c_idf": np.eye(128, dtype=np.float32),
        "c_idb": np.eye(128, dtype=ml_dtypes.bfloat16),
    }


_BUILD_CACHE = {}


def get_nc(phases=("ffn1", "attn", "ffn2")):
    key = tuple(phases)
    if key not in _BUILD_CACHE:
        _BUILD_CACHE[key] = Build(phases=phases).build()
    return _BUILD_CACHE[key]


def kernel(**inputs):
    consts = make_consts()
    nc = get_nc()
    in_maps = []
    for core in range(8):
        m = dict(consts)
        for k, v in inputs.items():
            v = np.ascontiguousarray(np.asarray(v))
            m[k] = v[2 * core:2 * core + 2] if k == "L0_attn_x" else v
        in_maps.append(m)
    res = run_bass_kernel_spmd(nc, in_maps, core_ids=list(range(8)))
    return np.concatenate([r["y"] for r in res.results], axis=0)


if __name__ == "__main__":
    bld = Build(phases=("ffn1",))
    nc = bld.build()
    print("build ok; instructions:", nc.n_instructions())


# version 06
# speedup vs baseline: 1.6615x; 1.0982x over previous
import numpy as np
import ml_dtypes
import concourse.bass as bass
import concourse.mybir as mybir
from concourse.bass_utils import run_bass_kernel_spmd

F32 = mybir.dt.float32
BF16 = mybir.dt.bfloat16
ALU = mybir.AluOpType
ACT = mybir.ActivationFunctionType
AX = mybir.AxisListType.X

D = 1024
S = 2048
NBATCH = 2
NCH = 8
H = 16
DH = 64
NE = 8
HID = 256
TS = 512
NTS = S // TS
QB = 256
EPS_LN = 1e-5
EPS_RMS = 1e-6
KB = 1024
BIG1024 = [1024, 1024]


class Ctr:
    __slots__ = ("h", "n", "name")
    def __init__(self, h, name):
        self.h = h; self.n = 0; self.name = name


class Build:
    def __init__(self, phases=("ffn1", "attn", "ffn2"), nbatch=NBATCH, debug=()):
        self.phases = phases
        self.nbatch = nbatch
        self.debug = list(debug)
        self.dbg_slots = {}
        self.nc = bass.Bass("TRN2", target_bir_lowering=False)
        self.plan = []
        self.segs = []
        self.last_wait = {}

    def op(self, eng, fn, waits=(), inc=None, inc_k=1, tag=""):
        waits2 = []
        for (c, v) in waits:
            if v <= 0:
                continue
            assert v <= c.n, f"forward wait on {c.name}: {v} > {c.n} ({tag})"
            key = (eng, id(c))
            if self.last_wait.get(key, -1) < v:
                self.last_wait[key] = v
                waits2.append((c.h, v))
        if inc is not None:
            inc.n += inc_k
        ic = inc
        def run(eh, waits2=waits2, fn=fn, ic=ic, inc_k=inc_k):
            for (hh, v) in waits2:
                eh.wait_ge(hh, v)
            r = fn(eh)
            if ic is not None:
                r.then_inc(ic.h, inc_k)
        self.plan.append((eng, run))
        return inc.n if inc is not None else None

    def dma(self, q, out_ap, in_ap, waits=(), tag="", **kw):
        c = self.sd if q == "sync" else self.sg
        return self.op(q, lambda e: e.dma_start(out_ap, in_ap, **kw),
                       waits=waits, inc=c, inc_k=16, tag=tag)

    def tmm(self, out, lhsT, rhs, start, stop, waits=(), inc=False, tp=None, tag=""):
        kw = {}
        if tp is not None:
            kw["tile_position"] = tp
        return self.op("tensor",
                       lambda e: e.matmul(out, lhsT, rhs, start=start, stop=stop, **kw),
                       waits=waits, inc=self.st if (inc or stop) else None, tag=tag)

    def ttr(self, out, in_, ident, waits=(), tag=""):
        return self.op("tensor", lambda e: e.transpose(out, in_, ident),
                       waits=waits, inc=self.st, tag=tag)

    def vop(self, fn, waits=(), tag=""):
        return self.op("vector", fn, waits=waits, inc=self.sv, tag=tag)

    def sop(self, fn, waits=(), tag=""):
        return self.op("scalar", fn, waits=waits, inc=self.ss, tag=tag)

    def claim_bank(self, k):
        w = self.bank_free[k]
        self.bank_free[k] = []
        return list(w)

    def bank_read(self, k, ctr, val):
        self.bank_free[k].append((ctr, val))

    def PSb(self, k, c0=0, c1=512):
        return self.PS[:, 512 * k + c0:512 * k + c1]

    def declare(self):
        nc = self.nc
        self.ins = {}
        specs = {
            "L0_attn_x": [self.nbatch, S, D], "L0_attn_wq": BIG1024,
            "L0_attn_bq": [D], "L0_attn_wk": BIG1024, "L0_attn_bk": [D],
            "L0_attn_wv": BIG1024, "L0_attn_bv": [D], "L0_attn_wo": BIG1024,
            "L0_attn_bo": [D], "L0_ffn_dw_kernel": [D, 1, 3], "L0_ffn_dw_bias": [D],
            "L0_ffn_pw_w": BIG1024, "L0_ffn_pw_b": [D], "L0_ffn_router_w": [8, D],
            "L0_ffn_router_b": [8], "L0_ffn_expert_w": [NE, D, D],
            "L0_ffn_expert_b": [NE, D], "L0_ffn_out_w": BIG1024, "L0_ffn_out_b": [D],
            "L0_ffn_norm_w": [D], "L0_ffn_ff_down_w": [HID, D],
            "L0_ffn_ff_down_b": [HID], "L0_ffn_ff_up_w": [D, HID], "L0_ffn_ff_up_b": [D],
            "c_cos": [128, S], "c_sin": [128, S], "c_perm": [128, 128],
            "c_ones": [128, 128], "c_idf": [128, 128], "c_idb": [128, 128],
        }
        bf16_names = {"c_cos", "c_sin", "c_perm", "c_ones", "c_idb"}
        for k, sh in specs.items():
            dt = BF16 if k in bf16_names else F32
            self.ins[k] = nc.dram_tensor(k, sh, dt, kind="ExternalInput").ap()
        self.out = nc.dram_tensor("y", [self.nbatch, S, D], F32, kind="ExternalOutput").ap()
        if self.debug:
            self.dbgt = nc.dram_tensor("dbg", [len(self.debug), 128, 512], F32,
                                       kind="ExternalOutput").ap()
        self.nat = {}
        self.img = {}
        for w in ["wq", "wk", "wv", "wo", "pw", "ow"] + [f"e{i}" for i in range(NE)]:
            self.nat[w] = nc.dram_tensor("nat_" + w, BIG1024, BF16, kind="Internal").ap()
            self.img[w] = nc.dram_tensor("img_" + w, [128, 8 * KB], BF16, kind="Internal").ap()
        self.nat["ffd"] = nc.dram_tensor("nat_ffd", [HID, D], BF16, kind="Internal").ap()
        self.img["ffd"] = nc.dram_tensor("img_ffd", [128, 8 * HID], BF16, kind="Internal").ap()
        self.nat["ffu"] = nc.dram_tensor("nat_ffu", [D, HID], BF16, kind="Internal").ap()
        self.img["ffu"] = nc.dram_tensor("img_ffu", [128, 2 * KB], BF16, kind="Internal").ap()

    def alloc(self):
        nc = self.nc
        A = nc.alloc_sbuf_tensor_at
        arena = nc.alloc_sbuf_tensor("ARENA", [128, 212800], mybir.dt.uint8)
        amloc = nc.lookup_mloc(arena)
        o = [amloc.addr]
        def at(name, shape, dtype):
            t = A(name, shape, dtype, offset=o[0])
            o[0] += int(np.prod(shape[1:])) * mybir.dt.size(dtype)
            o[0] = (o[0] + 31) & ~31
            return t
        self.R = at("R", [128, NCH * S], F32)
        self.cos = at("cosT", [128, S], BF16)
        self.sin = at("sinT", [128, S], BF16)
        self.perm = at("permT", [128, 128], BF16)
        self.onesb = at("onesb", [128, 128], BF16)
        self.idf = at("idf", [128, 128], F32)
        self.idb = at("idb", [128, 128], BF16)
        self.bias = at("biasc", [128, 240], F32)
        self.routimg = at("routimg", [128, 64], BF16)
        self.rbcol = at("rbcol", [128, 1], F32)
        self.bvrow = at("bvrow", [1, KB], BF16)
        self.S0 = at("S0", [128, 8 * KB], BF16)
        self.S1 = at("S1", [128, 8 * KB], BF16)
        self.S2 = at("S2", [128, 8 * KB], BF16)
        ua_base = o[0]
        self.UA = at("UA", [128, 16 * KB], BF16)
        ub = o[0]
        self.yb = A("yb", [128, NCH * (TS + 2)], F32, offset=ub)
        self.sres = A("sres", [128, NCH * TS], F32, offset=ub)
        self.comb = A("comb", [128, NCH * TS], F32, offset=ub + 16448)
        self.probs0 = A("probs0", [128, 2 * TS], BF16, offset=ub)
        self.probs1 = A("probs1", [128, 2 * TS], BF16, offset=ub + 2048)
        self.vhat = A("vhat", [128, 16 * 130], BF16, offset=ub + 4096)
        self.q1 = A("q1", [128, S], BF16, offset=ub + 8320)
        self.k1 = A("k1", [128, S], BF16, offset=ub + 12416)
        self.rsc = A("ropesc", [128, S], BF16, offset=ub + 16512)
        o[0] = ((ub + 33248) + 31) & ~31
        uc = o[0]
        self.row_m = A("row_m", [1, TS], F32, offset=uc)
        self.row_q = A("row_q", [1, TS], F32, offset=uc + 2048)
        self.row_t = A("row_t", [1, TS], F32, offset=uc + 4096)
        self.row_bf = A("row_bf", [1, 4 * TS], BF16, offset=uc + 6144)
        self.mb = A("mb", [128, TS], F32, offset=uc + 10240)
        self.rb_ = A("rb_", [128, TS], F32, offset=uc + 12288)
        self.silu = A("siluf", [128, TS], F32, offset=uc + 14336)
        self.rwrow = A("rwrow", [1, TS], BF16, offset=uc + 10240)
        self.rwb = A("rwb", [128, TS], BF16, offset=uc + 11264)
        self.lgf = A("lgf", [128, 32], F32, offset=uc + 16384)
        self.rwtok = A("rwtok", [128, 32], BF16, offset=uc + 16512)
        self.m1n = A("m1n", [128, 24], F32, offset=uc + 16576)
        self.cstage = A("cstage", [128, TS], BF16, offset=uc + 16672)
        self.quarter = A("quarter", [128, 2 * S], BF16, offset=uc)
        self.wotsl = A("wotsl", [128, S], BF16, offset=uc + 8192)
        self.stage = A("stage", [128, S], BF16, offset=uc + 12288)
        self.recipf = A("recipf", [128, 512], F32, offset=uc + 16384)
        self.denb = A("denb", [128, 512], BF16, offset=uc + 18432)
        self.xstage = A("xstage", [128, KB], F32, offset=uc + 12288)
        o[0] = uc + 19456
        if self.debug:
            self.dbgstage = A("dbgstage", [128, 512], F32, offset=ua_base + 28672)
            self.dbgstage2 = A("dbgstage2", [128, 512], F32, offset=ub + 20608)
        assert o[0] <= amloc.addr + 212800, o[0]
        self.PS = nc.alloc_psum_tensor("PS", [128, 4096], F32)
        self.bc = {}
        names = ["bq", "bk", "bv", "bo", "dwb", "pwb", "outb", "nw", "nwh",
                 "ffub", "ffubh"]
        for i, name in enumerate(names):
            self.bc[name] = (8 * i, 8)
        self.bc["ffdb"] = (88, 2)
        self.bc["ebc"] = (90, 64)
        self.bc["kern"] = (154, 24)
        self.bc["rbb"] = (178, 8)
        self.bc["c0"] = (186, 2)

    def bcol(self, name, j=0, w=1):
        c0, n = self.bc[name]
        assert j + w <= n, (name, j, w)
        return self.bias[:, c0 + j:c0 + j + w]

    def sems(self):
        nc = self.nc
        self._semctx = []
        def mk(name):
            g = nc.semaphore(name)
            h = g.__enter__()
            self._semctx.append(g)
            return Ctr(h, name)
        self.sd = mk("sd"); self.sg = mk("sg"); self.st = mk("st")
        self.sv = mk("sv"); self.ss = mk("ss"); self.so = mk("so")
        self.bank_free = {k: [] for k in range(8)}
        self.slot_free = {0: (self.st, 0), 1: (self.st, 0), 2: (self.st, 0)}
        self.eready = {1: 0, 2: 0}

    def dump(self, name, ap, bf16=False, stage2=False):
        if name not in self.debug:
            return
        slot = self.debug.index(name)
        stg = self.dbgstage2 if stage2 else self.dbgstage
        waits = [(c, c.n) for c in (self.st, self.sv, self.ss, self.sg)]
        nr, ncols = ap.shape[0], ap.shape[-1]
        self.vop(lambda e: e.tensor_copy(stg[0:nr, 0:ncols], ap),
                 waits=waits, tag="dbgc_" + name)
        v = self.sv.n
        self.dma("sync", self.dbgt[slot], stg[:, :],
                 waits=[(self.sv, v)], tag="dbg_" + name)
        self.vop(lambda e: e.tensor_copy(self.m1n[:, 3:4], self.m1n[:, 3:4]),
                 waits=[(self.sd, self.sd.n)], tag="dbgw_" + name)

    def prologue(self):
        ins = self.ins
        self.dma("sync", self.cos[:], ins["c_cos"], tag="ccos")
        self.dma("sync", self.sin[:], ins["c_sin"], tag="csin")
        self.dma("sync", self.perm[:], ins["c_perm"], tag="cperm")
        self.dma("sync", self.onesb[:], ins["c_ones"], tag="cones")
        self.dma("sync", self.idf[:], ins["c_idf"], tag="cidf")
        self.dma("sync", self.idb[:], ins["c_idb"], tag="cidb")
        for name, src in [("bq", "L0_attn_bq"), ("bk", "L0_attn_bk"),
                          ("bv", "L0_attn_bv"), ("bo", "L0_attn_bo"),
                          ("dwb", "L0_ffn_dw_bias"), ("pwb", "L0_ffn_pw_b"),
                          ("outb", "L0_ffn_out_b"), ("nw", "L0_ffn_norm_w"),
                          ("ffub", "L0_ffn_ff_up_b")]:
            c0, _ = self.bc[name]
            self.dma("sync", self.bias[:, c0:c0 + 8],
                     ins[src].rearrange("(c p) -> p c", p=128), tag="b_" + name, allow_slow_non_contiguous=True)
        c0, _ = self.bc["ffdb"]
        self.dma("sync", self.bias[:, c0:c0 + 2],
                 ins["L0_ffn_ff_down_b"].rearrange("(c p) -> p c", p=128), tag="b_ffdb", allow_slow_non_contiguous=True)
        c0, _ = self.bc["ebc"]
        self.dma("sync", self.bias[:, c0:c0 + 64].rearrange("p (e c) -> p e c", e=8),
                 ins["L0_ffn_expert_b"].rearrange("e (c p) -> p e c", p=128), tag="b_ebc", allow_slow_non_contiguous=True)
        c0, _ = self.bc["kern"]
        self.dma("sync", self.bias[:, c0:c0 + 24].rearrange("p (c k) -> p c k", c=8),
                 ins["L0_ffn_dw_kernel"].rearrange("(c p) o k -> p c (o k)", p=128),
                 tag="b_kern", allow_slow_non_contiguous=True)
        self.dma("sync", self.rbcol[0:8, 0:1],
                 ins["L0_ffn_router_b"].rearrange("(e o) -> e o", o=1), tag="b_rb",
                 allow_slow_non_contiguous=True)
        n_const = self.sd.n
        self.const_ready = n_const

        self.sop(lambda e: e.mul(self.bcol("nwh", 0, 8), self.bcol("nw", 0, 8), 0.5),
                 waits=[(self.sd, n_const)], tag="nwh")
        self.sop(lambda e: e.mul(self.bcol("ffubh", 0, 8), self.bcol("ffub", 0, 8), 0.5),
                 tag="ffubh")

        self.dma("sync", self.rbcol[0:8, 0:1],
                 ins["L0_ffn_router_b"].rearrange("(e o) -> e o", o=1), tag="b_rb",
                 allow_slow_non_contiguous=True)
        n_const = self.sd.n
        order = ["pw"] + [f"e{i}" for i in range(NE)] + ["ow", "ffd", "ffu",
                                                         "wq", "wk", "wv", "wo"]
        src = {"pw": self.ins["L0_ffn_pw_w"], "ow": self.ins["L0_ffn_out_w"],
               "ffd": self.ins["L0_ffn_ff_down_w"], "ffu": self.ins["L0_ffn_ff_up_w"],
               "wq": self.ins["L0_attn_wq"], "wk": self.ins["L0_attn_wk"],
               "wv": self.ins["L0_attn_wv"], "wo": self.ins["L0_attn_wo"]}
        for i in range(NE):
            src[f"e{i}"] = self.ins["L0_ffn_expert_w"][i]
        self.img_ready = {}
        s1_free = 0
        s2_free = 0
        for wname in order:
            a = src[wname]
            rows, ncol = a.shape
            nch = rows // 128
            cast_view = self.S1[:, 0:nch * ncol].rearrange("p (c n) -> p c n", c=nch)
            self.dma("gpsimd", cast_view, a.rearrange("(c p) n -> p c n", p=128),
                     waits=[(self.sd, s1_free)], tag=f"cast_{wname}")
            g_cast = self.sg.n
            if wname == "ffu":
                self.sop(lambda e: e.mul(self.S1[:, 0:2 * KB], self.S1[:, 0:2 * KB], 0.5),
                         waits=[(self.sg, g_cast)], tag="ffu_half")
                store_waits = [(self.ss, self.ss.n)]
            else:
                store_waits = [(self.sg, g_cast)]
            self.dma("sync", self.nat[wname].rearrange("(c p) n -> p c n", p=128),
                     cast_view, waits=store_waits, tag=f"nat_{wname}")
            d_nat = self.sd.n
            s1_free = d_nat
            nkc = ncol // 128
            for c in range(nkc):
                self.dma("sync", self.S2[:, c * rows:(c + 1) * rows],
                         self.nat[wname][:, c * 128:(c + 1) * 128], transpose=True,
                         waits=[(self.sd, max(d_nat, s2_free))] if c == 0 else (),
                         tag=f"tr_{wname}_{c}")
            d_tr = self.sd.n
            self.dma("sync", self.img[wname][:, 0:nkc * rows], self.S2[:, 0:nkc * rows],
                     waits=[(self.sd, d_tr)], tag=f"img_{wname}")
            s2_free = self.sd.n
            self.img_ready[wname] = self.sd.n
        self.dma("gpsimd", self.S1[0:8, 0:KB], self.ins["L0_ffn_router_w"],
                 waits=[(self.sd, s1_free)], tag="cast_rout")
        g_r = self.sg.n
        for c in range(8):
            bk = 2 + (c % 2)
            w = self.claim_bank(bk)
            self.tmm(self.PS[:, bk * 512:bk * 512 + 8],
                     self.S1[0:8, c * 128:(c + 1) * 128], self.idb[0:8, 0:8],
                     True, True, waits=w + [(self.sg, g_r)], tag=f"routT{c}")
            t_n = self.st.n
            self.sop(lambda e, c=c, bk=bk: e.copy(self.routimg[:, c * 8:(c + 1) * 8],
                                                  self.PS[:, bk * 512:bk * 512 + 8]),
                     waits=[(self.st, t_n)], tag=f"routE{c}")
            self.bank_read(bk, self.ss, self.ss.n)
        self.dma("gpsimd", self.bvrow[0:1, :],
                 self.ins["L0_attn_bv"].rearrange("(o d) -> o d", o=1), tag="bvrow")
        self.prologue_done = self.sd.n
        self.slot_free[1] = (self.st, self.st.n)
        self.prol_s1_free = (self.sd, s1_free)
        self.prol_s2_free = (self.sd, s2_free)
        self.prol_g = (self.sg, self.sg.n)

    def load_R(self, b):
        first = True
        for tt in range(16):
            waits = []
            if first and b > 0:
                waits.append((self.so, self.so.n))
            if not first:
                waits.append((self.st, self.last_xT))
            d_x = self.dma("sync", self.xstage[:],
                           self.ins["L0_attn_x"][b, tt * 128:(tt + 1) * 128, :],
                           waits=waits, tag=f"xin{b}_{tt}")
            first = False
            for c in range(NCH):
                bk = c % 4
                w = self.claim_bank(bk)
                self.ttr(self.PS[:, bk * 512:bk * 512 + 128],
                         self.xstage[:, c * 128:(c + 1) * 128], self.idf[:],
                         waits=w + [(self.sd, d_x)], tag=f"xT{tt}_{c}")
                t_n = self.st.n
                self.vop(lambda e, c=c, tt=tt, bk=bk: e.tensor_copy(
                    self.R[:, c * S + tt * 128:c * S + (tt + 1) * 128],
                    self.PS[:, bk * 512:bk * 512 + 128]),
                    waits=[(self.st, t_n)], tag=f"xE{tt}_{c}")
                self.bank_read(bk, self.sv, self.sv.n)
            self.last_xT = self.st.n

    def store_R(self, b):
        prev_store = self.so.n
        for tt in range(16):
            for c in range(NCH):
                bk = c % 4
                w = self.claim_bank(bk)
                self.ttr(self.PS[:, bk * 512:bk * 512 + 128],
                         self.R[:, c * S + tt * 128:c * S + (tt + 1) * 128], self.idf[:],
                         waits=w + ([(self.sv, self.sv.n)] if (tt == 0 and c == 0) else []),
                         tag=f"oT{tt}_{c}")
                t_n = self.st.n
                self.vop(lambda e, c=c, bk=bk: e.tensor_copy(
                    self.xstage[:, c * 128:(c + 1) * 128],
                    self.PS[:, bk * 512:bk * 512 + 128]),
                    waits=[(self.st, t_n)] + ([(self.so, prev_store)] if c == 0 else []),
                    tag=f"oE{tt}_{c}")
                self.bank_read(bk, self.sv, self.sv.n)
            v_n = self.sv.n
            self.op("sync", lambda e, tt=tt, b=b: e.dma_start(
                self.out[b, tt * 128:(tt + 1) * 128, :], self.xstage[:]),
                waits=[(self.sv, v_n)], inc=self.so, inc_k=16, tag=f"xout{tt}")
            prev_store = self.so.n

    def stats(self, get_chunk, want_mean, eps, tag, entry_waits=(), scratch=None):
        ua = self.UA
        if scratch is None:
            xbf = ua[:, 13 * KB:13 * KB + TS]
            sq = ua[:, 13 * KB + TS:13 * KB + 2 * TS]
        else:
            xbf, sq = scratch
        bsum, bsq = 4, 5
        wsum = self.claim_bank(bsum)
        wsq = self.claim_bank(bsq)
        t_prev = 0
        for c in range(NCH):
            src = get_chunk(c)
            cw = list(entry_waits) if c == 0 else [(self.st, t_prev)]
            self.vop(lambda e, src=src, xbf=xbf: e.tensor_copy(xbf, src),
                     waits=cw, tag=f"{tag}cast{c}")
            v_c = self.sv.n
            self.vop(lambda e, xbf=xbf, sq=sq: e.tensor_mul(sq, xbf, xbf),
                     tag=f"{tag}sq{c}")
            v_s = self.sv.n
            if want_mean:
                self.tmm(self.PSb(bsum)[0:1, 0:TS], self.onesb[:, 0:1], xbf,
                         c == 0, c == NCH - 1,
                         waits=(wsum if c == 0 else []) + [(self.sv, v_c)],
                         inc=True, tag=f"{tag}ms{c}")
            self.tmm(self.PSb(bsq)[0:1, 0:TS], self.onesb[:, 0:1], sq,
                     c == 0, c == NCH - 1,
                     waits=(wsq if c == 0 else []) + [(self.sv, v_s)],
                     inc=True, tag=f"{tag}qs{c}")
            t_prev = self.st.n
        t_done = self.st.n
        rm, rq, rt, rbf = self.row_m, self.row_q, self.row_t, self.row_bf
        if tag == "F00_0" and self.debug:
            self.sop(lambda e: e.copy(self.silu[0:1, :], self.PSb(bsq)[0:1, 0:TS]),
                     waits=[(self.st, t_done)], tag="dbgq")
            self.bank_read(bsq, self.ss, self.ss.n)
            self.dump("qsumP", self.silu[0:1, :])
            self.sop(lambda e: e.copy(self.silu[0:1, :], self.PSb(bsum)[0:1, 0:TS]),
                     waits=[(self.st, t_done)], tag="dbgm")
            self.bank_read(bsum, self.ss, self.ss.n)
            self.dump("msumP", self.silu[0:1, :])
        if want_mean:
            self.vop(lambda e: e.tensor_scalar_mul(rm[0:1, :], self.PSb(bsum)[0:1, 0:TS],
                                                   1.0 / D),
                     waits=[(self.st, t_done)], tag=f"{tag}m")
        self.vop(lambda e: e.tensor_scalar_mul(rq[0:1, :], self.PSb(bsq)[0:1, 0:TS],
                                               1.0 / D),
                 waits=[(self.st, t_done)], tag=f"{tag}q")
        self.bank_read(bsum, self.sv, self.sv.n)
        self.bank_read(bsq, self.sv, self.sv.n)
        if want_mean:
            self.vop(lambda e: e.tensor_mul(rt[0:1, :], rm[0:1, :], rm[0:1, :]),
                     tag=f"{tag}mm")
            self.vop(lambda e: e.tensor_sub(rq[0:1, :], rq[0:1, :], rt[0:1, :]),
                     tag=f"{tag}var")
        self.vop(lambda e: e.tensor_scalar_add(rq[0:1, :], rq[0:1, :], eps),
                 tag=f"{tag}eps")
        v_var = self.sv.n
        if tag == "F00_0":
            self.dump("varrow", rq[0:1, :])
        self.sop(lambda e: e.activation(rt[0:1, :], rq[0:1, :], ACT.Sqrt),
                 waits=[(self.sv, v_var)], tag=f"{tag}sd")
        s_sd = self.ss.n
        if tag == "F00_0" and self.debug:
            self.dump("sdrow", rt[0:1, :])
        self.vop(lambda e: e.reciprocal(rq[0:1, :], rt[0:1, :]),
                 waits=[(self.ss, s_sd)], tag=f"{tag}rc")
        self.vop(lambda e: e.tensor_copy(rbf[0:1, 2 * TS:3 * TS], rq[0:1, :]),
                 tag=f"{tag}rhi")
        self.vop(lambda e: e.tensor_sub(rbf[0:1, 3 * TS:4 * TS], rq[0:1, :],
                                        rbf[0:1, 2 * TS:3 * TS]), tag=f"{tag}rlo")
        if want_mean:
            self.vop(lambda e: e.tensor_copy(rbf[0:1, 0:TS], rm[0:1, :]),
                     tag=f"{tag}mhi")
            self.vop(lambda e: e.tensor_sub(rbf[0:1, TS:2 * TS], rm[0:1, :],
                                            rbf[0:1, 0:TS]), tag=f"{tag}mlo")
        v_rows = self.sv.n
        if tag == "F00_0":
            self.dump("rrow", rq[0:1, :])
            self.dump("rhilo", rbf[0:1, 2 * TS:3 * TS], bf16=True)
        plan = [(self.rb_, 2 * TS, 3 * TS, 6)]
        if want_mean:
            plan.append((self.mb, 0, TS, 7))
        for (dst, hi, lo, bk) in plan:
            w = self.claim_bank(bk)
            self.tmm(self.PSb(bk), self.onesb[0:1, :], rbf[0:1, hi:hi + TS],
                     True, False, waits=w + [(self.sv, v_rows)], inc=True,
                     tag=f"{tag}bc1")
            self.tmm(self.PSb(bk), self.onesb[0:1, :], rbf[0:1, lo:lo + TS],
                     False, True, tag=f"{tag}bc2")
            t_b = self.st.n
            self.vop(lambda e, dst=dst, bk=bk: e.tensor_copy(dst[:, :], self.PSb(bk)),
                     waits=[(self.st, t_b)], tag=f"{tag}bce")
            self.bank_read(bk, self.sv, self.sv.n)
        return self.sv.n

    def ffn(self, b, call):
        ua = self.UA
        conv_r = lambda c: ua[:, c * TS:(c + 1) * TS]
        x1_r = lambda c: ua[:, 4 * KB + c * TS:4 * KB + (c + 1) * TS]
        h_r = lambda c: ua[:, 8 * KB + c * TS:8 * KB + (c + 1) * TS]
        g_r = lambda m: ua[:, 12 * KB + m * TS:12 * KB + (m + 1) * TS]
        lgbf = ua[:, 15 * KB:15 * KB + 512]

        sem, val = self.slot_free[0]
        self.dma("sync", self.S0[:], self.img["pw"],
                 waits=[(self.sd, self.img_ready["pw"]), (sem, val)], tag=f"pwld{call}")
        pw_rdy = self.sd.n

        for ts in range(NTS):
            t0 = ts * TS
            Rc = lambda c: self.R[:, c * S + t0:c * S + t0 + TS]
            entry = [(self.st, self.st.n), (self.ss, self.ss.n)] if ts == 0 else ()
            if b == 0 and call == 0 and ts == 0:
                self.dump("R0", Rc(0))
            self.stats(Rc, True, EPS_LN, f"F{call}{b}_{ts}", entry_waits=entry)
            if b == 0 and call == 0 and ts == 0:
                self.dump("mb", self.mb[:, :])
                self.dump("rb", self.rb_[:, :])
            if ts == 0:
                self.vop(lambda e: e.memset(
                    self.yb[:, :].rearrange("p (c n) -> p c n", c=NCH)[:, :, 0:2], 0.0),
                    tag="halo0")
            for c in range(NCH):
                yv = self.yb[:, c * (TS + 2) + 2:c * (TS + 2) + 2 + TS]
                Rsrc = Rc(c)
                self.vop(lambda e, Rsrc=Rsrc, yv=yv: e.tensor_sub(yv, Rsrc, self.mb[:, :]),
                         tag=f"ysub{c}")
                self.vop(lambda e, yv=yv: e.tensor_mul(yv, yv, self.rb_[:, :]),
                         tag=f"ymul{c}")
            for c in range(NCH):
                yb0 = self.yb[:, c * (TS + 2):c * (TS + 2) + TS]
                yb1 = self.yb[:, c * (TS + 2) + 1:c * (TS + 2) + 1 + TS]
                yb2 = self.yb[:, c * (TS + 2) + 2:c * (TS + 2) + 2 + TS]
                self.vop(lambda e, yb0=yb0, c=c: e.tensor_scalar(
                    self.silu[:, :], yb0, self.bcol("kern", 3 * c), self.bcol("dwb", c),
                    ALU.mult, ALU.add), tag=f"cv0_{c}")
                self.vop(lambda e, yb1=yb1, c=c: e.scalar_tensor_tensor(
                    self.silu[:, :], yb1, self.bcol("kern", 3 * c + 1), self.silu[:, :],
                    ALU.mult, ALU.add), tag=f"cv1_{c}")
                self.vop(lambda e, yb2=yb2, c=c: e.scalar_tensor_tensor(
                    conv_r(c), yb2, self.bcol("kern", 3 * c + 2), self.silu[:, :],
                    ALU.mult, ALU.add), tag=f"cv2_{c}")
            v_conv = self.sv.n
            if b == 0 and call == 0 and ts == 0:
                self.dump("y0", self.yb[:, 2:2 + TS])
                self.dump("cv0", conv_r(0), bf16=True)
            if ts < NTS - 1:
                for c in range(NCH):
                    self.vop(lambda e, c=c: e.tensor_copy(
                        self.yb[:, c * (TS + 2):c * (TS + 2) + 2],
                        self.yb[:, c * (TS + 2) + TS:c * (TS + 2) + TS + 2]),
                        tag=f"halo{c}")
            v_silu_free = self.sv.n
            for half in range(2):
                t_pw = {}
                for j in range(4):
                    oc = half * 4 + j
                    w = self.claim_bank(j)
                    for kc in range(NCH):
                        self.tmm(self.PSb(j),
                                 self.S0[:, kc * KB + oc * 128:kc * KB + (oc + 1) * 128],
                                 conv_r(kc), kc == 0, kc == NCH - 1,
                                 waits=(w + [(self.sv, v_conv), (self.sd, pw_rdy)])
                                 if kc == 0 else [], tag=f"pw{oc}")
                    t_pw[j] = self.st.n
                s_si = {}
                for j in range(4):
                    oc = half * 4 + j
                    dst = self.UA[:, 13 * KB + j * TS:13 * KB + (j + 1) * TS]
                    self.sop(lambda e, j=j, oc=oc, dst=dst: e.activation(
                        dst, self.PSb(j), ACT.Silu, bias=self.bcol("pwb", oc)),
                        waits=[(self.st, t_pw[j]), (self.sv, v_silu_free)],
                        tag=f"psl{oc}")
                    s_si[j] = self.ss.n
                    self.bank_read(j, self.ss, s_si[j])
                for j in range(4):
                    oc = half * 4 + j
                    yv = self.yb[:, oc * (TS + 2) + 2:oc * (TS + 2) + 2 + TS]
                    srcb = self.UA[:, 13 * KB + j * TS:13 * KB + (j + 1) * TS]
                    self.vop(lambda e, oc=oc, yv=yv, srcb=srcb: e.tensor_add(
                        x1_r(oc), yv, srcb),
                        waits=[(self.ss, s_si[j])], tag=f"x1_{oc}")
                v_silu_free = self.sv.n
            v_x1 = self.sv.n
            if b == 0 and call == 0 and ts == 0:
                self.dump("x10", x1_r(0), bf16=True)
            self._dbg_rt = (b == 0 and call == 0 and ts == 0)
            self._router(v_x1, x1_r, lgbf)
            self._experts(call, ts, x1_r, conv_r, v_x1)
            self._moe_tail(b, call, ts, x1_r, h_r, g_r)

    def _router(self, v_x1, x1_r, lgbf):
        blg, blt = 4, 5
        if getattr(self, "_dbg_rt", False):
            self.dump("rimg", self.routimg[:, :], bf16=True)
            self.dump("rbbD", self.bias[:, self.bc["rbb"][0]:self.bc["rbb"][0] + 8])
        for tt in range(4):
            w = self.claim_bank(blg)
            for kc in range(NCH):
                self.tmm(self.PS[0:8, blg * 512:blg * 512 + 128],
                         self.routimg[:, kc * 8:(kc + 1) * 8],
                         x1_r(kc)[:, tt * 128:(tt + 1) * 128],
                         kc == 0, kc == NCH - 1,
                         waits=(w + [(self.sv, v_x1)]) if kc == 0 else [],
                         tag=f"rt{tt}")
            t_lg = self.st.n
            self.sop(lambda e, tt=tt: e.add(
                lgbf[0:8, tt * 128:(tt + 1) * 128],
                self.PS[0:8, blg * 512:blg * 512 + 128],
                self.rbcol[0:8, 0:1]),
                waits=[(self.st, t_lg)], tag=f"lgev{tt}")
            self.bank_read(blg, self.ss, self.ss.n)
        s_lg = self.ss.n
        if getattr(self, "_dbg_rt", False):
            self.dump("lgF", lgbf[0:8, :], bf16=True)
        for tt in range(4):
            w = self.claim_bank(blt)
            self.tmm(self.PS[:, blt * 512:blt * 512 + 8],
                     lgbf[0:8, tt * 128:(tt + 1) * 128], self.idb[0:8, 0:8],
                     True, True, waits=(w + [(self.ss, s_lg)]) if tt == 0 else w,
                     tag=f"lgT{tt}")
            t_lgT = self.st.n
            lg = self.lgf[:, tt * 8:(tt + 1) * 8]
            m1n = self.m1n[:, 0:1]
            m2n = self.m1n[:, 1:2]
            rec = self.m1n[:, 2:3]
            rec2 = self.m1n[:, 12:13]
            msk = self.m1n[:, 4:12]
            msk2 = self.m1n[:, 16:24]
            self.vop(lambda e, lg=lg: e.tensor_copy(
                lg, self.PS[:, blt * 512:blt * 512 + 8]),
                waits=[(self.st, t_lgT)], tag=f"lgadd{tt}")
            self.bank_read(blt, self.sv, self.sv.n)
            W = lambda: [(self.sv, self.sv.n)]
            self.vop(lambda e, lg=lg, m1n=m1n: e.reduce_max(m1n, lg, AX, negate=True),
                     waits=W(), tag=f"t1_{tt}")
            self.vop(lambda e, lg=lg, m1n=m1n, msk=msk: e.tensor_scalar(
                msk, lg, m1n, 0.0, ALU.add, ALU.is_ge), waits=W(), tag=f"i1_{tt}")
            self.vop(lambda e, lg=lg, msk=msk, msk2=msk2: e.scalar_tensor_tensor(
                msk2, msk, -1e30, lg, ALU.mult, ALU.add), waits=W(), tag=f"zz_{tt}")
            self.vop(lambda e, m2n=m2n, msk2=msk2: e.reduce_max(m2n, msk2, AX,
                                                               negate=True),
                     waits=W(), tag=f"t2_{tt}")
            self.vop(lambda e, lg=lg, m2n=m2n, msk=msk: e.tensor_scalar(
                msk, lg, m2n, 0.0, ALU.add, ALU.is_ge), waits=W(), tag=f"i2_{tt}")
            v_i = self.sv.n
            self.sop(lambda e, lg=lg, m1n=m1n: e.activation(lg, lg, ACT.Exp, bias=m1n),
                     waits=[(self.sv, v_i)], tag=f"gt_{tt}")
            s_g = self.ss.n
            self.vop(lambda e, lg=lg, msk=msk: e.tensor_mul(lg, lg, msk),
                     waits=[(self.ss, s_g)], tag=f"gm_{tt}")
            self.vop(lambda e, lg=lg, rec=rec: e.reduce_sum(rec, lg, AX),
                     waits=W(), tag=f"dn_{tt}")
            self.vop(lambda e, rec=rec, rec2=rec2: e.reciprocal(rec2, rec),
                     waits=W(), tag=f"rc_{tt}")
            self.vop(lambda e, lg=lg, rec2=rec2, tt=tt: e.tensor_scalar(
                self.rwtok[:, tt * 8:(tt + 1) * 8], lg, rec2, None, ALU.mult,
                ALU.bypass), waits=W(), tag=f"rw_{tt}")
        self.vop(lambda e: e.tensor_copy(self.lgf[:, 0:8], self.lgf[:, 0:8]),
                 waits=[(self.sv, self.sv.n)], tag="rwflush")
        self.v_rw = self.sv.n
        if getattr(self, "_dbg_rt", False):
            self.dump("lg", self.lgf[:, :])
            self.dump("rwt", self.rwtok[:, :], bf16=True)

    def _experts(self, call, ts, x1_r, conv_r, v_x1):
        v_rw = self.v_rw
        def eload(e_i):
            slot = self.S1 if e_i % 2 == 0 else self.S2
            k = 1 if e_i % 2 == 0 else 2
            sem, val = self.slot_free[k]
            wl = [(self.sd, self.img_ready[f"e{e_i}"]), (sem, val)]
            if not getattr(self, "_prol_clear", False):
                wl += [self.prol_s1_free, self.prol_s2_free, self.prol_g]
            self.dma("sync", slot[:], self.img[f"e{e_i}"], waits=wl, tag=f"el{e_i}")
            self.eready[k] = self.sd.n
            if e_i >= 6:
                self._prol_clear = True
        eload(0)
        eload(1)
        self._esilu_free = self.sv.n
        brb = 5
        rw_trw, rw_s, rw_tb, rw_v = {}, {}, {}, {}
        def emit_rwT(e_i):
            t_rw = {}
            for tt in range(4):
                bkt = 4 + (tt + 2) % 4
                w = self.claim_bank(bkt)
                self.tmm(self.PS[0:1, bkt * 512:bkt * 512 + 128],
                         self.rwtok[:, tt * 8 + e_i:tt * 8 + e_i + 1], self.idb[:],
                         True, True, waits=(w + [(self.sv, v_rw)]) if tt == 0 else w,
                         tag=f"rwT{e_i}")
                t_rw[tt] = self.st.n
            rw_trw[e_i] = t_rw
        def emit_rwr(e_i):
            t_rw = rw_trw[e_i]
            for tt in range(4):
                bkt = 4 + (tt + 2) % 4
                self.sop(lambda e, tt=tt, bkt=bkt: e.copy(
                    self.rwrow[0:1, tt * 128:(tt + 1) * 128],
                    self.PS[0:1, bkt * 512:bkt * 512 + 128]),
                    waits=[(self.st, t_rw[tt])], tag=f"rwr{e_i}")
                self.bank_read(bkt, self.ss, self.ss.n)
            rw_s[e_i] = self.ss.n
        def emit_rwbMM(e_i):
            w = self.claim_bank(brb)
            self.tmm(self.PSb(brb), self.onesb[0:1, :], self.rwrow[0:1, :], True, True,
                     waits=w + [(self.ss, rw_s[e_i])], tag=f"rwb{e_i}")
            rw_tb[e_i] = self.st.n
        def emit_rwbe(e_i):
            self.vop(lambda e: e.tensor_copy(self.rwb[:, :], self.PSb(brb)),
                     waits=[(self.st, rw_tb[e_i])], tag=f"rwbe{e_i}")
            rw_v[e_i] = self.sv.n
            self.bank_read(brb, self.sv, rw_v[e_i])
        emit_rwT(0)
        emit_rwr(0)
        emit_rwbMM(0)
        emit_rwbe(0)
        for e_i in range(NE):
            slot = self.S1 if e_i % 2 == 0 else self.S2
            k = 1 if e_i % 2 == 0 else 2
            v_rwb = rw_v[e_i]
            if getattr(self, "_dbg_rt", False) and e_i == 0:
                self.dump("rwb0", self.rwb[:, :], bf16=True)
            eri = self.eready[k]
            for half in range(2):
                t_e = {}
                for j in range(4):
                    oc = half * 4 + j
                    w = self.claim_bank(j)
                    for kc in range(NCH):
                        self.tmm(self.PSb(j),
                                 slot[:, kc * KB + oc * 128:kc * KB + (oc + 1) * 128],
                                 x1_r(kc), kc == 0, kc == NCH - 1,
                                 waits=(w + [(self.sd, eri), (self.sv, v_x1)])
                                 if kc == 0 else [], tag=f"ex{e_i}_{oc}")
                    t_e[j] = self.st.n
                if half == 0 and e_i + 1 < NE:
                    emit_rwT(e_i + 1)
                if half == 1:
                    self.slot_free[k] = (self.st, self.st.n)
                    if e_i + 2 < NE:
                        eload(e_i + 2)
                    if e_i + 1 < NE:
                        emit_rwbMM(e_i + 1)
                s_si = {}
                for j in range(4):
                    oc = half * 4 + j
                    dst = self.UA[:, 13 * KB + j * TS:13 * KB + (j + 1) * TS]
                    self.sop(lambda e, j=j, e_i=e_i, oc=oc, dst=dst: e.activation(
                        dst, self.PSb(j), ACT.Silu,
                        bias=self.bcol("ebc", e_i * 8 + oc)),
                        waits=[(self.st, t_e[j]), (self.sv, self._esilu_free)],
                        tag=f"es{e_i}_{oc}")
                    s_si[j] = self.ss.n
                    self.bank_read(j, self.ss, s_si[j])
                if half == 0 and e_i + 1 < NE:
                    emit_rwr(e_i + 1)
                for j in range(4):
                    oc = half * 4 + j
                    sc = conv_r(oc)
                    srcb = self.UA[:, 13 * KB + j * TS:13 * KB + (j + 1) * TS]
                    self.vop(lambda e, sc=sc, srcb=srcb: e.tensor_mul(
                        sc, srcb, self.rwb[:, :]),
                        waits=[(self.ss, s_si[j]), (self.sv, v_rwb)],
                        tag=f"em{e_i}_{oc}")
                    cb = self.comb[:, oc * TS:(oc + 1) * TS]
                    if e_i == 0:
                        self.vop(lambda e, sc=sc, cb=cb: e.tensor_copy(cb, sc),
                                 tag=f"ec{e_i}_{oc}")
                    else:
                        self.vop(lambda e, sc=sc, cb=cb: e.tensor_add(cb, cb, sc),
                                 tag=f"ea{e_i}_{oc}")
                if half == 1 and e_i + 1 < NE:
                    emit_rwbe(e_i + 1)
                self._esilu_free = self.sv.n

    def _moe_tail(self, b, call, ts, x1_r, h_r, g_r):
        v_comb = self.sv.n
        if getattr(self, "_dbg_rt", False):
            self.dump("cb0", self.comb[:, 0:TS])
        sem, val = self.slot_free[1]
        self.dma("sync", self.S1[:], self.img["ow"],
                 waits=[(self.sd, self.img_ready["ow"]), (sem, val)], tag="owld")
        ow_rdy = self.sd.n
        for wave in range(2):
            wbanks = []
            for bk in range(4):
                wbanks += self.claim_bank(bk)
            t_prev_kc = 0
            for kc in range(NCH):
                cs = self.cstage[:, 0:TS]
                cbsrc = self.comb[:, kc * TS:(kc + 1) * TS]
                cw = [(self.sv, v_comb)] if kc == 0 else [(self.st, t_prev_kc)]
                self.vop(lambda e, cs=cs, cbsrc=cbsrc: e.tensor_copy(cs, cbsrc),
                         waits=cw, tag=f"mc{wave}_{kc}")
                v_cs = self.sv.n
                for j in range(4):
                    oc = wave * 4 + j
                    self.tmm(self.PSb(j),
                             self.S1[:, kc * KB + oc * 128:kc * KB + (oc + 1) * 128],
                             cs, kc == 0, kc == NCH - 1,
                             waits=(wbanks + [(self.sd, ow_rdy), (self.sv, v_cs)])
                             if (kc == 0 and j == 0) else
                             ([(self.sv, v_cs)] if j == 0 else []),
                             inc=(j == 3), tag=f"mm{wave}_{kc}_{j}")
                t_prev_kc = self.st.n
            t_moe = self.st.n
            self.slot_free[1] = (self.st, t_moe)
            for j in range(4):
                oc = wave * 4 + j
                ps = self.PSb(j)
                sr = self.sres[:, oc * TS:(oc + 1) * TS]
                sb = h_r(oc)
                x1s = x1_r(oc)
                self.vop(lambda e, ps=ps, sr=sr, oc=oc, x1s=x1s: e.scalar_tensor_tensor(
                    sr, ps, self.bcol("outb", oc), x1s, ALU.add, ALU.add),
                    waits=[(self.st, t_moe)], tag=f"sr{oc}")
                self.vop(lambda e, sr=sr, sb=sb: e.tensor_copy(sb, sr), tag=f"sb{oc}")
                self.bank_read(j, self.sv, self.sv.n)
        if getattr(self, "_dbg_rt", False):
            self.dump("sr0", self.sres[:, 0:TS])
        ua = self.UA
        sq = ua[:, 13 * KB + TS:13 * KB + 2 * TS]
        bsq = 5
        w = self.claim_bank(bsq)
        t_prev = 0
        for c in range(NCH):
            self.vop(lambda e, c=c, sq=sq: e.tensor_mul(sq, h_r(c), h_r(c)),
                     waits=[(self.st, t_prev)] if c > 0 else [], tag=f"rq{c}")
            v_s = self.sv.n
            self.tmm(self.PSb(bsq)[0:1, 0:TS], self.onesb[:, 0:1], sq,
                     c == 0, c == NCH - 1,
                     waits=(w if c == 0 else []) + [(self.sv, v_s)], inc=True,
                     tag=f"rqs{c}")
            t_prev = self.st.n
        t_done = self.st.n
        rq, rt, rbf = self.row_q, self.row_t, self.row_bf
        self.vop(lambda e: e.tensor_scalar_mul(rq[0:1, :], self.PSb(bsq)[0:1, 0:TS],
                                               1.0 / D),
                 waits=[(self.st, t_done)], tag="rmq")
        self.bank_read(bsq, self.sv, self.sv.n)
        self.vop(lambda e: e.tensor_scalar_add(rq[0:1, :], rq[0:1, :], EPS_RMS),
                 tag="rme")
        v_var = self.sv.n
        self.sop(lambda e: e.activation(rt[0:1, :], rq[0:1, :], ACT.Sqrt),
                 waits=[(self.sv, v_var)], tag="rmsd")
        s_sd = self.ss.n
        self.vop(lambda e: e.reciprocal(rq[0:1, :], rt[0:1, :]),
                 waits=[(self.ss, s_sd)], tag="rmrc")
        if getattr(self, "_dbg_rt", False):
            self.dump("rmsrow", rq[0:1, :])
        self.vop(lambda e: e.tensor_copy(rbf[0:1, 2 * TS:3 * TS], rq[0:1, :]),
                 waits=[(self.st, getattr(self, "rbf_free", 0))], tag="rmhi")
        self.vop(lambda e: e.tensor_sub(rbf[0:1, 3 * TS:4 * TS], rq[0:1, :],
                                        rbf[0:1, 2 * TS:3 * TS]), tag="rmlo")
        v_rows = self.sv.n
        bk = 6
        w = self.claim_bank(bk)
        self.tmm(self.PSb(bk), self.onesb[0:1, :], rbf[0:1, 2 * TS:3 * TS], True, False,
                 waits=w + [(self.sv, v_rows)], inc=True, tag="rmb1")
        self.tmm(self.PSb(bk), self.onesb[0:1, :], rbf[0:1, 3 * TS:4 * TS], False, True,
                 tag="rmb2")
        t_b = self.st.n
        self.rbf_free = t_b
        psb = self.PSb(bk)
        self.vop(lambda e, psb=psb: e.tensor_copy(self.rb_[:, :], psb),
                 waits=[(self.st, t_b)], tag="rmbe")
        self.bank_read(bk, self.sv, self.sv.n)
        if getattr(self, "_dbg_rt", False):
            self.sop(lambda e, psb=psb: e.copy(self.mb[:, :], psb),
                     waits=[(self.st, t_b)], tag="dbgb6")
            self.bank_read(bk, self.ss, self.ss.n)
            self.dump("bk6D", self.mb[:, :])
            self.dump("rmhiD", rbf[0:1, 2 * TS:3 * TS], bf16=True)
            self.dump("rb2b", self.rb_[:, :])
        for oc in range(NCH):
            sr = self.sres[:, oc * TS:(oc + 1) * TS]
            self.vop(lambda e, sr=sr: e.tensor_mul(self.silu[:, :], sr, self.rb_[:, :]),
                     tag=f"h{oc}")
            self.vop(lambda e, oc=oc: e.tensor_scalar(
                h_r(oc), self.silu[:, :], self.bcol("nw", oc), None, ALU.mult,
                ALU.bypass), tag=f"hb{oc}")
            Rv = self.R[:, oc * S + ts * TS:oc * S + (ts + 1) * TS]
            self.vop(lambda e, Rv=Rv, oc=oc: e.scalar_tensor_tensor(
                Rv, self.silu[:, :], self.bcol("nwh", oc), Rv, ALU.mult, ALU.add),
                tag=f"Rh{oc}")
        v_h = self.sv.n
        if getattr(self, "_dbg_rt", False):
            self.dump("rb2", self.rb_[:, :])
            self.dump("h0", h_r(0), bf16=True)
        sem, val = self.slot_free[1]
        self.dma("sync", self.S1[:, 0:8 * HID], self.img["ffd"],
                 waits=[(self.sd, self.img_ready["ffd"]), (sem, val)], tag="ffdld")
        self.dma("sync", self.S1[:, 8 * HID:8 * HID + 2 * KB], self.img["ffu"],
                 waits=[(self.sd, self.img_ready["ffu"])], tag="ffuld")
        ff_rdy = self.sd.n
        s_gs = []
        for mc in range(2):
            bk = 4 + mc
            w = self.claim_bank(bk)
            for kc in range(NCH):
                self.tmm(self.PSb(bk),
                         self.S1[:, kc * HID + mc * 128:kc * HID + (mc + 1) * 128],
                         h_r(kc), kc == 0, kc == NCH - 1,
                         waits=(w + [(self.sd, ff_rdy), (self.sv, v_h)])
                         if kc == 0 else [], tag=f"fd{mc}")
            t_f = self.st.n
            self.sop(lambda e, bk=bk, mc=mc: e.activation(
                g_r(mc), self.PSb(bk), ACT.Gelu, bias=self.bcol("ffdb", mc)),
                waits=[(self.st, t_f)], tag=f"ge{mc}")
            self.bank_read(bk, self.ss, self.ss.n)
        s_g = self.ss.n
        if getattr(self, "_dbg_rt", False):
            self.dump("g0", g_r(0), bf16=True)
        for oc in range(NCH):
            bk = oc % 4
            w = self.claim_bank(bk)
            for hc in range(2):
                self.tmm(self.PSb(bk),
                         self.S1[:, 8 * HID + hc * KB + oc * 128:
                                 8 * HID + hc * KB + (oc + 1) * 128],
                         g_r(hc), hc == 0, hc == 1,
                         waits=(w + [(self.ss, s_g)]) if hc == 0 else [], tag=f"fu{oc}")
            t_u = self.st.n
            Rv = self.R[:, oc * S + ts * TS:oc * S + (ts + 1) * TS]
            self.vop(lambda e, bk=bk, Rv=Rv, oc=oc: e.scalar_tensor_tensor(
                Rv, self.PSb(bk), self.bcol("ffubh", oc), Rv, ALU.add, ALU.add),
                waits=[(self.st, t_u)], tag=f"Rf{oc}")
            self.bank_read(bk, self.sv, self.sv.n)
        self.slot_free[1] = (self.st, self.st.n)
        if getattr(self, "_dbg_rt", False):
            self.dump("Rf0", self.R[:, 0:TS])

    def attn(self, b):
        ua = self.UA
        yln = lambda c: ua[:, c * S:(c + 1) * S]
        q1, k1, rsc = self.q1, self.k1, self.rsc
        rf, db = self.recipf, self.denb
        self.vop(lambda e: e.memset(self.vhat[:, :], 1.0),
                 waits=[(self.st, self.st.n), (self.ss, self.ss.n)], tag="vhat1")
        for ts in range(NTS):
            t0 = ts * TS
            Rc = lambda c, t0=t0: self.R[:, c * S + t0:c * S + t0 + TS]
            entry = [(self.st, self.st.n), (self.ss, self.ss.n)] if ts == 0 else ()
            self.stats(Rc, True, EPS_LN, f"A{b}_{ts}", entry_waits=entry,
                       scratch=(self.probs0[:, 0:TS], self.probs0[:, TS:2 * TS]))
            for c in range(NCH):
                Rsrc = Rc(c)
                dst = yln(c)[:, t0:t0 + TS]
                self.vop(lambda e, Rsrc=Rsrc: e.tensor_sub(self.silu[:, :], Rsrc,
                                                           self.mb[:, :]),
                         tag=f"alns{c}")
                self.vop(lambda e, dst=dst: e.tensor_mul(dst, self.silu[:, :],
                                                         self.rb_[:, :]),
                         tag=f"alnm{c}")
        v_yln = self.sv.n
        slots = {}
        for i, w in enumerate(["wq", "wk", "wv"]):
            sl = [self.S0, self.S1, self.S2][i]
            sem, val = self.slot_free[i]
            self.dma("sync", sl[:], self.img[w],
                     waits=[(self.sd, self.img_ready[w]), (sem, val)],
                     tag=f"a_{w}ld")
            slots[w] = sl
        w_rdy = self.sd.n
        pair_idx = 0
        for hp in range(NCH):
            for (wname, buf, bs, bias) in [("wq", q1, 0, "bq"), ("wk", k1, 4, "bk")]:
                for nb in range(4):
                    wb = self.claim_bank(bs + nb)
                    for kc in range(NCH):
                        self.tmm(self.PSb(bs + nb),
                                 slots[wname][:, kc * KB + hp * 128:kc * KB + (hp + 1) * 128],
                                 yln(kc)[:, nb * TS:(nb + 1) * TS],
                                 kc == 0, kc == NCH - 1,
                                 waits=(wb + [(self.sd, w_rdy), (self.sv, v_yln)])
                                 if kc == 0 else [], tag=f"{wname}p{hp}_{nb}")
                t_p = self.st.n
                self.sop(lambda e, buf=buf, bs=bs, hp=hp, bias=bias: e.add(
                    buf[:, :], self.PS[:, bs * 512:bs * 512 + 2048],
                    self.bcol(bias, hp)),
                    waits=[(self.st, t_p)], tag=f"{wname}e{hp}")
                s_e = self.ss.n
                for nb in range(4):
                    self.bank_read(bs + nb, self.ss, s_e)
                for nb in range(4):
                    wb = self.claim_bank(bs + nb)
                    self.tmm(self.PSb(bs + nb), self.perm[:, :],
                             buf[:, nb * TS:(nb + 1) * TS], True, True,
                             waits=(wb + [(self.ss, s_e)]) if nb == 0 else wb,
                             tag=f"pm{hp}_{bs}_{nb}")
                t_pm = self.st.n
                ps = self.PS[:, bs * 512:bs * 512 + 2048]
                self.vop(lambda e, ps=ps: e.tensor_tensor(rsc[:, :], ps,
                                                          self.sin[:, :], ALU.mult),
                         waits=[(self.st, t_pm)], tag=f"rs{hp}_{bs}")
                for nb in range(4):
                    self.bank_read(bs + nb, self.sv, self.sv.n)
                self.vop(lambda e, buf=buf: e.tensor_mul(buf[:, :], buf[:, :],
                                                         self.cos[:, :]),
                         waits=[(self.st, t_pm)], tag=f"rc{hp}_{bs}")
                self.vop(lambda e, buf=buf: e.tensor_add(buf[:, :], buf[:, :],
                                                         rsc[:, :]),
                         tag=f"ra{hp}_{bs}")
            v_rope = self.sv.n
            if b == 0 and hp == 0 and self.debug:
                self.dump("ylnD", yln(0)[:, 0:512], stage2=True)
                self.dump("q1D", q1[:, 0:512], stage2=True)
                self.dump("k1D", k1[:, 0:512], stage2=True)
            for ktq in range(4):
                t_v = {}
                for dk in range(4):
                    kt = 4 * ktq + dk
                    wb = self.claim_bank(dk)
                    for kc in range(NCH):
                        self.tmm(self.PS[:, dk * 512:dk * 512 + 128],
                                 yln(kc)[:, kt * 128:(kt + 1) * 128],
                                 slots["wv"][:, kc * KB + hp * 128:kc * KB + (hp + 1) * 128],
                                 kc == 0, False,
                                 waits=wb if kc == 0 else [], tag=f"vp{hp}_{kt}")
                    self.tmm(self.PS[:, dk * 512:dk * 512 + 128], self.onesb[0:1, :],
                             self.bvrow[0:1, hp * 128:(hp + 1) * 128], False, True,
                             tag=f"vb{hp}_{kt}")
                    t_v[dk] = self.st.n
                for dk in range(4):
                    kt = 4 * ktq + dk
                    for h in range(2):
                        self.sop(lambda e, dk=dk, kt=kt, h=h: e.copy(
                            self.vhat[:, kt * 130 + 65 * h:kt * 130 + 65 * h + 64],
                            self.PS[:, dk * 512 + 64 * h:dk * 512 + 64 * h + 64]),
                            waits=[(self.st, t_v[dk])] if h == 0 else [],
                            tag=f"ve{hp}_{kt}_{h}")
                    self.bank_read(dk, self.ss, self.ss.n)
            s_vhat = self.ss.n
            W = lambda: [(self.sv, self.sv.n)]
            bk6 = 6
            for i, (buf, h) in enumerate([(q1, 0), (q1, 1), (k1, 0), (k1, 1)]):
                self.vop(lambda e, buf=buf, h=h: e.tensor_mul(
                    rsc[64 * h:64 * h + 64, :], buf[64 * h:64 * h + 64, :],
                    buf[64 * h:64 * h + 64, :]),
                    waits=W(), tag=f"nsq{hp}_{i}")
                v_sq = self.sv.n
                wbs = []
                for j in range(4):
                    wbs += self.claim_bank(4 + j)
                for j in range(4):
                    self.tmm(self.PS[0:1, (4 + j) * 512:(4 + j) * 512 + 512],
                             self.onesb[64 * h:64 * h + 64, 0:1],
                             rsc[64 * h:64 * h + 64, j * 512:(j + 1) * 512],
                             True, True,
                             waits=(wbs + [(self.sv, v_sq)]) if j == 0 else [],
                             inc=(j == 3), tag=f"nrm{hp}_{i}_{j}")
                t_n = self.st.n
                self.vop(lambda e, i=i: e.reduce_max(rf[0:1, 4 + i:5 + i],
                                                     self.PS[0:1, 2048:4096], AX),
                         waits=[(self.st, t_n)], tag=f"nmx{hp}_{i}")
                for j in range(4):
                    self.bank_read(4 + j, self.sv, self.sv.n)
            s_cells = self.sv.n
            self.vop(lambda e: e.tensor_copy(rf[0:1, 4:8], rf[0:1, 4:8]),
                     waits=[(self.sv, s_cells)], tag=f"nflush{hp}")
            if b == 0 and hp == 0 and self.debug:
                self.dump("celD", rf[0:1, 0:16], stage2=True)
            self.vop(lambda e: e.tensor_mul(rf[0:1, 8:10], rf[0:1, 4:6],
                                            rf[0:1, 6:8]),
                     waits=[(self.sv, self.sv.n)], tag=f"cm{hp}")
            v_cm = self.sv.n
            self.sop(lambda e: e.activation(rf[0:1, 10:12], rf[0:1, 8:10], ACT.Sqrt),
                     waits=[(self.sv, v_cm)], tag=f"cs{hp}")
            s_cs = self.ss.n
            self.vop(lambda e: e.tensor_scalar(db[0:1, 4:6], rf[0:1, 10:12],
                                               -0.125, None, ALU.mult, ALU.bypass),
                     waits=[(self.ss, s_cs)], tag=f"cnn{hp}")
            v_cn = self.sv.n
            for h in range(2):
                wb = self.claim_bank(bk6)
                self.tmm(self.PS[:, bk6 * 512:bk6 * 512 + 1], self.onesb[0:1, :],
                         db[0:1, 4 + h:5 + h], True, True,
                         waits=wb + [(self.sv, v_cn)], tag=f"cb{hp}_{h}")
                t_c = self.st.n
                self.vop(lambda e, h=h: e.tensor_copy(
                    self.bcol("c0", h), self.PS[:, bk6 * 512:bk6 * 512 + 1]),
                    waits=[(self.st, t_c), (self.ss, getattr(self, "c0_free", 0))],
                    tag=f"ce{hp}_{h}")
                self.bank_read(bk6, self.sv, self.sv.n)
            v_c0 = self.sv.n
            if b == 0 and hp == 0 and self.debug:
                self.dump("vhD", self.vhat[:, 0:260], stage2=True)
                self.dump("c0D", self.bcol("c0", 0, 2), stage2=True)
            probs_free = {h: (self.st, 0) for h in range(2)}
            PBASE = [0, 2, 6]
            for qb in range(4):
                avw = {0: self.claim_bank(4), 1: self.claim_bank(5)}
                def sc_mm(m, h, first, qb=qb):
                    pb = PBASE[(2 * m + h) % 3]
                    for dk in range(2):
                        kt = 2 * m + dk
                        bk = pb + dk
                        wb = self.claim_bank(bk)
                        self.tmm(self.PS[0:128, bk * 512:(bk + 1) * 512],
                                 k1[64 * h:64 * h + 64, kt * 128:(kt + 1) * 128],
                                 q1[64 * h:64 * h + 64, qb * 512:(qb + 1) * 512],
                                 True, True,
                                 waits=wb + ([(self.sv, v_c0), (self.sv, v_rope)]
                                             if first else []),
                                 tp=(64 * h, 0), tag=f"sc{hp}_{qb}_{kt}_{h}",
                                 inc=True)
                    return self.st.n
                t_sc = {(0, 0): sc_mm(0, 0, qb == 0), (0, 1): sc_mm(0, 1, False)}
                for m in range(8):
                    exp_done = {}
                    for h in range(2):
                        pb = PBASE[(2 * m + h) % 3]
                        prb = self.probs0 if h == 0 else self.probs1
                        sem, val = probs_free[h]
                        self.sop(lambda e, prb=prb, pb=pb, h=h: e.activation(
                            prb[:, 0:1024], self.PS[0:128, pb * 512:pb * 512 + 1024],
                            ACT.Exp, bias=self.bcol("c0", h), scale=0.125),
                            waits=[(self.st, t_sc[(m, h)]), (sem, val),
                                   (self.ss, s_vhat)],
                            tag=f"ex{hp}_{qb}_{m}_{h}")
                        exp_done[h] = self.ss.n
                        self.bank_read(pb, self.ss, self.ss.n)
                        self.bank_read(pb + 1, self.ss, self.ss.n)
                    if m + 1 < 8:
                        t_sc[(m + 1, 0)] = sc_mm(m + 1, 0, False)
                    for h in range(2):
                        prb = self.probs0 if h == 0 else self.probs1
                        for dk in range(2):
                            kt = 2 * m + dk
                            self.tmm(self.PS[0:65, (4 + h) * 512:(5 + h) * 512],
                                     self.vhat[:, kt * 130 + 65 * h:kt * 130 + 65 * (h + 1)],
                                     prb[:, dk * 512:(dk + 1) * 512],
                                     kt == 0, kt == 15,
                                     waits=(avw[h] if kt == 0 else []) +
                                     ([(self.ss, exp_done[h])] if dk == 0 else []),
                                     tag=f"av{hp}_{qb}_{kt}_{h}", inc=True)
                        probs_free[h] = (self.st, self.st.n)
                    if m + 1 < 8:
                        t_sc[(m + 1, 1)] = sc_mm(m + 1, 1, False)
                t_av = self.st.n
                if b == 0 and hp == 0 and qb == 0 and self.debug:
                    self.dump("pr0", self.probs0[:, 512:1024], stage2=True)
                for h in range(2):
                    avps = self.PS[:, (4 + h) * 512:(5 + h) * 512]
                    self.sop(lambda e, avps=avps: e.activation(
                        rf[64:65, :], avps[64:65, :], ACT.Ln),
                        waits=[(self.st, t_av)], tag=f"dn{hp}_{qb}_{h}")
                    self.sop(lambda e: e.activation(db[64:65, 0:512], rf[64:65, :],
                                                    ACT.Exp, scale=-1.0),
                             tag=f"dx{hp}_{qb}_{h}")
                    s_dn = self.ss.n
                    bkb = 6
                    wb = self.claim_bank(bkb)
                    self.tmm(self.PS[0:64, bkb * 512:bkb * 512 + 512],
                             self.onesb[64:65, 0:64], db[64:65, 0:512],
                             True, True, waits=wb + [(self.ss, s_dn)],
                             tag=f"dbc{hp}_{qb}_{h}", inc=True)
                    t_db = self.st.n
                    self.vop(lambda e: e.tensor_copy(rf[0:64, :],
                             self.PS[0:64, bkb * 512:bkb * 512 + 512]),
                             waits=[(self.st, t_db)], tag=f"dbe{hp}_{qb}_{h}")
                    v_dbe = self.sv.n
                    self.bank_read(bkb, self.sv, v_dbe)
                    if h == 0:
                        dst = self.quarter[0:64,
                                           (hp % 2) * S + qb * 512:(hp % 2) * S + qb * 512 + 512]
                    else:
                        dst = self.stage[0:64, qb * 512:qb * 512 + 512]
                    self.vop(lambda e, avps=avps, dst=dst: e.tensor_mul(
                        dst, avps[0:64, :], rf[0:64, :]),
                        waits=[(self.sv, v_dbe)], tag=f"qn{hp}_{qb}_{h}")
                    self.bank_read(4 + h, self.sv, self.sv.n)
            self.c0_free = self.ss.n
            if b == 0 and hp == 0 and self.debug:
                self.dump("dnD", db[64:65, 0:512], stage2=True)
                self.dump("qrD", self.quarter[0:64, 0:512], stage2=True)
            v_q = self.sv.n
            self.dma("gpsimd",
                     self.quarter[64:128, (hp % 2) * S:(hp % 2) * S + S],
                     self.stage[0:64, 0:S],
                     waits=[(self.sv, v_q), (self.sd, self.prologue_done)],
                     tag=f"hshift{hp}")
            self.qshift_done = self.sg.n
            if hp % 2 == 1:
                wlist = [(self.sd, self.img_ready["wo"]), (self.sg, self.qshift_done)]
                if hasattr(self, "wotsl_free"):
                    wlist.append((self.st, self.wotsl_free))
                self.dma("sync", self.wotsl[:, 0:KB],
                         self.img["wo"][:, (hp - 1) * KB:hp * KB],
                         waits=wlist, tag=f"wold{hp}a")
                self.dma("sync", self.wotsl[:, KB:2 * KB],
                         self.img["wo"][:, hp * KB:(hp + 1) * KB], tag=f"wold{hp}b")
                wo_rdy = self.sd.n
                for oc in range(NCH):
                    for tb in range(4):
                        bk = (oc * 4 + tb) % 2 + 6
                        wb = self.claim_bank(bk)
                        for kc2 in range(2):
                            self.tmm(self.PSb(bk),
                                     self.wotsl[:, kc2 * KB + oc * 128:kc2 * KB + (oc + 1) * 128],
                                     self.quarter[:, kc2 * S + tb * 512:kc2 * S + (tb + 1) * 512],
                                     kc2 == 0, kc2 == 1,
                                     waits=(wb + [(self.sd, wo_rdy),
                                                  (self.sg, self.qshift_done)])
                                     if kc2 == 0 else [], tag=f"op{hp}_{oc}_{tb}")
                        t_o = self.st.n
                        Rv = self.R[:, oc * S + tb * 512:oc * S + (tb + 1) * 512]
                        ps = self.PSb(bk)
                        if pair_idx == 0:
                            self.vop(lambda e, ps=ps, Rv=Rv, oc=oc: e.scalar_tensor_tensor(
                                Rv, ps, self.bcol("bo", oc), Rv, ALU.add, ALU.add),
                                waits=[(self.st, t_o)], tag=f"Ro{hp}_{oc}_{tb}")
                        else:
                            self.vop(lambda e, ps=ps, Rv=Rv: e.tensor_add(Rv, Rv, ps),
                                     waits=[(self.st, t_o)], tag=f"Ro{hp}_{oc}_{tb}")
                        self.bank_read(bk, self.sv, self.sv.n)
                self.wotsl_free = self.st.n
                pair_idx += 1
        self.slot_free[0] = (self.st, self.st.n)
        self.slot_free[1] = (self.st, self.st.n)
        self.slot_free[2] = (self.st, self.st.n)
        if b == 0 and self.debug:
            self.dump("RaD", self.R[:, 0:512], stage2=True)

    def seg(self):
        self.segs.append(len(self.plan))

    def build(self):
        self.declare()
        self.alloc()
        self.sems()
        self.prologue()
        self.seg()
        for b in range(self.nbatch):
            self.load_R(b)
            self.seg()
            for ph in self.phases:
                if ph == "ffn1":
                    self.ffn(b, 0)
                elif ph == "attn":
                    self.attn(b)
                elif ph == "ffn2":
                    self.ffn(b, 1)
                self.seg()
            self.store_R(b)
            self.seg()
        self.plan.append(("sync", lambda e: e.wait_ge(self.so.h, self.so.n)))
        nc = self.nc
        with nc.Block() as block:
            cur = 0
            while cur < len(self.plan):
                eng = self.plan[cur][0]
                group = []
                while cur < len(self.plan) and self.plan[cur][0] == eng:
                    group.append(self.plan[cur][1])
                    cur += 1
                def section(eh, group=group):
                    for f in group:
                        f(eh)
                getattr(block, eng)(section)
        for g in reversed(self._semctx):
            g.__exit__(None, None, None)
        return nc


def make_consts():
    j = np.arange(0, DH, 2, dtype=np.float64) / DH
    invf = 1.0 / (10000.0 ** j)
    t = np.arange(S, dtype=np.float64)
    fr = t[None, :] * invf[:, None]
    cos = np.repeat(np.cos(fr), 2, axis=0)
    sin = np.repeat(np.sin(fr), 2, axis=0)
    sin_eff = sin.copy()
    sin_eff[:32] *= -1.0
    cosT = np.concatenate([cos, cos], axis=0).astype(ml_dtypes.bfloat16)
    sinT = np.concatenate([sin_eff, sin_eff], axis=0).astype(ml_dtypes.bfloat16)
    perm = np.zeros((128, 128), dtype=np.float32)
    for blk in range(2):
        for i in range(64):
            perm[blk * 64 + (i + 32) % 64, blk * 64 + i] = 1.0
    return {
        "c_cos": np.ascontiguousarray(cosT),
        "c_sin": np.ascontiguousarray(sinT),
        "c_perm": perm.astype(ml_dtypes.bfloat16),
        "c_ones": np.ones((128, 128), dtype=ml_dtypes.bfloat16),
        "c_idf": np.eye(128, dtype=np.float32),
        "c_idb": np.eye(128, dtype=ml_dtypes.bfloat16),
    }


_BUILD_CACHE = {}


def get_nc(phases=("ffn1", "attn", "ffn2")):
    key = tuple(phases)
    if key not in _BUILD_CACHE:
        _BUILD_CACHE[key] = Build(phases=phases).build()
    return _BUILD_CACHE[key]


def kernel(**inputs):
    consts = make_consts()
    nc = get_nc()
    in_maps = []
    for core in range(8):
        m = dict(consts)
        for k, v in inputs.items():
            v = np.ascontiguousarray(np.asarray(v))
            m[k] = v[2 * core:2 * core + 2] if k == "L0_attn_x" else v
        in_maps.append(m)
    res = run_bass_kernel_spmd(nc, in_maps, core_ids=list(range(8)))
    return np.concatenate([r["y"] for r in res.results], axis=0)


if __name__ == "__main__":
    bld = Build(phases=("ffn1",))
    nc = bld.build()
    print("build ok; instructions:", nc.n_instructions())


# version 08
# speedup vs baseline: 1.6719x; 1.0062x over previous
import numpy as np
import ml_dtypes
import concourse.bass as bass
import concourse.mybir as mybir
from concourse.bass_utils import run_bass_kernel_spmd

F32 = mybir.dt.float32
BF16 = mybir.dt.bfloat16
ALU = mybir.AluOpType
ACT = mybir.ActivationFunctionType
AX = mybir.AxisListType.X

D = 1024
S = 2048
NBATCH = 2
NCH = 8
H = 16
DH = 64
NE = 8
HID = 256
TS = 512
NTS = S // TS
QB = 256
EPS_LN = 1e-5
EPS_RMS = 1e-6
KB = 1024
BIG1024 = [1024, 1024]


class Ctr:
    __slots__ = ("h", "n", "name")
    def __init__(self, h, name):
        self.h = h; self.n = 0; self.name = name


class Build:
    def __init__(self, phases=("ffn1", "attn", "ffn2"), nbatch=NBATCH, debug=()):
        self.phases = phases
        self.nbatch = nbatch
        self.debug = list(debug)
        self.dbg_slots = {}
        self.nc = bass.Bass("TRN2", target_bir_lowering=False)
        self.plan = []
        self.segs = []
        self.last_wait = {}

    def op(self, eng, fn, waits=(), inc=None, inc_k=1, tag=""):
        waits2 = []
        for (c, v) in waits:
            if v <= 0:
                continue
            assert v <= c.n, f"forward wait on {c.name}: {v} > {c.n} ({tag})"
            key = (eng, id(c))
            if self.last_wait.get(key, -1) < v:
                self.last_wait[key] = v
                waits2.append((c.h, v))
        if inc is not None:
            inc.n += inc_k
        ic = inc
        def run(eh, waits2=waits2, fn=fn, ic=ic, inc_k=inc_k):
            for (hh, v) in waits2:
                eh.wait_ge(hh, v)
            r = fn(eh)
            if ic is not None:
                r.then_inc(ic.h, inc_k)
        self.plan.append((eng, run))
        return inc.n if inc is not None else None

    def dma(self, q, out_ap, in_ap, waits=(), tag="", **kw):
        c = self.sd if q == "sync" else self.sg
        return self.op(q, lambda e: e.dma_start(out_ap, in_ap, **kw),
                       waits=waits, inc=c, inc_k=16, tag=tag)

    def tmm(self, out, lhsT, rhs, start, stop, waits=(), inc=False, tp=None, tag=""):
        kw = {}
        if tp is not None:
            kw["tile_position"] = tp
        return self.op("tensor",
                       lambda e: e.matmul(out, lhsT, rhs, start=start, stop=stop, **kw),
                       waits=waits, inc=self.st if (inc or stop) else None, tag=tag)

    def ttr(self, out, in_, ident, waits=(), tag=""):
        return self.op("tensor", lambda e: e.transpose(out, in_, ident),
                       waits=waits, inc=self.st, tag=tag)

    def vop(self, fn, waits=(), tag=""):
        return self.op("vector", fn, waits=waits, inc=self.sv, tag=tag)

    def sop(self, fn, waits=(), tag=""):
        return self.op("scalar", fn, waits=waits, inc=self.ss, tag=tag)

    def claim_bank(self, k):
        w = self.bank_free[k]
        self.bank_free[k] = []
        return list(w)

    def bank_read(self, k, ctr, val):
        self.bank_free[k].append((ctr, val))

    def PSb(self, k, c0=0, c1=512):
        return self.PS[:, 512 * k + c0:512 * k + c1]

    def declare(self):
        nc = self.nc
        self.ins = {}
        specs = {
            "L0_attn_x": [self.nbatch, S, D], "L0_attn_wq": BIG1024,
            "L0_attn_bq": [D], "L0_attn_wk": BIG1024, "L0_attn_bk": [D],
            "L0_attn_wv": BIG1024, "L0_attn_bv": [D], "L0_attn_wo": BIG1024,
            "L0_attn_bo": [D], "L0_ffn_dw_kernel": [D, 1, 3], "L0_ffn_dw_bias": [D],
            "L0_ffn_pw_w": BIG1024, "L0_ffn_pw_b": [D], "L0_ffn_router_w": [8, D],
            "L0_ffn_router_b": [8], "L0_ffn_expert_w": [NE, D, D],
            "L0_ffn_expert_b": [NE, D], "L0_ffn_out_w": BIG1024, "L0_ffn_out_b": [D],
            "L0_ffn_norm_w": [D], "L0_ffn_ff_down_w": [HID, D],
            "L0_ffn_ff_down_b": [HID], "L0_ffn_ff_up_w": [D, HID], "L0_ffn_ff_up_b": [D],
            "c_cos": [128, S], "c_sin": [128, S], "c_perm": [128, 128],
            "c_ones": [128, 128], "c_idf": [128, 128], "c_idb": [128, 128],
        }
        bf16_names = {"c_cos", "c_sin", "c_perm", "c_ones", "c_idb"}
        for k, sh in specs.items():
            dt = BF16 if k in bf16_names else F32
            self.ins[k] = nc.dram_tensor(k, sh, dt, kind="ExternalInput").ap()
        self.out = nc.dram_tensor("y", [self.nbatch, S, D], F32, kind="ExternalOutput").ap()
        if self.debug:
            self.dbgt = nc.dram_tensor("dbg", [len(self.debug), 128, 512], F32,
                                       kind="ExternalOutput").ap()
        self.nat = {}
        self.img = {}
        for w in ["wq", "wk", "wv", "wo", "pw", "ow"] + [f"e{i}" for i in range(NE)]:
            self.nat[w] = nc.dram_tensor("nat_" + w, BIG1024, BF16, kind="Internal").ap()
            self.img[w] = nc.dram_tensor("img_" + w, [128, 8 * KB], BF16, kind="Internal").ap()
        self.nat["ffd"] = nc.dram_tensor("nat_ffd", [HID, D], BF16, kind="Internal").ap()
        self.img["ffd"] = nc.dram_tensor("img_ffd", [128, 8 * HID], BF16, kind="Internal").ap()
        self.nat["ffu"] = nc.dram_tensor("nat_ffu", [D, HID], BF16, kind="Internal").ap()
        self.img["ffu"] = nc.dram_tensor("img_ffu", [128, 2 * KB], BF16, kind="Internal").ap()

    def alloc(self):
        nc = self.nc
        A = nc.alloc_sbuf_tensor_at
        arena = nc.alloc_sbuf_tensor("ARENA", [128, 212800], mybir.dt.uint8)
        amloc = nc.lookup_mloc(arena)
        o = [amloc.addr]
        def at(name, shape, dtype):
            t = A(name, shape, dtype, offset=o[0])
            o[0] += int(np.prod(shape[1:])) * mybir.dt.size(dtype)
            o[0] = (o[0] + 31) & ~31
            return t
        self.R = at("R", [128, NCH * S], F32)
        self.cos = at("cosT", [128, S], BF16)
        self.sin = at("sinT", [128, S], BF16)
        self.perm = at("permT", [128, 128], BF16)
        self.onesb = at("onesb", [128, 128], BF16)
        self.idf = at("idf", [128, 128], F32)
        self.idb = at("idb", [128, 128], BF16)
        self.bias = at("biasc", [128, 240], F32)
        self.routimg = at("routimg", [128, 64], BF16)
        self.rbcol = at("rbcol", [128, 1], F32)
        self.bvrow = at("bvrow", [1, KB], BF16)
        self.S0 = at("S0", [128, 8 * KB], BF16)
        self.S1 = at("S1", [128, 8 * KB], BF16)
        self.S2 = at("S2", [128, 8 * KB], BF16)
        ua_base = o[0]
        self.UA = at("UA", [128, 16 * KB], BF16)
        ub = o[0]
        self.yb = A("yb", [128, NCH * (TS + 2)], F32, offset=ub)
        self.sres = A("sres", [128, NCH * TS], F32, offset=ub)
        self.comb = A("comb", [128, NCH * TS], F32, offset=ub + 16448)
        self.probs0 = A("probs0", [128, 2 * TS], BF16, offset=ub)
        self.probs1 = A("probs1", [128, 2 * TS], BF16, offset=ub + 2048)
        self.vhat = A("vhat", [128, 16 * 130], BF16, offset=ub + 4096)
        self.q1 = A("q1", [128, S], BF16, offset=ub + 8320)
        self.k1 = A("k1", [128, S], BF16, offset=ub + 12416)
        self.rsc = A("ropesc", [128, S], BF16, offset=ub + 16512)
        o[0] = ((ub + 33248) + 31) & ~31
        uc = o[0]
        self.row_m = A("row_m", [1, TS], F32, offset=uc)
        self.row_q = A("row_q", [1, TS], F32, offset=uc + 2048)
        self.row_t = A("row_t", [1, TS], F32, offset=uc + 4096)
        self.row_bf = A("row_bf", [1, 4 * TS], BF16, offset=uc + 6144)
        self.mb = A("mb", [128, TS], F32, offset=uc + 10240)
        self.rb_ = A("rb_", [128, TS], F32, offset=uc + 12288)
        self.silu = A("siluf", [128, TS], F32, offset=uc + 14336)
        self.rwrow = A("rwrow", [1, TS], BF16, offset=uc + 10240)
        self.rwb = A("rwb", [128, TS], BF16, offset=uc + 11264)
        self.lgf = A("lgf", [128, 32], F32, offset=uc + 16384)
        self.rwtok = A("rwtok", [128, 32], BF16, offset=uc + 16512)
        self.m1n = A("m1n", [128, 24], F32, offset=uc + 16576)
        self.cstage = A("cstage", [128, TS], BF16, offset=uc + 16672)
        self.quarter = A("quarter", [128, 2 * S], BF16, offset=uc)
        self.wotsl = A("wotsl", [128, S], BF16, offset=uc + 8192)
        self.stage = A("stage", [128, S], BF16, offset=uc + 12288)
        self.recipf = A("recipf", [128, 512], F32, offset=uc + 16384)
        self.denb = A("denb", [128, 512], BF16, offset=uc + 18432)
        self.xstage = A("xstage", [128, KB], F32, offset=uc + 12288)
        o[0] = uc + 19456
        if self.debug:
            self.dbgstage = A("dbgstage", [128, 512], F32, offset=ua_base + 28672)
            self.dbgstage2 = A("dbgstage2", [128, 512], F32, offset=ub + 20608)
        assert o[0] <= amloc.addr + 212800, o[0]
        self.PS = nc.alloc_psum_tensor("PS", [128, 4096], F32)
        self.bc = {}
        names = ["bq", "bk", "bv", "bo", "dwb", "pwb", "outb", "nw", "nwh",
                 "ffub", "ffubh"]
        for i, name in enumerate(names):
            self.bc[name] = (8 * i, 8)
        self.bc["ffdb"] = (88, 2)
        self.bc["ebc"] = (90, 64)
        self.bc["kern"] = (154, 24)
        self.bc["rbb"] = (178, 8)
        self.bc["c0"] = (186, 2)

    def bcol(self, name, j=0, w=1):
        c0, n = self.bc[name]
        assert j + w <= n, (name, j, w)
        return self.bias[:, c0 + j:c0 + j + w]

    def sems(self):
        nc = self.nc
        self._semctx = []
        def mk(name):
            g = nc.semaphore(name)
            h = g.__enter__()
            self._semctx.append(g)
            return Ctr(h, name)
        self.sd = mk("sd"); self.sg = mk("sg"); self.st = mk("st")
        self.sv = mk("sv"); self.ss = mk("ss"); self.so = mk("so")
        self.bank_free = {k: [] for k in range(8)}
        self.slot_free = {0: (self.st, 0), 1: (self.st, 0), 2: (self.st, 0)}
        self.eready = {1: 0, 2: 0}

    def dump(self, name, ap, bf16=False, stage2=False):
        if name not in self.debug:
            return
        slot = self.debug.index(name)
        stg = self.dbgstage2 if stage2 else self.dbgstage
        waits = [(c, c.n) for c in (self.st, self.sv, self.ss, self.sg)]
        nr, ncols = ap.shape[0], ap.shape[-1]
        self.vop(lambda e: e.tensor_copy(stg[0:nr, 0:ncols], ap),
                 waits=waits, tag="dbgc_" + name)
        v = self.sv.n
        self.dma("sync", self.dbgt[slot], stg[:, :],
                 waits=[(self.sv, v)], tag="dbg_" + name)
        self.vop(lambda e: e.tensor_copy(self.m1n[:, 3:4], self.m1n[:, 3:4]),
                 waits=[(self.sd, self.sd.n)], tag="dbgw_" + name)

    def prologue(self):
        ins = self.ins
        self.dma("sync", self.cos[:], ins["c_cos"], tag="ccos")
        self.dma("sync", self.sin[:], ins["c_sin"], tag="csin")
        self.dma("sync", self.perm[:], ins["c_perm"], tag="cperm")
        self.dma("sync", self.onesb[:], ins["c_ones"], tag="cones")
        self.dma("sync", self.idf[:], ins["c_idf"], tag="cidf")
        self.dma("sync", self.idb[:], ins["c_idb"], tag="cidb")
        for name, src in [("bq", "L0_attn_bq"), ("bk", "L0_attn_bk"),
                          ("bv", "L0_attn_bv"), ("bo", "L0_attn_bo"),
                          ("dwb", "L0_ffn_dw_bias"), ("pwb", "L0_ffn_pw_b"),
                          ("outb", "L0_ffn_out_b"), ("nw", "L0_ffn_norm_w"),
                          ("ffub", "L0_ffn_ff_up_b")]:
            c0, _ = self.bc[name]
            self.dma("sync", self.bias[:, c0:c0 + 8],
                     ins[src].rearrange("(c p) -> p c", p=128), tag="b_" + name, allow_slow_non_contiguous=True)
        c0, _ = self.bc["ffdb"]
        self.dma("sync", self.bias[:, c0:c0 + 2],
                 ins["L0_ffn_ff_down_b"].rearrange("(c p) -> p c", p=128), tag="b_ffdb", allow_slow_non_contiguous=True)
        c0, _ = self.bc["ebc"]
        self.dma("sync", self.bias[:, c0:c0 + 64].rearrange("p (e c) -> p e c", e=8),
                 ins["L0_ffn_expert_b"].rearrange("e (c p) -> p e c", p=128), tag="b_ebc", allow_slow_non_contiguous=True)
        c0, _ = self.bc["kern"]
        self.dma("sync", self.bias[:, c0:c0 + 24].rearrange("p (c k) -> p c k", c=8),
                 ins["L0_ffn_dw_kernel"].rearrange("(c p) o k -> p c (o k)", p=128),
                 tag="b_kern", allow_slow_non_contiguous=True)
        self.dma("sync", self.rbcol[0:8, 0:1],
                 ins["L0_ffn_router_b"].rearrange("(e o) -> e o", o=1), tag="b_rb",
                 allow_slow_non_contiguous=True)
        n_const = self.sd.n
        self.const_ready = n_const

        self.sop(lambda e: e.mul(self.bcol("nwh", 0, 8), self.bcol("nw", 0, 8), 0.5),
                 waits=[(self.sd, n_const)], tag="nwh")
        self.sop(lambda e: e.mul(self.bcol("ffubh", 0, 8), self.bcol("ffub", 0, 8), 0.5),
                 tag="ffubh")

        self.dma("sync", self.rbcol[0:8, 0:1],
                 ins["L0_ffn_router_b"].rearrange("(e o) -> e o", o=1), tag="b_rb",
                 allow_slow_non_contiguous=True)
        n_const = self.sd.n
        order = ["pw"] + [f"e{i}" for i in range(NE)] + ["ow", "ffd", "ffu",
                                                         "wq", "wk", "wv", "wo"]
        src = {"pw": self.ins["L0_ffn_pw_w"], "ow": self.ins["L0_ffn_out_w"],
               "ffd": self.ins["L0_ffn_ff_down_w"], "ffu": self.ins["L0_ffn_ff_up_w"],
               "wq": self.ins["L0_attn_wq"], "wk": self.ins["L0_attn_wk"],
               "wv": self.ins["L0_attn_wv"], "wo": self.ins["L0_attn_wo"]}
        for i in range(NE):
            src[f"e{i}"] = self.ins["L0_ffn_expert_w"][i]
        self.img_ready = {}
        s1_free = 0
        s2_free = 0
        for wname in order:
            a = src[wname]
            rows, ncol = a.shape
            nch = rows // 128
            cast_view = self.S1[:, 0:nch * ncol].rearrange("p (c n) -> p c n", c=nch)
            self.dma("gpsimd", cast_view, a.rearrange("(c p) n -> p c n", p=128),
                     waits=[(self.sd, s1_free)], tag=f"cast_{wname}")
            g_cast = self.sg.n
            if wname == "ffu":
                self.sop(lambda e: e.mul(self.S1[:, 0:2 * KB], self.S1[:, 0:2 * KB], 0.5),
                         waits=[(self.sg, g_cast)], tag="ffu_half")
                store_waits = [(self.ss, self.ss.n)]
            else:
                store_waits = [(self.sg, g_cast)]
            self.dma("sync", self.nat[wname].rearrange("(c p) n -> p c n", p=128),
                     cast_view, waits=store_waits, tag=f"nat_{wname}")
            d_nat = self.sd.n
            s1_free = d_nat
            nkc = ncol // 128
            for c in range(nkc):
                self.dma("sync", self.S2[:, c * rows:(c + 1) * rows],
                         self.nat[wname][:, c * 128:(c + 1) * 128], transpose=True,
                         waits=[(self.sd, max(d_nat, s2_free))] if c == 0 else (),
                         tag=f"tr_{wname}_{c}")
            d_tr = self.sd.n
            self.dma("sync", self.img[wname][:, 0:nkc * rows], self.S2[:, 0:nkc * rows],
                     waits=[(self.sd, d_tr)], tag=f"img_{wname}")
            s2_free = self.sd.n
            self.img_ready[wname] = self.sd.n
        self.dma("gpsimd", self.S1[0:8, 0:KB], self.ins["L0_ffn_router_w"],
                 waits=[(self.sd, s1_free)], tag="cast_rout")
        g_r = self.sg.n
        for c in range(8):
            bk = 2 + (c % 2)
            w = self.claim_bank(bk)
            self.tmm(self.PS[:, bk * 512:bk * 512 + 8],
                     self.S1[0:8, c * 128:(c + 1) * 128], self.idb[0:8, 0:8],
                     True, True, waits=w + [(self.sg, g_r)], tag=f"routT{c}")
            t_n = self.st.n
            self.sop(lambda e, c=c, bk=bk: e.copy(self.routimg[:, c * 8:(c + 1) * 8],
                                                  self.PS[:, bk * 512:bk * 512 + 8]),
                     waits=[(self.st, t_n)], tag=f"routE{c}")
            self.bank_read(bk, self.ss, self.ss.n)
        self.dma("gpsimd", self.bvrow[0:1, :],
                 self.ins["L0_attn_bv"].rearrange("(o d) -> o d", o=1), tag="bvrow")
        self.prologue_done = self.sd.n
        self.slot_free[1] = (self.st, self.st.n)
        self.prol_s1_free = (self.sd, s1_free)
        self.prol_s2_free = (self.sd, s2_free)
        self.prol_g = (self.sg, self.sg.n)

    def load_R(self, b):
        first = True
        for tt in range(16):
            waits = []
            if first and b > 0:
                waits.append((self.so, self.so.n))
            if not first:
                waits.append((self.st, self.last_xT))
            d_x = self.dma("sync", self.xstage[:],
                           self.ins["L0_attn_x"][b, tt * 128:(tt + 1) * 128, :],
                           waits=waits, tag=f"xin{b}_{tt}")
            first = False
            for c in range(NCH):
                bk = c % 4
                w = self.claim_bank(bk)
                self.ttr(self.PS[:, bk * 512:bk * 512 + 128],
                         self.xstage[:, c * 128:(c + 1) * 128], self.idf[:],
                         waits=w + [(self.sd, d_x)], tag=f"xT{tt}_{c}")
                t_n = self.st.n
                self.vop(lambda e, c=c, tt=tt, bk=bk: e.tensor_copy(
                    self.R[:, c * S + tt * 128:c * S + (tt + 1) * 128],
                    self.PS[:, bk * 512:bk * 512 + 128]),
                    waits=[(self.st, t_n)], tag=f"xE{tt}_{c}")
                self.bank_read(bk, self.sv, self.sv.n)
            self.last_xT = self.st.n

    def store_R(self, b):
        prev_store = self.so.n
        for tt in range(16):
            for c in range(NCH):
                bk = c % 4
                w = self.claim_bank(bk)
                self.ttr(self.PS[:, bk * 512:bk * 512 + 128],
                         self.R[:, c * S + tt * 128:c * S + (tt + 1) * 128], self.idf[:],
                         waits=w + ([(self.sv, self.sv.n)] if (tt == 0 and c == 0) else []),
                         tag=f"oT{tt}_{c}")
                t_n = self.st.n
                self.vop(lambda e, c=c, bk=bk: e.tensor_copy(
                    self.xstage[:, c * 128:(c + 1) * 128],
                    self.PS[:, bk * 512:bk * 512 + 128]),
                    waits=[(self.st, t_n)] + ([(self.so, prev_store)] if c == 0 else []),
                    tag=f"oE{tt}_{c}")
                self.bank_read(bk, self.sv, self.sv.n)
            v_n = self.sv.n
            self.op("sync", lambda e, tt=tt, b=b: e.dma_start(
                self.out[b, tt * 128:(tt + 1) * 128, :], self.xstage[:]),
                waits=[(self.sv, v_n)], inc=self.so, inc_k=16, tag=f"xout{tt}")
            prev_store = self.so.n

    def stats(self, get_chunk, want_mean, eps, tag, entry_waits=(), scratch=None):
        ua = self.UA
        if scratch is None:
            xbf = ua[:, 13 * KB:13 * KB + TS]
            sq = ua[:, 13 * KB + TS:13 * KB + 2 * TS]
        else:
            xbf, sq = scratch
        bsum, bsq = 4, 5
        wsum = self.claim_bank(bsum)
        wsq = self.claim_bank(bsq)
        t_prev = 0
        for c in range(NCH):
            src = get_chunk(c)
            cw = list(entry_waits) if c == 0 else [(self.st, t_prev)]
            self.vop(lambda e, src=src, xbf=xbf: e.tensor_copy(xbf, src),
                     waits=cw, tag=f"{tag}cast{c}")
            v_c = self.sv.n
            self.vop(lambda e, xbf=xbf, sq=sq: e.tensor_mul(sq, xbf, xbf),
                     tag=f"{tag}sq{c}")
            v_s = self.sv.n
            if want_mean:
                self.tmm(self.PSb(bsum)[0:1, 0:TS], self.onesb[:, 0:1], xbf,
                         c == 0, c == NCH - 1,
                         waits=(wsum if c == 0 else []) + [(self.sv, v_c)],
                         inc=True, tag=f"{tag}ms{c}")
            self.tmm(self.PSb(bsq)[0:1, 0:TS], self.onesb[:, 0:1], sq,
                     c == 0, c == NCH - 1,
                     waits=(wsq if c == 0 else []) + [(self.sv, v_s)],
                     inc=True, tag=f"{tag}qs{c}")
            t_prev = self.st.n
        t_done = self.st.n
        rm, rq, rt, rbf = self.row_m, self.row_q, self.row_t, self.row_bf
        if tag == "F00_0" and self.debug:
            self.sop(lambda e: e.copy(self.silu[0:1, :], self.PSb(bsq)[0:1, 0:TS]),
                     waits=[(self.st, t_done)], tag="dbgq")
            self.bank_read(bsq, self.ss, self.ss.n)
            self.dump("qsumP", self.silu[0:1, :])
            self.sop(lambda e: e.copy(self.silu[0:1, :], self.PSb(bsum)[0:1, 0:TS]),
                     waits=[(self.st, t_done)], tag="dbgm")
            self.bank_read(bsum, self.ss, self.ss.n)
            self.dump("msumP", self.silu[0:1, :])
        if want_mean:
            self.vop(lambda e: e.tensor_scalar_mul(rm[0:1, :], self.PSb(bsum)[0:1, 0:TS],
                                                   1.0 / D),
                     waits=[(self.st, t_done)], tag=f"{tag}m")
        self.vop(lambda e: e.tensor_scalar_mul(rq[0:1, :], self.PSb(bsq)[0:1, 0:TS],
                                               1.0 / D),
                 waits=[(self.st, t_done)], tag=f"{tag}q")
        self.bank_read(bsum, self.sv, self.sv.n)
        self.bank_read(bsq, self.sv, self.sv.n)
        if want_mean:
            self.vop(lambda e: e.tensor_mul(rt[0:1, :], rm[0:1, :], rm[0:1, :]),
                     tag=f"{tag}mm")
            self.vop(lambda e: e.tensor_sub(rq[0:1, :], rq[0:1, :], rt[0:1, :]),
                     tag=f"{tag}var")
        self.vop(lambda e: e.tensor_scalar_add(rq[0:1, :], rq[0:1, :], eps),
                 tag=f"{tag}eps")
        v_var = self.sv.n
        if tag == "F00_0":
            self.dump("varrow", rq[0:1, :])
        self.sop(lambda e: e.activation(rt[0:1, :], rq[0:1, :], ACT.Ln),
                 waits=[(self.sv, v_var)], tag=f"{tag}sd")
        if tag == "F00_0" and self.debug:
            self.dump("sdrow", rt[0:1, :])
        self.sop(lambda e: e.activation(rq[0:1, :], rt[0:1, :], ACT.Exp, scale=-0.5),
                 tag=f"{tag}rc")
        s_sd = self.ss.n
        self.vop(lambda e: e.tensor_copy(rbf[0:1, 2 * TS:3 * TS], rq[0:1, :]),
                 waits=[(self.ss, s_sd)], tag=f"{tag}rhi")
        self.vop(lambda e: e.tensor_sub(rbf[0:1, 3 * TS:4 * TS], rq[0:1, :],
                                        rbf[0:1, 2 * TS:3 * TS]), tag=f"{tag}rlo")
        if want_mean:
            self.vop(lambda e: e.tensor_copy(rbf[0:1, 0:TS], rm[0:1, :]),
                     tag=f"{tag}mhi")
            self.vop(lambda e: e.tensor_sub(rbf[0:1, TS:2 * TS], rm[0:1, :],
                                            rbf[0:1, 0:TS]), tag=f"{tag}mlo")
        v_rows = self.sv.n
        if tag == "F00_0":
            self.dump("rrow", rq[0:1, :])
            self.dump("rhilo", rbf[0:1, 2 * TS:3 * TS], bf16=True)
        plan = [(self.rb_, 2 * TS, 3 * TS, 6)]
        if want_mean:
            plan.append((self.mb, 0, TS, 7))
        for (dst, hi, lo, bk) in plan:
            w = self.claim_bank(bk)
            self.tmm(self.PSb(bk), self.onesb[0:1, :], rbf[0:1, hi:hi + TS],
                     True, False, waits=w + [(self.sv, v_rows)], inc=True,
                     tag=f"{tag}bc1")
            self.tmm(self.PSb(bk), self.onesb[0:1, :], rbf[0:1, lo:lo + TS],
                     False, True, tag=f"{tag}bc2")
            t_b = self.st.n
            self.vop(lambda e, dst=dst, bk=bk: e.tensor_copy(dst[:, :], self.PSb(bk)),
                     waits=[(self.st, t_b)], tag=f"{tag}bce")
            self.bank_read(bk, self.sv, self.sv.n)
        return self.sv.n

    def ffn(self, b, call):
        ua = self.UA
        conv_r = lambda c: ua[:, c * TS:(c + 1) * TS]
        x1_r = lambda c: ua[:, 4 * KB + c * TS:4 * KB + (c + 1) * TS]
        h_r = lambda c: ua[:, 8 * KB + c * TS:8 * KB + (c + 1) * TS]
        g_r = lambda m: ua[:, 12 * KB + m * TS:12 * KB + (m + 1) * TS]
        lgbf = ua[:, 15 * KB:15 * KB + 512]

        sem, val = self.slot_free[0]
        self.dma("sync", self.S0[:], self.img["pw"],
                 waits=[(self.sd, self.img_ready["pw"]), (sem, val)], tag=f"pwld{call}")
        pw_rdy = self.sd.n

        for ts in range(NTS):
            t0 = ts * TS
            Rc = lambda c: self.R[:, c * S + t0:c * S + t0 + TS]
            entry = [(self.st, self.st.n), (self.ss, self.ss.n)] if ts == 0 else ()
            if b == 0 and call == 0 and ts == 0:
                self.dump("R0", Rc(0))
            self.stats(Rc, True, EPS_LN, f"F{call}{b}_{ts}", entry_waits=entry)
            if b == 0 and call == 0 and ts == 0:
                self.dump("mb", self.mb[:, :])
                self.dump("rb", self.rb_[:, :])
            if ts == 0:
                self.vop(lambda e: e.memset(
                    self.yb[:, :].rearrange("p (c n) -> p c n", c=NCH)[:, :, 0:2], 0.0),
                    tag="halo0")
            for c in range(NCH):
                yv = self.yb[:, c * (TS + 2) + 2:c * (TS + 2) + 2 + TS]
                Rsrc = Rc(c)
                self.vop(lambda e, Rsrc=Rsrc, yv=yv: e.tensor_sub(yv, Rsrc, self.mb[:, :]),
                         tag=f"ysub{c}")
                self.vop(lambda e, yv=yv: e.tensor_mul(yv, yv, self.rb_[:, :]),
                         tag=f"ymul{c}")
            for c in range(NCH):
                yb0 = self.yb[:, c * (TS + 2):c * (TS + 2) + TS]
                yb1 = self.yb[:, c * (TS + 2) + 1:c * (TS + 2) + 1 + TS]
                yb2 = self.yb[:, c * (TS + 2) + 2:c * (TS + 2) + 2 + TS]
                self.vop(lambda e, yb0=yb0, c=c: e.tensor_scalar(
                    self.silu[:, :], yb0, self.bcol("kern", 3 * c), self.bcol("dwb", c),
                    ALU.mult, ALU.add), tag=f"cv0_{c}")
                self.vop(lambda e, yb1=yb1, c=c: e.scalar_tensor_tensor(
                    self.silu[:, :], yb1, self.bcol("kern", 3 * c + 1), self.silu[:, :],
                    ALU.mult, ALU.add), tag=f"cv1_{c}")
                self.vop(lambda e, yb2=yb2, c=c: e.scalar_tensor_tensor(
                    conv_r(c), yb2, self.bcol("kern", 3 * c + 2), self.silu[:, :],
                    ALU.mult, ALU.add), tag=f"cv2_{c}")
            v_conv = self.sv.n
            if b == 0 and call == 0 and ts == 0:
                self.dump("y0", self.yb[:, 2:2 + TS])
                self.dump("cv0", conv_r(0), bf16=True)
            if ts < NTS - 1:
                for c in range(NCH):
                    self.vop(lambda e, c=c: e.tensor_copy(
                        self.yb[:, c * (TS + 2):c * (TS + 2) + 2],
                        self.yb[:, c * (TS + 2) + TS:c * (TS + 2) + TS + 2]),
                        tag=f"halo{c}")
            v_silu_free = self.sv.n
            for half in range(2):
                t_pw = {}
                for j in range(4):
                    oc = half * 4 + j
                    w = self.claim_bank(j)
                    for kc in range(NCH):
                        self.tmm(self.PSb(j),
                                 self.S0[:, kc * KB + oc * 128:kc * KB + (oc + 1) * 128],
                                 conv_r(kc), kc == 0, kc == NCH - 1,
                                 waits=(w + [(self.sv, v_conv), (self.sd, pw_rdy)])
                                 if kc == 0 else [], tag=f"pw{oc}")
                    t_pw[j] = self.st.n
                s_si = {}
                for j in range(4):
                    oc = half * 4 + j
                    dst = self.UA[:, 13 * KB + j * TS:13 * KB + (j + 1) * TS]
                    self.sop(lambda e, j=j, oc=oc, dst=dst: e.activation(
                        dst, self.PSb(j), ACT.Silu, bias=self.bcol("pwb", oc)),
                        waits=[(self.st, t_pw[j]), (self.sv, v_silu_free)],
                        tag=f"psl{oc}")
                    s_si[j] = self.ss.n
                    self.bank_read(j, self.ss, s_si[j])
                for j in range(4):
                    oc = half * 4 + j
                    yv = self.yb[:, oc * (TS + 2) + 2:oc * (TS + 2) + 2 + TS]
                    srcb = self.UA[:, 13 * KB + j * TS:13 * KB + (j + 1) * TS]
                    self.vop(lambda e, oc=oc, yv=yv, srcb=srcb: e.tensor_add(
                        x1_r(oc), yv, srcb),
                        waits=[(self.ss, s_si[j])], tag=f"x1_{oc}")
                v_silu_free = self.sv.n
            v_x1 = self.sv.n
            if b == 0 and call == 0 and ts == 0:
                self.dump("x10", x1_r(0), bf16=True)
            self._dbg_rt = (b == 0 and call == 0 and ts == 0)
            self._router(v_x1, x1_r, lgbf)
            self._experts(call, ts, x1_r, conv_r, v_x1)
            self._moe_tail(b, call, ts, x1_r, h_r, g_r)

    def _router(self, v_x1, x1_r, lgbf):
        blg, blt = 4, 5
        if getattr(self, "_dbg_rt", False):
            self.dump("rimg", self.routimg[:, :], bf16=True)
            self.dump("rbbD", self.bias[:, self.bc["rbb"][0]:self.bc["rbb"][0] + 8])
        for tt in range(4):
            w = self.claim_bank(blg)
            for kc in range(NCH):
                self.tmm(self.PS[0:8, blg * 512:blg * 512 + 128],
                         self.routimg[:, kc * 8:(kc + 1) * 8],
                         x1_r(kc)[:, tt * 128:(tt + 1) * 128],
                         kc == 0, kc == NCH - 1,
                         waits=(w + [(self.sv, v_x1)]) if kc == 0 else [],
                         tag=f"rt{tt}")
            t_lg = self.st.n
            self.sop(lambda e, tt=tt: e.add(
                lgbf[0:8, tt * 128:(tt + 1) * 128],
                self.PS[0:8, blg * 512:blg * 512 + 128],
                self.rbcol[0:8, 0:1]),
                waits=[(self.st, t_lg)], tag=f"lgev{tt}")
            self.bank_read(blg, self.ss, self.ss.n)
        s_lg = self.ss.n
        if getattr(self, "_dbg_rt", False):
            self.dump("lgF", lgbf[0:8, :], bf16=True)
        for tt in range(4):
            w = self.claim_bank(blt)
            self.tmm(self.PS[:, blt * 512:blt * 512 + 8],
                     lgbf[0:8, tt * 128:(tt + 1) * 128], self.idb[0:8, 0:8],
                     True, True, waits=(w + [(self.ss, s_lg)]) if tt == 0 else w,
                     tag=f"lgT{tt}")
            t_lgT = self.st.n
            lg = self.lgf[:, tt * 8:(tt + 1) * 8]
            m1n = self.m1n[:, 0:1]
            m2n = self.m1n[:, 1:2]
            rec = self.m1n[:, 2:3]
            rec2 = self.m1n[:, 12:13]
            msk = self.m1n[:, 4:12]
            msk2 = self.m1n[:, 16:24]
            self.vop(lambda e, lg=lg: e.tensor_copy(
                lg, self.PS[:, blt * 512:blt * 512 + 8]),
                waits=[(self.st, t_lgT)], tag=f"lgadd{tt}")
            self.bank_read(blt, self.sv, self.sv.n)
            W = lambda: [(self.sv, self.sv.n)]
            self.vop(lambda e, lg=lg, m1n=m1n: e.reduce_max(m1n, lg, AX, negate=True),
                     waits=W(), tag=f"t1_{tt}")
            self.vop(lambda e, lg=lg, m1n=m1n, msk=msk: e.tensor_scalar(
                msk, lg, m1n, 0.0, ALU.add, ALU.is_ge), waits=W(), tag=f"i1_{tt}")
            self.vop(lambda e, lg=lg, msk=msk, msk2=msk2: e.scalar_tensor_tensor(
                msk2, msk, -1e30, lg, ALU.mult, ALU.add), waits=W(), tag=f"zz_{tt}")
            self.vop(lambda e, m2n=m2n, msk2=msk2: e.reduce_max(m2n, msk2, AX,
                                                               negate=True),
                     waits=W(), tag=f"t2_{tt}")
            self.vop(lambda e, lg=lg, m2n=m2n, msk=msk: e.tensor_scalar(
                msk, lg, m2n, 0.0, ALU.add, ALU.is_ge), waits=W(), tag=f"i2_{tt}")
            v_i = self.sv.n
            self.sop(lambda e, lg=lg, m1n=m1n: e.activation(lg, lg, ACT.Exp, bias=m1n),
                     waits=[(self.sv, v_i)], tag=f"gt_{tt}")
            s_g = self.ss.n
            self.vop(lambda e, lg=lg, msk=msk: e.tensor_mul(lg, lg, msk),
                     waits=[(self.ss, s_g)], tag=f"gm_{tt}")
            self.vop(lambda e, lg=lg, rec=rec: e.reduce_sum(rec, lg, AX),
                     waits=W(), tag=f"dn_{tt}")
            self.vop(lambda e, rec=rec, rec2=rec2: e.reciprocal(rec2, rec),
                     waits=W(), tag=f"rc_{tt}")
            self.vop(lambda e, lg=lg, rec2=rec2, tt=tt: e.tensor_scalar(
                self.rwtok[:, tt * 8:(tt + 1) * 8], lg, rec2, None, ALU.mult,
                ALU.bypass), waits=W(), tag=f"rw_{tt}")
        self.vop(lambda e: e.tensor_copy(self.lgf[:, 0:8], self.lgf[:, 0:8]),
                 waits=[(self.sv, self.sv.n)], tag="rwflush")
        self.v_rw = self.sv.n
        if getattr(self, "_dbg_rt", False):
            self.dump("lg", self.lgf[:, :])
            self.dump("rwt", self.rwtok[:, :], bf16=True)

    def _experts(self, call, ts, x1_r, conv_r, v_x1):
        v_rw = self.v_rw
        def eload(e_i):
            slot = self.S1 if e_i % 2 == 0 else self.S2
            k = 1 if e_i % 2 == 0 else 2
            sem, val = self.slot_free[k]
            wl = [(self.sd, self.img_ready[f"e{e_i}"]), (sem, val)]
            if not getattr(self, "_prol_clear", False):
                wl += [self.prol_s1_free, self.prol_s2_free, self.prol_g]
            self.dma("sync", slot[:], self.img[f"e{e_i}"], waits=wl, tag=f"el{e_i}")
            self.eready[k] = self.sd.n
            if e_i >= 6:
                self._prol_clear = True
        eload(0)
        eload(1)
        self._esilu_free = self.sv.n
        brb = 5
        rw_trw, rw_s, rw_tb, rw_v = {}, {}, {}, {}
        def emit_rwT(e_i):
            t_rw = {}
            for tt in range(4):
                bkt = 4 + (tt + 2) % 4
                w = self.claim_bank(bkt)
                self.tmm(self.PS[0:1, bkt * 512:bkt * 512 + 128],
                         self.rwtok[:, tt * 8 + e_i:tt * 8 + e_i + 1], self.idb[:],
                         True, True, waits=(w + [(self.sv, v_rw)]) if tt == 0 else w,
                         tag=f"rwT{e_i}")
                t_rw[tt] = self.st.n
            rw_trw[e_i] = t_rw
        def emit_rwr(e_i):
            t_rw = rw_trw[e_i]
            for tt in range(4):
                bkt = 4 + (tt + 2) % 4
                self.sop(lambda e, tt=tt, bkt=bkt: e.copy(
                    self.rwrow[0:1, tt * 128:(tt + 1) * 128],
                    self.PS[0:1, bkt * 512:bkt * 512 + 128]),
                    waits=[(self.st, t_rw[tt])], tag=f"rwr{e_i}")
                self.bank_read(bkt, self.ss, self.ss.n)
            rw_s[e_i] = self.ss.n
        def emit_rwbMM(e_i):
            w = self.claim_bank(brb)
            self.tmm(self.PSb(brb), self.onesb[0:1, :], self.rwrow[0:1, :], True, True,
                     waits=w + [(self.ss, rw_s[e_i])], tag=f"rwb{e_i}")
            rw_tb[e_i] = self.st.n
        def emit_rwbe(e_i):
            self.vop(lambda e: e.tensor_copy(self.rwb[:, :], self.PSb(brb)),
                     waits=[(self.st, rw_tb[e_i])], tag=f"rwbe{e_i}")
            rw_v[e_i] = self.sv.n
            self.bank_read(brb, self.sv, rw_v[e_i])
        emit_rwT(0)
        emit_rwr(0)
        emit_rwbMM(0)
        emit_rwbe(0)
        for e_i in range(NE):
            slot = self.S1 if e_i % 2 == 0 else self.S2
            k = 1 if e_i % 2 == 0 else 2
            v_rwb = rw_v[e_i]
            if getattr(self, "_dbg_rt", False) and e_i == 0:
                self.dump("rwb0", self.rwb[:, :], bf16=True)
            eri = self.eready[k]
            for half in range(2):
                t_e = {}
                for j in range(4):
                    oc = half * 4 + j
                    w = self.claim_bank(j)
                    for kc in range(NCH):
                        self.tmm(self.PSb(j),
                                 slot[:, kc * KB + oc * 128:kc * KB + (oc + 1) * 128],
                                 x1_r(kc), kc == 0, kc == NCH - 1,
                                 waits=(w + [(self.sd, eri), (self.sv, v_x1)])
                                 if kc == 0 else [], tag=f"ex{e_i}_{oc}")
                    t_e[j] = self.st.n
                if half == 0 and e_i + 1 < NE:
                    emit_rwT(e_i + 1)
                if half == 1:
                    self.slot_free[k] = (self.st, self.st.n)
                    if e_i + 2 < NE:
                        eload(e_i + 2)
                    if e_i + 1 < NE:
                        emit_rwbMM(e_i + 1)
                s_si = {}
                for j in range(4):
                    oc = half * 4 + j
                    dst = self.UA[:, 13 * KB + j * TS:13 * KB + (j + 1) * TS]
                    self.sop(lambda e, j=j, e_i=e_i, oc=oc, dst=dst: e.activation(
                        dst, self.PSb(j), ACT.Silu,
                        bias=self.bcol("ebc", e_i * 8 + oc)),
                        waits=[(self.st, t_e[j]), (self.sv, self._esilu_free)],
                        tag=f"es{e_i}_{oc}")
                    s_si[j] = self.ss.n
                    self.bank_read(j, self.ss, s_si[j])
                if half == 0 and e_i + 1 < NE:
                    emit_rwr(e_i + 1)
                for j in range(4):
                    oc = half * 4 + j
                    sc = conv_r(oc)
                    srcb = self.UA[:, 13 * KB + j * TS:13 * KB + (j + 1) * TS]
                    self.vop(lambda e, sc=sc, srcb=srcb: e.tensor_mul(
                        sc, srcb, self.rwb[:, :]),
                        waits=[(self.ss, s_si[j]), (self.sv, v_rwb)],
                        tag=f"em{e_i}_{oc}")
                    cb = self.comb[:, oc * TS:(oc + 1) * TS]
                    if e_i == 0:
                        self.vop(lambda e, sc=sc, cb=cb: e.tensor_copy(cb, sc),
                                 tag=f"ec{e_i}_{oc}")
                    else:
                        self.vop(lambda e, sc=sc, cb=cb: e.tensor_add(cb, cb, sc),
                                 tag=f"ea{e_i}_{oc}")
                if half == 1 and e_i + 1 < NE:
                    emit_rwbe(e_i + 1)
                self._esilu_free = self.sv.n

    def _moe_tail(self, b, call, ts, x1_r, h_r, g_r):
        v_comb = self.sv.n
        if getattr(self, "_dbg_rt", False):
            self.dump("cb0", self.comb[:, 0:TS])
        sem, val = self.slot_free[1]
        self.dma("sync", self.S1[:], self.img["ow"],
                 waits=[(self.sd, self.img_ready["ow"]), (sem, val)], tag="owld")
        ow_rdy = self.sd.n
        for wave in range(2):
            wbanks = []
            for bk in range(4):
                wbanks += self.claim_bank(bk)
            t_prev_kc = 0
            for kc in range(NCH):
                cs = self.cstage[:, 0:TS]
                cbsrc = self.comb[:, kc * TS:(kc + 1) * TS]
                cw = [(self.sv, v_comb)] if kc == 0 else [(self.st, t_prev_kc)]
                self.vop(lambda e, cs=cs, cbsrc=cbsrc: e.tensor_copy(cs, cbsrc),
                         waits=cw, tag=f"mc{wave}_{kc}")
                v_cs = self.sv.n
                for j in range(4):
                    oc = wave * 4 + j
                    self.tmm(self.PSb(j),
                             self.S1[:, kc * KB + oc * 128:kc * KB + (oc + 1) * 128],
                             cs, kc == 0, kc == NCH - 1,
                             waits=(wbanks + [(self.sd, ow_rdy), (self.sv, v_cs)])
                             if (kc == 0 and j == 0) else
                             ([(self.sv, v_cs)] if j == 0 else []),
                             inc=(j == 3), tag=f"mm{wave}_{kc}_{j}")
                t_prev_kc = self.st.n
            t_moe = self.st.n
            self.slot_free[1] = (self.st, t_moe)
            for j in range(4):
                oc = wave * 4 + j
                ps = self.PSb(j)
                sr = self.sres[:, oc * TS:(oc + 1) * TS]
                sb = h_r(oc)
                x1s = x1_r(oc)
                self.vop(lambda e, ps=ps, sr=sr, oc=oc, x1s=x1s: e.scalar_tensor_tensor(
                    sr, ps, self.bcol("outb", oc), x1s, ALU.add, ALU.add),
                    waits=[(self.st, t_moe)], tag=f"sr{oc}")
                self.vop(lambda e, sr=sr, sb=sb: e.tensor_copy(sb, sr), tag=f"sb{oc}")
                self.bank_read(j, self.sv, self.sv.n)
        if getattr(self, "_dbg_rt", False):
            self.dump("sr0", self.sres[:, 0:TS])
        ua = self.UA
        sq = ua[:, 13 * KB + TS:13 * KB + 2 * TS]
        bsq = 5
        w = self.claim_bank(bsq)
        t_prev = 0
        for c in range(NCH):
            self.vop(lambda e, c=c, sq=sq: e.tensor_mul(sq, h_r(c), h_r(c)),
                     waits=[(self.st, t_prev)] if c > 0 else [], tag=f"rq{c}")
            v_s = self.sv.n
            self.tmm(self.PSb(bsq)[0:1, 0:TS], self.onesb[:, 0:1], sq,
                     c == 0, c == NCH - 1,
                     waits=(w if c == 0 else []) + [(self.sv, v_s)], inc=True,
                     tag=f"rqs{c}")
            t_prev = self.st.n
        t_done = self.st.n
        rq, rt, rbf = self.row_q, self.row_t, self.row_bf
        self.vop(lambda e: e.tensor_scalar_mul(rq[0:1, :], self.PSb(bsq)[0:1, 0:TS],
                                               1.0 / D),
                 waits=[(self.st, t_done)], tag="rmq")
        self.bank_read(bsq, self.sv, self.sv.n)
        self.vop(lambda e: e.tensor_scalar_add(rq[0:1, :], rq[0:1, :], EPS_RMS),
                 tag="rme")
        v_var = self.sv.n
        self.sop(lambda e: e.activation(rt[0:1, :], rq[0:1, :], ACT.Ln),
                 waits=[(self.sv, v_var)], tag="rmsd")
        self.sop(lambda e: e.activation(rq[0:1, :], rt[0:1, :], ACT.Exp, scale=-0.5),
                 tag="rmrc")
        s_sd = self.ss.n
        if getattr(self, "_dbg_rt", False):
            self.dump("rmsrow", rq[0:1, :])
        self.vop(lambda e: e.tensor_copy(rbf[0:1, 2 * TS:3 * TS], rq[0:1, :]),
                 waits=[(self.ss, s_sd), (self.st, getattr(self, "rbf_free", 0))],
                 tag="rmhi")
        self.vop(lambda e: e.tensor_sub(rbf[0:1, 3 * TS:4 * TS], rq[0:1, :],
                                        rbf[0:1, 2 * TS:3 * TS]), tag="rmlo")
        v_rows = self.sv.n
        bk = 6
        w = self.claim_bank(bk)
        self.tmm(self.PSb(bk), self.onesb[0:1, :], rbf[0:1, 2 * TS:3 * TS], True, False,
                 waits=w + [(self.sv, v_rows)], inc=True, tag="rmb1")
        self.tmm(self.PSb(bk), self.onesb[0:1, :], rbf[0:1, 3 * TS:4 * TS], False, True,
                 tag="rmb2")
        t_b = self.st.n
        self.rbf_free = t_b
        psb = self.PSb(bk)
        self.vop(lambda e, psb=psb: e.tensor_copy(self.rb_[:, :], psb),
                 waits=[(self.st, t_b)], tag="rmbe")
        self.bank_read(bk, self.sv, self.sv.n)
        if getattr(self, "_dbg_rt", False):
            self.sop(lambda e, psb=psb: e.copy(self.mb[:, :], psb),
                     waits=[(self.st, t_b)], tag="dbgb6")
            self.bank_read(bk, self.ss, self.ss.n)
            self.dump("bk6D", self.mb[:, :])
            self.dump("rmhiD", rbf[0:1, 2 * TS:3 * TS], bf16=True)
            self.dump("rb2b", self.rb_[:, :])
        for oc in range(NCH):
            sr = self.sres[:, oc * TS:(oc + 1) * TS]
            self.vop(lambda e, sr=sr: e.tensor_mul(self.silu[:, :], sr, self.rb_[:, :]),
                     tag=f"h{oc}")
            self.vop(lambda e, oc=oc: e.tensor_scalar(
                h_r(oc), self.silu[:, :], self.bcol("nw", oc), None, ALU.mult,
                ALU.bypass), tag=f"hb{oc}")
            Rv = self.R[:, oc * S + ts * TS:oc * S + (ts + 1) * TS]
            self.vop(lambda e, Rv=Rv, oc=oc: e.scalar_tensor_tensor(
                Rv, self.silu[:, :], self.bcol("nwh", oc), Rv, ALU.mult, ALU.add),
                tag=f"Rh{oc}")
        v_h = self.sv.n
        if getattr(self, "_dbg_rt", False):
            self.dump("rb2", self.rb_[:, :])
            self.dump("h0", h_r(0), bf16=True)
        sem, val = self.slot_free[1]
        self.dma("sync", self.S1[:, 0:8 * HID], self.img["ffd"],
                 waits=[(self.sd, self.img_ready["ffd"]), (sem, val)], tag="ffdld")
        self.dma("sync", self.S1[:, 8 * HID:8 * HID + 2 * KB], self.img["ffu"],
                 waits=[(self.sd, self.img_ready["ffu"])], tag="ffuld")
        ff_rdy = self.sd.n
        s_gs = []
        for mc in range(2):
            bk = 4 + mc
            w = self.claim_bank(bk)
            for kc in range(NCH):
                self.tmm(self.PSb(bk),
                         self.S1[:, kc * HID + mc * 128:kc * HID + (mc + 1) * 128],
                         h_r(kc), kc == 0, kc == NCH - 1,
                         waits=(w + [(self.sd, ff_rdy), (self.sv, v_h)])
                         if kc == 0 else [], tag=f"fd{mc}")
            t_f = self.st.n
            self.sop(lambda e, bk=bk, mc=mc: e.activation(
                g_r(mc), self.PSb(bk), ACT.Gelu, bias=self.bcol("ffdb", mc)),
                waits=[(self.st, t_f)], tag=f"ge{mc}")
            self.bank_read(bk, self.ss, self.ss.n)
        s_g = self.ss.n
        if getattr(self, "_dbg_rt", False):
            self.dump("g0", g_r(0), bf16=True)
        for oc in range(NCH):
            bk = oc % 4
            w = self.claim_bank(bk)
            for hc in range(2):
                self.tmm(self.PSb(bk),
                         self.S1[:, 8 * HID + hc * KB + oc * 128:
                                 8 * HID + hc * KB + (oc + 1) * 128],
                         g_r(hc), hc == 0, hc == 1,
                         waits=(w + [(self.ss, s_g)]) if hc == 0 else [], tag=f"fu{oc}")
            t_u = self.st.n
            Rv = self.R[:, oc * S + ts * TS:oc * S + (ts + 1) * TS]
            self.vop(lambda e, bk=bk, Rv=Rv, oc=oc: e.scalar_tensor_tensor(
                Rv, self.PSb(bk), self.bcol("ffubh", oc), Rv, ALU.add, ALU.add),
                waits=[(self.st, t_u)], tag=f"Rf{oc}")
            self.bank_read(bk, self.sv, self.sv.n)
        self.slot_free[1] = (self.st, self.st.n)
        if getattr(self, "_dbg_rt", False):
            self.dump("Rf0", self.R[:, 0:TS])

    def attn(self, b):
        ua = self.UA
        yln = lambda c: ua[:, c * S:(c + 1) * S]
        q1, k1, rsc = self.q1, self.k1, self.rsc
        rf, db = self.recipf, self.denb
        self.vop(lambda e: e.memset(self.vhat[:, :], 1.0),
                 waits=[(self.st, self.st.n), (self.ss, self.ss.n)], tag="vhat1")
        for ts in range(NTS):
            t0 = ts * TS
            Rc = lambda c, t0=t0: self.R[:, c * S + t0:c * S + t0 + TS]
            entry = [(self.st, self.st.n), (self.ss, self.ss.n)] if ts == 0 else ()
            self.stats(Rc, True, EPS_LN, f"A{b}_{ts}", entry_waits=entry,
                       scratch=(self.probs0[:, 0:TS], self.probs0[:, TS:2 * TS]))
            for c in range(NCH):
                Rsrc = Rc(c)
                dst = yln(c)[:, t0:t0 + TS]
                self.vop(lambda e, Rsrc=Rsrc: e.tensor_sub(self.silu[:, :], Rsrc,
                                                           self.mb[:, :]),
                         tag=f"alns{c}")
                self.vop(lambda e, dst=dst: e.tensor_mul(dst, self.silu[:, :],
                                                         self.rb_[:, :]),
                         tag=f"alnm{c}")
        v_yln = self.sv.n
        slots = {}
        for i, w in enumerate(["wq", "wk", "wv"]):
            sl = [self.S0, self.S1, self.S2][i]
            sem, val = self.slot_free[i]
            self.dma("sync", sl[:], self.img[w],
                     waits=[(self.sd, self.img_ready[w]), (sem, val)],
                     tag=f"a_{w}ld")
            slots[w] = sl
        w_rdy = self.sd.n
        pair_idx = 0
        for hp in range(NCH):
            for (wname, buf, bs, bias) in [("wq", q1, 0, "bq"), ("wk", k1, 4, "bk")]:
                for nb in range(4):
                    wb = self.claim_bank(bs + nb)
                    for kc in range(NCH):
                        self.tmm(self.PSb(bs + nb),
                                 slots[wname][:, kc * KB + hp * 128:kc * KB + (hp + 1) * 128],
                                 yln(kc)[:, nb * TS:(nb + 1) * TS],
                                 kc == 0, kc == NCH - 1,
                                 waits=(wb + [(self.sd, w_rdy), (self.sv, v_yln)])
                                 if kc == 0 else [], tag=f"{wname}p{hp}_{nb}")
                t_p = self.st.n
                self.sop(lambda e, buf=buf, bs=bs, hp=hp, bias=bias: e.add(
                    buf[:, :], self.PS[:, bs * 512:bs * 512 + 2048],
                    self.bcol(bias, hp)),
                    waits=[(self.st, t_p)], tag=f"{wname}e{hp}")
                s_e = self.ss.n
                for nb in range(4):
                    self.bank_read(bs + nb, self.ss, s_e)
                for nb in range(4):
                    wb = self.claim_bank(bs + nb)
                    self.tmm(self.PSb(bs + nb), self.perm[:, :],
                             buf[:, nb * TS:(nb + 1) * TS], True, True,
                             waits=(wb + [(self.ss, s_e)]) if nb == 0 else wb,
                             tag=f"pm{hp}_{bs}_{nb}")
                t_pm = self.st.n
                ps = self.PS[:, bs * 512:bs * 512 + 2048]
                self.vop(lambda e, ps=ps: e.tensor_tensor(rsc[:, :], ps,
                                                          self.sin[:, :], ALU.mult),
                         waits=[(self.st, t_pm)], tag=f"rs{hp}_{bs}")
                for nb in range(4):
                    self.bank_read(bs + nb, self.sv, self.sv.n)
                self.vop(lambda e, buf=buf: e.tensor_mul(buf[:, :], buf[:, :],
                                                         self.cos[:, :]),
                         waits=[(self.st, t_pm)], tag=f"rc{hp}_{bs}")
                self.vop(lambda e, buf=buf: e.tensor_add(buf[:, :], buf[:, :],
                                                         rsc[:, :]),
                         tag=f"ra{hp}_{bs}")
            v_rope = self.sv.n
            if b == 0 and hp == 0 and self.debug:
                self.dump("ylnD", yln(0)[:, 0:512], stage2=True)
                self.dump("q1D", q1[:, 0:512], stage2=True)
                self.dump("k1D", k1[:, 0:512], stage2=True)
            for ktq in range(4):
                t_v = {}
                for dk in range(4):
                    kt = 4 * ktq + dk
                    wb = self.claim_bank(dk)
                    for kc in range(NCH):
                        self.tmm(self.PS[:, dk * 512:dk * 512 + 128],
                                 yln(kc)[:, kt * 128:(kt + 1) * 128],
                                 slots["wv"][:, kc * KB + hp * 128:kc * KB + (hp + 1) * 128],
                                 kc == 0, False,
                                 waits=wb if kc == 0 else [], tag=f"vp{hp}_{kt}")
                    self.tmm(self.PS[:, dk * 512:dk * 512 + 128], self.onesb[0:1, :],
                             self.bvrow[0:1, hp * 128:(hp + 1) * 128], False, True,
                             tag=f"vb{hp}_{kt}")
                    t_v[dk] = self.st.n
                for dk in range(4):
                    kt = 4 * ktq + dk
                    for h in range(2):
                        self.sop(lambda e, dk=dk, kt=kt, h=h: e.copy(
                            self.vhat[:, kt * 130 + 65 * h:kt * 130 + 65 * h + 64],
                            self.PS[:, dk * 512 + 64 * h:dk * 512 + 64 * h + 64]),
                            waits=[(self.st, t_v[dk])] if h == 0 else [],
                            tag=f"ve{hp}_{kt}_{h}")
                    self.bank_read(dk, self.ss, self.ss.n)
            s_vhat = self.ss.n
            W = lambda: [(self.sv, self.sv.n)]
            bk6 = 6
            for i, (buf, h) in enumerate([(q1, 0), (q1, 1), (k1, 0), (k1, 1)]):
                self.vop(lambda e, buf=buf, h=h: e.tensor_mul(
                    rsc[64 * h:64 * h + 64, :], buf[64 * h:64 * h + 64, :],
                    buf[64 * h:64 * h + 64, :]),
                    waits=W(), tag=f"nsq{hp}_{i}")
                v_sq = self.sv.n
                wbs = []
                for j in range(4):
                    wbs += self.claim_bank(4 + j)
                for j in range(4):
                    self.tmm(self.PS[0:1, (4 + j) * 512:(4 + j) * 512 + 512],
                             self.onesb[64 * h:64 * h + 64, 0:1],
                             rsc[64 * h:64 * h + 64, j * 512:(j + 1) * 512],
                             True, True,
                             waits=(wbs + [(self.sv, v_sq)]) if j == 0 else [],
                             inc=(j == 3), tag=f"nrm{hp}_{i}_{j}")
                t_n = self.st.n
                self.vop(lambda e, i=i: e.reduce_max(rf[0:1, 4 + i:5 + i],
                                                     self.PS[0:1, 2048:4096], AX),
                         waits=[(self.st, t_n)], tag=f"nmx{hp}_{i}")
                for j in range(4):
                    self.bank_read(4 + j, self.sv, self.sv.n)
            s_cells = self.sv.n
            self.vop(lambda e: e.tensor_copy(rf[0:1, 4:8], rf[0:1, 4:8]),
                     waits=[(self.sv, s_cells)], tag=f"nflush{hp}")
            if b == 0 and hp == 0 and self.debug:
                self.dump("celD", rf[0:1, 0:16], stage2=True)
            self.vop(lambda e: e.tensor_mul(rf[0:1, 8:10], rf[0:1, 4:6],
                                            rf[0:1, 6:8]),
                     waits=[(self.sv, self.sv.n)], tag=f"cm{hp}")
            v_cm = self.sv.n
            self.sop(lambda e: e.activation(rf[0:1, 10:12], rf[0:1, 8:10], ACT.Sqrt),
                     waits=[(self.sv, v_cm)], tag=f"cs{hp}")
            s_cs = self.ss.n
            self.vop(lambda e: e.tensor_scalar(db[0:1, 4:6], rf[0:1, 10:12],
                                               -0.125, None, ALU.mult, ALU.bypass),
                     waits=[(self.ss, s_cs)], tag=f"cnn{hp}")
            v_cn = self.sv.n
            for h in range(2):
                wb = self.claim_bank(bk6)
                self.tmm(self.PS[:, bk6 * 512:bk6 * 512 + 1], self.onesb[0:1, :],
                         db[0:1, 4 + h:5 + h], True, True,
                         waits=wb + [(self.sv, v_cn)], tag=f"cb{hp}_{h}")
                t_c = self.st.n
                self.vop(lambda e, h=h: e.tensor_copy(
                    self.bcol("c0", h), self.PS[:, bk6 * 512:bk6 * 512 + 1]),
                    waits=[(self.st, t_c), (self.ss, getattr(self, "c0_free", 0))],
                    tag=f"ce{hp}_{h}")
                self.bank_read(bk6, self.sv, self.sv.n)
            v_c0 = self.sv.n
            if b == 0 and hp == 0 and self.debug:
                self.dump("vhD", self.vhat[:, 0:260], stage2=True)
                self.dump("c0D", self.bcol("c0", 0, 2), stage2=True)
            probs_free = {h: (self.st, 0) for h in range(2)}
            PBASE = [0, 2, 6]
            for qb in range(4):
                avw = {0: self.claim_bank(4), 1: self.claim_bank(5)}
                def sc_mm(m, h, first, qb=qb):
                    pb = PBASE[(2 * m + h) % 3]
                    for dk in range(2):
                        kt = 2 * m + dk
                        bk = pb + dk
                        wb = self.claim_bank(bk)
                        self.tmm(self.PS[0:128, bk * 512:(bk + 1) * 512],
                                 k1[64 * h:64 * h + 64, kt * 128:(kt + 1) * 128],
                                 q1[64 * h:64 * h + 64, qb * 512:(qb + 1) * 512],
                                 True, True,
                                 waits=wb + ([(self.sv, v_c0), (self.sv, v_rope)]
                                             if first else []),
                                 tp=(64 * h, 0), tag=f"sc{hp}_{qb}_{kt}_{h}",
                                 inc=True)
                    return self.st.n
                t_sc = {(0, 0): sc_mm(0, 0, qb == 0), (0, 1): sc_mm(0, 1, False)}
                for m in range(8):
                    exp_done = {}
                    for h in range(2):
                        pb = PBASE[(2 * m + h) % 3]
                        prb = self.probs0 if h == 0 else self.probs1
                        sem, val = probs_free[h]
                        self.sop(lambda e, prb=prb, pb=pb, h=h: e.activation(
                            prb[:, 0:1024], self.PS[0:128, pb * 512:pb * 512 + 1024],
                            ACT.Exp, bias=self.bcol("c0", h), scale=0.125),
                            waits=[(self.st, t_sc[(m, h)]), (sem, val),
                                   (self.ss, s_vhat)],
                            tag=f"ex{hp}_{qb}_{m}_{h}")
                        exp_done[h] = self.ss.n
                        self.bank_read(pb, self.ss, self.ss.n)
                        self.bank_read(pb + 1, self.ss, self.ss.n)
                    if m + 1 < 8:
                        t_sc[(m + 1, 0)] = sc_mm(m + 1, 0, False)
                    for h in range(2):
                        prb = self.probs0 if h == 0 else self.probs1
                        for dk in range(2):
                            kt = 2 * m + dk
                            self.tmm(self.PS[0:65, (4 + h) * 512:(5 + h) * 512],
                                     self.vhat[:, kt * 130 + 65 * h:kt * 130 + 65 * (h + 1)],
                                     prb[:, dk * 512:(dk + 1) * 512],
                                     kt == 0, kt == 15,
                                     waits=(avw[h] if kt == 0 else []) +
                                     ([(self.ss, exp_done[h])] if dk == 0 else []),
                                     tag=f"av{hp}_{qb}_{kt}_{h}", inc=True)
                        probs_free[h] = (self.st, self.st.n)
                    if m + 1 < 8:
                        t_sc[(m + 1, 1)] = sc_mm(m + 1, 1, False)
                t_av = self.st.n
                if b == 0 and hp == 0 and qb == 0 and self.debug:
                    self.dump("pr0", self.probs0[:, 512:1024], stage2=True)
                for h in range(2):
                    avps = self.PS[:, (4 + h) * 512:(5 + h) * 512]
                    self.sop(lambda e, avps=avps: e.activation(
                        rf[64:65, :], avps[64:65, :], ACT.Ln),
                        waits=[(self.st, t_av)], tag=f"dn{hp}_{qb}_{h}")
                    self.sop(lambda e: e.activation(db[64:65, 0:512], rf[64:65, :],
                                                    ACT.Exp, scale=-1.0),
                             tag=f"dx{hp}_{qb}_{h}")
                    s_dn = self.ss.n
                    bkb = 6
                    wb = self.claim_bank(bkb)
                    self.tmm(self.PS[0:64, bkb * 512:bkb * 512 + 512],
                             self.onesb[64:65, 0:64], db[64:65, 0:512],
                             True, True, waits=wb + [(self.ss, s_dn)],
                             tag=f"dbc{hp}_{qb}_{h}", inc=True)
                    t_db = self.st.n
                    self.vop(lambda e: e.tensor_copy(rf[0:64, :],
                             self.PS[0:64, bkb * 512:bkb * 512 + 512]),
                             waits=[(self.st, t_db)], tag=f"dbe{hp}_{qb}_{h}")
                    v_dbe = self.sv.n
                    self.bank_read(bkb, self.sv, v_dbe)
                    if h == 0:
                        dst = self.quarter[0:64,
                                           (hp % 2) * S + qb * 512:(hp % 2) * S + qb * 512 + 512]
                    else:
                        dst = self.stage[0:64, qb * 512:qb * 512 + 512]
                    self.vop(lambda e, avps=avps, dst=dst: e.tensor_mul(
                        dst, avps[0:64, :], rf[0:64, :]),
                        waits=[(self.sv, v_dbe)], tag=f"qn{hp}_{qb}_{h}")
                    self.bank_read(4 + h, self.sv, self.sv.n)
            self.c0_free = self.ss.n
            if b == 0 and hp == 0 and self.debug:
                self.dump("dnD", db[64:65, 0:512], stage2=True)
                self.dump("qrD", self.quarter[0:64, 0:512], stage2=True)
            v_q = self.sv.n
            self.dma("gpsimd",
                     self.quarter[64:128, (hp % 2) * S:(hp % 2) * S + S],
                     self.stage[0:64, 0:S],
                     waits=[(self.sv, v_q), (self.sd, self.prologue_done)],
                     tag=f"hshift{hp}")
            self.qshift_done = self.sg.n
            if hp % 2 == 1:
                wlist = [(self.sd, self.img_ready["wo"]), (self.sg, self.qshift_done)]
                if hasattr(self, "wotsl_free"):
                    wlist.append((self.st, self.wotsl_free))
                self.dma("sync", self.wotsl[:, 0:KB],
                         self.img["wo"][:, (hp - 1) * KB:hp * KB],
                         waits=wlist, tag=f"wold{hp}a")
                self.dma("sync", self.wotsl[:, KB:2 * KB],
                         self.img["wo"][:, hp * KB:(hp + 1) * KB], tag=f"wold{hp}b")
                wo_rdy = self.sd.n
                for oc in range(NCH):
                    for tb in range(4):
                        bk = (oc * 4 + tb) % 2 + 6
                        wb = self.claim_bank(bk)
                        for kc2 in range(2):
                            self.tmm(self.PSb(bk),
                                     self.wotsl[:, kc2 * KB + oc * 128:kc2 * KB + (oc + 1) * 128],
                                     self.quarter[:, kc2 * S + tb * 512:kc2 * S + (tb + 1) * 512],
                                     kc2 == 0, kc2 == 1,
                                     waits=(wb + [(self.sd, wo_rdy),
                                                  (self.sg, self.qshift_done)])
                                     if kc2 == 0 else [], tag=f"op{hp}_{oc}_{tb}")
                        t_o = self.st.n
                        Rv = self.R[:, oc * S + tb * 512:oc * S + (tb + 1) * 512]
                        ps = self.PSb(bk)
                        if pair_idx == 0:
                            self.vop(lambda e, ps=ps, Rv=Rv, oc=oc: e.scalar_tensor_tensor(
                                Rv, ps, self.bcol("bo", oc), Rv, ALU.add, ALU.add),
                                waits=[(self.st, t_o)], tag=f"Ro{hp}_{oc}_{tb}")
                        else:
                            self.vop(lambda e, ps=ps, Rv=Rv: e.tensor_add(Rv, Rv, ps),
                                     waits=[(self.st, t_o)], tag=f"Ro{hp}_{oc}_{tb}")
                        self.bank_read(bk, self.sv, self.sv.n)
                self.wotsl_free = self.st.n
                pair_idx += 1
        self.slot_free[0] = (self.st, self.st.n)
        self.slot_free[1] = (self.st, self.st.n)
        self.slot_free[2] = (self.st, self.st.n)
        if b == 0 and self.debug:
            self.dump("RaD", self.R[:, 0:512], stage2=True)

    def seg(self):
        self.segs.append(len(self.plan))

    def build(self):
        self.declare()
        self.alloc()
        self.sems()
        self.prologue()
        self.seg()
        for b in range(self.nbatch):
            self.load_R(b)
            self.seg()
            for ph in self.phases:
                if ph == "ffn1":
                    self.ffn(b, 0)
                elif ph == "attn":
                    self.attn(b)
                elif ph == "ffn2":
                    self.ffn(b, 1)
                self.seg()
            self.store_R(b)
            self.seg()
        self.plan.append(("sync", lambda e: e.wait_ge(self.so.h, self.so.n)))
        nc = self.nc
        with nc.Block() as block:
            cur = 0
            while cur < len(self.plan):
                eng = self.plan[cur][0]
                group = []
                while cur < len(self.plan) and self.plan[cur][0] == eng:
                    group.append(self.plan[cur][1])
                    cur += 1
                def section(eh, group=group):
                    for f in group:
                        f(eh)
                getattr(block, eng)(section)
        for g in reversed(self._semctx):
            g.__exit__(None, None, None)
        return nc


def make_consts():
    j = np.arange(0, DH, 2, dtype=np.float64) / DH
    invf = 1.0 / (10000.0 ** j)
    t = np.arange(S, dtype=np.float64)
    fr = t[None, :] * invf[:, None]
    cos = np.repeat(np.cos(fr), 2, axis=0)
    sin = np.repeat(np.sin(fr), 2, axis=0)
    sin_eff = sin.copy()
    sin_eff[:32] *= -1.0
    cosT = np.concatenate([cos, cos], axis=0).astype(ml_dtypes.bfloat16)
    sinT = np.concatenate([sin_eff, sin_eff], axis=0).astype(ml_dtypes.bfloat16)
    perm = np.zeros((128, 128), dtype=np.float32)
    for blk in range(2):
        for i in range(64):
            perm[blk * 64 + (i + 32) % 64, blk * 64 + i] = 1.0
    return {
        "c_cos": np.ascontiguousarray(cosT),
        "c_sin": np.ascontiguousarray(sinT),
        "c_perm": perm.astype(ml_dtypes.bfloat16),
        "c_ones": np.ones((128, 128), dtype=ml_dtypes.bfloat16),
        "c_idf": np.eye(128, dtype=np.float32),
        "c_idb": np.eye(128, dtype=ml_dtypes.bfloat16),
    }


_BUILD_CACHE = {}


def get_nc(phases=("ffn1", "attn", "ffn2")):
    key = tuple(phases)
    if key not in _BUILD_CACHE:
        _BUILD_CACHE[key] = Build(phases=phases).build()
    return _BUILD_CACHE[key]


def kernel(**inputs):
    consts = make_consts()
    nc = get_nc()
    in_maps = []
    for core in range(8):
        m = dict(consts)
        for k, v in inputs.items():
            v = np.ascontiguousarray(np.asarray(v))
            m[k] = v[2 * core:2 * core + 2] if k == "L0_attn_x" else v
        in_maps.append(m)
    res = run_bass_kernel_spmd(nc, in_maps, core_ids=list(range(8)))
    return np.concatenate([r["y"] for r in res.results], axis=0)


if __name__ == "__main__":
    bld = Build(phases=("ffn1",))
    nc = bld.build()
    print("build ok; instructions:", nc.n_instructions())
